# version 02
# baseline (speedup 1.0000x reference)
import numpy as np
import ml_dtypes
from contextlib import ExitStack
import concourse.bass as bass
import concourse.mybir as mybir
from concourse.bass_utils import run_bass_kernel_spmd
from concourse.alu_op_type import AluOpType as ALU

F32 = mybir.dt.float32
BF16 = mybir.dt.bfloat16
AF = mybir.ActivationFunctionType
AX = mybir.AxisListType.X
BF = ml_dtypes.bfloat16

S = 4096
DM = 1024
NCH = 8
TG = 512
NTG = S // TG
NEG = -1e9

ENGS = ("sync", "gpsimd", "vector", "scalar", "tensor")


class Prog:

    def __init__(self):
        self.q = {e: [] for e in ENGS}
        self.cnt = {}
        self.waited = {e: {} for e in ENGS}
        self.sems = set(ENGS)

    def wait(self, eng, tok):
        if tok is None:
            return
        o, v = tok
        if v <= self.waited[eng].get(o, 0):
            return
        self.waited[eng][o] = v
        self.q[eng].append(lambda h, SEM, o=o, v=v: h.wait_ge(SEM[o], v))

    def op(self, eng, build, inc=0, sem=None, waits=()):
        sem = sem or eng
        self.sems.add(sem)
        for t in waits:
            self.wait(eng, t)
        if inc:
            self.cnt[sem] = self.cnt.get(sem, 0) + inc
        v = self.cnt.get(sem, 0)

        def f(h, SEM, build=build, inc=inc, sem=sem):
            ins = build(h)
            if inc:
                ins.then_inc(SEM[sem], inc)
        self.q[eng].append(f)
        return (sem, v)

    def tok(self, sem):
        return (sem, self.cnt.get(sem, 0))


def finalize(nc, p, ctx):
    SEM = {e: ctx.enter_context(nc.semaphore(f"S_{e}")) for e in sorted(p.sems)}
    block = ctx.enter_context(nc.Block())
    for e in ENGS:
        if not p.q[e]:
            continue
        queue = p.q[e]

        def body(h, queue=queue):
            for f in queue:
                f(h, SEM)
        getattr(block, e)(body)


class PsumPool:
    def __init__(self, p, ctx, nc, n=8):
        self.tiles = [ctx.enter_context(nc.psum_tensor(f"pb{i}", [128, 512], F32))
                      for i in range(n)]
        self.release = [None] * n
        self.held = set()
        self.i = 0
        self.p = p

    def get(self, hold=False):
        for _ in range(len(self.tiles)):
            i = self.i
            self.i = (self.i + 1) % len(self.tiles)
            if i not in self.held:
                break
        else:
            raise RuntimeError("all psum banks held")
        if hold:
            self.held.add(i)
        if self.release[i] is not None:
            self.p.wait("tensor", self.release[i])
            self.release[i] = None
        return self.tiles[i], i

    def done(self, i, token):
        self.release[i] = token
        self.held.discard(i)


class Launch:

    def __init__(self, name):
        self.nc = bass.Bass("TRN2", target_bir_lowering=False)
        self.ctx = ExitStack()
        self.ctx.__enter__()
        self.p = Prog()
        self.pool = None
        self.name = name
        self.prefix = ""
        self.arena = None
        self.arena_off = 0
        self.arena_max = 0
        self._sb_names = set()

    def dram_in(self, name, shape, dtype=F32):
        return self.nc.dram_tensor(self.prefix + name, list(shape), dtype,
                                   kind="ExternalInput").ap()

    def dram_out(self, name, shape, dtype=F32):
        return self.nc.dram_tensor(self.prefix + name, list(shape), dtype,
                                   kind="ExternalOutput").ap()

    def dram_scratch(self, name, shape, dtype=BF16):
        return self.nc.dram_tensor(self.prefix + name, list(shape), dtype).ap()

    def use_arena(self, cols_bf16):
        self.arena = self.sb("shared_arena", [128, cols_bf16], BF16)
        self.arena_cols = cols_bf16

    def arena_reset(self):
        self.arena_max = max(self.arena_max, self.arena_off)
        self.arena_off = 0

    def sb(self, name, shape, dtype):
        name = self.prefix + name
        assert name not in self._sb_names, name
        self._sb_names.add(name)
        if self.arena is not None:
            P, Cc = shape
            bcols = Cc * (2 if dtype == F32 else 1)
            off = self.arena_off
            self.arena_off = off + bcols + (bcols & 1)
            assert self.arena_off <= self.arena_cols, (
                f"arena overflow at {name}: {self.arena_off} > {self.arena_cols}")
            ap = self.arena[:, off:off + bcols]
            if dtype == F32:
                ap = ap.bitcast(F32)
            return ap[0:P, :]
        return self.ctx.enter_context(self.nc.sbuf_tensor(name, list(shape), dtype))

    def init_pool(self):
        if self.pool is None:
            self.pool = PsumPool(self.p, self.ctx, self.nc)

    def fence(self):
        p = self.p
        if not hasattr(self, "_fb"):
            self.prefix, sv = "", self.prefix
            self._fb = self.ctx.enter_context(
                self.nc.sbuf_tensor("fence_buf", [1, 8], F32))
            self._fbb = self.ctx.enter_context(
                self.nc.sbuf_tensor("fence_bufb", [1, 8], BF16))
            p.op("gpsimd", lambda h: h.memset(self._fbb[:], 1.0), inc=1)
            self.prefix = sv
        fb, fbb = self._fb, self._fbb
        tv = p.op("vector", lambda h: h.tensor_copy(fb[0:1, 0:1], fb[0:1, 1:2]), inc=1)
        ts_ = p.op("scalar", lambda h: h.copy(fb[0:1, 2:3], fb[0:1, 3:4]), inc=1)
        fps, ifp = self.pool.get()
        tt_ = p.op("tensor", lambda h, fps=fps: h.matmul(fps[0:1, 0:1], fbb[0:1, 0:1],
                                                         fbb[0:1, 1:2], start=True,
                                                         stop=True), inc=1)
        self.pool.done(ifp, tt_)
        joins = [tv, ts_, tt_]
        for sem_name in sorted(p.cnt.keys()):
            if sem_name not in ENGS:
                joins.append((sem_name, p.cnt[sem_name]))
        tg_ = p.op("gpsimd", lambda h: h.memset(fb[0:1, 4:5], 0.0), inc=1,
                   waits=joins)
        for e in ENGS:
            p.wait(e, tg_)
        return tg_

    def done(self):
        finalize(self.nc, self.p, self.ctx)
        self.ctx.__exit__(None, None, None)
        return self.nc


def load_slabs(p, eng, sb, hbm, n, nk, toff=0, col0=0, waits=(), sem="ld_w"):
    tok = None
    for fc in range(nk):
        ap = bass.AP(hbm.tensor, (toff + fc * 128) * n, [[n, 128], [1, n]])
        tok = p.op(eng, lambda h, ap=ap, fc=fc: h.dma_start(
            out=sb[:, col0 + fc * n:col0 + (fc + 1) * n], in_=ap), inc=16,
            sem=sem, waits=waits if fc == 0 else ())
    return tok



class Common:

    def __init__(self, L, x_hbm, with_xn_full=False, xt_ap=None, xsq_ap=None):
        self.L = L
        p, nc = L.p, L.nc
        self.x_hbm = x_hbm
        self.xt = xt_ap if xt_ap is not None else L.sb("c_xt", [128, NCH * TG], F32)
        self.xsq = xsq_ap if xsq_ap is not None else L.sb("c_xsq", [128, NCH * TG], F32)
        self.with_xn_full = with_xn_full
        if with_xn_full:
            self.xn = L.sb("c_xn", [128, NCH * S], BF16)
        else:
            self.xn = L.sb("c_xn", [128, NCH * TG], BF16)
        self.rows = L.sb("c_rows", [1, 3080], F32)
        self.ab = L.sb("c_ab", [128, 2 * TG], F32)
        self.ones_col = L.sb("c_onescol", [128, 1], F32)
        self.ones_row = L.sb("c_onesrow", [1, 128], F32)
        self.ident = L.sb("c_ident", [128, 128], F32)
        self.xt_free = []
        self.xn_free = None
        self.xsq_free = None
        self.x_tok = None
        self.mu_tok = None
        self.ab_tok = None
        self.xn_tok = None
        self.cpool = L.sb("c_cpool", [128, 8], F32)
        self._cvals = {}
        p.op("gpsimd", lambda h: h.memset(self.ones_col[:], 1.0), inc=1)
        p.op("gpsimd", lambda h: h.memset(self.ones_row[:], 1.0), inc=1)
        self.const_tok = p.tok("gpsimd")
        ident_h = L.dram_in("c_identin", [128, 128], F32)
        self.ident_tok = p.op("sync", lambda h: h.dma_start(out=self.ident[:], in_=ident_h[:, :]), inc=16, sem="ld_w")
        self.ident_bf = L.sb("c_identbf", [128, 128], BF16)
        self.identbf_tok = p.op("gpsimd", lambda h: h.dma_start(out=self.ident_bf[:], in_=ident_h[:, :]), inc=16, sem="ld_idbf")

    def constant(self, value, parts=128):
        if value not in self._cvals:
            col = len(self._cvals)
            assert col < 8
            self._cvals[value] = col
            tok = self.L.p.op("gpsimd", lambda h, col=col, value=value: h.memset(
                self.cpool[:, col:col + 1], value), inc=1)
            self.const_tok = tok
        col = self._cvals[value]
        return self.cpool[0:parts, col:col + 1]

    def xslice(self, fc, width=TG):
        return self.xt[:, fc * TG:fc * TG + width]

    def xnslice(self, fc, tg=None, width=TG, off=0):
        if self.with_xn_full:
            return self.xn[:, fc * S + tg * TG + off: fc * S + tg * TG + off + width]
        return self.xn[:, fc * TG + off: fc * TG + off + width]

    def load_x(self, tg):
        p = self.L.p
        w = list(self.xt_free)
        for fc in range(NCH):
            ap = bass.AP(self.x_hbm.tensor, fc * 128 * S + tg * TG,
                         [[S, 128], [1, TG]])
            self.x_tok = p.op("sync", lambda h, ap=ap, fc=fc: h.dma_start(
                out=self.xt[:, fc * TG:(fc + 1) * TG], in_=ap), inc=16,
                sem="ld_x", waits=w if fc == 0 else ())
        return self.x_tok

    def ln(self, tg, pool, affine=None):
        p = self.L.p
        R = self.rows
        mu, e2, m2 = R[0:1, 0:512], R[0:1, 512:1024], R[0:1, 1024:1536]
        var, std = e2, m2
        rstd, bneg = R[0:1, 1536:2048], R[0:1, 2048:2560]
        sq_toks = []
        w0 = [self.x_tok, self.xsq_free] if self.xsq_free else [self.x_tok]
        for fc in range(NCH):
            t = p.op("scalar", lambda h, fc=fc: h.activation(
                self.xsq[:, fc * TG:(fc + 1) * TG], self.xslice(fc), AF.Square),
                inc=1, waits=w0 if fc == 0 else [self.x_tok])
            sq_toks.append(t)
        psA, iA = pool.get()
        for fc in range(NCH):
            tokA = p.op("tensor", lambda h, fc=fc, psA=psA: h.matmul(
                psA[0:1, :], self.ones_col[:], self.xslice(fc),
                start=(fc == 0), stop=(fc == NCH - 1)),
                inc=(1 if fc == NCH - 1 else 0),
                waits=[self.x_tok, self.const_tok] if fc == 0 else ())
        psB, iB = pool.get()
        for fc in range(NCH):
            tokB = p.op("tensor", lambda h, fc=fc, psB=psB: h.matmul(
                psB[0:1, :], self.ones_col[:], self.xsq[:, fc * TG:(fc + 1) * TG],
                start=(fc == 0), stop=(fc == NCH - 1)),
                inc=(1 if fc == NCH - 1 else 0),
                waits=[sq_toks[fc]])
        self.xsq_free = tokB
        t1 = p.op("vector", lambda h: h.tensor_scalar(mu, psA[0:1, :], 1.0 / DM, None, ALU.mult),
                  inc=1, waits=[tokA])
        t2 = p.op("vector", lambda h: h.tensor_scalar(e2, psB[0:1, :], 1.0 / DM, None, ALU.mult),
                  inc=1, waits=[tokB])
        pool.done(iA, t1)
        pool.done(iB, t2)
        t3 = p.op("vector", lambda h: h.tensor_tensor(m2, mu, mu, ALU.mult), inc=1,
                  waits=[t1])
        t4 = p.op("vector", lambda h: h.tensor_tensor(var, e2, m2, ALU.subtract), inc=1,
                  waits=[t2, t3])
        epsap = self.constant(1e-5, parts=1)
        t5 = p.op("scalar", lambda h, epsap=epsap: h.activation(
            std, var, AF.Sqrt, bias=epsap), inc=1, waits=[t4, self.const_tok])
        t6 = p.op("vector", lambda h: h.reciprocal(rstd, std), inc=1, waits=[t5])
        t7 = p.op("vector", lambda h: h.scalar_tensor_tensor(bneg, mu, -1.0, rstd,
                                                            ALU.mult, ALU.mult),
                  inc=1, waits=[t6])
        self.mu_tok = t7
        psC, iC = pool.get()
        tC = p.op("tensor", lambda h: h.matmul(psC[:, :], self.ones_row[:], rstd,
                                               start=True, stop=True), inc=1,
                  waits=[t6, self.ident_tok])
        psD, iD = pool.get()
        tD = p.op("tensor", lambda h: h.matmul(psD[:, :], self.ones_row[:], bneg,
                                               start=True, stop=True), inc=1,
                  waits=[t7])
        ea = p.op("vector", lambda h: h.tensor_copy(self.ab[:, 0:TG], psC[:, :]), inc=1,
                  waits=[tC, self.ab_tok] if self.ab_tok else [tC])
        eb = p.op("vector", lambda h: h.tensor_copy(self.ab[:, TG:2 * TG], psD[:, :]),
                  inc=1, waits=[tD])
        pool.done(iC, ea)
        pool.done(iD, eb)
        a_b = self.ab[:, 0:TG]
        b_b = self.ab[:, TG:2 * TG]
        xn_toks = []
        for fc in range(NCH):
            scr = self.xsq[:, fc * TG:(fc + 1) * TG]
            tA = p.op("vector", lambda h, fc=fc, scr=scr: h.tensor_tensor(
                scr, self.xslice(fc), a_b, ALU.mult), inc=1,
                waits=[ea, eb, self.xsq_free, self.x_tok] if fc == 0 else [])
            dst = self.xnslice(fc, tg)
            if affine is None:
                tB = p.op("vector", lambda h, dst=dst, scr=scr: h.tensor_tensor(
                    dst, scr, b_b, ALU.add), inc=1,
                    waits=[tA] + ([self.xn_free] if (fc == 0 and self.xn_free) else []))
            else:
                gam, bet, gtok = affine
                tB0 = p.op("vector", lambda h, scr=scr: h.tensor_tensor(
                    scr, scr, b_b, ALU.add), inc=1,
                    waits=[tA] + ([self.xn_free] if (fc == 0 and self.xn_free) else []))
                tB = p.op("vector", lambda h, dst=dst, scr=scr, fc=fc, gam=gam,
                          bet=bet: h.tensor_scalar(
                    dst, scr, gam[:, fc:fc + 1], bet[:, fc:fc + 1], ALU.mult,
                    ALU.add), inc=1, waits=[tB0, gtok])
            xn_toks.append(tB)
        self.xsq_free = xn_toks[-1]
        self.xn_tok = xn_toks[-1]
        self.ab_tok = xn_toks[-1]
        self.xt_free = [xn_toks[-1]]
        return xn_toks



def declare_ffn0(L):
    sv, L.prefix = L.prefix, ""
    D = {
        "sW1": L.dram_in("sW1", [DM, 512]),
        "sW2": L.dram_in("sW2", [512, DM]),
        "rW1": L.dram_in("rW1", [8, DM, 512]),
        "rW2": L.dram_in("rW2", [8, 512, DM]),
        "rb2": L.dram_in("rb2", [8, DM]),
        "sb1l": L.dram_in("sb1l", [128, 4]),
        "sb2l": L.dram_in("sb2l", [128, 8]),
        "rb1l": L.dram_in("rb1l", [128, 32]),
        "rWT": L.dram_in("rWT", [DM, 8]),
        "ebl": L.dram_in("ebl", [8, 1]),
        "sels": L.dram_in("sels", [8, 1024]),
    }
    L.prefix = sv
    return D


def emit_ffn0(L, x_hbm, y_hbm, D):
    nc, p = L.nc, L.p
    H = 512
    NHC = H // 128
    sW1h, sW2h, rW1h, rW2h, rb2h = D["sW1"], D["sW2"], D["rW1"], D["rW2"], D["rb2"]
    sb1h, sb2h, rb1h, rWTh, ebh, selsh = (D["sb1l"], D["sb2l"], D["rb1l"],
                                          D["rWT"], D["ebl"], D["sels"])

    sW1 = L.sb("s_sW1", [128, NCH * H], BF16)
    sW2 = L.sb("s_sW2", [128, NHC * DM], BF16)
    rW1 = L.sb("s_rW1", [128, NCH * H], BF16)
    rW2 = L.sb("s_rW2", [128, 8 * NHC * DM], BF16)
    rb2 = L.sb("s_rb2", [8, DM], BF16)
    sb1 = L.sb("s_sb1", [128, 4], F32)
    sb2 = L.sb("s_sb2", [128, 8], F32)
    rb1 = L.sb("s_rb1", [128, 32], F32)
    rWT = L.sb("s_rWT", [128, NCH * 8], F32)
    eb = L.sb("s_eb", [8, 1], F32)
    sels = L.sb("s_sels", [8, 1024], BF16)
    colsum = L.sb("s_colsum", [8, 1], F32)

    load_slabs(p, "gpsimd", sW1, sW1h, H, NCH, sem="ld_wg")
    load_slabs(p, "gpsimd", sW2, sW2h, DM, NHC, sem="ld_wg")
    for e in range(8):
        load_slabs(p, "gpsimd", rW2, rW2h, DM, NHC, toff=e * H, col0=e * NHC * DM, sem="ld_wg")
    p.op("gpsimd", lambda h: h.dma_start(out=rb2[:], in_=rb2h[:, :]), inc=16, sem="ld_wg")
    p.op("sync", lambda h: h.dma_start(out=sb1[:], in_=sb1h[:, :]), inc=16, sem="ld_w")
    p.op("sync", lambda h: h.dma_start(out=sb2[:], in_=sb2h[:, :]), inc=16, sem="ld_w")
    p.op("sync", lambda h: h.dma_start(out=rb1[:], in_=rb1h[:, :]), inc=16, sem="ld_w")
    load_slabs(p, "sync", rWT, rWTh, 8, NCH)
    p.op("sync", lambda h: h.dma_start(out=eb[:], in_=ebh[:, :]), inc=16, sem="ld_w")
    p.op("gpsimd", lambda h: h.dma_start(out=sels[:], in_=selsh[:, :]), inc=16, sem="ld_wg")
    wt_gp = p.tok("ld_wg")

    C = Common(L, x_hbm)
    wt_sync = p.tok("ld_w")
    C.ident_tok = wt_sync
    L.init_pool()
    pool = L.pool

    psK, iK = pool.get()
    for fc in range(NCH):
        tk = p.op("tensor", lambda h, fc=fc: h.matmul(
            psK[0:1, 0:8], C.ones_col[:], rWT[:, fc * 8:(fc + 1) * 8],
            start=(fc == 0), stop=(fc == NCH - 1)),
            inc=(1 if fc == NCH - 1 else 0),
            waits=[wt_sync, wt_gp, C.const_tok] if fc == 0 else ())
    csr = C.rows[0:1, 2560:2568]
    e1 = p.op("vector", lambda h: h.tensor_copy(csr, psK[0:1, 0:8]), inc=1, waits=[tk])
    pool.done(iK, e1)
    psK2, iK2 = pool.get()
    tk2 = p.op("tensor", lambda h: h.transpose(psK2[0:8, 0:1], csr, C.ident[0:1, 0:1]),
               inc=1, waits=[e1, C.ident_tok])
    e2c = p.op("vector", lambda h: h.tensor_copy(colsum[:], psK2[0:8, 0:1]), inc=1,
               waits=[tk2])
    pool.done(iK2, e2c)
    colsum_tok = e2c

    htld = L.sb("htld", [128, 32 * TG], BF16)
    h1b = L.sb("h1b", [128, NHC * TG], BF16)
    combe = L.sb("combe", [128, 2 * TG], BF16)
    zbufs = L.sb("zb", [8, 5 * TG], F32)
    combf = L.sb("combf", [8, TG], F32)
    combbf = L.sb("combbf", [8, TG], BF16)
    ztok = L.sb("ztok", [128, 4 * 8], F32)
    vcols = L.sb("vcols", [128, 8], F32)
    eqb = L.sb("eqb", [128, 4 * 8], F32)
    outb = L.sb("outb", [128, 2 * 512], F32)

    z = zbufs[0:8, 0:TG]
    relu = zbufs[0:8, TG:2 * TG]
    ach = zbufs[0:8, 2 * TG:3 * TG]
    sp = zbufs[0:8, 3 * TG:4 * TG]
    act = zbufs[0:8, 4 * TG:5 * TG]
    maskb = relu

    out_free = [None, None]
    rW1_free = None
    htld_free = None
    combe_free = [None, None]

    for tg in range(NTG):
        C.load_x(tg)
        C.ln(tg, pool)
        mu = C.rows[0:1, 0:512]
        rstd = C.rows[0:1, 1536:2048]

        psR, iR = pool.get()
        for fc in range(NCH):
            tR = p.op("tensor", lambda h, fc=fc, psR=psR: h.matmul(
                psR[0:8, :], rWT[:, fc * 8:(fc + 1) * 8], C.xslice(fc),
                start=(fc == 0), stop=(fc == NCH - 1)),
                inc=(1 if fc == NCH - 1 else 0),
                waits=[C.x_tok, wt_sync] if fc == 0 else ())
        psM, iM = pool.get()
        tM = p.op("tensor", lambda h, psM=psM: h.matmul(
            psM[0:8, :], C.ones_row[0:1, 0:8], mu, start=True, stop=True), inc=1,
            waits=[C.mu_tok])
        psS, iS = pool.get()
        tS = p.op("tensor", lambda h, psS=psS: h.matmul(
            psS[0:8, :], C.ones_row[0:1, 0:8], rstd, start=True, stop=True), inc=1,
            waits=[C.mu_tok])
        zt1 = p.op("vector", lambda h, psM=psM: h.tensor_scalar(ach, psM[0:8, :], colsum[:], None,
                                                       ALU.mult), inc=1,
                   waits=[tM, colsum_tok])
        pool.done(iM, zt1)
        zt2 = p.op("vector", lambda h, psR=psR: h.tensor_tensor(sp, psR[0:8, :], ach, ALU.subtract),
                   inc=1, waits=[tR, zt1])
        pool.done(iR, zt2)
        zt3 = p.op("vector", lambda h, psS=psS: h.tensor_tensor(relu, sp, psS[0:8, :], ALU.mult),
                   inc=1, waits=[tS, zt2])
        pool.done(iS, zt3)
        zt4 = p.op("vector", lambda h: h.tensor_scalar(z, relu, eb[:], None, ALU.add),
                   inc=1, waits=[zt3, wt_sync])
        st1 = p.op("scalar", lambda h: h.activation(ach, z, AF.Abs), inc=1, waits=[zt4])
        st2 = p.op("scalar", lambda h: h.activation(ach, ach, AF.Exp, scale=-1.0), inc=1,
                   waits=[st1])
        st3 = p.op("scalar", lambda h: h.activation(ach, ach, AF.Ln, bias=1.0), inc=1,
                   waits=[st2])
        rt = p.op("vector", lambda h: h.tensor_scalar(relu, z, 0.0, None, ALU.max),
                  inc=1, waits=[zt4])
        sp1 = p.op("vector", lambda h: h.tensor_tensor(sp, relu, ach, ALU.add), inc=1,
                   waits=[rt, st3])
        st4 = p.op("scalar", lambda h: h.activation(act, sp, AF.Ln), inc=1, waits=[sp1])
        halfap = C.constant(0.5, parts=8)
        st5 = p.op("scalar", lambda h, halfap=halfap: h.activation(
            act, act, AF.Exp, scale=halfap), inc=1, waits=[st4, C.const_tok])
        zt_toks = []
        for tt in range(4):
            psT, iT = pool.get()
            tT = p.op("tensor", lambda h, tt=tt, psT=psT: h.transpose(
                psT[:, 0:8], z[:, tt * 128:(tt + 1) * 128], C.ident[0:8, 0:8]),
                inc=1, waits=[zt4, C.ident_tok])
            eT = p.op("vector", lambda h, tt=tt, psT=psT: h.tensor_copy(
                ztok[:, tt * 8:(tt + 1) * 8], psT[:, 0:8]), inc=1, waits=[tT])
            pool.done(iT, eT)
            zt_toks.append(eT)
        v2toks = []
        for tt in range(4):
            zt_ap = ztok[:, tt * 8:(tt + 1) * 8]
            v1 = vcols[:, tt:tt + 1]
            v2 = vcols[:, 4 + tt:5 + tt]
            eq = eqb[:, tt * 8:(tt + 1) * 8]
            a = p.op("vector", lambda h, zt_ap=zt_ap, v1=v1: h.tensor_reduce(
                v1, zt_ap, axis=AX, op=ALU.max), inc=1, waits=[zt_toks[tt]])
            b = p.op("vector", lambda h, zt_ap=zt_ap, v1=v1, eq=eq: h.tensor_scalar(
                eq, zt_ap, v1, None, ALU.is_equal), inc=1, waits=[a])
            c = p.op("vector", lambda h, zt_ap=zt_ap, eq=eq: h.scalar_tensor_tensor(
                eq, eq, -1e30, zt_ap, ALU.mult, ALU.add), inc=1, waits=[b])
            d = p.op("vector", lambda h, eq=eq, v2=v2: h.tensor_reduce(
                v2, eq, axis=AX, op=ALU.max), inc=1, waits=[c])
            v2toks.append(d)
        psV, iV = pool.get()
        for tt in range(4):
            tV = p.op("tensor", lambda h, tt=tt, psV=psV: h.transpose(
                psV[0:1, tt * 128:(tt + 1) * 128], vcols[:, 4 + tt:5 + tt], C.ident[:]),
                inc=1, waits=[v2toks[tt]])
        v2r = C.rows[0:1, 2568:3080]
        eV = p.op("vector", lambda h, psV=psV: h.tensor_copy(v2r, psV[0:1, 0:512]), inc=1,
                  waits=[tV])
        pool.done(iV, eV)
        psV2, iV2 = pool.get()
        tV2 = p.op("tensor", lambda h, psV2=psV2: h.matmul(psV2[0:8, :], C.ones_row[0:1, 0:8], v2r,
                                                start=True, stop=True), inc=1,
                   waits=[eV])
        mk = p.op("vector", lambda h, psV2=psV2: h.tensor_tensor(maskb, z, psV2[0:8, :], ALU.is_ge),
                  inc=1, waits=[tV2, st5])
        pool.done(iV2, mk)
        cb1 = p.op("vector", lambda h: h.tensor_tensor(combf[:, :], maskb, act, ALU.mult),
                   inc=1, waits=[mk])
        cb2 = p.op("vector", lambda h: h.tensor_copy(combbf[:, :], combf[:, :]), inc=1, waits=[cb1])

        for hc in range(NHC):
            psH, iH = pool.get()
            for fc in range(NCH):
                tH = p.op("tensor", lambda h, hc=hc, fc=fc, psH=psH: h.matmul(
                    psH[:, :], sW1[:, fc * H + hc * 128: fc * H + (hc + 1) * 128],
                    C.xnslice(fc, tg), start=(fc == 0), stop=(fc == NCH - 1)),
                    inc=(1 if fc == NCH - 1 else 0),
                    waits=[C.xn_tok, wt_gp] if fc == 0 else ())
            eH = p.op("scalar", lambda h, hc=hc, psH=psH: h.activation(
                h1b[:, hc * TG:(hc + 1) * TG], psH[:, :], AF.Silu,
                bias=sb1[:, hc:hc + 1]), inc=1, waits=[tH, wt_sync])
            pool.done(iH, eH)
        h1_tok = eH

        ht_toks = []
        for e in range(8):
            rW1_tok = load_slabs(p, "gpsimd", rW1, rW1h, H, NCH, toff=e * DM,
                                 waits=[rW1_free] if rW1_free else (), sem="ld_r1")
            par = e % 2
            cslc = combe[:, par * TG:(par + 1) * TG]
            psE, iE = pool.get()
            tE = p.op("tensor", lambda h, e=e, psE=psE: h.matmul(
                psE[:, :], sels[:, e * 128:(e + 1) * 128], combbf[:, :],
                start=True, stop=True), inc=1, waits=[cb2, wt_gp])
            w = [tE] + ([combe_free[par]] if combe_free[par] else [])
            cbe = p.op("vector", lambda h, psE=psE, cslc=cslc: h.tensor_copy(
                cslc, psE[:, :]), inc=1, waits=w)
            pool.done(iE, cbe)
            for hc in range(NHC):
                psQ, iQ = pool.get()
                for fc in range(NCH):
                    tQ = p.op("tensor", lambda h, hc=hc, fc=fc, psQ=psQ: h.matmul(
                        psQ[:, :], rW1[:, fc * H + hc * 128: fc * H + (hc + 1) * 128],
                        C.xnslice(fc, tg), start=(fc == 0), stop=(fc == NCH - 1)),
                        inc=(1 if fc == NCH - 1 else 0),
                        waits=[rW1_tok, C.xn_tok] if fc == 0 else ())
                dst = htld[:, (e * NHC + hc) * TG:(e * NHC + hc + 1) * TG]
                w = [tQ] + ([htld_free] if (e == 0 and hc == 0 and htld_free) else [])
                eQ = p.op("scalar", lambda h, e=e, hc=hc, psQ=psQ, dst=dst: h.activation(
                    dst, psQ[:, :], AF.Silu, bias=rb1[:, e * 4 + hc:e * 4 + hc + 1]),
                    inc=1, waits=w)
                pool.done(iQ, eQ)
                mQ = p.op("vector", lambda h, dst=dst, cslc=cslc: h.tensor_tensor(
                    dst, dst, cslc, ALU.mult), inc=1,
                    waits=[eQ, cbe])
                ht_toks.append(mQ)
                combe_free[par] = mQ
            rW1_free = tQ
        ht_all = ht_toks[-1]

        st_toks = []
        for dc in range(NCH):
            psY, iY = pool.get()
            first = True
            for hc in range(NHC):
                tY = p.op("tensor", lambda h, dc=dc, hc=hc, psY=psY, first=first: h.matmul(
                    psY[:, :], sW2[:, hc * DM + dc * 128: hc * DM + (dc + 1) * 128],
                    h1b[:, hc * TG:(hc + 1) * TG], start=first, stop=False),
                    waits=[h1_tok] if first else ())
                first = False
            for e in range(8):
                for hc in range(NHC):
                    tY = p.op("tensor", lambda h, dc=dc, e=e, hc=hc, psY=psY: h.matmul(
                        psY[:, :],
                        rW2[:, (e * NHC + hc) * DM + dc * 128: (e * NHC + hc) * DM + (dc + 1) * 128],
                        htld[:, (e * NHC + hc) * TG:(e * NHC + hc + 1) * TG],
                        start=False, stop=False),
                        waits=[ht_all] if (e == 0 and hc == 0) else ())
            tY = p.op("tensor", lambda h, dc=dc, psY=psY: h.matmul(
                psY[:, :], rb2[:, dc * 128:(dc + 1) * 128], combbf[:, :],
                start=False, stop=True), inc=1, waits=[])
            par = dc % 2
            ob = outb[:, par * 512:(par + 1) * 512]
            w = [tY] + ([out_free[par]] if out_free[par] else [])
            eY = p.op("vector", lambda h, dc=dc, psY=psY, ob=ob: h.scalar_tensor_tensor(
                ob, psY[:, :], sb2[:, dc:dc + 1], C.xslice(dc), ALU.add, ALU.add),
                inc=1, waits=w)
            pool.done(iY, eY)
            yap = bass.AP(y_hbm.tensor, dc * 128 * S + tg * TG, [[S, 128], [1, TG]])
            sT = p.op("sync", lambda h, yap=yap, ob=ob: h.dma_start(out=yap, in_=ob),
                      inc=16, sem=f"st{par}", waits=[eY])
            out_free[par] = sT
            st_toks.append(sT)
        C.xt_free = [eY]
        C.xn_free = tQ
        htld_free = tY

    return None


def build_ffn0():
    L = Launch("ffn0")
    x_hbm = L.dram_in("x", [DM, S])
    y_hbm = L.dram_out("y", [DM, S])
    D = declare_ffn0(L)
    emit_ffn0(L, x_hbm, y_hbm, D)
    L.p.wait("sync", L.p.tok("st0"))
    L.p.wait("sync", L.p.tok("st1"))
    return L.done()



def ffn0_host_inputs(inp, prefix="L0_ffn_"):
    g = lambda k: np.asarray(inp[prefix + k], dtype=np.float32)
    sb1 = g("sb1").reshape(4, 128).T.copy()
    sb2 = g("sb2").reshape(8, 128).T.copy()
    rb1 = g("rb1").reshape(8, 4, 128).transpose(2, 0, 1).reshape(128, 32).copy()
    sels = np.zeros((8, 1024), dtype=np.float32)
    for e in range(8):
        sels[e, e * 128:(e + 1) * 128] = 1.0
    return {
        "sW1": g("sW1"), "sW2": g("sW2"), "rW1": g("rW1"), "rW2": g("rW2"),
        "rb2": g("rb2"), "sb1l": sb1, "sb2l": sb2, "rb1l": rb1,
        "rWT": np.ascontiguousarray(g("routing_W").T),
        "ebl": g("expert_bias").reshape(8, 1),
        "sels": sels, "c_identin": np.eye(128, dtype=np.float32),
    }


def run_launch(nc, shared, per_core, n=8):
    in_maps = []
    for i in range(n):
        m = dict(shared)
        m.update(per_core[i])
        in_maps.append(m)
    res = run_bass_kernel_spmd(nc, in_maps, core_ids=list(range(n)))
    return res.results






def emit_attn_heads(L, C, pool, cfg):
    p = L.p
    NH = cfg["n_heads"]
    NT = S // 128
    NQG = S // TG
    kv_of = cfg["kv_of"]
    q_hbm, k_hbm, v_hbm, o_hbm = cfg["q_hbm"], cfg["k_hbm"], cfg["v_hbm"], cfg["o_hbm"]
    maskB = cfg["maskB"]
    maskB_tok = cfg["maskB_tok"]
    spill_toks = cfg["spill_toks"]
    scale_ap = C.constant(cfg["scale"])

    qb = [L.sb("at_q0", [65, S], BF16), L.sb("at_q1", [65, S], BF16)]
    kb = [L.sb("at_k0", [65, S], BF16), L.sb("at_k1", [65, S], BF16)]
    vb = [L.sb("at_v0", [128, NT * 65], BF16), L.sb("at_v1", [128, NT * 65], BF16)]
    ebuf = [L.sb("at_e0", [128, TG], BF16), L.sb("at_e1", [128, TG], BF16)]
    ost = [L.sb("at_o0", [64, TG], BF16), L.sb("at_o1", [64, TG], BF16)]
    scols = L.sb("at_scols", [128, 8], F32)
    mcols = L.sb("at_mcols", [128, 4], F32)
    rrow = L.sb("at_rrow", [1, TG], F32)
    rb_sb = L.sb("at_rbsb", [64, TG], F32)

    for par in range(2):
        p.op("gpsimd", lambda h, par=par: h.memset(kb[par][64:65, :], 1.0), inc=1)
    ones_tok = p.tok("gpsimd")

    q_free = [None, None]
    k_free = [None, None]
    v_free = [None, None]
    e_free = [None, None]
    o_free = [None, None]
    scols_free = None
    mcols_free = None
    rrow_free = None
    rb_free = None
    prev_head_pe = None

    for hh in range(NH):
        par = hh % 2
        g = kv_of(hh)
        qap = bass.AP(q_hbm.tensor, hh * 64 * S, [[S, 64], [1, S]])
        kap = bass.AP(k_hbm.tensor, g * 64 * S, [[S, 64], [1, S]])
        vap = bass.AP(v_hbm.tensor, g * 128 * (NT * 65), [[NT * 65, 128], [1, NT * 65]])
        w0 = list(spill_toks) + ([q_free[par]] if q_free[par] else [])
        qtok = p.op("sync", lambda h, qap=qap, par=par: h.dma_start(
            out=qb[par][0:64, :], in_=qap), inc=16, sem=f"ld_q{par}", waits=w0)
        w1 = [k_free[par]] if k_free[par] else []
        ktok = p.op("sync", lambda h, kap=kap, par=par: h.dma_start(
            out=kb[par][0:64, :], in_=kap), inc=16, sem=f"ld_k{par}", waits=w1)
        w2 = [v_free[par]] if v_free[par] else []
        vtok = p.op("sync", lambda h, vap=vap, par=par: h.dma_start(
            out=vb[par][:, :], in_=vap), inc=16, sem=f"ld_v{par}", waits=w2)

        ngm_toks = [None] * NQG
        fms = [None] * 4
        for qt in range(NT):
            nkg = qt // 4 + 1
            rd = None
            for kg in range(nkg):
                ps, ib = pool.get()
                wA = [qtok, ktok, ones_tok] if (qt == 0 and kg == 0) else []
                if prev_head_pe is not None and qt == 0 and kg == 0:
                    wA.append(prev_head_pe)
                t1 = p.op("tensor", lambda h, ps=ps, par=par, qt=qt, kg=kg: h.matmul(
                    ps[:, :], qb[par][0:64, qt * 128:(qt + 1) * 128],
                    kb[par][0:64, kg * TG:(kg + 1) * TG], start=True, stop=True),
                    inc=1, waits=wA)
                wR = [t1]
                if kg == 0 and scols_free is not None:
                    wR.append(scols_free)
                rd = p.op("vector", lambda h, ps=ps, kg=kg: h.tensor_reduce(
                    scols[:, kg:kg + 1], ps[:, :], axis=AX, op=ALU.max), inc=1,
                    waits=wR)
                pool.done(ib, rd)
            wF = [rd]
            if qt % 4 == 0 and mcols_free is not None:
                wF.append(mcols_free)
            fms[qt % 4] = p.op("vector", lambda h, qt=qt, nkg=nkg: h.tensor_reduce(
                mcols[:, qt % 4:qt % 4 + 1], scols[:, 0:nkg], axis=AX, op=ALU.max),
                inc=1, waits=wF)
            scols_free = fms[qt % 4]
            if qt % 4 == 3:
                qg = qt // 4
                psT, iT = pool.get()
                for j in range(4):
                    tT = p.op("tensor", lambda h, psT=psT, j=j: h.transpose(
                        psT[0:1, j * 128:(j + 1) * 128], mcols[:, j:j + 1],
                        C.ident[:]), inc=1, waits=[fms[j], C.ident_tok])
                mcols_free = tT
                ngm = p.op("vector", lambda h, psT=psT, par=par, qg=qg: h.tensor_scalar(
                    qb[par][64:65, qg * TG:(qg + 1) * TG], psT[0:1, 0:TG], -1.0,
                    None, ALU.mult), inc=1, waits=[tT])
                pool.done(iT, ngm)
                ngm_toks[qg] = ngm

        for qg in range(NQG):
            po, io = pool.get(hold=True)
            nkt = 4 * qg + 4
            vm = None
            for kt in range(nkt):
                psb, ib = pool.get()
                wM = [ngm_toks[qg]] if kt == 0 else []
                m1 = p.op("tensor", lambda h, psb=psb, par=par, kt=kt, qg=qg: h.matmul(
                    psb[:, :], kb[par][0:65, kt * 128:(kt + 1) * 128],
                    qb[par][0:65, qg * TG:(qg + 1) * TG], start=True, stop=True),
                    inc=1, waits=wM)
                if kt >= 4 * qg:
                    j = kt - 4 * qg
                    msk = p.op("vector", lambda h, psb=psb, j=j: h.tensor_tensor(
                        psb[:, :], psb[:, :], maskB[:, j * TG:(j + 1) * TG], ALU.add),
                        inc=1, waits=[m1, maskB_tok])
                    ew = msk
                else:
                    ew = m1
                epar = kt % 2
                wE = [ew, C.const_tok]
                if e_free[epar] is not None:
                    wE.append(e_free[epar])
                ex = p.op("scalar", lambda h, psb=psb, epar=epar: h.activation(
                    ebuf[epar][:, :], psb[:, :], AF.Exp, scale=scale_ap), inc=1,
                    waits=wE)
                pool.done(ib, ex)
                vm = p.op("tensor", lambda h, po=po, par=par, kt=kt, epar=epar,
                          st=(kt == 0), sp=(kt == nkt - 1): h.matmul(
                    po[0:65, :], vb[par][:, kt * 65:(kt + 1) * 65],
                    ebuf[epar][:, :], start=st, stop=sp), inc=1, waits=[ex, vtok])
                e_free[epar] = vm
            wRc = [vm] + ([rrow_free] if rrow_free else [])
            rc = p.op("vector", lambda h, po=po: h.reciprocal(rrow[0:1, :],
                                                             po[64:65, :]),
                      inc=1, waits=wRc)
            psr, ir = pool.get()
            tb = p.op("tensor", lambda h, psr=psr: h.matmul(
                psr[0:64, :], C.ones_row[0:1, 0:64], rrow[0:1, :],
                start=True, stop=True), inc=1, waits=[rc, C.const_tok])
            rrow_free = tb
            wC1 = [tb] + ([rb_free] if rb_free else [])
            c1 = p.op("vector", lambda h, psr=psr: h.tensor_copy(rb_sb[:, :],
                                                                psr[0:64, :]),
                      inc=1, waits=wC1)
            pool.done(ir, c1)
            opar = qg % 2
            wC2 = [c1] + ([o_free[opar]] if o_free[opar] else [])
            c2 = p.op("vector", lambda h, po=po, opar=opar: h.tensor_tensor(
                ost[opar][:, :], po[0:64, :], rb_sb[:, :], ALU.mult), inc=1,
                waits=wC2)
            pool.done(io, c2)
            rb_free = c2
            oap = bass.AP(o_hbm.tensor, hh * 64 * S + qg * TG, [[S, 64], [1, TG]])
            so = p.op("sync", lambda h, oap=oap, opar=opar: h.dma_start(
                out=oap, in_=ost[opar][:, :]), inc=16, sem=f"st_o{opar}",
                waits=[c2])
            o_free[opar] = so
        q_free[par] = m1
        k_free[par] = m1
        v_free[par] = vm
        prev_head_pe = m1
    return [p.tok("st_o0"), p.tok("st_o1")]


def emit_wout(L, C, pool, cfg):
    p = L.p
    o_hbm, y_hbm = cfg["o_hbm"], cfg["y_hbm"]
    Wo_sb, Wo_tok = cfg["Wo_sb"], cfg["Wo_tok"]
    bout_sb, bout_tok = cfg.get("bout_sb"), cfg.get("bout_tok")
    o_done = cfg["o_done"]
    ot = L.sb("wo_ot", [128, NCH * TG], BF16)
    outb = L.sb("wo_outb", [128, 2 * TG], F32)
    ot_free = None
    out_free = [None, None]
    last = None
    for tg in range(NTG):
        C.load_x(tg)
        w0 = list(o_done) + ([ot_free] if ot_free else [])
        for fc in range(NCH):
            oap = bass.AP(o_hbm.tensor, fc * 128 * S + tg * TG, [[S, 128], [1, TG]])
            otok = p.op("sync", lambda h, oap=oap, fc=fc: h.dma_start(
                out=ot[:, fc * TG:(fc + 1) * TG], in_=oap), inc=16, sem="ld_o",
                waits=w0 if fc == 0 else ())
        for dc in range(NCH):
            psY, iY = pool.get()
            for fc in range(NCH):
                tY = p.op("tensor", lambda h, dc=dc, fc=fc, psY=psY: h.matmul(
                    psY[:, :], Wo_sb[:, fc * DM + dc * 128: fc * DM + (dc + 1) * 128],
                    ot[:, fc * TG:(fc + 1) * TG], start=(fc == 0), stop=(fc == NCH - 1)),
                    inc=(1 if fc == NCH - 1 else 0),
                    waits=[otok, Wo_tok, C.x_tok] if fc == 0 else ())
            par = dc % 2
            ob = outb[:, par * TG:(par + 1) * TG]
            w = [tY] + ([out_free[par]] if out_free[par] else [])
            if bout_sb is not None:
                w.append(bout_tok)
                eY = p.op("vector", lambda h, dc=dc, psY=psY, ob=ob: h.scalar_tensor_tensor(
                    ob, psY[:, :], bout_sb[:, dc:dc + 1], C.xslice(dc), ALU.add,
                    ALU.add), inc=1, waits=w)
            else:
                eY = p.op("vector", lambda h, dc=dc, psY=psY, ob=ob: h.tensor_tensor(
                    ob, psY[:, :], C.xslice(dc), ALU.add), inc=1, waits=w)
            pool.done(iY, eY)
            yap = bass.AP(y_hbm.tensor, dc * 128 * S + tg * TG, [[S, 128], [1, TG]])
            sT = p.op("sync", lambda h, yap=yap, ob=ob: h.dma_start(out=yap, in_=ob),
                      inc=16, sem=f"st{par}", waits=[eY])
            out_free[par] = sT
            last = eY
        C.xt_free = [last]
        ot_free = tY
    p.wait("sync", p.tok("st0"))
    p.wait("sync", p.tok("st1"))


def make_maskB(L, C):
    mh = L.dram_in("maskb_in", [128, 4 * TG], F32)
    msb = L.sb("c_maskB", [128, 4 * TG], F32)
    tok = L.p.op("sync", lambda h: h.dma_start(out=msb[:], in_=mh[:, :]), inc=16,
                 sem="ld_w")
    return msb, tok


def host_maskB(anti=False):
    m = np.zeros((128, 4 * TG), dtype=np.float32)
    for j in range(4):
        tile = np.zeros((128, TG), dtype=np.float32)
        for sub in range(4):
            if sub < j:
                tile[:, sub * 128:(sub + 1) * 128] = 0.0 if anti else NEG
            elif sub == j:
                kk = np.arange(128)[:, None]
                qq = np.arange(128)[None, :]
                keep = (kk >= qq) if anti else (kk <= qq)
                tile[:, sub * 128:(sub + 1) * 128] = np.where(keep, 0.0, NEG)
            else:
                tile[:, sub * 128:(sub + 1) * 128] = NEG if anti else 0.0
        m[:, j * TG:(j + 1) * TG] = tile
    return m



def declare_l0a(L):
    sv, L.prefix = L.prefix, ""
    D = {
        "Wqa": L.dram_in("Wqa", [DM, 64]),
        "Wqb": L.dram_in("Wqb", [64, DM]),
        "Wkva": L.dram_in("Wkva", [DM, 32]),
        "Wkp": L.dram_in("Wkp", [32, 256]),
        "Wvp": L.dram_in("Wvp", [32, 256]),
        "Wout": L.dram_in("Wout", [DM, DM]),
        "boutl": L.dram_in("boutl", [128, 8]),
        "Pperm": L.dram_in("Pperm", [128, 128]),
        "ropeC": L.dram_in("ropeC", [64, S]),
        "ropeS": L.dram_in("ropeS", [64, S]),
    }
    L.prefix = sv
    return D


def emit_l0a(L, x_hbm, y_hbm, D):
    nc, p = L.nc, L.p
    QL, KVL, KVH = 64, 32, 4
    Wqah, Wqbh, Wkvah, Wkh, Wvh = (D["Wqa"], D["Wqb"], D["Wkva"], D["Wkp"],
                                   D["Wvp"])
    Wouth, bouth, Ph, Cth, Sth = (D["Wout"], D["boutl"], D["Pperm"], D["ropeC"],
                                  D["ropeS"])

    q_hbm = L.dram_scratch("q_sp", [DM, S])
    k_hbm = L.dram_scratch("k_sp", [KVH * 64, S])
    v_hbm = L.dram_scratch("v_sp", [KVH, 128, (S // 128) * 65])
    o_hbm = L.dram_scratch("o_sp", [DM, S])

    Wqa = L.sb("s_Wqa", [128, NCH * QL], BF16)
    Wqb = L.sb("s_Wqb", [64, DM], BF16)
    Wkva = L.sb("s_Wkva", [128, NCH * KVL], BF16)
    Wk = L.sb("s_Wk", [32, 256], BF16)
    Wv = L.sb("s_Wv", [32, 256], BF16)
    Wout = L.sb("s_Wout", [128, NCH * DM], BF16)
    bout = L.sb("s_bout", [128, 8], F32)
    Pp = L.sb("s_P", [128, 128], BF16)
    Ct = L.sb("s_ropeC", [64, S], F32)
    St = L.sb("s_ropeS", [64, S], F32)

    load_slabs(p, "gpsimd", Wqa, Wqah, QL, NCH, sem="ld_wg")
    p.op("gpsimd", lambda h: h.dma_start(out=Wqb[:], in_=Wqbh[:, :]), inc=16, sem="ld_wg")
    load_slabs(p, "gpsimd", Wkva, Wkvah, KVL, NCH, sem="ld_wg")
    p.op("gpsimd", lambda h: h.dma_start(out=Wk[:], in_=Wkh[:, :]), inc=16, sem="ld_wg")
    p.op("gpsimd", lambda h: h.dma_start(out=Wv[:], in_=Wvh[:, :]), inc=16, sem="ld_wg")
    load_slabs(p, "gpsimd", Wout, Wouth, DM, NCH, sem="ld_wg")
    p.op("gpsimd", lambda h: h.dma_start(out=Pp[:], in_=Ph[:, :]), inc=16, sem="ld_wg")
    p.op("sync", lambda h: h.dma_start(out=bout[:], in_=bouth[:, :]), inc=16, sem="ld_w")
    p.op("sync", lambda h: h.dma_start(out=Ct[:], in_=Cth[:, :]), inc=16, sem="ld_w")
    p.op("sync", lambda h: h.dma_start(out=St[:], in_=Sth[:, :]), inc=16, sem="ld_w")

    C = Common(L, x_hbm)
    maskB, maskB_tok = make_maskB(L, C)
    wt_s = p.tok("ld_w")
    C.ident_tok = wt_s
    wt_g = p.tok("ld_wg")
    L.init_pool()
    pool = L.pool

    qlb = L.sb("pj_qlb", [64, TG], BF16)
    kvlb = L.sb("pj_kvlb", [32, TG], BF16)
    qtmp = L.sb("pj_qtmp", [128, 2 * TG], BF16)
    r1 = L.sb("pj_r1", [128, 2 * TG], F32)
    r2 = L.sb("pj_r2", [128, 2 * TG], F32)
    qout = L.sb("pj_qout", [128, 2 * TG], BF16)
    vstage = L.sb("pj_vst", [128, 2 * 260], BF16)
    for pr in range(2):
        for g in range(4):
            p.op("gpsimd", lambda h, pr=pr, g=g: h.memset(
                vstage[:, pr * 260 + g * 65 + 64: pr * 260 + g * 65 + 65], 1.0), inc=1)
    vst_const = p.tok("gpsimd")

    qlb_free = None
    kvlb_free = None
    qtmp_free = [None, None]
    r1_free = [None, None]
    r2_free = [None, None]
    qout_free = [None, None]
    vst_free = [None, None]

    def rope_chunk(psq_pair, dst_ap, tgc, par, extra_w=()):
        psq, iq, tokq = psq_pair
        qt_ap = qtmp[:, par * TG:(par + 1) * TG]
        w = [tokq] + ([qtmp_free[par]] if qtmp_free[par] else [])
        ev = p.op("vector", lambda h, psq=psq, qt_ap=qt_ap: h.tensor_copy(qt_ap, psq[:, :]),
                  inc=1, waits=w)
        psp, ip = pool.get()
        tp = p.op("tensor", lambda h, psp=psp, qt_ap=qt_ap: h.matmul(
            psp[:, :], Pp[:, :], qt_ap, start=True, stop=True), inc=1,
            waits=[ev, wt_g])
        qtmp_free[par] = tp
        r1a = r1[:, par * TG:(par + 1) * TG]
        r2a = r2[:, par * TG:(par + 1) * TG]
        ct = Ct[:, tgc * TG:(tgc + 1) * TG]
        st_ = St[:, tgc * TG:(tgc + 1) * TG]
        w1 = [tokq, wt_s] + ([r1_free[par]] if r1_free[par] else []) + list(extra_w)
        ta = p.op("vector", lambda h, psq=psq, r1a=r1a, ct=ct: h.tensor_tensor(
            r1a[0:64, :], psq[0:64, :], ct, ALU.mult), inc=1, waits=w1)
        tb_ = p.op("vector", lambda h, psq=psq, r1a=r1a, ct=ct: h.tensor_tensor(
            r1a[64:128, :], psq[64:128, :], ct, ALU.mult), inc=1, waits=[ta])
        pool.done(iq, tb_)
        w2 = [tp] + ([r2_free[par]] if r2_free[par] else [])
        tc = p.op("vector", lambda h, psp=psp, r2a=r2a, st_=st_: h.tensor_tensor(
            r2a[0:64, :], psp[0:64, :], st_, ALU.mult), inc=1, waits=w2)
        td = p.op("vector", lambda h, psp=psp, r2a=r2a, st_=st_: h.tensor_tensor(
            r2a[64:128, :], psp[64:128, :], st_, ALU.mult), inc=1, waits=[tc])
        pool.done(ip, td)
        te = p.op("vector", lambda h, r1a=r1a, r2a=r2a, dst_ap=dst_ap: h.tensor_tensor(
            dst_ap, r1a, r2a, ALU.add), inc=1, waits=[tb_, td])
        r1_free[par] = te
        r2_free[par] = te
        return te

    for tg in range(NTG):
        C.load_x(tg)
        C.ln(tg, pool)
        psql, iql = pool.get()
        for fc in range(NCH):
            tql = p.op("tensor", lambda h, psql=psql, fc=fc: h.matmul(
                psql[0:64, :], Wqa[:, fc * QL:(fc + 1) * QL], C.xnslice(fc, tg),
                start=(fc == 0), stop=(fc == NCH - 1)),
                inc=(1 if fc == NCH - 1 else 0),
                waits=[C.xn_tok, wt_g] if fc == 0 else ())
        w = [tql] + ([qlb_free] if qlb_free else [])
        eql = p.op("vector", lambda h, psql=psql: h.tensor_copy(qlb[:, :], psql[0:64, :]),
                   inc=1, waits=w)
        pool.done(iql, eql)
        pskl, ikl = pool.get()
        for fc in range(NCH):
            tkl = p.op("tensor", lambda h, pskl=pskl, fc=fc: h.matmul(
                pskl[0:32, :], Wkva[:, fc * KVL:(fc + 1) * KVL], C.xnslice(fc, tg),
                start=(fc == 0), stop=(fc == NCH - 1)),
                inc=(1 if fc == NCH - 1 else 0))
        w = [tkl] + ([kvlb_free] if kvlb_free else [])
        ekl = p.op("vector", lambda h, pskl=pskl: h.tensor_copy(kvlb[:, :], pskl[0:32, :]),
                   inc=1, waits=w)
        pool.done(ikl, ekl)
        for c in range(NCH):
            par = c % 2
            psq, iq = pool.get()
            tq = p.op("tensor", lambda h, psq=psq, c=c: h.matmul(
                psq[:, :], Wqb[:, c * 128:(c + 1) * 128], qlb[:, :],
                start=True, stop=True), inc=1, waits=[eql])
            dst = qout[:, par * TG:(par + 1) * TG]
            wq = [qout_free[par]] if qout_free[par] else []
            te = rope_chunk((psq, iq, tq), dst, tg, par, extra_w=wq)
            qap2 = bass.AP(q_hbm.tensor, c * 128 * S + tg * TG, [[S, 128], [1, TG]])
            sq = p.op("sync", lambda h, qap2=qap2, dst=dst: h.dma_start(
                out=qap2, in_=dst), inc=16, sem=f"sp_q{par}", waits=[te])
            qout_free[par] = sq
            qlb_free = tq
        for c2 in range(2):
            par = c2 % 2
            psk, ik = pool.get()
            tk = p.op("tensor", lambda h, psk=psk, c2=c2: h.matmul(
                psk[:, :], Wk[:, c2 * 128:(c2 + 1) * 128], kvlb[:, :],
                start=True, stop=True), inc=1, waits=[ekl])
            dst = qout[:, par * TG:(par + 1) * TG]
            wq = [qout_free[par]] if qout_free[par] else []
            te = rope_chunk((psk, ik, tk), dst, tg, par, extra_w=wq)
            kap2 = bass.AP(k_hbm.tensor, c2 * 128 * S + tg * TG, [[S, 128], [1, TG]])
            sk = p.op("sync", lambda h, kap2=kap2, dst=dst: h.dma_start(
                out=kap2, in_=dst), inc=16, sem=f"sp_q{par}", waits=[te])
            qout_free[par] = sk
        for tt_ in range(4):
            par = tt_ % 2
            psv, iv = pool.get()
            tv = p.op("tensor", lambda h, psv=psv, tt_=tt_: h.matmul(
                psv[:, 0:256], kvlb[:, tt_ * 128:(tt_ + 1) * 128], Wv[:, :],
                start=True, stop=True), inc=1, waits=[ekl, wt_g])
            kt = tg * 4 + tt_
            wv = [tv, vst_const] + ([vst_free[par]] if vst_free[par] else [])
            for g in range(4):
                ev = p.op("vector", lambda h, psv=psv, g=g, par=par: h.tensor_copy(
                    vstage[:, par * 260 + g * 65: par * 260 + g * 65 + 64],
                    psv[:, g * 64:(g + 1) * 64]), inc=1,
                    waits=wv if g == 0 else ())
            pool.done(iv, ev)
            for g in range(4):
                vap2 = bass.AP(v_hbm.tensor, g * 128 * (32 * 65) + kt * 65,
                               [[32 * 65, 128], [1, 65]])
                sv = p.op("sync", lambda h, vap2=vap2, g=g, par=par: h.dma_start(
                    out=vap2, in_=vstage[:, par * 260 + g * 65: par * 260 + (g + 1) * 65]),
                    inc=16, sem=f"sp_v{par}", waits=[ev] if g == 0 else ())
            vst_free[par] = sv
            kvlb_free = tv

    spill_toks = [p.tok("sp_q0"), p.tok("sp_q1"), p.tok("sp_v0"), p.tok("sp_v1")]
    o_done = emit_attn_heads(L, C, pool, dict(
        n_heads=16, kv_of=lambda hh: hh // 4, scale=0.125,
        q_hbm=q_hbm, k_hbm=k_hbm, v_hbm=v_hbm, o_hbm=o_hbm,
        maskB=maskB, maskB_tok=maskB_tok, spill_toks=spill_toks))
    emit_wout(L, C, pool, dict(o_hbm=o_hbm, y_hbm=y_hbm, Wo_sb=Wout, Wo_tok=wt_g,
                               bout_sb=bout, bout_tok=wt_s, o_done=o_done))
    return None


def build_attn_l0a():
    L = Launch("l0a")
    x_hbm = L.dram_in("x", [DM, S])
    y_hbm = L.dram_out("y", [DM, S])
    emit_l0a(L, x_hbm, y_hbm, declare_l0a(L))
    return L.done()


def l0a_host_inputs(inp):
    g = lambda k: np.asarray(inp["L0_attn_a_" + k], dtype=np.float32)
    Wkvb = g("Wkvb").reshape(32, 4, 64, 2)
    inv = 1.0 / (10000.0 ** (np.arange(0, 64, 2, dtype=np.float64) / 64))
    t = np.arange(S, dtype=np.float64)
    fr = np.outer(t, inv)
    Crep = np.repeat(np.cos(fr).T, 2, axis=0).astype(np.float32)
    Srep = np.repeat(np.sin(fr).T, 2, axis=0).astype(np.float32)
    sign = np.tile(np.array([-1.0, 1.0], dtype=np.float32), 32)[:, None]
    P = np.zeros((128, 128), dtype=np.float32)
    for i in range(128):
        j = (i // 64) * 64 + ((i % 64) ^ 1)
        P[i, j] = 1.0
    return {
        "Wqa": g("Wqa"), "Wqb": g("Wqb"), "Wkva": g("Wkva"),
        "Wkp": np.ascontiguousarray(Wkvb[:, :, :, 0].reshape(32, 256)),
        "Wvp": np.ascontiguousarray(Wkvb[:, :, :, 1].reshape(32, 256)),
        "Wout": g("Wout"),
        "boutl": g("bout").reshape(8, 128).T.copy(),
        "Pperm": P, "ropeC": Crep, "ropeS": Srep * sign,
        "maskb_in": host_maskB(anti=False),
        "c_identin": np.eye(128, dtype=np.float32),
    }



def declare_l1a(L):
    sv, L.prefix = L.prefix, ""
    D = {
        "WQ": L.dram_in("WQ", [DM, DM]),
        "WK": L.dram_in("WK", [DM, DM]),
        "WV": L.dram_in("WV", [DM, DM]),
        "WOt": L.dram_in("WOt", [DM, DM]),
        "bql": L.dram_in("bql", [128, 8]),
        "bkl": L.dram_in("bkl", [128, 8]),
        "bvrow": L.dram_in("bvrow", [1, DM]),
        "bol": L.dram_in("bol", [128, 8]),
    }
    L.prefix = sv
    return D


def emit_l1a(L, x_hbm, y_hbm, D):
    nc, p = L.nc, L.p
    WQh, WKh, WVh, WOh = D["WQ"], D["WK"], D["WV"], D["WOt"]
    bqh, bkh, bvh, boh = D["bql"], D["bkl"], D["bvrow"], D["bol"]

    q_hbm = L.dram_scratch("q_sp", [DM, S])
    k_hbm = L.dram_scratch("k_sp", [DM, S])
    v_hbm = L.dram_scratch("v_sp", [16, 128, (S // 128) * 65])
    o_hbm = L.dram_scratch("o_sp", [DM, S])

    WQ = L.sb("s_WQ", [128, NCH * DM], BF16)
    WK = L.sb("s_WK", [128, NCH * DM], BF16)
    WV = L.sb("s_WV", [128, NCH * DM], BF16)
    WO = L.sb("s_WO", [128, NCH * DM], BF16)
    bq = L.sb("s_bq", [128, 8], F32)
    bk = L.sb("s_bk", [128, 8], F32)
    bo = L.sb("s_bo", [128, 8], F32)
    bvr = L.sb("s_bvr", [1, DM], BF16)
    onesbf = L.sb("s_onesbf", [1, 128], BF16)

    load_slabs(p, "gpsimd", WQ, WQh, DM, NCH, sem="ld_wg")
    load_slabs(p, "gpsimd", WK, WKh, DM, NCH, sem="ld_wg")
    load_slabs(p, "gpsimd", WV, WVh, DM, NCH, sem="ld_wg")
    load_slabs(p, "gpsimd", WO, WOh, DM, NCH, sem="ld_wg")
    p.op("gpsimd", lambda h: h.dma_start(out=bvr[:], in_=bvh[:, :]), inc=16, sem="ld_wg")
    p.op("sync", lambda h: h.dma_start(out=bq[:], in_=bqh[:, :]), inc=16, sem="ld_w")
    p.op("sync", lambda h: h.dma_start(out=bk[:], in_=bkh[:, :]), inc=16, sem="ld_w")
    p.op("sync", lambda h: h.dma_start(out=bo[:], in_=boh[:, :]), inc=16, sem="ld_w")
    p.op("gpsimd", lambda h: h.memset(onesbf[:], 1.0), inc=1)
    ones_tok = p.tok("gpsimd")

    C = Common(L, x_hbm)
    maskB, maskB_tok = make_maskB(L, C)
    wt_s = p.tok("ld_w")
    C.ident_tok = wt_s
    wt_g = p.tok("ld_wg")
    L.init_pool()
    pool = L.pool

    qst = L.sb("pj_qst", [128, 2 * TG], BF16)
    vstage = L.sb("pj_vst", [128, 2 * 520], BF16)
    for pr in range(2):
        for g in range(8):
            p.op("gpsimd", lambda h, pr=pr, g=g: h.memset(
                vstage[:, pr * 520 + g * 65 + 64: pr * 520 + g * 65 + 65], 1.0), inc=1)
    vst_const = p.tok("gpsimd")

    qst_free = [None, None]
    vst_free = [None, None]
    for tg in range(NTG):
        C.load_x(tg)
        C.ln(tg, pool)
        for (W_sb, b_sb, dsth, pfx) in ((WQ, bq, q_hbm, "q"), (WK, bk, k_hbm, "k")):
            for c in range(NCH):
                par = c % 2
                ps, ib = pool.get()
                for fc in range(NCH):
                    tq = p.op("tensor", lambda h, ps=ps, fc=fc, c=c, W_sb=W_sb: h.matmul(
                        ps[:, :], W_sb[:, fc * DM + c * 128: fc * DM + (c + 1) * 128],
                        C.xnslice(fc, tg), start=(fc == 0), stop=(fc == NCH - 1)),
                        inc=(1 if fc == NCH - 1 else 0),
                        waits=[C.xn_tok, wt_g] if fc == 0 else ())
                dst = qst[:, par * TG:(par + 1) * TG]
                w = [tq, wt_s] + ([qst_free[par]] if qst_free[par] else [])
                ev = p.op("vector", lambda h, ps=ps, dst=dst, c=c, b_sb=b_sb: h.tensor_scalar(
                    dst, ps[:, :], b_sb[:, c:c + 1], None, ALU.add), inc=1, waits=w)
                pool.done(ib, ev)
                qap2 = bass.AP(dsth.tensor, c * 128 * S + tg * TG, [[S, 128], [1, TG]])
                sq = p.op("sync", lambda h, qap2=qap2, dst=dst: h.dma_start(
                    out=qap2, in_=dst), inc=16, sem=f"sp_{pfx}{par}", waits=[ev])
                qst_free[par] = sq
        for tt_ in range(4):
            for half in range(2):
                par = (tt_ * 2 + half) % 2
                psv, iv = pool.get()
                for fc in range(NCH):
                    tv = p.op("tensor", lambda h, psv=psv, fc=fc, half=half, tt_=tt_: h.matmul(
                        psv[:, :], C.xnslice(fc, tg, width=128, off=tt_ * 128),
                        WV[:, fc * DM + half * 512: fc * DM + (half + 1) * 512],
                        start=(fc == 0), stop=False),
                        waits=[C.xn_tok, wt_g] if fc == 0 else ())
                tv = p.op("tensor", lambda h, psv=psv, half=half: h.matmul(
                    psv[:, :], onesbf[0:1, 0:128], bvr[0:1, half * 512:(half + 1) * 512],
                    start=False, stop=True), inc=1, waits=[ones_tok])
                kt = tg * 4 + tt_
                wv = [tv, vst_const] + ([vst_free[par]] if vst_free[par] else [])
                for gg in range(8):
                    ev = p.op("vector", lambda h, psv=psv, gg=gg, par=par: h.tensor_copy(
                        vstage[:, par * 520 + gg * 65: par * 520 + gg * 65 + 64],
                        psv[:, gg * 64:(gg + 1) * 64]), inc=1,
                        waits=wv if gg == 0 else ())
                pool.done(iv, ev)
                for gg in range(8):
                    g = half * 8 + gg
                    vap2 = bass.AP(v_hbm.tensor, g * 128 * (32 * 65) + kt * 65,
                                   [[32 * 65, 128], [1, 65]])
                    sv = p.op("sync", lambda h, vap2=vap2, gg=gg, par=par: h.dma_start(
                        out=vap2,
                        in_=vstage[:, par * 520 + gg * 65: par * 520 + (gg + 1) * 65]),
                        inc=16, sem=f"sp_v{par}", waits=[ev] if gg == 0 else ())
                vst_free[par] = sv

    spill_toks = [p.tok("sp_q0"), p.tok("sp_q1"), p.tok("sp_k0"), p.tok("sp_k1"),
                  p.tok("sp_v0"), p.tok("sp_v1")]
    o_done = emit_attn_heads(L, C, pool, dict(
        n_heads=16, kv_of=lambda hh: hh, scale=0.125,
        q_hbm=q_hbm, k_hbm=k_hbm, v_hbm=v_hbm, o_hbm=o_hbm,
        maskB=maskB, maskB_tok=maskB_tok, spill_toks=spill_toks))
    emit_wout(L, C, pool, dict(o_hbm=o_hbm, y_hbm=y_hbm, Wo_sb=WO, Wo_tok=wt_g,
                               bout_sb=bo, bout_tok=wt_s, o_done=o_done))
    return None


def build_attn_l1a():
    L = Launch("l1a")
    x_hbm = L.dram_in("x", [DM, S])
    y_hbm = L.dram_out("y", [DM, S])
    emit_l1a(L, x_hbm, y_hbm, declare_l1a(L))
    return L.done()


def l1a_host_inputs(inp):
    g = lambda k: np.asarray(inp["L1_attn_a_" + k], dtype=np.float32)
    WQ = np.ascontiguousarray(g("W_q").transpose(1, 0, 2).reshape(DM, DM))
    WK = np.ascontiguousarray(g("W_k").transpose(1, 0, 2).reshape(DM, DM))
    WV = np.ascontiguousarray(g("W_v").transpose(1, 0, 2).reshape(DM, DM))
    return {
        "WQ": WQ, "WK": WK, "WV": WV,
        "WOt": np.ascontiguousarray(g("W_o").T),
        "bql": g("b_q").reshape(-1).reshape(8, 128).T.copy(),
        "bkl": g("b_k").reshape(-1).reshape(8, 128).T.copy(),
        "bvrow": g("b_v").reshape(1, DM).copy(),
        "bol": g("b_o").reshape(8, 128).T.copy(),
        "maskb_in": host_maskB(anti=False),
        "c_identin": np.eye(128, dtype=np.float32),
    }



def declare_l0b(L):
    sv, L.prefix = L.prefix, ""
    D = {
        "Wqkv": L.dram_in("Wqkv", [DM, 3 * DM]),
        "maskb_anti": L.dram_in("maskb_anti", [128, 4 * TG]),
    }
    L.prefix = sv
    return D


def emit_l0b(L, x_hbm, y_hbm, D):
    nc, p = L.nc, L.p
    Wh = D["Wqkv"]
    NT = S // 128

    q_hbm = L.dram_scratch("q_sp", [DM, S])
    k_hbm = L.dram_scratch("k_sp", [DM, S])
    v_hbm = L.dram_scratch("v_sp", [NT, 128, 1025])

    W = L.sb("s_W", [128, NCH * 3 * DM], BF16)
    load_slabs(p, "gpsimd", W, Wh, 3 * DM, NCH, sem="ld_wg")
    onesbf = L.sb("s_onesbf", [1, 128], BF16)
    onescolbf = L.sb("s_onescolbf", [128, 1], BF16)
    p.op("gpsimd", lambda h: h.memset(onesbf[:], 1.0), inc=1)
    p.op("gpsimd", lambda h: h.memset(onescolbf[:], 1.0), inc=1)

    C = Common(L, x_hbm)
    msb = L.sb("c_maskB", [128, 4 * TG], F32)
    maskB_tok = p.op("sync", lambda h: h.dma_start(out=msb[:], in_=D["maskb_anti"][:, :]),
                     inc=16, sem="ld_w")
    maskB = msb
    wt_s = p.tok("ld_w")
    C.ident_tok = wt_s
    wt_g = p.tok("ld_wg")
    scale_ap = C.constant(1.0 / 32.0)
    ones_tok = p.tok("gpsimd")
    L.init_pool()
    pool = L.pool

    qst = L.sb("pj_qst", [128, 2 * TG], BF16)
    vstage = L.sb("pj_vst", [128, 2 * 1025], BF16)
    for pr in range(2):
        p.op("gpsimd", lambda h, pr=pr: h.memset(
            vstage[:, pr * 1025 + 1024: pr * 1025 + 1025], 1.0), inc=1)
    vst_const = p.tok("gpsimd")

    qst_free = [None, None]
    vst_free = [None, None]
    for tg in range(NTG):
        C.load_x(tg)
        C.ln(tg, pool)
        for part, dsth, pfx in ((0, q_hbm, "q"), (1, k_hbm, "k")):
            for c in range(NCH):
                par = c % 2
                ps, ib = pool.get()
                for fc in range(NCH):
                    tq = p.op("tensor", lambda h, ps=ps, fc=fc, c=c, part=part: h.matmul(
                        ps[:, :],
                        W[:, fc * 3 * DM + part * DM + c * 128: fc * 3 * DM + part * DM + (c + 1) * 128],
                        C.xnslice(fc, tg), start=(fc == 0), stop=(fc == NCH - 1)),
                        inc=(1 if fc == NCH - 1 else 0),
                        waits=[C.xn_tok, wt_g] if fc == 0 else ())
                dst = qst[:, par * TG:(par + 1) * TG]
                w = [tq] + ([qst_free[par]] if qst_free[par] else [])
                ev = p.op("vector", lambda h, ps=ps, dst=dst: h.tensor_copy(dst, ps[:, :]),
                          inc=1, waits=w)
                pool.done(ib, ev)
                qap2 = bass.AP(dsth.tensor, c * 128 * S + tg * TG, [[S, 128], [1, TG]])
                sq = p.op("sync", lambda h, qap2=qap2, dst=dst: h.dma_start(
                    out=qap2, in_=dst), inc=16, sem=f"sp_{pfx}{par}", waits=[ev])
                qst_free[par] = sq
        for tt_ in range(4):
            par = tt_ % 2
            psv, iv = pool.get()
            psv2, iv2 = pool.get()
            evs = []
            for half in range(2):
                psx = psv if half == 0 else psv2
                for fc in range(NCH):
                    tv = p.op("tensor", lambda h, psx=psx, fc=fc, half=half, tt_=tt_: h.matmul(
                        psx[:, :], C.xnslice(fc, tg, width=128, off=tt_ * 128),
                        W[:, fc * 3 * DM + 2 * DM + half * 512: fc * 3 * DM + 2 * DM + (half + 1) * 512],
                        start=(fc == 0), stop=(fc == NCH - 1)),
                        inc=(1 if fc == NCH - 1 else 0),
                        waits=[C.xn_tok] if fc == 0 else ())
                w = [tv, vst_const] + ([vst_free[par]] if (half == 0 and vst_free[par]) else [])
                ev = p.op("vector", lambda h, psx=psx, half=half, par=par: h.tensor_copy(
                    vstage[:, par * 1025 + half * 512: par * 1025 + (half + 1) * 512],
                    psx[:, :]), inc=1, waits=w)
                evs.append(ev)
            pool.done(iv, evs[0])
            pool.done(iv2, evs[1])
            kt = tg * 4 + tt_
            vap2 = bass.AP(v_hbm.tensor, kt * 128 * 1025, [[1025, 128], [1, 1025]])
            sv = p.op("sync", lambda h, vap2=vap2, par=par: h.dma_start(
                out=vap2, in_=vstage[:, par * 1025:(par + 1) * 1025]), inc=16,
                sem=f"sp_v{par}", waits=[evs[0], evs[1]])
            vst_free[par] = sv

    spill = [p.tok("sp_q0"), p.tok("sp_q1"), p.tok("sp_k0"), p.tok("sp_k1"),
             p.tok("sp_v0"), p.tok("sp_v1")]

    qg_b = [L.sb("at_q0", [128, NCH * TG], BF16), L.sb("at_q1", [128, NCH * TG], BF16)]
    kt_b = [L.sb("at_k0", [128, NCH * 128], BF16), L.sb("at_k1", [128, NCH * 128], BF16)]
    vt_b = [L.sb("at_v0", [128, 1025], BF16), L.sb("at_v1", [128, 1025], BF16)]
    esb = L.sb("at_e", [128, NT * TG], BF16)
    negm = L.sb("at_negm", [1, S], BF16)
    mcols = L.sb("at_mcols", [128, NT], F32)
    scol = L.sb("at_scol", [128, 1], F32)
    rrow = L.sb("at_rrow", [1, TG], F32)
    rb_sb = L.sb("at_rb", [128, TG], F32)
    outb = L.sb("at_outb", [128, 2 * TG], F32)

    qb_free = [None, None]
    kb_free = [None, None]
    vb_free = [None, None]
    scol_free = None
    mcols_free = None

    def kload(kt, src, dst_b, free, pfx):
        par = kt % 2
        ap = bass.AP(src.tensor, kt * 128, [[128 * S, NCH], [S, 128], [1, 128]])
        tok = None
        for fc in range(NCH):
            a2 = bass.AP(src.tensor, fc * 128 * S + kt * 128, [[S, 128], [1, 128]])
            w = list(spill) + ([free[par]] if free[par] else [])
            tok = p.op("sync", lambda h, a2=a2, fc=fc, par=par: h.dma_start(
                out=dst_b[par][:, fc * 128:(fc + 1) * 128], in_=a2), inc=16,
                sem=f"ld_{pfx}{par}", waits=w if fc == 0 else ())
        return tok, par

    red_tok = {}
    for kg in range(NCH):
        par = kg % 2
        ktoks = []
        for fc in range(NCH):
            a2 = bass.AP(k_hbm.tensor, fc * 128 * S + kg * TG, [[S, 128], [1, TG]])
            w = list(spill) + ([qb_free[par]] if qb_free[par] else [])
            ktok = p.op("sync", lambda h, a2=a2, fc=fc, par=par: h.dma_start(
                out=qg_b[par][:, fc * TG:(fc + 1) * TG], in_=a2), inc=16,
                sem=f"ld_kg{par}", waits=w if fc == 0 else ())
        for qt in range(4 * kg + 4):
            qpar = qt % 2
            qtoks = []
            for fc in range(NCH):
                a2 = bass.AP(q_hbm.tensor, fc * 128 * S + qt * 128, [[S, 128], [1, 128]])
                w = list(spill) + ([kb_free[qpar]] if kb_free[qpar] else [])
                qtok = p.op("sync", lambda h, a2=a2, fc=fc, qpar=qpar: h.dma_start(
                    out=kt_b[qpar][:, fc * 128:(fc + 1) * 128], in_=a2), inc=16,
                    sem=f"ld_qt{qpar}", waits=w if fc == 0 else ())
            ps, ib = pool.get()
            for fc in range(NCH):
                t1 = p.op("tensor", lambda h, ps=ps, fc=fc, par=par, qpar=qpar: h.matmul(
                    ps[:, :], kt_b[qpar][:, fc * 128:(fc + 1) * 128],
                    qg_b[par][:, fc * TG:(fc + 1) * TG],
                    start=(fc == 0), stop=(fc == NCH - 1)),
                    inc=(1 if fc == NCH - 1 else 0),
                    waits=[qtok, ktok] if fc == 0 else ())
            kb_free[qpar] = t1
            w = [t1] + ([scol_free] if scol_free else [])
            rtok = p.op("vector", lambda h, ps=ps: h.tensor_reduce(
                scol[:, :], ps[:, :], axis=AX, op=ALU.max), inc=1, waits=w)
            pool.done(ib, rtok)
            if kg == qt // 4:
                w = [rtok] + ([mcols_free] if mcols_free else [])
                mtok = p.op("vector", lambda h, qt=qt: h.tensor_copy(
                    mcols[:, qt:qt + 1], scol[:, :]), inc=1, waits=w)
            else:
                mtok = p.op("vector", lambda h, qt=qt: h.tensor_tensor(
                    mcols[:, qt:qt + 1], mcols[:, qt:qt + 1], scol[:, :], ALU.max),
                    inc=1, waits=[rtok])
            scol_free = mtok
            red_tok[qt] = mtok
        qb_free[par] = t1
    ngm_toks = [None] * NTG
    for qg in range(NTG):
        psT, iT = pool.get()
        for j in range(4):
            tT = p.op("tensor", lambda h, psT=psT, j=j, qg=qg: h.transpose(
                psT[0:1, j * 128:(j + 1) * 128], mcols[:, 4 * qg + j:4 * qg + j + 1],
                C.ident[:]), inc=1, waits=[red_tok[4 * qg + j], C.ident_tok])
        mcols_free = tT
        ngm = p.op("vector", lambda h, psT=psT, qg=qg: h.tensor_scalar(
            negm[0:1, qg * TG:(qg + 1) * TG], psT[0:1, 0:TG], -1.0, None, ALU.mult),
            inc=1, waits=[tT])
        pool.done(iT, ngm)
        ngm_toks[qg] = ngm

    qb_free = [None, None]
    kb_free = [None, None]
    e_free = None
    out_free = [None, None]
    rb_free = None
    rrow_free = None
    for qg in range(NTG):
        par = qg % 2
        for fc in range(NCH):
            a2 = bass.AP(q_hbm.tensor, fc * 128 * S + qg * TG, [[S, 128], [1, TG]])
            w = [ngm_toks[qg]] + ([qb_free[par]] if qb_free[par] else [])
            qtok = p.op("sync", lambda h, a2=a2, fc=fc, par=par: h.dma_start(
                out=qg_b[par][:, fc * TG:(fc + 1) * TG], in_=a2), inc=16,
                sem=f"ld_qg{par}", waits=w if fc == 0 else ())
        kts = list(range(4 * qg, NT))
        psl, il = pool.get(hold=True)
        for ki, kt in enumerate(kts):
            ktok, kpar = kload(kt, k_hbm, kt_b, kb_free, "kb")
            ps, ib = pool.get()
            for fc in range(NCH):
                m1 = p.op("tensor", lambda h, ps=ps, fc=fc, kpar=kpar, par=par: h.matmul(
                    ps[:, :], kt_b[kpar][:, fc * 128:(fc + 1) * 128],
                    qg_b[par][:, fc * TG:(fc + 1) * TG], start=(fc == 0), stop=False),
                    waits=[ktok, qtok] if fc == 0 else ())
            m2 = p.op("tensor", lambda h, ps=ps, qg=qg: h.matmul(
                ps[:, :], onesbf[0:1, 0:128], negm[0:1, qg * TG:(qg + 1) * TG],
                start=False, stop=True), inc=1, waits=[ones_tok])
            kb_free[kpar] = m2
            if kt < 4 * qg + 4:
                j = kt - 4 * qg
                msk = p.op("vector", lambda h, ps=ps, j=j: h.tensor_tensor(
                    ps[:, :], ps[:, :], maskB[:, j * TG:(j + 1) * TG], ALU.add),
                    inc=1, waits=[m2, maskB_tok])
                ew = msk
            else:
                ew = m2
            wE = [ew, C.const_tok] + ([e_free] if (ki == 0 and e_free) else [])
            ex = p.op("scalar", lambda h, ps=ps, ki=ki: h.activation(
                esb[:, ki * TG:(ki + 1) * TG], ps[:, :], AF.Exp, scale=scale_ap),
                inc=1, waits=wE)
            pool.done(ib, ex)
            tl = p.op("tensor", lambda h, psl=psl, ki=ki, last=(ki == len(kts) - 1): h.matmul(
                psl[0:1, :], onescolbf[:, :], esb[:, ki * TG:(ki + 1) * TG],
                start=(ki == 0), stop=last), inc=(1 if ki == len(kts) - 1 else 0),
                waits=[ex])
        wR = [tl] + ([rrow_free] if rrow_free else [])
        rc = p.op("vector", lambda h, psl=psl: h.reciprocal(rrow[0:1, :], psl[0:1, :]),
                  inc=1, waits=wR)
        pool.done(il, rc)
        psr, ir = pool.get()
        tb = p.op("tensor", lambda h, psr=psr: h.matmul(
            psr[:, :], C.ones_row[:, :], rrow[0:1, :], start=True, stop=True),
            inc=1, waits=[rc])
        rrow_free = tb
        wC = [tb] + ([rb_free] if rb_free else [])
        c1 = p.op("vector", lambda h, psr=psr: h.tensor_copy(rb_sb[:, :], psr[:, :]),
                  inc=1, waits=wC)
        pool.done(ir, c1)
        C.load_x(qg)
        ops = []
        for dc in range(NCH):
            po, io = pool.get(hold=True)
            ops.append((po, io))
        vm = None
        for ki, kt in enumerate(kts):
            vpar = kt % 2
            vap2 = bass.AP(v_hbm.tensor, kt * 128 * 1025, [[1025, 128], [1, 1025]])
            w = list(spill) + ([vb_free[vpar]] if vb_free[vpar] else [])
            vtok = p.op("sync", lambda h, vap2=vap2, vpar=vpar: h.dma_start(
                out=vt_b[vpar][:, :], in_=vap2), inc=16, sem=f"ld_v{vpar}", waits=w)
            for dc in range(NCH):
                po, io = ops[dc]
                vm = p.op("tensor", lambda h, po=po, vpar=vpar, dc=dc, ki=ki,
                          st=(ki == 0), sp=(ki == len(kts) - 1): h.matmul(
                    po[:, :], vt_b[vpar][:, dc * 128:(dc + 1) * 128],
                    esb[:, ki * TG:(ki + 1) * TG], start=st, stop=sp),
                    inc=(1 if dc == NCH - 1 else 0), waits=[vtok] if dc == 0 else ())
            vb_free[vpar] = vm
        e_free = vm
        for dc in range(NCH):
            po, io = ops[dc]
            opar = dc % 2
            ob = outb[:, opar * TG:(opar + 1) * TG]
            w = [vm, c1, C.x_tok] + ([out_free[opar]] if out_free[opar] else [])
            e1 = p.op("vector", lambda h, po=po, ob=ob: h.tensor_tensor(
                ob, po[:, :], rb_sb[:, :], ALU.mult), inc=1, waits=w)
            pool.done(io, e1)
            e2 = p.op("vector", lambda h, ob=ob, dc=dc: h.tensor_tensor(
                ob, ob, C.xslice(dc), ALU.add), inc=1, waits=[e1])
            rb_free = e1
            yap = bass.AP(y_hbm.tensor, dc * 128 * S + qg * TG, [[S, 128], [1, TG]])
            sT = p.op("sync", lambda h, yap=yap, ob=ob: h.dma_start(out=yap, in_=ob),
                      inc=16, sem=f"st{opar}", waits=[e2])
            out_free[opar] = sT
        C.xt_free = [e2]
        qb_free[par] = m1
    return None


def build_attn_l0b():
    L = Launch("l0b")
    x_hbm = L.dram_in("x", [DM, S])
    y_hbm = L.dram_out("y", [DM, S])
    emit_l0b(L, x_hbm, y_hbm, declare_l0b(L))
    L.p.wait("sync", L.p.tok("st0"))
    L.p.wait("sync", L.p.tok("st1"))
    return L.done()


def l0b_host_inputs(inp):
    return {
        "Wqkv": np.asarray(inp["L0_attn_b_W"], dtype=np.float32),
        "maskb_anti": host_maskB(anti=True),
        "c_identin": np.eye(128, dtype=np.float32),
    }



def declare_ffn1(L):
    sv, L.prefix = L.prefix, ""
    D = {
        "W1": L.dram_in("W1", [8, DM, 4096]),
        "W2": L.dram_in("W2", [8, 4096, DM]),
        "rWT1": L.dram_in("rWT1", [DM, 8]),
        "gaml": L.dram_in("gaml", [128, 8]),
        "betl": L.dram_in("betl", [128, 8]),
        "sels1": L.dram_in("sels1", [8, 1024]),
    }
    L.prefix = sv
    return D


def emit_ffn1(L, x_hbm, y_hbm, D):
    nc, p = L.nc, L.p
    H1 = 4096
    NHC1 = H1 // 128
    W1h, W2h, rWTh = D["W1"], D["W2"], D["rWT1"]
    gamh, beth, selsh = D["gaml"], D["betl"], D["sels1"]
    xn_sp = L.dram_scratch("xn_sp", [DM, S])

    rWT = L.sb("s_rWT", [128, NCH * 8], F32)
    rGT = L.sb("s_rGT", [128, NCH * 8], F32)
    gam = L.sb("s_gam", [128, 8], F32)
    bet = L.sb("s_bet", [128, 8], F32)
    sels = L.sb("s_sels", [8, 1024], BF16)
    colsg = L.sb("s_colsg", [8, 1], F32)
    cbeta = L.sb("s_cbeta", [8, 1], F32)
    W1 = L.sb("s_W1", [128, NCH * H1], BF16)
    W2 = L.sb("s_W2", [128, NHC1 * DM], BF16)
    arenaA = L.sb("arenaA", [128, 16384], BF16)
    arenaB = L.sb("arenaB", [128, 4096], BF16)

    load_slabs(p, "sync", rWT, rWTh, 8, NCH)
    p.op("sync", lambda h: h.dma_start(out=gam[:], in_=gamh[:, :]), inc=16, sem="ld_w")
    p.op("sync", lambda h: h.dma_start(out=bet[:], in_=beth[:, :]), inc=16, sem="ld_w")
    p.op("gpsimd", lambda h: h.dma_start(out=sels[:], in_=selsh[:, :]), inc=16, sem="ld_wg")
    wt_g = p.tok("ld_wg")

    C = Common(L, x_hbm,
               xt_ap=arenaA[:, 0:8192].bitcast(F32),
               xsq_ap=arenaA[:, 8192:16384].bitcast(F32))
    wt_s = p.tok("ld_w")
    C.ident_tok = wt_s
    L.init_pool()
    pool = L.pool

    rg_toks = []
    for fc in range(NCH):
        t = p.op("vector", lambda h, fc=fc: h.tensor_scalar(
            rGT[:, fc * 8:(fc + 1) * 8], rWT[:, fc * 8:(fc + 1) * 8],
            gam[:, fc:fc + 1], None, ALU.mult), inc=1, waits=[wt_s])
        rg_toks.append(t)
    psK, iK = pool.get()
    for fc in range(NCH):
        tk = p.op("tensor", lambda h, psK=psK, fc=fc: h.matmul(
            psK[0:1, 0:8], C.ones_col[:], rGT[:, fc * 8:(fc + 1) * 8],
            start=(fc == 0), stop=(fc == NCH - 1)),
            inc=(1 if fc == NCH - 1 else 0),
            waits=[rg_toks[-1], C.const_tok] if fc == 0 else ())
    csr = C.rows[0:1, 2560:2568]
    e1 = p.op("vector", lambda h: h.tensor_copy(csr, psK[0:1, 0:8]), inc=1, waits=[tk])
    pool.done(iK, e1)
    psK2, iK2 = pool.get()
    tk2 = p.op("tensor", lambda h, psK2=psK2: h.transpose(
        psK2[0:8, 0:1], csr, C.ident[0:1, 0:1]), inc=1, waits=[e1, C.ident_tok])
    e2c = p.op("vector", lambda h, psK2=psK2: h.tensor_copy(colsg[:], psK2[0:8, 0:1]),
               inc=1, waits=[tk2])
    pool.done(iK2, e2c)
    rb_toks = []
    for fc in range(NCH):
        t = p.op("vector", lambda h, fc=fc: h.tensor_scalar(
            C.xsq[:, fc * 8:fc * 8 + 8], rWT[:, fc * 8:(fc + 1) * 8],
            bet[:, fc:fc + 1], None, ALU.mult), inc=1, waits=[wt_s, e2c])
        rb_toks.append(t)
    psK3, iK3 = pool.get()
    for fc in range(NCH):
        tk3 = p.op("tensor", lambda h, psK3=psK3, fc=fc: h.matmul(
            psK3[0:1, 0:8], C.ones_col[:], C.xsq[:, fc * 8:fc * 8 + 8],
            start=(fc == 0), stop=(fc == NCH - 1)),
            inc=(1 if fc == NCH - 1 else 0),
            waits=[rb_toks[-1]] if fc == 0 else ())
    e3 = p.op("vector", lambda h, psK3=psK3: h.tensor_copy(csr, psK3[0:1, 0:8]),
              inc=1, waits=[tk3])
    pool.done(iK3, e3)
    psK4, iK4 = pool.get()
    tk4 = p.op("tensor", lambda h, psK4=psK4: h.transpose(
        psK4[0:8, 0:1], csr, C.ident[0:1, 0:1]), inc=1, waits=[e3])
    e4 = p.op("vector", lambda h, psK4=psK4: h.tensor_copy(cbeta[:], psK4[0:8, 0:1]),
              inc=1, waits=[tk4])
    pool.done(iK4, e4)
    pre_tok = e4
    C.xsq_free = tk3

    zview = arenaB[:, 0:4096].bitcast(F32)
    z = zview[0:8, 0:TG]
    relu = zview[0:8, TG:2 * TG]
    ach = zview[0:8, 2 * TG:3 * TG]
    comb_all = L.sb("comb_all", [8, S], BF16)
    ztok = L.sb("ztok", [128, 4 * 8], F32)
    vcols = L.sb("vcols", [128, 16], F32)
    eqb = L.sb("eqb", [128, 4 * 8], F32)
    xnt = C.xn
    combe = L.sb("combe", [128, TG], BF16)
    htld = arenaA[:, 0:16384]
    ytile = arenaB[:, 0:2048].bitcast(F32)
    outb = arenaB[:, 2048:4096].bitcast(F32)
    inv15 = C.constant(1.0 / 1.5, parts=128)

    xn_st = [None] * NTG
    comb_toks = [None] * NTG
    ycp_toks = {}
    for tg in range(NTG):
        C.load_x(tg)
        toks = C.ln(tg, pool, affine=(gam, bet, wt_s))
        mu = C.rows[0:1, 0:512]
        rstd = C.rows[0:1, 1536:2048]
        for fc in range(NCH):
            ap = bass.AP(xn_sp.tensor, fc * 128 * S + tg * TG, [[S, 128], [1, TG]])
            st_x = p.op("sync", lambda h, ap=ap, fc=fc: h.dma_start(
                out=ap, in_=C.xnslice(fc, tg)), inc=16, sem="sp_xn",
                waits=[toks[fc]])
        xn_st[tg] = st_x
        C.xn_free = st_x
        for fc in range(NCH):
            scr = C.xsq[:, fc * TG:(fc + 1) * TG]
            tYI = p.op("vector", lambda h, scr=scr, fc=fc: h.tensor_tensor(
                scr, scr, C.xslice(fc), ALU.add), inc=1, waits=[toks[fc]])
            ap = bass.AP(y_hbm.tensor, fc * 128 * S + tg * TG, [[S, 128], [1, TG]])
            yc = p.op("sync", lambda h, ap=ap, scr=scr: h.dma_start(
                out=ap, in_=scr), inc=16, sem="st_y0", waits=[tYI])
            ycp_toks[(tg, fc)] = yc
        C.xsq_free = yc
        psR, iR = pool.get()
        for fc in range(NCH):
            tR = p.op("tensor", lambda h, psR=psR, fc=fc: h.matmul(
                psR[0:8, :], rGT[:, fc * 8:(fc + 1) * 8], C.xslice(fc),
                start=(fc == 0), stop=(fc == NCH - 1)),
                inc=(1 if fc == NCH - 1 else 0),
                waits=[C.x_tok, rg_toks[-1]] if fc == 0 else ())
        psM, iM = pool.get()
        tM = p.op("tensor", lambda h, psM=psM: h.matmul(
            psM[0:8, :], C.ones_row[0:1, 0:8], mu, start=True, stop=True), inc=1,
            waits=[C.mu_tok])
        psS, iS = pool.get()
        tS = p.op("tensor", lambda h, psS=psS: h.matmul(
            psS[0:8, :], C.ones_row[0:1, 0:8], rstd, start=True, stop=True), inc=1,
            waits=[C.mu_tok])
        zt1 = p.op("vector", lambda h, psM=psM: h.tensor_scalar(
            ach, psM[0:8, :], colsg[:], None, ALU.mult), inc=1, waits=[tM, pre_tok])
        pool.done(iM, zt1)
        zt2 = p.op("vector", lambda h, psR=psR: h.tensor_tensor(
            relu, psR[0:8, :], ach, ALU.subtract), inc=1, waits=[tR, zt1])
        pool.done(iR, zt2)
        zt3 = p.op("vector", lambda h, psS=psS: h.tensor_tensor(
            ach, relu, psS[0:8, :], ALU.mult), inc=1, waits=[tS, zt2])
        pool.done(iS, zt3)
        zt4 = p.op("vector", lambda h: h.tensor_scalar(z, ach, cbeta[:], None,
                                                       ALU.add), inc=1, waits=[zt3])
        ctoks = []
        for tt in range(4):
            psT, iT = pool.get()
            tT = p.op("tensor", lambda h, psT=psT, tt=tt: h.transpose(
                psT[:, 0:8], z[:, tt * 128:(tt + 1) * 128], C.ident[0:8, 0:8]),
                inc=1, waits=[zt4, C.ident_tok])
            eT = p.op("vector", lambda h, psT=psT, tt=tt: h.tensor_copy(
                ztok[:, tt * 8:(tt + 1) * 8], psT[:, 0:8]), inc=1, waits=[tT])
            pool.done(iT, eT)
            zt_ap = ztok[:, tt * 8:(tt + 1) * 8]
            v1 = vcols[:, tt:tt + 1]
            v2 = vcols[:, 8 + tt:9 + tt]
            nv1 = vcols[:, 4 + tt:5 + tt]
            eq = eqb[:, tt * 8:(tt + 1) * 8]
            a = p.op("vector", lambda h, zt_ap=zt_ap, v1=v1: h.tensor_reduce(
                v1, zt_ap, axis=AX, op=ALU.max), inc=1, waits=[eT])
            b = p.op("vector", lambda h, zt_ap=zt_ap, v1=v1, eq=eq: h.tensor_scalar(
                eq, zt_ap, v1, None, ALU.is_equal), inc=1, waits=[a])
            c = p.op("vector", lambda h, zt_ap=zt_ap, eq=eq: h.scalar_tensor_tensor(
                eq, eq, -1e30, zt_ap, ALU.mult, ALU.add), inc=1, waits=[b])
            d = p.op("vector", lambda h, eq=eq, v2=v2: h.tensor_reduce(
                v2, eq, axis=AX, op=ALU.max), inc=1, waits=[c])
            nb = p.op("vector", lambda h, v1=v1, nv1=nv1: h.tensor_scalar(
                nv1, v1, -1.0 / 1.5, None, ALU.mult), inc=1, waits=[a])
            pe = p.op("scalar", lambda h, zt_ap=zt_ap, eq=eq, nv1=nv1: h.activation(
                eq, zt_ap, AF.Exp, bias=nv1, scale=inv15), inc=1,
                waits=[d, nb, C.const_tok])
            ps_ = p.op("vector", lambda h, eq=eq, tt=tt: h.tensor_reduce(
                vcols[:, 12 + tt:13 + tt], eq, axis=AX, op=ALU.add), inc=1,
                waits=[pe])
            rr = p.op("vector", lambda h, tt=tt: h.reciprocal(
                vcols[:, 12 + tt:13 + tt], vcols[:, 12 + tt:13 + tt]), inc=1,
                waits=[ps_])
            g1 = p.op("vector", lambda h, eq=eq, tt=tt: h.tensor_scalar(
                eq, eq, vcols[:, 12 + tt:13 + tt], None, ALU.mult), inc=1,
                waits=[rr])
            g2 = p.op("vector", lambda h, zt_ap=zt_ap, v2=v2: h.tensor_scalar(
                zt_ap, zt_ap, v2, None, ALU.is_ge), inc=1, waits=[g1])
            g3 = p.op("vector", lambda h, eq=eq, zt_ap=zt_ap: h.tensor_tensor(
                eq, eq, zt_ap, ALU.mult), inc=1, waits=[g2])
            psU, iU = pool.get()
            tU = p.op("tensor", lambda h, psU=psU, eq=eq, tt=tt: h.transpose(
                psU[0:8, 0:128], eq, C.ident[:]), inc=1, waits=[g3])
            cT = p.op("vector", lambda h, psU=psU, tg=tg, tt=tt: h.tensor_copy(
                comb_all[0:8, tg * TG + tt * 128: tg * TG + (tt + 1) * 128],
                psU[0:8, 0:128]), inc=1, waits=[tU])
            pool.done(iU, cT)
            ctoks.append(cT)
        comb_toks[tg] = ctoks[-1]
        C.xt_free = C.xt_free + [yc, tR]

    xn_done = [p.tok("sp_xn"), p.tok("st_y0"), p.tok("tensor"),
               p.tok("vector"), p.tok("scalar")]

    xnt_free = [None, None]
    combe_free = [None, None]
    htld_free = None
    yld_free = [None, None]
    out_free = [None, None]
    ystore = {}
    w1_free = None
    w2_free = None
    for e in range(8):
        w = list(xn_done) + ([w1_free] if w1_free else [])
        w1tok = load_slabs(p, "gpsimd", W1, W1h, H1, NCH, toff=e * DM, sem="ld_w1",
                           waits=w)
        w = [w2_free] if w2_free else []
        w2tok = load_slabs(p, "gpsimd", W2, W2h, DM, NHC1, toff=e * H1, sem="ld_w2",
                           waits=w)
        for tg in range(NTG):
            par = tg % 2
            for fc in range(NCH):
                ap = bass.AP(xn_sp.tensor, fc * 128 * S + tg * TG, [[S, 128], [1, TG]])
                wx = list(xn_done) + ([xnt_free[0]] if xnt_free[0] else [])
                xtok = p.op("sync", lambda h, ap=ap, fc=fc: h.dma_start(
                    out=xnt[:, fc * TG:(fc + 1) * TG],
                    in_=ap), inc=16, sem="ld_xn0",
                    waits=wx if fc == 0 else ())
            psE, iE = pool.get()
            tE = p.op("tensor", lambda h, psE=psE, e=e, tg=tg: h.matmul(
                psE[:, :], sels[:, e * 128:(e + 1) * 128],
                comb_all[0:8, tg * TG:(tg + 1) * TG], start=True, stop=True),
                inc=1, waits=[comb_toks[tg], wt_g])
            wc = [tE] + ([combe_free[0]] if combe_free[0] else [])
            cbe = p.op("vector", lambda h, psE=psE: h.tensor_copy(
                combe[:, :], psE[:, :]), inc=1, waits=wc)
            pool.done(iE, cbe)
            for hc in range(NHC1):
                psQ, iQ = pool.get()
                for fc in range(NCH):
                    tQ = p.op("tensor", lambda h, psQ=psQ, fc=fc, hc=hc: h.matmul(
                        psQ[:, :], W1[:, fc * H1 + hc * 128: fc * H1 + (hc + 1) * 128],
                        xnt[:, fc * TG:(fc + 1) * TG],
                        start=(fc == 0), stop=(fc == NCH - 1)),
                        inc=(1 if fc == NCH - 1 else 0),
                        waits=[xtok, w1tok] if fc == 0 else ())
                dst = htld[:, hc * TG:(hc + 1) * TG]
                w = [tQ] + ([htld_free] if (hc == 0 and htld_free) else [])
                eQ = p.op("scalar", lambda h, psQ=psQ, dst=dst: h.activation(
                    dst, psQ[:, :], AF.Silu), inc=1, waits=w)
                pool.done(iQ, eQ)
                mQ = p.op("vector", lambda h, dst=dst: h.tensor_tensor(
                    dst, dst, combe[:, :], ALU.mult), inc=1,
                    waits=[eQ, cbe])
            ht_all = mQ
            for dc in range(NCH):
                ypar = dc % 2
                yap = bass.AP(y_hbm.tensor, dc * 128 * S + tg * TG, [[S, 128], [1, TG]])
                wly = [ycp_toks[(tg, dc)]] + ([yld_free[ypar]] if yld_free[ypar] else [])
                if (tg, dc) in ystore:
                    wly.append(ystore[(tg, dc)])
                yl = p.op("sync", lambda h, yap=yap, ypar=ypar: h.dma_start(
                    out=ytile[:, ypar * TG:(ypar + 1) * TG], in_=yap), inc=16,
                    sem=f"ld_y{ypar}", waits=wly)
                psY, iY = pool.get()
                for hc in range(NHC1):
                    tY = p.op("tensor", lambda h, psY=psY, hc=hc, dc=dc: h.matmul(
                        psY[:, :], W2[:, hc * DM + dc * 128: hc * DM + (dc + 1) * 128],
                        htld[:, hc * TG:(hc + 1) * TG], start=(hc == 0),
                        stop=(hc == NHC1 - 1)), inc=(1 if hc == NHC1 - 1 else 0),
                        waits=[ht_all, w2tok] if hc == 0 else ())
                ob = outb[:, ypar * TG:(ypar + 1) * TG]
                w = [tY, yl] + ([out_free[ypar]] if out_free[ypar] else [])
                eY = p.op("vector", lambda h, psY=psY, ob=ob, ypar=ypar: h.tensor_tensor(
                    ob, psY[:, :], ytile[:, ypar * TG:(ypar + 1) * TG], ALU.add),
                    inc=1, waits=w)
                pool.done(iY, eY)
                yld_free[ypar] = eY
                sT = p.op("sync", lambda h, yap=yap, ob=ob: h.dma_start(
                    out=yap, in_=ob), inc=16, sem=f"st_y{ypar + 1}", waits=[eY])
                out_free[ypar] = sT
                ystore[(tg, dc)] = sT
            htld_free = tY
            xnt_free[0] = tQ
            combe_free[0] = mQ
        w1_free = tQ
        w2_free = tY
    return None


def build_ffn1():
    L = Launch("ffn1")
    x_hbm = L.dram_in("x", [DM, S])
    y_hbm = L.dram_out("y", [DM, S])
    emit_ffn1(L, x_hbm, y_hbm, declare_ffn1(L))
    L.p.wait("sync", L.p.tok("st_y1"))
    L.p.wait("sync", L.p.tok("st_y2"))
    return L.done()


def ffn1_host_inputs(inp):
    g = lambda k: np.asarray(inp["L1_ffn_" + k], dtype=np.float32)
    sels = np.zeros((8, 1024), dtype=np.float32)
    for e in range(8):
        sels[e, e * 128:(e + 1) * 128] = 1.0
    return {
        "W1": g("W1"), "W2": g("W2"),
        "rWT1": np.ascontiguousarray(g("router_w").T),
        "gaml": g("gamma").reshape(8, 128).T.copy(),
        "betl": g("beta").reshape(8, 128).T.copy(),
        "sels1": sels,
        "c_identin": np.eye(128, dtype=np.float32),
    }



def declare_l1b(L):
    sv, L.prefix = L.prefix, ""
    D = {
        "Wq": L.dram_in("Wq", [DM, DM]),
        "Wk": L.dram_in("Wk", [DM, DM]),
        "Wv": L.dram_in("Wv", [DM, 2048]),
        "Wg": L.dram_in("Wg", [DM, 2048]),
        "Wo": L.dram_in("Wo", [2048, DM]),
        "gnl": L.dram_in("gnl", [128, 16]),
        "ropeC2": L.dram_in("ropeC2", [128, S]),
        "ropeS2": L.dram_in("ropeS2", [128, S]),
        "dmaskT": L.dram_in("dmaskT", [64, 4 * 64]),
        "cqcol": L.dram_in("cqcol", [64, 4]),
        "ckcol": L.dram_in("ckcol", [64, 4]),
    }
    L.prefix = sv
    return D


def emit_l1b(L, x_hbm, y_hbm, D):
    nc, p = L.nc, L.p
    DV = 2048
    Wqh, Wkh, Wvh, Wgh, Woh = D["Wq"], D["Wk"], D["Wv"], D["Wg"], D["Wo"]
    gnlh, Cth, Sth = D["gnl"], D["ropeC2"], D["ropeS2"]
    dmh, cqh, ckh = D["dmaskT"], D["cqcol"], D["ckcol"]

    q_hbm = L.dram_scratch("q_sp", [DM, S])
    k_hbm = L.dram_scratch("k_sp", [DM, S])
    v_hbm = L.dram_scratch("v_sp", [4, 32, 128, 512])
    g_hbm = L.dram_scratch("g_sp", [4, 32, 128, 512])
    o2_hbm = L.dram_scratch("o2_sp", [DV, S])

    GAM = [1.0 - 2.0 ** (-5.0 - hh) for hh in range(4)]

    Wq = L.sb("s_Wq", [128, NCH * DM], BF16)
    Wk = L.sb("s_Wk", [128, NCH * DM], BF16)
    arenaA = L.sb("arenaA", [128, 32768], BF16)
    Wv = arenaA[:, 0:16384]
    Wg = arenaA[:, 16384:32768]
    Wo = arenaA[:, 0:16384]
    o2t = arenaA[:, 16384:24576]
    Ct = L.sb("s_ropeC", [128, S], BF16)
    St = L.sb("s_ropeS", [128, S], BF16)
    dmk = L.sb("s_dmaskT", [64, 4 * 64], F32)
    cqc = L.sb("s_cqcol", [64, 4], F32)
    ckc = L.sb("s_ckcol", [64, 4], F32)
    gnl = L.sb("s_gnl", [128, 16], F32)

    load_slabs(p, "gpsimd", Wq, Wqh, DM, NCH, sem="ld_wg")
    load_slabs(p, "gpsimd", Wk, Wkh, DM, NCH, sem="ld_wg")
    load_slabs(p, "gpsimd", Wv, Wvh, DV, NCH, sem="ld_wg")
    load_slabs(p, "gpsimd", Wg, Wgh, DV, NCH, sem="ld_wg")
    p.op("gpsimd", lambda h: h.dma_start(out=Ct[:], in_=Cth[:, :]), inc=16, sem="ld_wg")
    p.op("gpsimd", lambda h: h.dma_start(out=St[:], in_=Sth[:, :]), inc=16, sem="ld_wg")
    p.op("sync", lambda h: h.dma_start(out=dmk[:], in_=dmh[:, :]), inc=16, sem="ld_w")
    p.op("sync", lambda h: h.dma_start(out=cqc[:], in_=cqh[:, :]), inc=16, sem="ld_w")
    p.op("sync", lambda h: h.dma_start(out=ckc[:], in_=ckh[:, :]), inc=16, sem="ld_w")
    p.op("sync", lambda h: h.dma_start(out=gnl[:], in_=gnlh[:, :]), inc=16, sem="ld_w")

    C = Common(L, x_hbm)
    wt_s = p.tok("ld_w")
    C.ident_tok = wt_s
    wt_g = p.tok("ld_wg")
    L.init_pool()
    pool = L.pool

    r1s = L.sb("pj_r1", [128, 2 * TG], F32)
    r2s = L.sb("pj_r2", [128, 2 * TG], F32)
    qout = L.sb("pj_qout", [128, 2 * TG], BF16)
    vgst = L.sb("pj_vgst", [128, 2 * TG], BF16)
    r1_free = [None, None]
    qout_free = [None, None]
    vg_free = [None, None]

    for tg in range(NTG):
        C.load_x(tg)
        C.ln(tg, pool)
        for part, Wsb, dsth, pfx, rope in ((0, Wq, q_hbm, "q", True),
                                           (1, Wk, k_hbm, "k", True)):
            for hh in range(4):
                psA, iA = pool.get()
                psB, iB = pool.get()
                for half, psx in ((0, psA), (1, psB)):
                    c = 2 * hh + half
                    for fc in range(NCH):
                        tq = p.op("tensor", lambda h, psx=psx, fc=fc, c=c, Wsb=Wsb: h.matmul(
                            psx[:, :], Wsb[:, fc * DM + c * 128: fc * DM + (c + 1) * 128],
                            C.xnslice(fc, tg), start=(fc == 0), stop=(fc == NCH - 1)),
                            inc=(1 if fc == NCH - 1 else 0),
                            waits=[C.xn_tok, wt_g] if fc == 0 else ())
                ctg = Ct[:, tg * TG:(tg + 1) * TG]
                stg = St[:, tg * TG:(tg + 1) * TG]
                par = hh % 2
                r1a = r1s[:, par * TG:(par + 1) * TG]
                r2a = r2s[:, par * TG:(par + 1) * TG]
                qo = qout[:, par * TG:(par + 1) * TG]
                w0 = [tq] + ([r1_free[par]] if r1_free[par] else [])
                e1 = p.op("vector", lambda h, psA=psA, r1a=r1a, ctg=ctg: h.tensor_tensor(
                    r1a, psA[:, :], ctg, ALU.mult), inc=1, waits=w0)
                e2 = p.op("vector", lambda h, psB=psB, r2a=r2a, stg=stg: h.tensor_tensor(
                    r2a, psB[:, :], stg, ALU.mult), inc=1, waits=[e1])
                w1 = [e2] + ([qout_free[par]] if qout_free[par] else [])
                e3 = p.op("vector", lambda h, r1a=r1a, r2a=r2a, qo=qo: h.tensor_tensor(
                    qo, r1a, r2a, ALU.subtract), inc=1, waits=w1)
                ap1 = bass.AP(dsth.tensor, 2 * hh * 128 * S + tg * TG, [[S, 128], [1, TG]])
                s1 = p.op("sync", lambda h, ap1=ap1, qo=qo: h.dma_start(
                    out=ap1, in_=qo), inc=16, sem=f"sp_{pfx}{par}", waits=[e3])
                e4 = p.op("vector", lambda h, psB=psB, r1a=r1a, ctg=ctg: h.tensor_tensor(
                    r1a, psB[:, :], ctg, ALU.mult), inc=1, waits=[e3])
                pool.done(iB, e4)
                e5 = p.op("vector", lambda h, psA=psA, r2a=r2a, stg=stg: h.tensor_tensor(
                    r2a, psA[:, :], stg, ALU.mult), inc=1, waits=[e4])
                pool.done(iA, e5)
                w2 = [e5, s1]
                e6 = p.op("vector", lambda h, r1a=r1a, r2a=r2a, qo=qo: h.tensor_tensor(
                    qo, r1a, r2a, ALU.add), inc=1, waits=w2)
                ap2 = bass.AP(dsth.tensor, (2 * hh + 1) * 128 * S + tg * TG,
                              [[S, 128], [1, TG]])
                s2 = p.op("sync", lambda h, ap2=ap2, qo=qo: h.dma_start(
                    out=ap2, in_=qo), inc=16, sem=f"sp_{pfx}{par}", waits=[e6])
                qout_free[par] = s2
                r1_free[par] = e6
        for j2 in range(4):
            for hh in range(4):
                for which, Wsb2, dsth2, act in ((0, Wv, v_hbm, False),
                                                (1, Wg, g_hbm, True)):
                    par = (j2 * 4 + hh * 2 + which) % 2
                    psv, iv = pool.get()
                    for fc in range(NCH):
                        tv = p.op("tensor", lambda h, psv=psv, fc=fc, hh=hh, j2=j2,
                                  Wsb2=Wsb2: h.matmul(
                            psv[:, :], C.xnslice(fc, tg, width=128, off=j2 * 128),
                            Wsb2[:, fc * DV + hh * 512: fc * DV + (hh + 1) * 512],
                            start=(fc == 0), stop=(fc == NCH - 1)),
                            inc=(1 if fc == NCH - 1 else 0),
                            waits=[C.xn_tok] if fc == 0 else ())
                    vg = vgst[:, par * TG:(par + 1) * TG]
                    w = [tv] + ([vg_free[par]] if vg_free[par] else [])
                    if act:
                        ev = p.op("scalar", lambda h, psv=psv, vg=vg: h.activation(
                            vg, psv[:, :], AF.Silu), inc=1, waits=w)
                    else:
                        ev = p.op("vector", lambda h, psv=psv, vg=vg: h.tensor_copy(
                            vg, psv[:, :]), inc=1, waits=w)
                    pool.done(iv, ev)
                    jj2 = tg * 4 + j2
                    ap = bass.AP(dsth2.tensor, hh * 32 * 128 * 512 + jj2 * 128 * 512,
                                 [[512, 128], [1, 512]])
                    sv = p.op("sync", lambda h, ap=ap, vg=vg: h.dma_start(
                        out=ap, in_=vg), inc=16, sem=f"sp_v{par}", waits=[ev])
                    vg_free[par] = sv

    spill = [p.tok("sp_q0"), p.tok("sp_q1"), p.tok("sp_k0"), p.tok("sp_k1"),
             p.tok("sp_v0"), p.tok("sp_v1")]

    qh = Wq
    kh = Wk
    vgt = L.sb("sc_vgt", [64, 4 * TG], BF16)
    Ssb = L.sb("sc_S", [128, 2 * TG], F32)
    Sbf = L.sb("sc_Sbf", [128, 2 * TG], BF16)
    abuf = L.sb("sc_abuf", [64, 64], BF16)
    ktok_sb = L.sb("sc_ktok", [64, 256], BF16)
    vk_sb = L.sb("sc_vk", [64, TG], BF16)
    of32 = L.sb("sc_of32", [64, TG], F32)
    sqs = L.sb("sc_sqs", [64, TG], F32)
    scol = L.sb("sc_scol", [64, 4], F32)
    o2bf = L.sb("sc_o2", [64, TG], BF16)
    o2st = L.sb("sc_o2st", [128, 2 * 4 * 64], BF16)

    qh_free = None
    kh_free = None
    vg_free2 = [None, None]
    st_o2 = [None, None]
    prev_sc = None
    for hh in range(4):
        w = list(spill) + ([qh_free] if qh_free else [])
        for half in range(2):
            ap = bass.AP(q_hbm.tensor, (2 * hh + half) * 128 * S, [[S, 128], [1, S]])
            qtok = p.op("sync", lambda h, ap=ap, half=half: h.dma_start(
                out=qh[:, half * S:(half + 1) * S], in_=ap), inc=16, sem="ld_qh",
                waits=w if half == 0 else ())
        w = [kh_free] if kh_free else []
        for half in range(2):
            ap = bass.AP(k_hbm.tensor, (2 * hh + half) * 128 * S, [[S, 128], [1, S]])
            ktok2 = p.op("sync", lambda h, ap=ap, half=half: h.dma_start(
                out=kh[:, half * S:(half + 1) * S], in_=ap), inc=16, sem="ld_kh",
                waits=w if half == 0 else ())
        z1 = p.op("gpsimd", lambda h: h.memset(Ssb[:, :], 0.0), inc=1,
                  waits=[prev_sc] if prev_sc else [])
        z2 = p.op("gpsimd", lambda h: h.memset(Sbf[:, :], 0.0), inc=1)
        sbf_tok = p.tok("gpsimd")
        gC = GAM[hh] ** 64
        dmsl = dmk[0:64, hh * 64:(hh + 1) * 64]
        cqsl = cqc[0:64, hh:hh + 1]
        cksl = ckc[0:64, hh:hh + 1]
        st_upd = None
        sbf_upd = sbf_tok
        for j in range(64):
            j2 = j // 2
            hlf = j % 2
            vpar = j % 2
            for wh, src in ((0, v_hbm), (1, g_hbm)):
                ap = bass.AP(src.tensor,
                             hh * 32 * 128 * 512 + j2 * 128 * 512 + hlf * 64 * 512,
                             [[512, 64], [1, 512]])
                wv = list(spill) + ([vg_free2[vpar]] if vg_free2[vpar] else [])
                vgtok = p.op("sync", lambda h, ap=ap, wh=wh, vpar=vpar: h.dma_start(
                    out=vgt[:, (vpar * 2 + wh) * TG:(vpar * 2 + wh + 1) * TG],
                    in_=ap), inc=16, sem=f"ld_vg{vpar}",
                    waits=wv if wh == 0 else ())
            cur_vg = vgtok
            vsl = vgt[:, vpar * 2 * TG:(vpar * 2 + 1) * TG]
            gsl = vgt[:, (vpar * 2 + 1) * TG:(vpar * 2 + 2) * TG]
            qsl = [qh[:, half * S + j * 64:half * S + (j + 1) * 64] for half in range(2)]
            ksl = [kh[:, half * S + j * 64:half * S + (j + 1) * 64] for half in range(2)]
            ps1, i1 = pool.get()
            for half in range(2):
                t1 = p.op("tensor", lambda h, ps1=ps1, half=half, j=j, qsl=qsl,
                          ksl=ksl: h.matmul(
                    ps1[0:64, 0:64], ksl[half], qsl[half], start=(half == 0),
                    stop=(half == 1)), inc=(1 if half == 1 else 0),
                    waits=[qtok, ktok2, vgtok] if (j == 0 and half == 0) else ())
            am = p.op("vector", lambda h, ps1=ps1, dmsl=dmsl: h.tensor_tensor(
                abuf[:, :], ps1[0:64, 0:64], dmsl, ALU.mult), inc=1,
                waits=[t1, wt_s])
            pool.done(i1, am)
            ps2, i2 = pool.get()
            t2 = p.op("tensor", lambda h, ps2=ps2, vsl=vsl: h.matmul(
                ps2[0:64, :], abuf[:, :], vsl, start=True, stop=True), inc=1,
                waits=[am, cur_vg])
            ps3, i3 = pool.get()
            for half in range(2):
                t3 = p.op("tensor", lambda h, ps3=ps3, half=half, qsl=qsl: h.matmul(
                    ps3[0:64, :], qsl[half], Sbf[:, half * TG:(half + 1) * TG],
                    start=(half == 0), stop=(half == 1)),
                    inc=(1 if half == 1 else 0),
                    waits=[sbf_upd] if half == 0 else ())
            c1 = p.op("vector", lambda h, ps2=ps2: h.tensor_copy(of32[:, :],
                                                                 ps2[0:64, :]),
                      inc=1, waits=[t2])
            pool.done(i2, c1)
            c2 = p.op("vector", lambda h, ps3=ps3, cqsl=cqsl: h.scalar_tensor_tensor(
                of32[:, :], ps3[0:64, :], cqsl, of32[:, :], ALU.mult, ALU.add),
                inc=1, waits=[t3, c1])
            pool.done(i3, c2)
            sq = p.op("scalar", lambda h: h.activation(sqs[:, :], of32[:, :],
                                                       AF.Square), inc=1, waits=[c2])
            rs = p.op("vector", lambda h: h.tensor_reduce(
                scol[:, 0:1], sqs[:, :], axis=AX, op=ALU.add), inc=1, waits=[sq])
            ms = p.op("vector", lambda h: h.tensor_scalar(
                scol[:, 1:2], scol[:, 0:1], 1.0 / 512.0, None, ALU.mult), inc=1,
                waits=[rs])
            epsap = C.constant(1e-5, parts=64)
            sr = p.op("scalar", lambda h, epsap=epsap: h.activation(
                scol[:, 2:3], scol[:, 1:2], AF.Sqrt, bias=epsap), inc=1,
                waits=[ms, C.const_tok])
            rc = p.op("vector", lambda h: h.reciprocal(scol[:, 3:4], scol[:, 2:3]),
                      inc=1, waits=[sr])
            n1 = p.op("vector", lambda h: h.tensor_scalar(
                of32[:, :], of32[:, :], scol[:, 3:4], None, ALU.mult), inc=1,
                waits=[rc])
            g1 = p.op("vector", lambda h, gsl=gsl: h.tensor_tensor(
                o2bf[:, :], of32[:, :], gsl, ALU.mult), inc=1, waits=[n1])
            opar = j % 2
            ps4, i4 = pool.get()
            ps4b = ps4[:, :].bitcast(BF16)
            ev_last = None
            for sub in range(4):
                t4 = p.op("tensor", lambda h, ps4b=ps4b, sub=sub: h.transpose(
                    ps4b[:, sub * 64:(sub + 1) * 64],
                    o2bf[:, sub * 128:(sub + 1) * 128], C.ident_bf[0:64, 0:64]),
                    inc=1, waits=[g1, C.identbf_tok])
            w = [t4] + ([st_o2[opar]] if st_o2[opar] else [])
            ev = p.op("vector", lambda h, ps4b=ps4b, opar=opar: h.tensor_copy(
                o2st[:, opar * 256:(opar + 1) * 256], ps4b[:, 0:256]), inc=1,
                waits=w)
            pool.done(i4, ev)
            for sub in range(4):
                ap = bass.AP(o2_hbm.tensor, (hh * 512 + sub * 128) * S + j * 64,
                             [[S, 128], [1, 64]])
                so = p.op("sync", lambda h, ap=ap, sub=sub, opar=opar: h.dma_start(
                    out=ap, in_=o2st[:, opar * 256 + sub * 64:opar * 256 + (sub + 1) * 64]),
                    inc=16, sem=f"st_o2{opar}", waits=[ev] if sub == 0 else ())
            st_o2[opar] = so
            ps5, i5 = pool.get()
            ps5b = ps5[:, :].bitcast(BF16)
            for half in range(2):
                t5 = p.op("tensor", lambda h, ps5b=ps5b, half=half, ksl=ksl: h.transpose(
                    ps5b[0:64, half * 128:(half + 1) * 128], ksl[half],
                    C.ident_bf[:, :]), inc=1,
                    waits=[am, C.identbf_tok] if half == 0 else [am])
            kt_ev = p.op("vector", lambda h, ps5b=ps5b: h.tensor_copy(
                ktok_sb[:, :], ps5b[0:64, 0:256]), inc=1, waits=[t5])
            pool.done(i5, kt_ev)
            vk = p.op("vector", lambda h, vsl=vsl, cksl=cksl: h.tensor_scalar(
                vk_sb[:, :], vsl, cksl, None, ALU.mult), inc=1,
                waits=[kt_ev, wt_s, cur_vg])
            ps6, i6 = pool.get()
            ps7, i7 = pool.get()
            for half, psx in ((0, ps6), (1, ps7)):
                t6 = p.op("tensor", lambda h, psx=psx, half=half: h.matmul(
                    psx[:, :], ktok_sb[:, half * 128:(half + 1) * 128], vk_sb[:, :],
                    start=True, stop=True), inc=1, waits=[kt_ev, vk])
            su1 = p.op("vector", lambda h, ps6=ps6, gC=gC: h.scalar_tensor_tensor(
                Ssb[:, 0:TG], Ssb[:, 0:TG], gC, ps6[:, :], ALU.mult, ALU.add),
                inc=1, waits=[t6])
            pool.done(i6, su1)
            su2 = p.op("vector", lambda h, ps7=ps7, gC=gC: h.scalar_tensor_tensor(
                Ssb[:, TG:2 * TG], Ssb[:, TG:2 * TG], gC, ps7[:, :], ALU.mult,
                ALU.add), inc=1, waits=[t6])
            pool.done(i7, su2)
            sb1_ = p.op("vector", lambda h: h.tensor_copy(Sbf[:, 0:TG], Ssb[:, 0:TG]),
                        inc=1, waits=[su1, t3])
            sb2_ = p.op("vector", lambda h: h.tensor_copy(Sbf[:, TG:2 * TG],
                                                          Ssb[:, TG:2 * TG]),
                        inc=1, waits=[su2])
            sbf_upd = sb2_
            if hlf == 1:
                vg_free2[vpar] = vk
        qh_free = t3
        kh_free = t6
        prev_sc = sb2_

    o2_done = [p.tok("st_o20"), p.tok("st_o21")]

    wo_tok = load_slabs(p, "gpsimd", Wo, Woh, DM, 16, sem="ld_wo",
                        waits=list(o2_done) + [p.tok("vector"), p.tok("tensor")])
    gn_toks = []
    for vc in range(16):
        t = p.op("vector", lambda h, vc=vc: h.tensor_scalar(
            Wo[:, vc * DM:(vc + 1) * DM], Wo[:, vc * DM:(vc + 1) * DM],
            gnl[:, vc:vc + 1], None, ALU.mult), inc=1,
            waits=[wo_tok, wt_s])
        gn_toks.append(t)
    wo_ready = gn_toks[-1]

    ot_free = None
    out_free = [None, None]
    for tg in range(NTG):
        C.load_x(tg)
        w0 = [wo_ready] + list(o2_done) + ([ot_free] if ot_free else [])
        for vc in range(16):
            oap = bass.AP(o2_hbm.tensor, vc * 128 * S + tg * TG, [[S, 128], [1, TG]])
            otok = p.op("sync", lambda h, oap=oap, vc=vc: h.dma_start(
                out=o2t[:, vc * TG:(vc + 1) * TG], in_=oap), inc=16, sem="ld_o2",
                waits=w0 if vc == 0 else ())
        for dc in range(NCH):
            psY, iY = pool.get()
            for vc in range(16):
                tY = p.op("tensor", lambda h, psY=psY, vc=vc, dc=dc: h.matmul(
                    psY[:, :], Wo[:, vc * DM + dc * 128: vc * DM + (dc + 1) * 128],
                    o2t[:, vc * TG:(vc + 1) * TG], start=(vc == 0), stop=(vc == 15)),
                    inc=(1 if vc == 15 else 0),
                    waits=[otok, wo_ready, C.x_tok] if vc == 0 else ())
            opar = dc % 2
            ob = r1s[:, opar * TG:(opar + 1) * TG]
            w = [tY] + ([out_free[opar]] if out_free[opar] else [])
            eY = p.op("vector", lambda h, psY=psY, ob=ob, dc=dc: h.tensor_tensor(
                ob, psY[:, :], C.xslice(dc), ALU.add), inc=1, waits=w)
            pool.done(iY, eY)
            yap = bass.AP(y_hbm.tensor, dc * 128 * S + tg * TG, [[S, 128], [1, TG]])
            sT = p.op("sync", lambda h, yap=yap, ob=ob: h.dma_start(out=yap, in_=ob),
                      inc=16, sem=f"st{opar}", waits=[eY])
            out_free[opar] = sT
        C.xt_free = [eY]
        ot_free = tY
    return None


def build_attn_l1b():
    L = Launch("l1b")
    x_hbm = L.dram_in("x", [DM, S])
    y_hbm = L.dram_out("y", [DM, S])
    emit_l1b(L, x_hbm, y_hbm, declare_l1b(L))
    L.p.wait("sync", L.p.tok("st0"))
    L.p.wait("sync", L.p.tok("st1"))
    return L.done()


def l1b_host_inputs(inp):
    g = lambda k: np.asarray(inp["L1_attn_b_" + k], dtype=np.float32)
    inv = 1.0 / (10000.0 ** (np.arange(0, 256, 2, dtype=np.float64) / 256))
    t = np.arange(S, dtype=np.float64)
    fr = np.outer(inv, t)
    GAM = [1.0 - 2.0 ** (-5.0 - hh) for hh in range(4)]
    idx = np.arange(64)
    dm = np.zeros((64, 4 * 64), dtype=np.float32)
    cq = np.zeros((64, 4), dtype=np.float32)
    ck = np.zeros((64, 4), dtype=np.float32)
    for hh in range(4):
        rel = idx[None, :] - idx[:, None]
        D = np.where(rel >= 0, GAM[hh] ** np.maximum(rel, 0), 0.0)
        dm[:, hh * 64:(hh + 1) * 64] = (D / 16.0).astype(np.float32)
        cq[:, hh] = (GAM[hh] ** (idx + 1) / 16.0).astype(np.float32)
        ck[:, hh] = (GAM[hh] ** (63 - idx)).astype(np.float32)
    gn = g("gn_weight")
    gnl = np.zeros((128, 16), dtype=np.float32)
    for vc in range(16):
        gnl[:, vc] = gn[(vc % 4) * 128:(vc % 4 + 1) * 128]
    return {
        "Wq": g("Wq"), "Wk": g("Wk"), "Wv": g("Wv"), "Wg": g("Wg"), "Wo": g("Wo"),
        "gnl": gnl,
        "ropeC2": np.cos(fr).astype(np.float32), "ropeS2": np.sin(fr).astype(np.float32),
        "dmaskT": dm, "cqcol": cq, "ckcol": ck,
        "c_identin": np.eye(128, dtype=np.float32),
    }



def build_fused():
    L = Launch("fused")
    x0 = L.dram_in("x", [DM, S])
    y8 = L.dram_out("y", [DM, S])
    acts = [L.dram_scratch(f"act{i}", [DM, S], F32) for i in range(7)]
    D_l0a = declare_l0a(L)
    D_ffn0 = declare_ffn0(L)
    D_l0b = declare_l0b(L)
    D_l1a = declare_l1a(L)
    D_ffn1 = declare_ffn1(L)
    D_l1b = declare_l1b(L)
    L.use_arena(106000)
    seq = [(emit_l0a, D_l0a), (emit_ffn0, D_ffn0), (emit_l0b, D_l0b),
           (emit_ffn0, D_ffn0), (emit_l1a, D_l1a), (emit_ffn1, D_ffn1),
           (emit_l1b, D_l1b), (emit_ffn1, D_ffn1)]
    cur = x0
    for i, (fn, D) in enumerate(seq):
        L.prefix = f"s{i}_"
        out = y8 if i == 7 else acts[i]
        fn(L, cur, out, D)
        L.arena_reset()
        L.prefix = ""
        L.fence()
        cur = out
    return L.done()


def fused_host_inputs(inputs):
    hm = {}
    for d in (l0a_host_inputs(inputs), ffn0_host_inputs(inputs),
              l0b_host_inputs(inputs), l1a_host_inputs(inputs),
              ffn1_host_inputs(inputs), l1b_host_inputs(inputs)):
        for k, v in d.items():
            if k in ("c_identin", "maskb_in"):
                continue
            hm[k] = v
    eye = np.eye(128, dtype=np.float32)
    for i in range(8):
        hm[f"s{i}_c_identin"] = eye
    hm["s0_maskb_in"] = host_maskB(anti=False)
    hm["s4_maskb_in"] = host_maskB(anti=False)
    return hm


def kernel(**inputs):
    x = np.asarray(inputs["L0_attn_a_x"], dtype=np.float32)
    nc = build_fused()
    shared = fused_host_inputs(inputs)
    in_maps = []
    for i in range(8):
        m = dict(shared)
        m["x"] = np.ascontiguousarray(x[i].T)
        in_maps.append(m)
    res = run_bass_kernel_spmd(nc, in_maps, core_ids=list(range(8)))
    return np.stack([np.ascontiguousarray(res.results[i]["y"]).T
                     for i in range(8)]).astype(np.float32)


def kernel_unfused(**inputs):
    x = np.asarray(inputs["L0_attn_a_x"], dtype=np.float32)
    cur = [np.ascontiguousarray(x[i].T) for i in range(8)]

    nc_l0a = build_attn_l0a()
    nc_ffn0 = build_ffn0()
    nc_l0b = build_attn_l0b()
    nc_l1a = build_attn_l1a()
    nc_ffn1 = build_ffn1()
    nc_l1b = build_attn_l1b()

    steps = [
        (nc_l0a, l0a_host_inputs(inputs)),
        (nc_ffn0, ffn0_host_inputs(inputs)),
        (nc_l0b, l0b_host_inputs(inputs)),
        (nc_ffn0, ffn0_host_inputs(inputs)),
        (nc_l1a, l1a_host_inputs(inputs)),
        (nc_ffn1, ffn1_host_inputs(inputs)),
        (nc_l1b, l1b_host_inputs(inputs)),
        (nc_ffn1, ffn1_host_inputs(inputs)),
    ]
    for nc, shared in steps:
        in_maps = []
        for i in range(8):
            m = dict(shared)
            m["x"] = cur[i]
            in_maps.append(m)
        res = run_bass_kernel_spmd(nc, in_maps, core_ids=list(range(8)))
        cur = [np.ascontiguousarray(res.results[i]["y"]) for i in range(8)]
    return np.stack([c.T for c in cur]).astype(np.float32)


# version 03
# speedup vs baseline: 1.2473x; 1.2473x over previous
import numpy as np
import ml_dtypes
from contextlib import ExitStack
import concourse.bass as bass
import concourse.mybir as mybir
from concourse.bass_utils import run_bass_kernel_spmd
from concourse.alu_op_type import AluOpType as ALU

F32 = mybir.dt.float32
BF16 = mybir.dt.bfloat16
AF = mybir.ActivationFunctionType
AX = mybir.AxisListType.X
BF = ml_dtypes.bfloat16

S = 4096
DM = 1024
NCH = 8
TG = 512
NTG = S // TG
NEG = -1e9

ENGS = ("sync", "gpsimd", "vector", "scalar", "tensor")


class Prog:

    def __init__(self):
        self.q = {e: [] for e in ENGS}
        self.cnt = {}
        self.waited = {e: {} for e in ENGS}
        self.sems = set(ENGS)

    def wait(self, eng, tok):
        if tok is None:
            return
        o, v = tok
        if v <= self.waited[eng].get(o, 0):
            return
        self.waited[eng][o] = v
        self.q[eng].append(lambda h, SEM, o=o, v=v: h.wait_ge(SEM[o], v))

    def op(self, eng, build, inc=0, sem=None, waits=()):
        sem = sem or eng
        self.sems.add(sem)
        for t in waits:
            self.wait(eng, t)
        if inc:
            self.cnt[sem] = self.cnt.get(sem, 0) + inc
        v = self.cnt.get(sem, 0)

        def f(h, SEM, build=build, inc=inc, sem=sem):
            ins = build(h)
            if inc:
                ins.then_inc(SEM[sem], inc)
        self.q[eng].append(f)
        return (sem, v)

    def tok(self, sem):
        return (sem, self.cnt.get(sem, 0))


def finalize(nc, p, ctx):
    SEM = {e: ctx.enter_context(nc.semaphore(f"S_{e}")) for e in sorted(p.sems)}
    block = ctx.enter_context(nc.Block())
    for e in ENGS:
        if not p.q[e]:
            continue
        queue = p.q[e]

        def body(h, queue=queue):
            for f in queue:
                f(h, SEM)
        getattr(block, e)(body)


class PsumPool:
    def __init__(self, p, ctx, nc, n=8):
        self.tiles = [ctx.enter_context(nc.psum_tensor(f"pb{i}", [128, 512], F32))
                      for i in range(n)]
        self.release = [None] * n
        self.held = set()
        self.i = 0
        self.p = p

    def get(self, hold=False):
        for _ in range(len(self.tiles)):
            i = self.i
            self.i = (self.i + 1) % len(self.tiles)
            if i not in self.held:
                break
        else:
            raise RuntimeError("all psum banks held")
        if hold:
            self.held.add(i)
        if self.release[i] is not None:
            self.p.wait("tensor", self.release[i])
            self.release[i] = None
        return self.tiles[i], i

    def done(self, i, token):
        self.release[i] = token
        self.held.discard(i)


class Launch:

    def __init__(self, name):
        self.nc = bass.Bass("TRN2", target_bir_lowering=False)
        self.ctx = ExitStack()
        self.ctx.__enter__()
        self.p = Prog()
        self.pool = None
        self.name = name
        self.prefix = ""
        self.arena = None
        self.arena_off = 0
        self.arena_max = 0
        self._sb_names = set()

    def dram_in(self, name, shape, dtype=F32):
        return self.nc.dram_tensor(self.prefix + name, list(shape), dtype,
                                   kind="ExternalInput").ap()

    def dram_out(self, name, shape, dtype=F32):
        return self.nc.dram_tensor(self.prefix + name, list(shape), dtype,
                                   kind="ExternalOutput").ap()

    def dram_scratch(self, name, shape, dtype=BF16):
        return self.nc.dram_tensor(self.prefix + name, list(shape), dtype).ap()

    def use_arena(self, cols_bf16):
        self.arena = self.sb("shared_arena", [128, cols_bf16], BF16)
        self.arena_cols = cols_bf16

    def arena_reset(self):
        self.arena_max = max(self.arena_max, self.arena_off)
        self.arena_off = 0

    def sb(self, name, shape, dtype):
        name = self.prefix + name
        assert name not in self._sb_names, name
        self._sb_names.add(name)
        if self.arena is not None:
            P, Cc = shape
            bcols = Cc * (2 if dtype == F32 else 1)
            off = self.arena_off
            self.arena_off = off + bcols + (bcols & 1)
            assert self.arena_off <= self.arena_cols, (
                f"arena overflow at {name}: {self.arena_off} > {self.arena_cols}")
            ap = self.arena[:, off:off + bcols]
            if dtype == F32:
                ap = ap.bitcast(F32)
            return ap[0:P, :]
        return self.ctx.enter_context(self.nc.sbuf_tensor(name, list(shape), dtype))

    def init_pool(self):
        if self.pool is None:
            self.pool = PsumPool(self.p, self.ctx, self.nc)

    def fence(self):
        p = self.p
        if not hasattr(self, "_fb"):
            self.prefix, sv = "", self.prefix
            self._fb = self.ctx.enter_context(
                self.nc.sbuf_tensor("fence_buf", [1, 8], F32))
            self._fbb = self.ctx.enter_context(
                self.nc.sbuf_tensor("fence_bufb", [1, 8], BF16))
            p.op("gpsimd", lambda h: h.memset(self._fbb[:], 1.0), inc=1)
            self.prefix = sv
        fb, fbb = self._fb, self._fbb
        tv = p.op("vector", lambda h: h.tensor_copy(fb[0:1, 0:1], fb[0:1, 1:2]), inc=1)
        ts_ = p.op("scalar", lambda h: h.copy(fb[0:1, 2:3], fb[0:1, 3:4]), inc=1)
        fps, ifp = self.pool.get()
        tt_ = p.op("tensor", lambda h, fps=fps: h.matmul(fps[0:1, 0:1], fbb[0:1, 0:1],
                                                         fbb[0:1, 1:2], start=True,
                                                         stop=True), inc=1)
        self.pool.done(ifp, tt_)
        joins = [tv, ts_, tt_]
        for sem_name in sorted(p.cnt.keys()):
            if sem_name not in ENGS:
                joins.append((sem_name, p.cnt[sem_name]))
        tg_ = p.op("gpsimd", lambda h: h.memset(fb[0:1, 4:5], 0.0), inc=1,
                   waits=joins)
        for e in ENGS:
            p.wait(e, tg_)
        return tg_

    def done(self):
        finalize(self.nc, self.p, self.ctx)
        self.ctx.__exit__(None, None, None)
        return self.nc


def load_slabs(p, eng, sb, hbm, n, nk, toff=0, col0=0, waits=(), sem="ld_w"):
    tok = None
    for fc in range(nk):
        ap = bass.AP(hbm.tensor, (toff + fc * 128) * n, [[n, 128], [1, n]])
        tok = p.op(eng, lambda h, ap=ap, fc=fc: h.dma_start(
            out=sb[:, col0 + fc * n:col0 + (fc + 1) * n], in_=ap), inc=16,
            sem=sem, waits=waits if fc == 0 else ())
    return tok



class Common:

    def __init__(self, L, x_hbm, with_xn_full=False, xt_ap=None, xsq_ap=None,
                 rows_cols=3080):
        self.L = L
        p, nc = L.p, L.nc
        self.x_hbm = x_hbm
        self.xt = xt_ap if xt_ap is not None else L.sb("c_xt", [128, NCH * TG], F32)
        self.xsq = xsq_ap if xsq_ap is not None else L.sb("c_xsq", [128, NCH * TG], F32)
        self.with_xn_full = with_xn_full
        if with_xn_full:
            self.xn = L.sb("c_xn", [128, NCH * S], BF16)
        else:
            self.xn = L.sb("c_xn", [128, NCH * TG], BF16)
        self.rows = L.sb("c_rows", [1, rows_cols], F32)
        self.ab = L.sb("c_ab", [128, 2 * TG], F32)
        self.ones_col = L.sb("c_onescol", [128, 1], F32)
        self.ones_row = L.sb("c_onesrow", [1, 128], F32)
        self.ident = L.sb("c_ident", [128, 128], F32)
        self.xt_free = []
        self.xn_free = None
        self.xsq_free = None
        self.x_tok = None
        self.mu_tok = None
        self.ab_tok = None
        self.xn_tok = None
        self.cpool = L.sb("c_cpool", [128, 8], F32)
        self._cvals = {}
        p.op("gpsimd", lambda h: h.memset(self.ones_col[:], 1.0), inc=1)
        p.op("gpsimd", lambda h: h.memset(self.ones_row[:], 1.0), inc=1)
        self.const_tok = p.tok("gpsimd")
        ident_h = L.dram_in("c_identin", [128, 128], F32)
        self.ident_tok = p.op("sync", lambda h: h.dma_start(out=self.ident[:], in_=ident_h[:, :]), inc=16, sem="ld_w")
        self.ident_bf = L.sb("c_identbf", [128, 128], BF16)
        self.identbf_tok = p.op("gpsimd", lambda h: h.dma_start(out=self.ident_bf[:], in_=ident_h[:, :]), inc=16, sem="ld_idbf")

    def constant(self, value, parts=128):
        if value not in self._cvals:
            col = len(self._cvals)
            assert col < 8
            self._cvals[value] = col
            tok = self.L.p.op("gpsimd", lambda h, col=col, value=value: h.memset(
                self.cpool[:, col:col + 1], value), inc=1)
            self.const_tok = tok
        col = self._cvals[value]
        return self.cpool[0:parts, col:col + 1]

    def xslice(self, fc, width=TG):
        return self.xt[:, fc * TG:fc * TG + width]

    def xnslice(self, fc, tg=None, width=TG, off=0):
        if self.with_xn_full:
            return self.xn[:, fc * S + tg * TG + off: fc * S + tg * TG + off + width]
        return self.xn[:, fc * TG + off: fc * TG + off + width]

    def load_x(self, tg):
        p = self.L.p
        w = list(self.xt_free)
        for fc in range(NCH):
            ap = bass.AP(self.x_hbm.tensor, fc * 128 * S + tg * TG,
                         [[S, 128], [1, TG]])
            self.x_tok = p.op("sync", lambda h, ap=ap, fc=fc: h.dma_start(
                out=self.xt[:, fc * TG:(fc + 1) * TG], in_=ap), inc=16,
                sem="ld_x", waits=w if fc == 0 else ())
        return self.x_tok

    def ln(self, tg, pool, affine=None):
        p = self.L.p
        R = self.rows
        mu, e2, m2 = R[0:1, 0:512], R[0:1, 512:1024], R[0:1, 1024:1536]
        var, std = e2, m2
        rstd, bneg = R[0:1, 1536:2048], R[0:1, 2048:2560]
        sq_toks = []
        w0 = [self.x_tok, self.xsq_free] if self.xsq_free else [self.x_tok]
        for fc in range(NCH):
            t = p.op("scalar", lambda h, fc=fc: h.activation(
                self.xsq[:, fc * TG:(fc + 1) * TG], self.xslice(fc), AF.Square),
                inc=1, waits=w0 if fc == 0 else [self.x_tok])
            sq_toks.append(t)
        psA, iA = pool.get()
        for fc in range(NCH):
            tokA = p.op("tensor", lambda h, fc=fc, psA=psA: h.matmul(
                psA[0:1, :], self.ones_col[:], self.xslice(fc),
                start=(fc == 0), stop=(fc == NCH - 1)),
                inc=(1 if fc == NCH - 1 else 0),
                waits=[self.x_tok, self.const_tok] if fc == 0 else ())
        psB, iB = pool.get()
        for fc in range(NCH):
            tokB = p.op("tensor", lambda h, fc=fc, psB=psB: h.matmul(
                psB[0:1, :], self.ones_col[:], self.xsq[:, fc * TG:(fc + 1) * TG],
                start=(fc == 0), stop=(fc == NCH - 1)),
                inc=(1 if fc == NCH - 1 else 0),
                waits=[sq_toks[fc]])
        self.xsq_free = tokB
        t1 = p.op("vector", lambda h: h.tensor_scalar(mu, psA[0:1, :], 1.0 / DM, None, ALU.mult),
                  inc=1, waits=[tokA])
        t2 = p.op("vector", lambda h: h.tensor_scalar(e2, psB[0:1, :], 1.0 / DM, None, ALU.mult),
                  inc=1, waits=[tokB])
        pool.done(iA, t1)
        pool.done(iB, t2)
        t3 = p.op("vector", lambda h: h.tensor_tensor(m2, mu, mu, ALU.mult), inc=1,
                  waits=[t1])
        t4 = p.op("vector", lambda h: h.tensor_tensor(var, e2, m2, ALU.subtract), inc=1,
                  waits=[t2, t3])
        epsap = self.constant(1e-5, parts=1)
        t5 = p.op("scalar", lambda h, epsap=epsap: h.activation(
            std, var, AF.Sqrt, bias=epsap), inc=1, waits=[t4, self.const_tok])
        t6 = p.op("vector", lambda h: h.reciprocal(rstd, std), inc=1, waits=[t5])
        t7 = p.op("vector", lambda h: h.scalar_tensor_tensor(bneg, mu, -1.0, rstd,
                                                            ALU.mult, ALU.mult),
                  inc=1, waits=[t6])
        self.mu_tok = t7
        psC, iC = pool.get()
        tC = p.op("tensor", lambda h: h.matmul(psC[:, :], self.ones_row[:], rstd,
                                               start=True, stop=True), inc=1,
                  waits=[t6, self.ident_tok])
        psD, iD = pool.get()
        tD = p.op("tensor", lambda h: h.matmul(psD[:, :], self.ones_row[:], bneg,
                                               start=True, stop=True), inc=1,
                  waits=[t7])
        ea = p.op("vector", lambda h: h.tensor_copy(self.ab[:, 0:TG], psC[:, :]), inc=1,
                  waits=[tC, self.ab_tok] if self.ab_tok else [tC])
        eb = p.op("vector", lambda h: h.tensor_copy(self.ab[:, TG:2 * TG], psD[:, :]),
                  inc=1, waits=[tD])
        pool.done(iC, ea)
        pool.done(iD, eb)
        a_b = self.ab[:, 0:TG]
        b_b = self.ab[:, TG:2 * TG]
        xn_toks = []
        for fc in range(NCH):
            scr = self.xsq[:, fc * TG:(fc + 1) * TG]
            tA = p.op("vector", lambda h, fc=fc, scr=scr: h.tensor_tensor(
                scr, self.xslice(fc), a_b, ALU.mult), inc=1,
                waits=[ea, eb, self.xsq_free, self.x_tok] if fc == 0 else [])
            dst = self.xnslice(fc, tg)
            if affine is None:
                tB = p.op("vector", lambda h, dst=dst, scr=scr: h.tensor_tensor(
                    dst, scr, b_b, ALU.add), inc=1,
                    waits=[tA] + ([self.xn_free] if (fc == 0 and self.xn_free) else []))
            else:
                gam, bet, gtok = affine
                tB0 = p.op("vector", lambda h, scr=scr: h.tensor_tensor(
                    scr, scr, b_b, ALU.add), inc=1,
                    waits=[tA] + ([self.xn_free] if (fc == 0 and self.xn_free) else []))
                tB = p.op("vector", lambda h, dst=dst, scr=scr, fc=fc, gam=gam,
                          bet=bet: h.tensor_scalar(
                    dst, scr, gam[:, fc:fc + 1], bet[:, fc:fc + 1], ALU.mult,
                    ALU.add), inc=1, waits=[tB0, gtok])
            xn_toks.append(tB)
        self.xsq_free = xn_toks[-1]
        self.xn_tok = xn_toks[-1]
        self.ab_tok = xn_toks[-1]
        self.xt_free = [xn_toks[-1]]
        return xn_toks



def declare_ffn0(L):
    sv, L.prefix = L.prefix, ""
    D = {
        "sW1": L.dram_in("sW1", [DM, 512]),
        "sW2": L.dram_in("sW2", [512, DM]),
        "rW1": L.dram_in("rW1", [8, DM, 512]),
        "rW2": L.dram_in("rW2", [8, 512, DM]),
        "rb2": L.dram_in("rb2", [8, DM]),
        "sb1l": L.dram_in("sb1l", [128, 4]),
        "sb2l": L.dram_in("sb2l", [128, 8]),
        "rb1l": L.dram_in("rb1l", [128, 32]),
        "rWT": L.dram_in("rWT", [DM, 8]),
        "ebl": L.dram_in("ebl", [8, 1]),
        "sels": L.dram_in("sels", [8, 1024]),
    }
    L.prefix = sv
    return D


def emit_ffn0(L, x_hbm, y_hbm, D):
    nc, p = L.nc, L.p
    H = 512
    NHC = H // 128
    sW1h, sW2h, rW1h, rW2h, rb2h = D["sW1"], D["sW2"], D["rW1"], D["rW2"], D["rb2"]
    sb1h, sb2h, rb1h, rWTh, ebh, selsh = (D["sb1l"], D["sb2l"], D["rb1l"],
                                          D["rWT"], D["ebl"], D["sels"])

    sW1 = L.sb("s_sW1", [128, NCH * H], BF16)
    sW2 = L.sb("s_sW2", [128, NHC * DM], BF16)
    rW1 = L.sb("s_rW1", [128, NCH * H], BF16)
    rW2 = L.sb("s_rW2", [128, 8 * NHC * DM], BF16)
    rb2 = L.sb("s_rb2", [8, DM], BF16)
    sb1 = L.sb("s_sb1", [128, 4], F32)
    sb2 = L.sb("s_sb2", [128, 8], F32)
    rb1 = L.sb("s_rb1", [128, 32], F32)
    rWT = L.sb("s_rWT", [128, NCH * 8], F32)
    eb = L.sb("s_eb", [8, 1], F32)
    sels = L.sb("s_sels", [8, 1024], BF16)
    colsum = L.sb("s_colsum", [8, 1], F32)

    load_slabs(p, "gpsimd", sW1, sW1h, H, NCH, sem="ld_wg")
    load_slabs(p, "gpsimd", sW2, sW2h, DM, NHC, sem="ld_wg")
    for e in range(8):
        load_slabs(p, "gpsimd", rW2, rW2h, DM, NHC, toff=e * H, col0=e * NHC * DM, sem="ld_wg")
    p.op("gpsimd", lambda h: h.dma_start(out=rb2[:], in_=rb2h[:, :]), inc=16, sem="ld_wg")
    p.op("sync", lambda h: h.dma_start(out=sb1[:], in_=sb1h[:, :]), inc=16, sem="ld_w")
    p.op("sync", lambda h: h.dma_start(out=sb2[:], in_=sb2h[:, :]), inc=16, sem="ld_w")
    p.op("sync", lambda h: h.dma_start(out=rb1[:], in_=rb1h[:, :]), inc=16, sem="ld_w")
    load_slabs(p, "sync", rWT, rWTh, 8, NCH)
    p.op("sync", lambda h: h.dma_start(out=eb[:], in_=ebh[:, :]), inc=16, sem="ld_w")
    p.op("gpsimd", lambda h: h.dma_start(out=sels[:], in_=selsh[:, :]), inc=16, sem="ld_wg")
    wt_gp = p.tok("ld_wg")

    C = Common(L, x_hbm)
    wt_sync = p.tok("ld_w")
    C.ident_tok = wt_sync
    L.init_pool()
    pool = L.pool

    psK, iK = pool.get()
    for fc in range(NCH):
        tk = p.op("tensor", lambda h, fc=fc: h.matmul(
            psK[0:1, 0:8], C.ones_col[:], rWT[:, fc * 8:(fc + 1) * 8],
            start=(fc == 0), stop=(fc == NCH - 1)),
            inc=(1 if fc == NCH - 1 else 0),
            waits=[wt_sync, wt_gp, C.const_tok] if fc == 0 else ())
    csr = C.rows[0:1, 2560:2568]
    e1 = p.op("vector", lambda h: h.tensor_copy(csr, psK[0:1, 0:8]), inc=1, waits=[tk])
    pool.done(iK, e1)
    psK2, iK2 = pool.get()
    tk2 = p.op("tensor", lambda h: h.transpose(psK2[0:8, 0:1], csr, C.ident[0:1, 0:1]),
               inc=1, waits=[e1, C.ident_tok])
    e2c = p.op("vector", lambda h: h.tensor_copy(colsum[:], psK2[0:8, 0:1]), inc=1,
               waits=[tk2])
    pool.done(iK2, e2c)
    colsum_tok = e2c

    htld = L.sb("htld", [128, 32 * TG], BF16)
    h1b = L.sb("h1b", [128, NHC * TG], BF16)
    combe = L.sb("combe", [128, 2 * TG], BF16)
    zbufs = L.sb("zb", [8, 5 * TG], F32)
    combf = L.sb("combf", [8, TG], F32)
    combbf = L.sb("combbf", [8, TG], BF16)
    ztok = L.sb("ztok", [128, 4 * 8], F32)
    vcols = L.sb("vcols", [128, 8], F32)
    eqb = L.sb("eqb", [128, 4 * 8], F32)
    outb = L.sb("outb", [128, 2 * 512], F32)

    z = zbufs[0:8, 0:TG]
    relu = zbufs[0:8, TG:2 * TG]
    ach = zbufs[0:8, 2 * TG:3 * TG]
    sp = zbufs[0:8, 3 * TG:4 * TG]
    act = zbufs[0:8, 4 * TG:5 * TG]
    maskb = relu

    out_free = [None, None]
    rW1_free = None
    htld_free = None
    combe_free = [None, None]

    for tg in range(NTG):
        C.load_x(tg)
        C.ln(tg, pool)
        mu = C.rows[0:1, 0:512]
        rstd = C.rows[0:1, 1536:2048]

        psR, iR = pool.get()
        for fc in range(NCH):
            tR = p.op("tensor", lambda h, fc=fc, psR=psR: h.matmul(
                psR[0:8, :], rWT[:, fc * 8:(fc + 1) * 8], C.xslice(fc),
                start=(fc == 0), stop=(fc == NCH - 1)),
                inc=(1 if fc == NCH - 1 else 0),
                waits=[C.x_tok, wt_sync] if fc == 0 else ())
        psM, iM = pool.get()
        tM = p.op("tensor", lambda h, psM=psM: h.matmul(
            psM[0:8, :], C.ones_row[0:1, 0:8], mu, start=True, stop=True), inc=1,
            waits=[C.mu_tok])
        psS, iS = pool.get()
        tS = p.op("tensor", lambda h, psS=psS: h.matmul(
            psS[0:8, :], C.ones_row[0:1, 0:8], rstd, start=True, stop=True), inc=1,
            waits=[C.mu_tok])
        zt1 = p.op("vector", lambda h, psM=psM: h.tensor_scalar(ach, psM[0:8, :], colsum[:], None,
                                                       ALU.mult), inc=1,
                   waits=[tM, colsum_tok])
        pool.done(iM, zt1)
        zt2 = p.op("vector", lambda h, psR=psR: h.tensor_tensor(sp, psR[0:8, :], ach, ALU.subtract),
                   inc=1, waits=[tR, zt1])
        pool.done(iR, zt2)
        zt3 = p.op("vector", lambda h, psS=psS: h.tensor_tensor(relu, sp, psS[0:8, :], ALU.mult),
                   inc=1, waits=[tS, zt2])
        pool.done(iS, zt3)
        zt4 = p.op("vector", lambda h: h.tensor_scalar(z, relu, eb[:], None, ALU.add),
                   inc=1, waits=[zt3, wt_sync])
        st1 = p.op("scalar", lambda h: h.activation(ach, z, AF.Abs), inc=1, waits=[zt4])
        st2 = p.op("scalar", lambda h: h.activation(ach, ach, AF.Exp, scale=-1.0), inc=1,
                   waits=[st1])
        st3 = p.op("scalar", lambda h: h.activation(ach, ach, AF.Ln, bias=1.0), inc=1,
                   waits=[st2])
        rt = p.op("vector", lambda h: h.tensor_scalar(relu, z, 0.0, None, ALU.max),
                  inc=1, waits=[zt4])
        sp1 = p.op("vector", lambda h: h.tensor_tensor(sp, relu, ach, ALU.add), inc=1,
                   waits=[rt, st3])
        st4 = p.op("scalar", lambda h: h.activation(act, sp, AF.Ln), inc=1, waits=[sp1])
        halfap = C.constant(0.5, parts=8)
        st5 = p.op("scalar", lambda h, halfap=halfap: h.activation(
            act, act, AF.Exp, scale=halfap), inc=1, waits=[st4, C.const_tok])
        zt_toks = []
        for tt in range(4):
            psT, iT = pool.get()
            tT = p.op("tensor", lambda h, tt=tt, psT=psT: h.transpose(
                psT[:, 0:8], z[:, tt * 128:(tt + 1) * 128], C.ident[0:8, 0:8]),
                inc=1, waits=[zt4, C.ident_tok])
            eT = p.op("vector", lambda h, tt=tt, psT=psT: h.tensor_copy(
                ztok[:, tt * 8:(tt + 1) * 8], psT[:, 0:8]), inc=1, waits=[tT])
            pool.done(iT, eT)
            zt_toks.append(eT)
        v2toks = []
        for tt in range(4):
            zt_ap = ztok[:, tt * 8:(tt + 1) * 8]
            v1 = vcols[:, tt:tt + 1]
            v2 = vcols[:, 4 + tt:5 + tt]
            eq = eqb[:, tt * 8:(tt + 1) * 8]
            a = p.op("vector", lambda h, zt_ap=zt_ap, v1=v1: h.tensor_reduce(
                v1, zt_ap, axis=AX, op=ALU.max), inc=1, waits=[zt_toks[tt]])
            b = p.op("vector", lambda h, zt_ap=zt_ap, v1=v1, eq=eq: h.tensor_scalar(
                eq, zt_ap, v1, None, ALU.is_equal), inc=1, waits=[a])
            c = p.op("vector", lambda h, zt_ap=zt_ap, eq=eq: h.scalar_tensor_tensor(
                eq, eq, -1e30, zt_ap, ALU.mult, ALU.add), inc=1, waits=[b])
            d = p.op("vector", lambda h, eq=eq, v2=v2: h.tensor_reduce(
                v2, eq, axis=AX, op=ALU.max), inc=1, waits=[c])
            v2toks.append(d)
        psV, iV = pool.get()
        for tt in range(4):
            tV = p.op("tensor", lambda h, tt=tt, psV=psV: h.transpose(
                psV[0:1, tt * 128:(tt + 1) * 128], vcols[:, 4 + tt:5 + tt], C.ident[:]),
                inc=1, waits=[v2toks[tt]])
        v2r = C.rows[0:1, 2568:3080]
        eV = p.op("vector", lambda h, psV=psV: h.tensor_copy(v2r, psV[0:1, 0:512]), inc=1,
                  waits=[tV])
        pool.done(iV, eV)
        psV2, iV2 = pool.get()
        tV2 = p.op("tensor", lambda h, psV2=psV2: h.matmul(psV2[0:8, :], C.ones_row[0:1, 0:8], v2r,
                                                start=True, stop=True), inc=1,
                   waits=[eV])
        mk = p.op("vector", lambda h, psV2=psV2: h.tensor_tensor(maskb, z, psV2[0:8, :], ALU.is_ge),
                  inc=1, waits=[tV2, st5])
        pool.done(iV2, mk)
        cb1 = p.op("vector", lambda h: h.tensor_tensor(combf[:, :], maskb, act, ALU.mult),
                   inc=1, waits=[mk])
        cb2 = p.op("vector", lambda h: h.tensor_copy(combbf[:, :], combf[:, :]), inc=1, waits=[cb1])

        for hc in range(NHC):
            psH, iH = pool.get()
            for fc in range(NCH):
                tH = p.op("tensor", lambda h, hc=hc, fc=fc, psH=psH: h.matmul(
                    psH[:, :], sW1[:, fc * H + hc * 128: fc * H + (hc + 1) * 128],
                    C.xnslice(fc, tg), start=(fc == 0), stop=(fc == NCH - 1)),
                    inc=(1 if fc == NCH - 1 else 0),
                    waits=[C.xn_tok, wt_gp] if fc == 0 else ())
            eH = p.op("scalar", lambda h, hc=hc, psH=psH: h.activation(
                h1b[:, hc * TG:(hc + 1) * TG], psH[:, :], AF.Silu,
                bias=sb1[:, hc:hc + 1]), inc=1, waits=[tH, wt_sync])
            pool.done(iH, eH)
        h1_tok = eH

        ht_toks = []
        for e in range(8):
            rW1_tok = load_slabs(p, "gpsimd", rW1, rW1h, H, NCH, toff=e * DM,
                                 waits=[rW1_free] if rW1_free else (), sem="ld_r1")
            par = e % 2
            cslc = combe[:, par * TG:(par + 1) * TG]
            psE, iE = pool.get()
            tE = p.op("tensor", lambda h, e=e, psE=psE: h.matmul(
                psE[:, :], sels[:, e * 128:(e + 1) * 128], combbf[:, :],
                start=True, stop=True), inc=1, waits=[cb2, wt_gp])
            w = [tE] + ([combe_free[par]] if combe_free[par] else [])
            cbe = p.op("vector", lambda h, psE=psE, cslc=cslc: h.tensor_copy(
                cslc, psE[:, :]), inc=1, waits=w)
            pool.done(iE, cbe)
            for hc in range(NHC):
                psQ, iQ = pool.get()
                for fc in range(NCH):
                    tQ = p.op("tensor", lambda h, hc=hc, fc=fc, psQ=psQ: h.matmul(
                        psQ[:, :], rW1[:, fc * H + hc * 128: fc * H + (hc + 1) * 128],
                        C.xnslice(fc, tg), start=(fc == 0), stop=(fc == NCH - 1)),
                        inc=(1 if fc == NCH - 1 else 0),
                        waits=[rW1_tok, C.xn_tok] if fc == 0 else ())
                dst = htld[:, (e * NHC + hc) * TG:(e * NHC + hc + 1) * TG]
                w = [tQ] + ([htld_free] if (e == 0 and hc == 0 and htld_free) else [])
                eQ = p.op("scalar", lambda h, e=e, hc=hc, psQ=psQ, dst=dst: h.activation(
                    dst, psQ[:, :], AF.Silu, bias=rb1[:, e * 4 + hc:e * 4 + hc + 1]),
                    inc=1, waits=w)
                pool.done(iQ, eQ)
                mQ = p.op("vector", lambda h, dst=dst, cslc=cslc: h.tensor_tensor(
                    dst, dst, cslc, ALU.mult), inc=1,
                    waits=[eQ, cbe])
                ht_toks.append(mQ)
                combe_free[par] = mQ
            rW1_free = tQ
        ht_all = ht_toks[-1]

        st_toks = []
        for dc in range(NCH):
            psY, iY = pool.get()
            first = True
            for hc in range(NHC):
                tY = p.op("tensor", lambda h, dc=dc, hc=hc, psY=psY, first=first: h.matmul(
                    psY[:, :], sW2[:, hc * DM + dc * 128: hc * DM + (dc + 1) * 128],
                    h1b[:, hc * TG:(hc + 1) * TG], start=first, stop=False),
                    waits=[h1_tok] if first else ())
                first = False
            for e in range(8):
                for hc in range(NHC):
                    tY = p.op("tensor", lambda h, dc=dc, e=e, hc=hc, psY=psY: h.matmul(
                        psY[:, :],
                        rW2[:, (e * NHC + hc) * DM + dc * 128: (e * NHC + hc) * DM + (dc + 1) * 128],
                        htld[:, (e * NHC + hc) * TG:(e * NHC + hc + 1) * TG],
                        start=False, stop=False),
                        waits=[ht_all] if (e == 0 and hc == 0) else ())
            tY = p.op("tensor", lambda h, dc=dc, psY=psY: h.matmul(
                psY[:, :], rb2[:, dc * 128:(dc + 1) * 128], combbf[:, :],
                start=False, stop=True), inc=1, waits=[])
            par = dc % 2
            ob = outb[:, par * 512:(par + 1) * 512]
            w = [tY] + ([out_free[par]] if out_free[par] else [])
            eY = p.op("vector", lambda h, dc=dc, psY=psY, ob=ob: h.scalar_tensor_tensor(
                ob, psY[:, :], sb2[:, dc:dc + 1], C.xslice(dc), ALU.add, ALU.add),
                inc=1, waits=w)
            pool.done(iY, eY)
            yap = bass.AP(y_hbm.tensor, dc * 128 * S + tg * TG, [[S, 128], [1, TG]])
            sT = p.op("sync", lambda h, yap=yap, ob=ob: h.dma_start(out=yap, in_=ob),
                      inc=16, sem=f"st{par}", waits=[eY])
            out_free[par] = sT
            st_toks.append(sT)
        C.xt_free = [eY]
        C.xn_free = tQ
        htld_free = tY

    return None


def build_ffn0():
    L = Launch("ffn0")
    x_hbm = L.dram_in("x", [DM, S])
    y_hbm = L.dram_out("y", [DM, S])
    D = declare_ffn0(L)
    emit_ffn0(L, x_hbm, y_hbm, D)
    L.p.wait("sync", L.p.tok("st0"))
    L.p.wait("sync", L.p.tok("st1"))
    return L.done()



def ffn0_host_inputs(inp, prefix="L0_ffn_"):
    g = lambda k: np.asarray(inp[prefix + k], dtype=np.float32)
    sb1 = g("sb1").reshape(4, 128).T.copy()
    sb2 = g("sb2").reshape(8, 128).T.copy()
    rb1 = g("rb1").reshape(8, 4, 128).transpose(2, 0, 1).reshape(128, 32).copy()
    sels = np.zeros((8, 1024), dtype=np.float32)
    for e in range(8):
        sels[e, e * 128:(e + 1) * 128] = 1.0
    return {
        "sW1": g("sW1"), "sW2": g("sW2"), "rW1": g("rW1"), "rW2": g("rW2"),
        "rb2": g("rb2"), "sb1l": sb1, "sb2l": sb2, "rb1l": rb1,
        "rWT": np.ascontiguousarray(g("routing_W").T),
        "ebl": g("expert_bias").reshape(8, 1),
        "sels": sels, "c_identin": np.eye(128, dtype=np.float32),
    }


def run_launch(nc, shared, per_core, n=8):
    in_maps = []
    for i in range(n):
        m = dict(shared)
        m.update(per_core[i])
        in_maps.append(m)
    res = run_bass_kernel_spmd(nc, in_maps, core_ids=list(range(n)))
    return res.results






def emit_attn_heads(L, C, pool, cfg):
    p = L.p
    NH = cfg["n_heads"]
    NT = S // 128
    NQG = S // TG
    kv_of = cfg["kv_of"]
    q_hbm, k_hbm, v_hbm, o_hbm = cfg["q_hbm"], cfg["k_hbm"], cfg["v_hbm"], cfg["o_hbm"]
    maskB = cfg["maskB"]
    maskB_tok = cfg["maskB_tok"]
    spill_toks = cfg["spill_toks"]
    scale_ap = C.constant(cfg["scale"])

    qb = [L.sb("at_q0", [65, S], BF16), L.sb("at_q1", [65, S], BF16)]
    kb = [L.sb("at_k0", [65, S], BF16), L.sb("at_k1", [65, S], BF16)]
    vb = [L.sb("at_v0", [128, NT * 65], BF16), L.sb("at_v1", [128, NT * 65], BF16)]
    esb = cfg.get("esb")
    if esb is None:
        esb = L.sb("at_esb", [128, NT * TG], BF16)
    nsq = L.sb("at_nsq", [64, S], BF16)
    nrow = L.sb("at_nrow", [1, TG], F32)
    kcols = L.sb("at_kcols", [1, 16], F32)
    ost = [L.sb("at_o0", [64, TG], BF16), L.sb("at_o1", [64, TG], BF16)]
    rrow = L.sb("at_rrow", [1, TG], F32)
    rb_sb = L.sb("at_rbsb", [64, TG], F32)
    onescol_bf = L.sb("at_onescolbf", [64, 1], BF16)

    for par in range(2):
        p.op("gpsimd", lambda h, par=par: h.memset(kb[par][64:65, :], 1.0), inc=1)
    p.op("gpsimd", lambda h: h.memset(onescol_bf[:], 1.0), inc=1)
    ones_tok = p.tok("gpsimd")

    q_free = [None, None]
    k_free = [None, None]
    v_free = [None, None]
    e_free = None
    o_free = [None, None]
    rrow_free = None
    rb_free = None
    nsq_free = None
    nrow_free = None
    kmax_tok = None
    prev_g = None

    for hh in range(NH):
        par = hh % 2
        g = kv_of(hh)
        qap = bass.AP(q_hbm.tensor, hh * 64 * S, [[S, 64], [1, S]])
        kap = bass.AP(k_hbm.tensor, g * 64 * S, [[S, 64], [1, S]])
        vap = bass.AP(v_hbm.tensor, g * 128 * (NT * 65), [[NT * 65, 128], [1, NT * 65]])
        w0 = list(spill_toks) + ([q_free[par]] if q_free[par] else [])
        qtok = p.op("sync", lambda h, qap=qap, par=par: h.dma_start(
            out=qb[par][0:64, :], in_=qap), inc=16, sem=f"ld_q{par}", waits=w0)
        w1 = [k_free[par]] if k_free[par] else []
        ktok = p.op("sync", lambda h, kap=kap, par=par: h.dma_start(
            out=kb[par][0:64, :], in_=kap), inc=16, sem=f"ld_k{par}", waits=w1)
        w2 = [v_free[par]] if v_free[par] else []
        vtok = p.op("sync", lambda h, vap=vap, par=par: h.dma_start(
            out=vb[par][:, :], in_=vap), inc=16, sem=f"ld_v{par}", waits=w2)

        if g != prev_g:
            prev_g = g
            wk = [ktok] + ([nsq_free] if nsq_free else [])
            ksq = p.op("scalar", lambda h, par=par: h.activation(
                nsq[:, :], kb[par][0:64, :], AF.Square), inc=1, waits=wk)
            rtoks = []
            for g8 in range(NCH):
                psK, iK = pool.get()
                tk = p.op("tensor", lambda h, psK=psK, g8=g8: h.matmul(
                    psK[0:1, :], onescol_bf[:, :], nsq[:, g8 * TG:(g8 + 1) * TG],
                    start=True, stop=True), inc=1,
                    waits=[ksq, ones_tok] if g8 == 0 else [ksq])
                w = [tk] + ([nrow_free] if (g8 == 0 and nrow_free) else [])
                rk = p.op("vector", lambda h, psK=psK, g8=g8: h.tensor_reduce(
                    kcols[0:1, g8:g8 + 1], psK[0:1, :], axis=AX, op=ALU.max),
                    inc=1, waits=w)
                pool.done(iK, rk)
                rtoks.append(rk)
            km1 = p.op("vector", lambda h: h.tensor_reduce(
                kcols[0:1, 8:9], kcols[0:1, 0:8], axis=AX, op=ALU.max), inc=1,
                waits=[rtoks[-1]])
            kmax_tok = p.op("scalar", lambda h: h.activation(
                kcols[0:1, 9:10], kcols[0:1, 8:9], AF.Sqrt), inc=1, waits=[km1])
            nsq_free = rtoks[-1]
        kmaxap = kcols[0:1, 9:10]

        wq = [qtok] + ([nsq_free] if nsq_free else [])
        qsq = p.op("scalar", lambda h, par=par: h.activation(
            nsq[:, :], qb[par][0:64, :], AF.Square), inc=1, waits=wq)
        brows = [None] * NQG
        for g8 in range(NQG):
            psQ, iQ = pool.get()
            tq = p.op("tensor", lambda h, psQ=psQ, g8=g8: h.matmul(
                psQ[0:1, :], onescol_bf[:, :], nsq[:, g8 * TG:(g8 + 1) * TG],
                start=True, stop=True), inc=1, waits=[qsq])
            w = [tq] + ([nrow_free] if nrow_free else [])
            sq_ = p.op("scalar", lambda h, psQ=psQ: h.activation(
                nrow[0:1, :], psQ[0:1, :], AF.Sqrt), inc=1, waits=w)
            pool.done(iQ, sq_)
            br = p.op("vector", lambda h, par=par, g8=g8, kmaxap=kmaxap: h.tensor_scalar(
                qb[par][64:65, g8 * TG:(g8 + 1) * TG], nrow[0:1, :], kmaxap, -1.0,
                ALU.mult, ALU.mult), inc=1, waits=[sq_, kmax_tok])
            nrow_free = br
            brows[g8] = br
        nsq_free = tq

        for qg in range(NQG):
            if cfg.get("anti"):
                kts = list(range(4 * qg, NT))
            else:
                kts = list(range(0, 4 * qg + 4))
            ex_toks = []
            for ki, kt in enumerate(kts):
                psb, ib = pool.get()
                wM = [brows[qg]] if ki == 0 else []
                m1 = p.op("tensor", lambda h, psb=psb, par=par, kt=kt, qg=qg: h.matmul(
                    psb[:, :], kb[par][0:65, kt * 128:(kt + 1) * 128],
                    qb[par][0:65, qg * TG:(qg + 1) * TG], start=True, stop=True),
                    inc=1, waits=wM)
                dj = kt - 4 * qg
                if 0 <= dj < 4:
                    msk = p.op("vector", lambda h, psb=psb, dj=dj: h.tensor_tensor(
                        psb[:, :], psb[:, :], maskB[:, dj * TG:(dj + 1) * TG],
                        ALU.add), inc=1, waits=[m1, maskB_tok])
                    ew = msk
                else:
                    ew = m1
                wE = [ew, C.const_tok] + ([e_free] if (ki == 0 and e_free) else [])
                ex = p.op("scalar", lambda h, psb=psb, ki=ki: h.activation(
                    esb[:, ki * TG:(ki + 1) * TG], psb[:, :], AF.Exp,
                    scale=scale_ap), inc=1, waits=wE)
                pool.done(ib, ex)
                ex_toks.append(ex)
            po, io = pool.get(hold=True)
            vm = None
            for ki, kt in enumerate(kts):
                vm = p.op("tensor", lambda h, po=po, par=par, kt=kt, ki=ki,
                          st=(ki == 0), sp=(ki == len(kts) - 1): h.matmul(
                    po[0:65, :], vb[par][:, kt * 65:(kt + 1) * 65],
                    esb[:, ki * TG:(ki + 1) * TG], start=st, stop=sp), inc=1,
                    waits=[ex_toks[ki], vtok] if ki == 0 else [ex_toks[ki]])
            e_free = vm
            wRc = [vm] + ([rrow_free] if rrow_free else [])
            rc = p.op("vector", lambda h, po=po: h.reciprocal(rrow[0:1, :],
                                                             po[64:65, :]),
                      inc=1, waits=wRc)
            psr, ir = pool.get()
            tb = p.op("tensor", lambda h, psr=psr: h.matmul(
                psr[0:64, :], C.ones_row[0:1, 0:64], rrow[0:1, :],
                start=True, stop=True), inc=1, waits=[rc, C.const_tok])
            rrow_free = tb
            wC1 = [tb] + ([rb_free] if rb_free else [])
            c1 = p.op("vector", lambda h, psr=psr: h.tensor_copy(rb_sb[:, :],
                                                                psr[0:64, :]),
                      inc=1, waits=wC1)
            pool.done(ir, c1)
            opar = qg % 2
            wC2 = [c1] + ([o_free[opar]] if o_free[opar] else [])
            c2 = p.op("vector", lambda h, po=po, opar=opar: h.tensor_tensor(
                ost[opar][:, :], po[0:64, :], rb_sb[:, :], ALU.mult), inc=1,
                waits=wC2)
            pool.done(io, c2)
            rb_free = c2
            oap = bass.AP(o_hbm.tensor, hh * 64 * S + qg * TG, [[S, 64], [1, TG]])
            so = p.op("sync", lambda h, oap=oap, opar=opar: h.dma_start(
                out=oap, in_=ost[opar][:, :]), inc=16, sem=f"st_o{opar}",
                waits=[c2])
            o_free[opar] = so
        q_free[par] = m1
        k_free[par] = m1
        v_free[par] = vm
    return [p.tok("st_o0"), p.tok("st_o1")]


def emit_wout(L, C, pool, cfg):
    p = L.p
    o_hbm, y_hbm = cfg["o_hbm"], cfg["y_hbm"]
    Wo_sb, Wo_tok = cfg["Wo_sb"], cfg["Wo_tok"]
    bout_sb, bout_tok = cfg.get("bout_sb"), cfg.get("bout_tok")
    o_done = cfg["o_done"]
    ot = cfg.get("ot_ap")
    if ot is None:
        ot = L.sb("wo_ot", [128, NCH * TG], BF16)
        outb = L.sb("wo_outb", [128, 2 * TG], F32)
    else:
        outb = cfg["outb_ap"]
    ot_free = None
    out_free = [None, None]
    last = None
    for tg in range(NTG):
        C.load_x(tg)
        w0 = list(o_done) + ([ot_free] if ot_free else [])
        for fc in range(NCH):
            oap = bass.AP(o_hbm.tensor, fc * 128 * S + tg * TG, [[S, 128], [1, TG]])
            otok = p.op("sync", lambda h, oap=oap, fc=fc: h.dma_start(
                out=ot[:, fc * TG:(fc + 1) * TG], in_=oap), inc=16, sem="ld_o",
                waits=w0 if fc == 0 else ())
        for dc in range(NCH):
            psY, iY = pool.get()
            for fc in range(NCH):
                tY = p.op("tensor", lambda h, dc=dc, fc=fc, psY=psY: h.matmul(
                    psY[:, :], Wo_sb[:, fc * DM + dc * 128: fc * DM + (dc + 1) * 128],
                    ot[:, fc * TG:(fc + 1) * TG], start=(fc == 0), stop=(fc == NCH - 1)),
                    inc=(1 if fc == NCH - 1 else 0),
                    waits=[otok, Wo_tok, C.x_tok] if fc == 0 else ())
            par = dc % 2
            ob = outb[:, par * TG:(par + 1) * TG]
            w = [tY] + ([out_free[par]] if out_free[par] else [])
            if bout_sb is not None:
                w.append(bout_tok)
                eY = p.op("vector", lambda h, dc=dc, psY=psY, ob=ob: h.scalar_tensor_tensor(
                    ob, psY[:, :], bout_sb[:, dc:dc + 1], C.xslice(dc), ALU.add,
                    ALU.add), inc=1, waits=w)
            else:
                eY = p.op("vector", lambda h, dc=dc, psY=psY, ob=ob: h.tensor_tensor(
                    ob, psY[:, :], C.xslice(dc), ALU.add), inc=1, waits=w)
            pool.done(iY, eY)
            yap = bass.AP(y_hbm.tensor, dc * 128 * S + tg * TG, [[S, 128], [1, TG]])
            sT = p.op("sync", lambda h, yap=yap, ob=ob: h.dma_start(out=yap, in_=ob),
                      inc=16, sem=f"st{par}", waits=[eY])
            out_free[par] = sT
            last = eY
        C.xt_free = [last]
        ot_free = tY
    p.wait("sync", p.tok("st0"))
    p.wait("sync", p.tok("st1"))


def make_maskB(L, C):
    mh = L.dram_in("maskb_in", [128, 4 * TG], F32)
    msb = L.sb("c_maskB", [128, 4 * TG], F32)
    tok = L.p.op("sync", lambda h: h.dma_start(out=msb[:], in_=mh[:, :]), inc=16,
                 sem="ld_w")
    return msb, tok


def host_maskB(anti=False):
    m = np.zeros((128, 4 * TG), dtype=np.float32)
    for j in range(4):
        tile = np.zeros((128, TG), dtype=np.float32)
        for sub in range(4):
            if sub < j:
                tile[:, sub * 128:(sub + 1) * 128] = 0.0 if anti else NEG
            elif sub == j:
                kk = np.arange(128)[:, None]
                qq = np.arange(128)[None, :]
                keep = (kk >= qq) if anti else (kk <= qq)
                tile[:, sub * 128:(sub + 1) * 128] = np.where(keep, 0.0, NEG)
            else:
                tile[:, sub * 128:(sub + 1) * 128] = NEG if anti else 0.0
        m[:, j * TG:(j + 1) * TG] = tile
    return m



def declare_l0a(L):
    sv, L.prefix = L.prefix, ""
    D = {
        "Wqa": L.dram_in("Wqa", [DM, 64]),
        "Wqb": L.dram_in("Wqb", [64, DM]),
        "Wkva": L.dram_in("Wkva", [DM, 32]),
        "Wkp": L.dram_in("Wkp", [32, 256]),
        "Wvp": L.dram_in("Wvp", [32, 256]),
        "Wout": L.dram_in("Wout", [DM, DM]),
        "boutl": L.dram_in("boutl", [128, 8]),
        "Pperm": L.dram_in("Pperm", [128, 128]),
        "ropeC": L.dram_in("ropeC", [64, S]),
        "ropeS": L.dram_in("ropeS", [64, S]),
    }
    L.prefix = sv
    return D


def emit_l0a(L, x_hbm, y_hbm, D):
    nc, p = L.nc, L.p
    QL, KVL, KVH = 64, 32, 4
    Wqah, Wqbh, Wkvah, Wkh, Wvh = (D["Wqa"], D["Wqb"], D["Wkva"], D["Wkp"],
                                   D["Wvp"])
    Wouth, bouth, Ph, Cth, Sth = (D["Wout"], D["boutl"], D["Pperm"], D["ropeC"],
                                  D["ropeS"])

    q_hbm = L.dram_scratch("q_sp", [DM, S])
    k_hbm = L.dram_scratch("k_sp", [KVH * 64, S])
    v_hbm = L.dram_scratch("v_sp", [KVH, 128, (S // 128) * 65])
    o_hbm = L.dram_scratch("o_sp", [DM, S])
    arenaE = L.sb("arenaE", [128, 16384], BF16)

    Wqa = L.sb("s_Wqa", [128, NCH * QL], BF16)
    Wqb = L.sb("s_Wqb", [64, DM], BF16)
    Wkva = L.sb("s_Wkva", [128, NCH * KVL], BF16)
    Wk = L.sb("s_Wk", [32, 256], BF16)
    Wv = L.sb("s_Wv", [32, 256], BF16)
    Wout = L.sb("s_Wout", [128, NCH * DM], BF16)
    bout = L.sb("s_bout", [128, 8], F32)
    Pp = L.sb("s_P", [128, 128], BF16)
    Ct = L.sb("s_ropeC", [64, S], BF16)
    St = L.sb("s_ropeS", [64, S], BF16)

    load_slabs(p, "gpsimd", Wqa, Wqah, QL, NCH, sem="ld_wg")
    p.op("gpsimd", lambda h: h.dma_start(out=Wqb[:], in_=Wqbh[:, :]), inc=16, sem="ld_wg")
    load_slabs(p, "gpsimd", Wkva, Wkvah, KVL, NCH, sem="ld_wg")
    p.op("gpsimd", lambda h: h.dma_start(out=Wk[:], in_=Wkh[:, :]), inc=16, sem="ld_wg")
    p.op("gpsimd", lambda h: h.dma_start(out=Wv[:], in_=Wvh[:, :]), inc=16, sem="ld_wg")
    load_slabs(p, "gpsimd", Wout, Wouth, DM, NCH, sem="ld_wg")
    p.op("gpsimd", lambda h: h.dma_start(out=Pp[:], in_=Ph[:, :]), inc=16, sem="ld_wg")
    p.op("sync", lambda h: h.dma_start(out=bout[:], in_=bouth[:, :]), inc=16, sem="ld_w")
    p.op("gpsimd", lambda h: h.dma_start(out=Ct[:, :], in_=Cth[:, :]), inc=16, sem="ld_wg")
    p.op("gpsimd", lambda h: h.dma_start(out=St[:, :], in_=Sth[:, :]), inc=16, sem="ld_wg")

    C = Common(L, x_hbm, xsq_ap=arenaE[:, 0:8192].bitcast(F32))
    maskB, maskB_tok = make_maskB(L, C)
    wt_s = p.tok("ld_w")
    C.ident_tok = wt_s
    wt_g = p.tok("ld_wg")
    L.init_pool()
    pool = L.pool
    cfg_esb = arenaE[:, 0:16384]

    qlb = L.sb("pj_qlb", [64, TG], BF16)
    kvlb = L.sb("pj_kvlb", [32, TG], BF16)
    qtmp = L.sb("pj_qtmp", [128, 2 * TG], BF16)
    r1 = L.sb("pj_r1", [128, 2 * TG], F32)
    r2 = L.sb("pj_r2", [128, 2 * TG], F32)
    qout = L.sb("pj_qout", [128, 2 * TG], BF16)
    vstage = L.sb("pj_vst", [128, 2 * 260], BF16)
    for pr in range(2):
        for g in range(4):
            p.op("gpsimd", lambda h, pr=pr, g=g: h.memset(
                vstage[:, pr * 260 + g * 65 + 64: pr * 260 + g * 65 + 65], 1.0), inc=1)
    vst_const = p.tok("gpsimd")

    qlb_free = None
    kvlb_free = None
    qtmp_free = [None, None]
    r1_free = [None, None]
    r2_free = [None, None]
    qout_free = [None, None]
    vst_free = [None, None]

    def rope_chunk(psq_pair, dst_ap, tgc, par, extra_w=()):
        psq, iq, tokq = psq_pair
        qt_ap = qtmp[:, par * TG:(par + 1) * TG]
        w = [tokq] + ([qtmp_free[par]] if qtmp_free[par] else [])
        ev = p.op("vector", lambda h, psq=psq, qt_ap=qt_ap: h.tensor_copy(qt_ap, psq[:, :]),
                  inc=1, waits=w)
        psp, ip = pool.get()
        tp = p.op("tensor", lambda h, psp=psp, qt_ap=qt_ap: h.matmul(
            psp[:, :], Pp[:, :], qt_ap, start=True, stop=True), inc=1,
            waits=[ev, wt_g])
        qtmp_free[par] = tp
        r1a = r1[:, par * TG:(par + 1) * TG]
        r2a = r2[:, par * TG:(par + 1) * TG]
        ct = Ct[:, tgc * TG:(tgc + 1) * TG]
        st_ = St[:, tgc * TG:(tgc + 1) * TG]
        w1 = [tokq, wt_s, wt_g] + ([r1_free[par]] if r1_free[par] else []) + list(extra_w)
        ta = p.op("vector", lambda h, psq=psq, r1a=r1a, ct=ct: h.tensor_tensor(
            r1a[0:64, :], psq[0:64, :], ct, ALU.mult), inc=1, waits=w1)
        tb_ = p.op("vector", lambda h, psq=psq, r1a=r1a, ct=ct: h.tensor_tensor(
            r1a[64:128, :], psq[64:128, :], ct, ALU.mult), inc=1, waits=[ta])
        pool.done(iq, tb_)
        w2 = [tp] + ([r2_free[par]] if r2_free[par] else [])
        tc = p.op("vector", lambda h, psp=psp, r2a=r2a, st_=st_: h.tensor_tensor(
            r2a[0:64, :], psp[0:64, :], st_, ALU.mult), inc=1, waits=w2)
        td = p.op("vector", lambda h, psp=psp, r2a=r2a, st_=st_: h.tensor_tensor(
            r2a[64:128, :], psp[64:128, :], st_, ALU.mult), inc=1, waits=[tc])
        pool.done(ip, td)
        te = p.op("vector", lambda h, r1a=r1a, r2a=r2a, dst_ap=dst_ap: h.tensor_tensor(
            dst_ap, r1a, r2a, ALU.add), inc=1, waits=[tb_, td])
        r1_free[par] = te
        r2_free[par] = te
        return te

    for tg in range(NTG):
        C.load_x(tg)
        C.ln(tg, pool)
        psql, iql = pool.get()
        for fc in range(NCH):
            tql = p.op("tensor", lambda h, psql=psql, fc=fc: h.matmul(
                psql[0:64, :], Wqa[:, fc * QL:(fc + 1) * QL], C.xnslice(fc, tg),
                start=(fc == 0), stop=(fc == NCH - 1)),
                inc=(1 if fc == NCH - 1 else 0),
                waits=[C.xn_tok, wt_g] if fc == 0 else ())
        w = [tql] + ([qlb_free] if qlb_free else [])
        eql = p.op("vector", lambda h, psql=psql: h.tensor_copy(qlb[:, :], psql[0:64, :]),
                   inc=1, waits=w)
        pool.done(iql, eql)
        pskl, ikl = pool.get()
        for fc in range(NCH):
            tkl = p.op("tensor", lambda h, pskl=pskl, fc=fc: h.matmul(
                pskl[0:32, :], Wkva[:, fc * KVL:(fc + 1) * KVL], C.xnslice(fc, tg),
                start=(fc == 0), stop=(fc == NCH - 1)),
                inc=(1 if fc == NCH - 1 else 0))
        w = [tkl] + ([kvlb_free] if kvlb_free else [])
        ekl = p.op("vector", lambda h, pskl=pskl: h.tensor_copy(kvlb[:, :], pskl[0:32, :]),
                   inc=1, waits=w)
        pool.done(ikl, ekl)
        for c in range(NCH):
            par = c % 2
            psq, iq = pool.get()
            tq = p.op("tensor", lambda h, psq=psq, c=c: h.matmul(
                psq[:, :], Wqb[:, c * 128:(c + 1) * 128], qlb[:, :],
                start=True, stop=True), inc=1, waits=[eql])
            dst = qout[:, par * TG:(par + 1) * TG]
            wq = [qout_free[par]] if qout_free[par] else []
            te = rope_chunk((psq, iq, tq), dst, tg, par, extra_w=wq)
            qap2 = bass.AP(q_hbm.tensor, c * 128 * S + tg * TG, [[S, 128], [1, TG]])
            sq = p.op("sync", lambda h, qap2=qap2, dst=dst: h.dma_start(
                out=qap2, in_=dst), inc=16, sem=f"sp_q{par}", waits=[te])
            qout_free[par] = sq
            qlb_free = tq
        for c2 in range(2):
            par = c2 % 2
            psk, ik = pool.get()
            tk = p.op("tensor", lambda h, psk=psk, c2=c2: h.matmul(
                psk[:, :], Wk[:, c2 * 128:(c2 + 1) * 128], kvlb[:, :],
                start=True, stop=True), inc=1, waits=[ekl])
            dst = qout[:, par * TG:(par + 1) * TG]
            wq = [qout_free[par]] if qout_free[par] else []
            te = rope_chunk((psk, ik, tk), dst, tg, par, extra_w=wq)
            kap2 = bass.AP(k_hbm.tensor, c2 * 128 * S + tg * TG, [[S, 128], [1, TG]])
            sk = p.op("sync", lambda h, kap2=kap2, dst=dst: h.dma_start(
                out=kap2, in_=dst), inc=16, sem=f"sp_q{par}", waits=[te])
            qout_free[par] = sk
        for tt_ in range(4):
            par = tt_ % 2
            psv, iv = pool.get()
            tv = p.op("tensor", lambda h, psv=psv, tt_=tt_: h.matmul(
                psv[:, 0:256], kvlb[:, tt_ * 128:(tt_ + 1) * 128], Wv[:, :],
                start=True, stop=True), inc=1, waits=[ekl, wt_g])
            kt = tg * 4 + tt_
            wv = [tv, vst_const] + ([vst_free[par]] if vst_free[par] else [])
            for g in range(4):
                ev = p.op("vector", lambda h, psv=psv, g=g, par=par: h.tensor_copy(
                    vstage[:, par * 260 + g * 65: par * 260 + g * 65 + 64],
                    psv[:, g * 64:(g + 1) * 64]), inc=1,
                    waits=wv if g == 0 else ())
            pool.done(iv, ev)
            for g in range(4):
                vap2 = bass.AP(v_hbm.tensor, g * 128 * (32 * 65) + kt * 65,
                               [[32 * 65, 128], [1, 65]])
                sv = p.op("sync", lambda h, vap2=vap2, g=g, par=par: h.dma_start(
                    out=vap2, in_=vstage[:, par * 260 + g * 65: par * 260 + (g + 1) * 65]),
                    inc=16, sem=f"sp_v{par}", waits=[ev] if g == 0 else ())
            vst_free[par] = sv
            kvlb_free = tv

    spill_toks = [p.tok("sp_q0"), p.tok("sp_q1"), p.tok("sp_v0"), p.tok("sp_v1")]
    o_done = emit_attn_heads(L, C, pool, dict(
        n_heads=16, kv_of=lambda hh: hh // 4, scale=0.125,
        q_hbm=q_hbm, k_hbm=k_hbm, v_hbm=v_hbm, o_hbm=o_hbm, esb=cfg_esb,
        maskB=maskB, maskB_tok=maskB_tok, spill_toks=spill_toks))
    emit_wout(L, C, pool, dict(o_hbm=o_hbm, y_hbm=y_hbm, Wo_sb=Wout, Wo_tok=wt_g,
                               bout_sb=bout, bout_tok=wt_s, o_done=o_done, ot_ap=cfg_esb[:, 0:4096],
                               outb_ap=cfg_esb[:, 4096:8192].bitcast(F32)))
    return None


def build_attn_l0a():
    L = Launch("l0a")
    x_hbm = L.dram_in("x", [DM, S])
    y_hbm = L.dram_out("y", [DM, S])
    emit_l0a(L, x_hbm, y_hbm, declare_l0a(L))
    return L.done()


def l0a_host_inputs(inp):
    g = lambda k: np.asarray(inp["L0_attn_a_" + k], dtype=np.float32)
    Wkvb = g("Wkvb").reshape(32, 4, 64, 2)
    inv = 1.0 / (10000.0 ** (np.arange(0, 64, 2, dtype=np.float64) / 64))
    t = np.arange(S, dtype=np.float64)
    fr = np.outer(t, inv)
    Crep = np.repeat(np.cos(fr).T, 2, axis=0).astype(np.float32)
    Srep = np.repeat(np.sin(fr).T, 2, axis=0).astype(np.float32)
    sign = np.tile(np.array([-1.0, 1.0], dtype=np.float32), 32)[:, None]
    P = np.zeros((128, 128), dtype=np.float32)
    for i in range(128):
        j = (i // 64) * 64 + ((i % 64) ^ 1)
        P[i, j] = 1.0
    return {
        "Wqa": g("Wqa"), "Wqb": g("Wqb"), "Wkva": g("Wkva"),
        "Wkp": np.ascontiguousarray(Wkvb[:, :, :, 0].reshape(32, 256)),
        "Wvp": np.ascontiguousarray(Wkvb[:, :, :, 1].reshape(32, 256)),
        "Wout": g("Wout"),
        "boutl": g("bout").reshape(8, 128).T.copy(),
        "Pperm": P, "ropeC": Crep, "ropeS": Srep * sign,
        "maskb_in": host_maskB(anti=False),
        "c_identin": np.eye(128, dtype=np.float32),
    }



def declare_l1a(L):
    sv, L.prefix = L.prefix, ""
    D = {
        "WQ": L.dram_in("WQ", [DM, DM]),
        "WK": L.dram_in("WK", [DM, DM]),
        "WV": L.dram_in("WV", [DM, DM]),
        "WOt": L.dram_in("WOt", [DM, DM]),
        "bql": L.dram_in("bql", [128, 8]),
        "bkl": L.dram_in("bkl", [128, 8]),
        "bvrow": L.dram_in("bvrow", [1, DM]),
        "bol": L.dram_in("bol", [128, 8]),
    }
    L.prefix = sv
    return D


def emit_l1a(L, x_hbm, y_hbm, D):
    nc, p = L.nc, L.p
    WQh, WKh, WVh, WOh = D["WQ"], D["WK"], D["WV"], D["WOt"]
    bqh, bkh, bvh, boh = D["bql"], D["bkl"], D["bvrow"], D["bol"]

    q_hbm = L.dram_scratch("q_sp", [DM, S])
    k_hbm = L.dram_scratch("k_sp", [DM, S])
    v_hbm = L.dram_scratch("v_sp", [16, 128, (S // 128) * 65])
    o_hbm = L.dram_scratch("o_sp", [DM, S])
    arenaE = L.sb("arenaE", [128, 16384], BF16)

    WQ = L.sb("s_WQ", [128, NCH * DM], BF16)
    WK = L.sb("s_WK", [128, NCH * DM], BF16)
    WV = L.sb("s_WV", [128, NCH * DM], BF16)
    WO = L.sb("s_WO", [128, NCH * DM], BF16)
    bq = L.sb("s_bq", [128, 8], F32)
    bk = L.sb("s_bk", [128, 8], F32)
    bo = L.sb("s_bo", [128, 8], F32)
    bvr = L.sb("s_bvr", [1, DM], BF16)
    onesbf = L.sb("s_onesbf", [1, 128], BF16)

    load_slabs(p, "gpsimd", WQ, WQh, DM, NCH, sem="ld_wg")
    load_slabs(p, "gpsimd", WK, WKh, DM, NCH, sem="ld_wg")
    load_slabs(p, "gpsimd", WV, WVh, DM, NCH, sem="ld_wg")
    load_slabs(p, "gpsimd", WO, WOh, DM, NCH, sem="ld_wg")
    p.op("gpsimd", lambda h: h.dma_start(out=bvr[:], in_=bvh[:, :]), inc=16, sem="ld_wg")
    p.op("sync", lambda h: h.dma_start(out=bq[:], in_=bqh[:, :]), inc=16, sem="ld_w")
    p.op("sync", lambda h: h.dma_start(out=bk[:], in_=bkh[:, :]), inc=16, sem="ld_w")
    p.op("sync", lambda h: h.dma_start(out=bo[:], in_=boh[:, :]), inc=16, sem="ld_w")
    p.op("gpsimd", lambda h: h.memset(onesbf[:], 1.0), inc=1)
    ones_tok = p.tok("gpsimd")

    C = Common(L, x_hbm, xsq_ap=arenaE[:, 0:8192].bitcast(F32), rows_cols=2568)
    maskB, maskB_tok = make_maskB(L, C)
    wt_s = p.tok("ld_w")
    C.ident_tok = wt_s
    wt_g = p.tok("ld_wg")
    L.init_pool()
    pool = L.pool
    cfg_esb = arenaE[:, 0:16384]

    qst = L.sb("pj_qst", [128, 2 * TG], BF16)
    vstage = L.sb("pj_vst", [128, 2 * 520], BF16)
    for pr in range(2):
        for g in range(8):
            p.op("gpsimd", lambda h, pr=pr, g=g: h.memset(
                vstage[:, pr * 520 + g * 65 + 64: pr * 520 + g * 65 + 65], 1.0), inc=1)
    vst_const = p.tok("gpsimd")

    qst_free = [None, None]
    vst_free = [None, None]
    for tg in range(NTG):
        C.load_x(tg)
        C.ln(tg, pool)
        for (W_sb, b_sb, dsth, pfx) in ((WQ, bq, q_hbm, "q"), (WK, bk, k_hbm, "k")):
            for c in range(NCH):
                par = c % 2
                ps, ib = pool.get()
                for fc in range(NCH):
                    tq = p.op("tensor", lambda h, ps=ps, fc=fc, c=c, W_sb=W_sb: h.matmul(
                        ps[:, :], W_sb[:, fc * DM + c * 128: fc * DM + (c + 1) * 128],
                        C.xnslice(fc, tg), start=(fc == 0), stop=(fc == NCH - 1)),
                        inc=(1 if fc == NCH - 1 else 0),
                        waits=[C.xn_tok, wt_g] if fc == 0 else ())
                dst = qst[:, par * TG:(par + 1) * TG]
                w = [tq, wt_s] + ([qst_free[par]] if qst_free[par] else [])
                ev = p.op("vector", lambda h, ps=ps, dst=dst, c=c, b_sb=b_sb: h.tensor_scalar(
                    dst, ps[:, :], b_sb[:, c:c + 1], None, ALU.add), inc=1, waits=w)
                pool.done(ib, ev)
                qap2 = bass.AP(dsth.tensor, c * 128 * S + tg * TG, [[S, 128], [1, TG]])
                sq = p.op("sync", lambda h, qap2=qap2, dst=dst: h.dma_start(
                    out=qap2, in_=dst), inc=16, sem=f"sp_{pfx}{par}", waits=[ev])
                qst_free[par] = sq
        for tt_ in range(4):
            for half in range(2):
                par = (tt_ * 2 + half) % 2
                psv, iv = pool.get()
                for fc in range(NCH):
                    tv = p.op("tensor", lambda h, psv=psv, fc=fc, half=half, tt_=tt_: h.matmul(
                        psv[:, :], C.xnslice(fc, tg, width=128, off=tt_ * 128),
                        WV[:, fc * DM + half * 512: fc * DM + (half + 1) * 512],
                        start=(fc == 0), stop=False),
                        waits=[C.xn_tok, wt_g] if fc == 0 else ())
                tv = p.op("tensor", lambda h, psv=psv, half=half: h.matmul(
                    psv[:, :], onesbf[0:1, 0:128], bvr[0:1, half * 512:(half + 1) * 512],
                    start=False, stop=True), inc=1, waits=[ones_tok])
                kt = tg * 4 + tt_
                wv = [tv, vst_const] + ([vst_free[par]] if vst_free[par] else [])
                for gg in range(8):
                    ev = p.op("vector", lambda h, psv=psv, gg=gg, par=par: h.tensor_copy(
                        vstage[:, par * 520 + gg * 65: par * 520 + gg * 65 + 64],
                        psv[:, gg * 64:(gg + 1) * 64]), inc=1,
                        waits=wv if gg == 0 else ())
                pool.done(iv, ev)
                for gg in range(8):
                    g = half * 8 + gg
                    vap2 = bass.AP(v_hbm.tensor, g * 128 * (32 * 65) + kt * 65,
                                   [[32 * 65, 128], [1, 65]])
                    sv = p.op("sync", lambda h, vap2=vap2, gg=gg, par=par: h.dma_start(
                        out=vap2,
                        in_=vstage[:, par * 520 + gg * 65: par * 520 + (gg + 1) * 65]),
                        inc=16, sem=f"sp_v{par}", waits=[ev] if gg == 0 else ())
                vst_free[par] = sv

    spill_toks = [p.tok("sp_q0"), p.tok("sp_q1"), p.tok("sp_k0"), p.tok("sp_k1"),
                  p.tok("sp_v0"), p.tok("sp_v1")]
    o_done = emit_attn_heads(L, C, pool, dict(
        n_heads=16, kv_of=lambda hh: hh, scale=0.125,
        q_hbm=q_hbm, k_hbm=k_hbm, v_hbm=v_hbm, o_hbm=o_hbm, esb=cfg_esb,
        maskB=maskB, maskB_tok=maskB_tok, spill_toks=spill_toks))
    emit_wout(L, C, pool, dict(o_hbm=o_hbm, y_hbm=y_hbm, Wo_sb=WO, Wo_tok=wt_g,
                               bout_sb=bo, bout_tok=wt_s, o_done=o_done, ot_ap=cfg_esb[:, 0:4096],
                               outb_ap=cfg_esb[:, 4096:8192].bitcast(F32)))
    return None


def build_attn_l1a():
    L = Launch("l1a")
    x_hbm = L.dram_in("x", [DM, S])
    y_hbm = L.dram_out("y", [DM, S])
    emit_l1a(L, x_hbm, y_hbm, declare_l1a(L))
    return L.done()


def l1a_host_inputs(inp):
    g = lambda k: np.asarray(inp["L1_attn_a_" + k], dtype=np.float32)
    WQ = np.ascontiguousarray(g("W_q").transpose(1, 0, 2).reshape(DM, DM))
    WK = np.ascontiguousarray(g("W_k").transpose(1, 0, 2).reshape(DM, DM))
    WV = np.ascontiguousarray(g("W_v").transpose(1, 0, 2).reshape(DM, DM))
    return {
        "WQ": WQ, "WK": WK, "WV": WV,
        "WOt": np.ascontiguousarray(g("W_o").T),
        "bql": g("b_q").reshape(-1).reshape(8, 128).T.copy(),
        "bkl": g("b_k").reshape(-1).reshape(8, 128).T.copy(),
        "bvrow": g("b_v").reshape(1, DM).copy(),
        "bol": g("b_o").reshape(8, 128).T.copy(),
        "maskb_in": host_maskB(anti=False),
        "c_identin": np.eye(128, dtype=np.float32),
    }



def declare_l0b(L):
    sv, L.prefix = L.prefix, ""
    D = {
        "Wqkv": L.dram_in("Wqkv", [DM, 3 * DM]),
        "maskb_anti": L.dram_in("maskb_anti", [128, 4 * TG]),
    }
    L.prefix = sv
    return D


def emit_l0b(L, x_hbm, y_hbm, D):
    nc, p = L.nc, L.p
    Wh = D["Wqkv"]
    NT = S // 128

    q_hbm = L.dram_scratch("q_sp", [DM, S])
    k_hbm = L.dram_scratch("k_sp", [DM, S])
    v_hbm = L.dram_scratch("v_sp", [NT, 128, 1025])

    W = L.sb("s_W", [128, NCH * 3 * DM], BF16)
    load_slabs(p, "gpsimd", W, Wh, 3 * DM, NCH, sem="ld_wg")
    onesbf = L.sb("s_onesbf", [1, 128], BF16)
    onescolbf = L.sb("s_onescolbf", [128, 1], BF16)
    p.op("gpsimd", lambda h: h.memset(onesbf[:], 1.0), inc=1)
    p.op("gpsimd", lambda h: h.memset(onescolbf[:], 1.0), inc=1)

    C = Common(L, x_hbm)
    msb = L.sb("c_maskB", [128, 4 * TG], F32)
    maskB_tok = p.op("sync", lambda h: h.dma_start(out=msb[:], in_=D["maskb_anti"][:, :]),
                     inc=16, sem="ld_w")
    maskB = msb
    wt_s = p.tok("ld_w")
    C.ident_tok = wt_s
    wt_g = p.tok("ld_wg")
    scale_ap = C.constant(1.0 / 32.0)
    ones_tok = p.tok("gpsimd")
    L.init_pool()
    pool = L.pool

    qst = L.sb("pj_qst", [128, 2 * TG], BF16)
    vstage = L.sb("pj_vst", [128, 2 * 1025], BF16)
    for pr in range(2):
        p.op("gpsimd", lambda h, pr=pr: h.memset(
            vstage[:, pr * 1025 + 1024: pr * 1025 + 1025], 1.0), inc=1)
    vst_const = p.tok("gpsimd")

    qst_free = [None, None]
    vst_free = [None, None]
    for tg in range(NTG):
        C.load_x(tg)
        C.ln(tg, pool)
        for part, dsth, pfx in ((0, q_hbm, "q"), (1, k_hbm, "k")):
            for c in range(NCH):
                par = c % 2
                ps, ib = pool.get()
                for fc in range(NCH):
                    tq = p.op("tensor", lambda h, ps=ps, fc=fc, c=c, part=part: h.matmul(
                        ps[:, :],
                        W[:, fc * 3 * DM + part * DM + c * 128: fc * 3 * DM + part * DM + (c + 1) * 128],
                        C.xnslice(fc, tg), start=(fc == 0), stop=(fc == NCH - 1)),
                        inc=(1 if fc == NCH - 1 else 0),
                        waits=[C.xn_tok, wt_g] if fc == 0 else ())
                dst = qst[:, par * TG:(par + 1) * TG]
                w = [tq] + ([qst_free[par]] if qst_free[par] else [])
                ev = p.op("vector", lambda h, ps=ps, dst=dst: h.tensor_copy(dst, ps[:, :]),
                          inc=1, waits=w)
                pool.done(ib, ev)
                qap2 = bass.AP(dsth.tensor, c * 128 * S + tg * TG, [[S, 128], [1, TG]])
                sq = p.op("sync", lambda h, qap2=qap2, dst=dst: h.dma_start(
                    out=qap2, in_=dst), inc=16, sem=f"sp_{pfx}{par}", waits=[ev])
                qst_free[par] = sq
        for tt_ in range(4):
            par = tt_ % 2
            psv, iv = pool.get()
            psv2, iv2 = pool.get()
            evs = []
            for half in range(2):
                psx = psv if half == 0 else psv2
                for fc in range(NCH):
                    tv = p.op("tensor", lambda h, psx=psx, fc=fc, half=half, tt_=tt_: h.matmul(
                        psx[:, :], C.xnslice(fc, tg, width=128, off=tt_ * 128),
                        W[:, fc * 3 * DM + 2 * DM + half * 512: fc * 3 * DM + 2 * DM + (half + 1) * 512],
                        start=(fc == 0), stop=(fc == NCH - 1)),
                        inc=(1 if fc == NCH - 1 else 0),
                        waits=[C.xn_tok] if fc == 0 else ())
                w = [tv, vst_const] + ([vst_free[par]] if (half == 0 and vst_free[par]) else [])
                ev = p.op("vector", lambda h, psx=psx, half=half, par=par: h.tensor_copy(
                    vstage[:, par * 1025 + half * 512: par * 1025 + (half + 1) * 512],
                    psx[:, :]), inc=1, waits=w)
                evs.append(ev)
            pool.done(iv, evs[0])
            pool.done(iv2, evs[1])
            kt = tg * 4 + tt_
            vap2 = bass.AP(v_hbm.tensor, kt * 128 * 1025, [[1025, 128], [1, 1025]])
            sv = p.op("sync", lambda h, vap2=vap2, par=par: h.dma_start(
                out=vap2, in_=vstage[:, par * 1025:(par + 1) * 1025]), inc=16,
                sem=f"sp_v{par}", waits=[evs[0], evs[1]])
            vst_free[par] = sv

    spill = [p.tok("sp_q0"), p.tok("sp_q1"), p.tok("sp_k0"), p.tok("sp_k1"),
             p.tok("sp_v0"), p.tok("sp_v1")]

    qg_b = [L.sb("at_q0", [128, NCH * TG], BF16), L.sb("at_q1", [128, NCH * TG], BF16)]
    kt_b = [L.sb("at_k0", [128, NCH * 128], BF16), L.sb("at_k1", [128, NCH * 128], BF16)]
    vt_b = [L.sb("at_v0", [128, 1025], BF16), L.sb("at_v1", [128, 1025], BF16)]
    esb = L.sb("at_e", [128, NT * TG], BF16)
    negm = L.sb("at_negm", [1, S], BF16)
    mcols = L.sb("at_mcols", [128, NT], F32)
    scol = L.sb("at_scol", [128, 1], F32)
    rrow = L.sb("at_rrow", [1, TG], F32)
    rb_sb = L.sb("at_rb", [128, TG], F32)
    outb = L.sb("at_outb", [128, 2 * TG], F32)

    qb_free = [None, None]
    kb_free = [None, None]
    vb_free = [None, None]
    scol_free = None
    mcols_free = None

    def kload(kt, src, dst_b, free, pfx):
        par = kt % 2
        ap = bass.AP(src.tensor, kt * 128, [[128 * S, NCH], [S, 128], [1, 128]])
        tok = None
        for fc in range(NCH):
            a2 = bass.AP(src.tensor, fc * 128 * S + kt * 128, [[S, 128], [1, 128]])
            w = list(spill) + ([free[par]] if free[par] else [])
            tok = p.op("sync", lambda h, a2=a2, fc=fc, par=par: h.dma_start(
                out=dst_b[par][:, fc * 128:(fc + 1) * 128], in_=a2), inc=16,
                sem=f"ld_{pfx}{par}", waits=w if fc == 0 else ())
        return tok, par

    red_tok = {}
    for kg in range(NCH):
        par = kg % 2
        ktoks = []
        for fc in range(NCH):
            a2 = bass.AP(k_hbm.tensor, fc * 128 * S + kg * TG, [[S, 128], [1, TG]])
            w = list(spill) + ([qb_free[par]] if qb_free[par] else [])
            ktok = p.op("sync", lambda h, a2=a2, fc=fc, par=par: h.dma_start(
                out=qg_b[par][:, fc * TG:(fc + 1) * TG], in_=a2), inc=16,
                sem=f"ld_kg{par}", waits=w if fc == 0 else ())
        for qt in range(4 * kg + 4):
            qpar = qt % 2
            qtoks = []
            for fc in range(NCH):
                a2 = bass.AP(q_hbm.tensor, fc * 128 * S + qt * 128, [[S, 128], [1, 128]])
                w = list(spill) + ([kb_free[qpar]] if kb_free[qpar] else [])
                qtok = p.op("sync", lambda h, a2=a2, fc=fc, qpar=qpar: h.dma_start(
                    out=kt_b[qpar][:, fc * 128:(fc + 1) * 128], in_=a2), inc=16,
                    sem=f"ld_qt{qpar}", waits=w if fc == 0 else ())
            ps, ib = pool.get()
            for fc in range(NCH):
                t1 = p.op("tensor", lambda h, ps=ps, fc=fc, par=par, qpar=qpar: h.matmul(
                    ps[:, :], kt_b[qpar][:, fc * 128:(fc + 1) * 128],
                    qg_b[par][:, fc * TG:(fc + 1) * TG],
                    start=(fc == 0), stop=(fc == NCH - 1)),
                    inc=(1 if fc == NCH - 1 else 0),
                    waits=[qtok, ktok] if fc == 0 else ())
            kb_free[qpar] = t1
            w = [t1] + ([scol_free] if scol_free else [])
            rtok = p.op("vector", lambda h, ps=ps: h.tensor_reduce(
                scol[:, :], ps[:, :], axis=AX, op=ALU.max), inc=1, waits=w)
            pool.done(ib, rtok)
            if kg == qt // 4:
                w = [rtok] + ([mcols_free] if mcols_free else [])
                mtok = p.op("vector", lambda h, qt=qt: h.tensor_copy(
                    mcols[:, qt:qt + 1], scol[:, :]), inc=1, waits=w)
            else:
                mtok = p.op("vector", lambda h, qt=qt: h.tensor_tensor(
                    mcols[:, qt:qt + 1], mcols[:, qt:qt + 1], scol[:, :], ALU.max),
                    inc=1, waits=[rtok])
            scol_free = mtok
            red_tok[qt] = mtok
        qb_free[par] = t1
    ngm_toks = [None] * NTG
    for qg in range(NTG):
        psT, iT = pool.get()
        for j in range(4):
            tT = p.op("tensor", lambda h, psT=psT, j=j, qg=qg: h.transpose(
                psT[0:1, j * 128:(j + 1) * 128], mcols[:, 4 * qg + j:4 * qg + j + 1],
                C.ident[:]), inc=1, waits=[red_tok[4 * qg + j], C.ident_tok])
        mcols_free = tT
        ngm = p.op("vector", lambda h, psT=psT, qg=qg: h.tensor_scalar(
            negm[0:1, qg * TG:(qg + 1) * TG], psT[0:1, 0:TG], -1.0, None, ALU.mult),
            inc=1, waits=[tT])
        pool.done(iT, ngm)
        ngm_toks[qg] = ngm

    qb_free = [None, None]
    kb_free = [None, None]
    e_free = None
    out_free = [None, None]
    rb_free = None
    rrow_free = None
    for qg in range(NTG):
        par = qg % 2
        for fc in range(NCH):
            a2 = bass.AP(q_hbm.tensor, fc * 128 * S + qg * TG, [[S, 128], [1, TG]])
            w = [ngm_toks[qg]] + ([qb_free[par]] if qb_free[par] else [])
            qtok = p.op("sync", lambda h, a2=a2, fc=fc, par=par: h.dma_start(
                out=qg_b[par][:, fc * TG:(fc + 1) * TG], in_=a2), inc=16,
                sem=f"ld_qg{par}", waits=w if fc == 0 else ())
        kts = list(range(4 * qg, NT))
        psl, il = pool.get(hold=True)
        for ki, kt in enumerate(kts):
            ktok, kpar = kload(kt, k_hbm, kt_b, kb_free, "kb")
            ps, ib = pool.get()
            for fc in range(NCH):
                m1 = p.op("tensor", lambda h, ps=ps, fc=fc, kpar=kpar, par=par: h.matmul(
                    ps[:, :], kt_b[kpar][:, fc * 128:(fc + 1) * 128],
                    qg_b[par][:, fc * TG:(fc + 1) * TG], start=(fc == 0), stop=False),
                    waits=[ktok, qtok] if fc == 0 else ())
            m2 = p.op("tensor", lambda h, ps=ps, qg=qg: h.matmul(
                ps[:, :], onesbf[0:1, 0:128], negm[0:1, qg * TG:(qg + 1) * TG],
                start=False, stop=True), inc=1, waits=[ones_tok])
            kb_free[kpar] = m2
            if kt < 4 * qg + 4:
                j = kt - 4 * qg
                msk = p.op("vector", lambda h, ps=ps, j=j: h.tensor_tensor(
                    ps[:, :], ps[:, :], maskB[:, j * TG:(j + 1) * TG], ALU.add),
                    inc=1, waits=[m2, maskB_tok])
                ew = msk
            else:
                ew = m2
            wE = [ew, C.const_tok] + ([e_free] if (ki == 0 and e_free) else [])
            ex = p.op("scalar", lambda h, ps=ps, ki=ki: h.activation(
                esb[:, ki * TG:(ki + 1) * TG], ps[:, :], AF.Exp, scale=scale_ap),
                inc=1, waits=wE)
            pool.done(ib, ex)
            tl = p.op("tensor", lambda h, psl=psl, ki=ki, last=(ki == len(kts) - 1): h.matmul(
                psl[0:1, :], onescolbf[:, :], esb[:, ki * TG:(ki + 1) * TG],
                start=(ki == 0), stop=last), inc=(1 if ki == len(kts) - 1 else 0),
                waits=[ex])
        wR = [tl] + ([rrow_free] if rrow_free else [])
        rc = p.op("vector", lambda h, psl=psl: h.reciprocal(rrow[0:1, :], psl[0:1, :]),
                  inc=1, waits=wR)
        pool.done(il, rc)
        psr, ir = pool.get()
        tb = p.op("tensor", lambda h, psr=psr: h.matmul(
            psr[:, :], C.ones_row[:, :], rrow[0:1, :], start=True, stop=True),
            inc=1, waits=[rc])
        rrow_free = tb
        wC = [tb] + ([rb_free] if rb_free else [])
        c1 = p.op("vector", lambda h, psr=psr: h.tensor_copy(rb_sb[:, :], psr[:, :]),
                  inc=1, waits=wC)
        pool.done(ir, c1)
        C.load_x(qg)
        ops = []
        for dc in range(NCH):
            po, io = pool.get(hold=True)
            ops.append((po, io))
        vm = None
        for ki, kt in enumerate(kts):
            vpar = kt % 2
            vap2 = bass.AP(v_hbm.tensor, kt * 128 * 1025, [[1025, 128], [1, 1025]])
            w = list(spill) + ([vb_free[vpar]] if vb_free[vpar] else [])
            vtok = p.op("sync", lambda h, vap2=vap2, vpar=vpar: h.dma_start(
                out=vt_b[vpar][:, :], in_=vap2), inc=16, sem=f"ld_v{vpar}", waits=w)
            for dc in range(NCH):
                po, io = ops[dc]
                vm = p.op("tensor", lambda h, po=po, vpar=vpar, dc=dc, ki=ki,
                          st=(ki == 0), sp=(ki == len(kts) - 1): h.matmul(
                    po[:, :], vt_b[vpar][:, dc * 128:(dc + 1) * 128],
                    esb[:, ki * TG:(ki + 1) * TG], start=st, stop=sp),
                    inc=(1 if dc == NCH - 1 else 0), waits=[vtok] if dc == 0 else ())
            vb_free[vpar] = vm
        e_free = vm
        for dc in range(NCH):
            po, io = ops[dc]
            opar = dc % 2
            ob = outb[:, opar * TG:(opar + 1) * TG]
            w = [vm, c1, C.x_tok] + ([out_free[opar]] if out_free[opar] else [])
            e1 = p.op("vector", lambda h, po=po, ob=ob: h.tensor_tensor(
                ob, po[:, :], rb_sb[:, :], ALU.mult), inc=1, waits=w)
            pool.done(io, e1)
            e2 = p.op("vector", lambda h, ob=ob, dc=dc: h.tensor_tensor(
                ob, ob, C.xslice(dc), ALU.add), inc=1, waits=[e1])
            rb_free = e1
            yap = bass.AP(y_hbm.tensor, dc * 128 * S + qg * TG, [[S, 128], [1, TG]])
            sT = p.op("sync", lambda h, yap=yap, ob=ob: h.dma_start(out=yap, in_=ob),
                      inc=16, sem=f"st{opar}", waits=[e2])
            out_free[opar] = sT
        C.xt_free = [e2]
        qb_free[par] = m1
    return None


def build_attn_l0b():
    L = Launch("l0b")
    x_hbm = L.dram_in("x", [DM, S])
    y_hbm = L.dram_out("y", [DM, S])
    emit_l0b(L, x_hbm, y_hbm, declare_l0b(L))
    L.p.wait("sync", L.p.tok("st0"))
    L.p.wait("sync", L.p.tok("st1"))
    return L.done()


def l0b_host_inputs(inp):
    return {
        "Wqkv": np.asarray(inp["L0_attn_b_W"], dtype=np.float32),
        "maskb_anti": host_maskB(anti=True),
        "c_identin": np.eye(128, dtype=np.float32),
    }



def declare_ffn1(L):
    sv, L.prefix = L.prefix, ""
    D = {
        "W1": L.dram_in("W1", [8, DM, 4096]),
        "W2": L.dram_in("W2", [8, 4096, DM]),
        "rWT1": L.dram_in("rWT1", [DM, 8]),
        "gaml": L.dram_in("gaml", [128, 8]),
        "betl": L.dram_in("betl", [128, 8]),
        "sels1": L.dram_in("sels1", [8, 1024]),
    }
    L.prefix = sv
    return D


def emit_ffn1(L, x_hbm, y_hbm, D):
    nc, p = L.nc, L.p
    H1 = 4096
    NHC1 = H1 // 128
    W1h, W2h, rWTh = D["W1"], D["W2"], D["rWT1"]
    gamh, beth, selsh = D["gaml"], D["betl"], D["sels1"]
    xn_sp = L.dram_scratch("xn_sp", [DM, S])

    rWT = L.sb("s_rWT", [128, NCH * 8], F32)
    rGT = L.sb("s_rGT", [128, NCH * 8], F32)
    gam = L.sb("s_gam", [128, 8], F32)
    bet = L.sb("s_bet", [128, 8], F32)
    sels = L.sb("s_sels", [8, 1024], BF16)
    colsg = L.sb("s_colsg", [8, 1], F32)
    cbeta = L.sb("s_cbeta", [8, 1], F32)
    W1 = L.sb("s_W1", [128, NCH * H1], BF16)
    W2 = L.sb("s_W2", [128, NHC1 * DM], BF16)
    arenaA = L.sb("arenaA", [128, 16384], BF16)
    arenaB = L.sb("arenaB", [128, 4096], BF16)

    load_slabs(p, "sync", rWT, rWTh, 8, NCH)
    p.op("sync", lambda h: h.dma_start(out=gam[:], in_=gamh[:, :]), inc=16, sem="ld_w")
    p.op("sync", lambda h: h.dma_start(out=bet[:], in_=beth[:, :]), inc=16, sem="ld_w")
    p.op("gpsimd", lambda h: h.dma_start(out=sels[:], in_=selsh[:, :]), inc=16, sem="ld_wg")
    wt_g = p.tok("ld_wg")

    C = Common(L, x_hbm,
               xt_ap=arenaA[:, 0:8192].bitcast(F32),
               xsq_ap=arenaA[:, 8192:16384].bitcast(F32))
    wt_s = p.tok("ld_w")
    C.ident_tok = wt_s
    L.init_pool()
    pool = L.pool

    rg_toks = []
    for fc in range(NCH):
        t = p.op("vector", lambda h, fc=fc: h.tensor_scalar(
            rGT[:, fc * 8:(fc + 1) * 8], rWT[:, fc * 8:(fc + 1) * 8],
            gam[:, fc:fc + 1], None, ALU.mult), inc=1, waits=[wt_s])
        rg_toks.append(t)
    psK, iK = pool.get()
    for fc in range(NCH):
        tk = p.op("tensor", lambda h, psK=psK, fc=fc: h.matmul(
            psK[0:1, 0:8], C.ones_col[:], rGT[:, fc * 8:(fc + 1) * 8],
            start=(fc == 0), stop=(fc == NCH - 1)),
            inc=(1 if fc == NCH - 1 else 0),
            waits=[rg_toks[-1], C.const_tok] if fc == 0 else ())
    csr = C.rows[0:1, 2560:2568]
    e1 = p.op("vector", lambda h: h.tensor_copy(csr, psK[0:1, 0:8]), inc=1, waits=[tk])
    pool.done(iK, e1)
    psK2, iK2 = pool.get()
    tk2 = p.op("tensor", lambda h, psK2=psK2: h.transpose(
        psK2[0:8, 0:1], csr, C.ident[0:1, 0:1]), inc=1, waits=[e1, C.ident_tok])
    e2c = p.op("vector", lambda h, psK2=psK2: h.tensor_copy(colsg[:], psK2[0:8, 0:1]),
               inc=1, waits=[tk2])
    pool.done(iK2, e2c)
    rb_toks = []
    for fc in range(NCH):
        t = p.op("vector", lambda h, fc=fc: h.tensor_scalar(
            C.xsq[:, fc * 8:fc * 8 + 8], rWT[:, fc * 8:(fc + 1) * 8],
            bet[:, fc:fc + 1], None, ALU.mult), inc=1, waits=[wt_s, e2c])
        rb_toks.append(t)
    psK3, iK3 = pool.get()
    for fc in range(NCH):
        tk3 = p.op("tensor", lambda h, psK3=psK3, fc=fc: h.matmul(
            psK3[0:1, 0:8], C.ones_col[:], C.xsq[:, fc * 8:fc * 8 + 8],
            start=(fc == 0), stop=(fc == NCH - 1)),
            inc=(1 if fc == NCH - 1 else 0),
            waits=[rb_toks[-1]] if fc == 0 else ())
    e3 = p.op("vector", lambda h, psK3=psK3: h.tensor_copy(csr, psK3[0:1, 0:8]),
              inc=1, waits=[tk3])
    pool.done(iK3, e3)
    psK4, iK4 = pool.get()
    tk4 = p.op("tensor", lambda h, psK4=psK4: h.transpose(
        psK4[0:8, 0:1], csr, C.ident[0:1, 0:1]), inc=1, waits=[e3])
    e4 = p.op("vector", lambda h, psK4=psK4: h.tensor_copy(cbeta[:], psK4[0:8, 0:1]),
              inc=1, waits=[tk4])
    pool.done(iK4, e4)
    pre_tok = e4
    C.xsq_free = tk3

    zview = arenaB[:, 0:4096].bitcast(F32)
    z = zview[0:8, 0:TG]
    relu = zview[0:8, TG:2 * TG]
    ach = zview[0:8, 2 * TG:3 * TG]
    comb_all = L.sb("comb_all", [8, S], BF16)
    ztok = L.sb("ztok", [128, 4 * 8], F32)
    vcols = L.sb("vcols", [128, 16], F32)
    eqb = L.sb("eqb", [128, 4 * 8], F32)
    xnt = C.xn
    combe = L.sb("combe", [128, TG], BF16)
    htld = arenaA[:, 0:16384]
    ytile = arenaB[:, 0:2048].bitcast(F32)
    outb = arenaB[:, 2048:4096].bitcast(F32)
    inv15 = C.constant(1.0 / 1.5, parts=128)

    xn_st = [None] * NTG
    comb_toks = [None] * NTG
    ycp_toks = {}
    for tg in range(NTG):
        C.load_x(tg)
        toks = C.ln(tg, pool, affine=(gam, bet, wt_s))
        mu = C.rows[0:1, 0:512]
        rstd = C.rows[0:1, 1536:2048]
        for fc in range(NCH):
            ap = bass.AP(xn_sp.tensor, fc * 128 * S + tg * TG, [[S, 128], [1, TG]])
            st_x = p.op("sync", lambda h, ap=ap, fc=fc: h.dma_start(
                out=ap, in_=C.xnslice(fc, tg)), inc=16, sem="sp_xn",
                waits=[toks[fc]])
        xn_st[tg] = st_x
        C.xn_free = st_x
        for fc in range(NCH):
            scr = C.xsq[:, fc * TG:(fc + 1) * TG]
            tYI = p.op("vector", lambda h, scr=scr, fc=fc: h.tensor_tensor(
                scr, scr, C.xslice(fc), ALU.add), inc=1, waits=[toks[fc]])
            ap = bass.AP(y_hbm.tensor, fc * 128 * S + tg * TG, [[S, 128], [1, TG]])
            yc = p.op("sync", lambda h, ap=ap, scr=scr: h.dma_start(
                out=ap, in_=scr), inc=16, sem="st_y0", waits=[tYI])
            ycp_toks[(tg, fc)] = yc
        C.xsq_free = yc
        psR, iR = pool.get()
        for fc in range(NCH):
            tR = p.op("tensor", lambda h, psR=psR, fc=fc: h.matmul(
                psR[0:8, :], rGT[:, fc * 8:(fc + 1) * 8], C.xslice(fc),
                start=(fc == 0), stop=(fc == NCH - 1)),
                inc=(1 if fc == NCH - 1 else 0),
                waits=[C.x_tok, rg_toks[-1]] if fc == 0 else ())
        psM, iM = pool.get()
        tM = p.op("tensor", lambda h, psM=psM: h.matmul(
            psM[0:8, :], C.ones_row[0:1, 0:8], mu, start=True, stop=True), inc=1,
            waits=[C.mu_tok])
        psS, iS = pool.get()
        tS = p.op("tensor", lambda h, psS=psS: h.matmul(
            psS[0:8, :], C.ones_row[0:1, 0:8], rstd, start=True, stop=True), inc=1,
            waits=[C.mu_tok])
        zt1 = p.op("vector", lambda h, psM=psM: h.tensor_scalar(
            ach, psM[0:8, :], colsg[:], None, ALU.mult), inc=1, waits=[tM, pre_tok])
        pool.done(iM, zt1)
        zt2 = p.op("vector", lambda h, psR=psR: h.tensor_tensor(
            relu, psR[0:8, :], ach, ALU.subtract), inc=1, waits=[tR, zt1])
        pool.done(iR, zt2)
        zt3 = p.op("vector", lambda h, psS=psS: h.tensor_tensor(
            ach, relu, psS[0:8, :], ALU.mult), inc=1, waits=[tS, zt2])
        pool.done(iS, zt3)
        zt4 = p.op("vector", lambda h: h.tensor_scalar(z, ach, cbeta[:], None,
                                                       ALU.add), inc=1, waits=[zt3])
        ctoks = []
        for tt in range(4):
            psT, iT = pool.get()
            tT = p.op("tensor", lambda h, psT=psT, tt=tt: h.transpose(
                psT[:, 0:8], z[:, tt * 128:(tt + 1) * 128], C.ident[0:8, 0:8]),
                inc=1, waits=[zt4, C.ident_tok])
            eT = p.op("vector", lambda h, psT=psT, tt=tt: h.tensor_copy(
                ztok[:, tt * 8:(tt + 1) * 8], psT[:, 0:8]), inc=1, waits=[tT])
            pool.done(iT, eT)
            zt_ap = ztok[:, tt * 8:(tt + 1) * 8]
            v1 = vcols[:, tt:tt + 1]
            v2 = vcols[:, 8 + tt:9 + tt]
            nv1 = vcols[:, 4 + tt:5 + tt]
            eq = eqb[:, tt * 8:(tt + 1) * 8]
            a = p.op("vector", lambda h, zt_ap=zt_ap, v1=v1: h.tensor_reduce(
                v1, zt_ap, axis=AX, op=ALU.max), inc=1, waits=[eT])
            b = p.op("vector", lambda h, zt_ap=zt_ap, v1=v1, eq=eq: h.tensor_scalar(
                eq, zt_ap, v1, None, ALU.is_equal), inc=1, waits=[a])
            c = p.op("vector", lambda h, zt_ap=zt_ap, eq=eq: h.scalar_tensor_tensor(
                eq, eq, -1e30, zt_ap, ALU.mult, ALU.add), inc=1, waits=[b])
            d = p.op("vector", lambda h, eq=eq, v2=v2: h.tensor_reduce(
                v2, eq, axis=AX, op=ALU.max), inc=1, waits=[c])
            nb = p.op("vector", lambda h, v1=v1, nv1=nv1: h.tensor_scalar(
                nv1, v1, -1.0 / 1.5, None, ALU.mult), inc=1, waits=[a])
            pe = p.op("scalar", lambda h, zt_ap=zt_ap, eq=eq, nv1=nv1: h.activation(
                eq, zt_ap, AF.Exp, bias=nv1, scale=inv15), inc=1,
                waits=[d, nb, C.const_tok])
            ps_ = p.op("vector", lambda h, eq=eq, tt=tt: h.tensor_reduce(
                vcols[:, 12 + tt:13 + tt], eq, axis=AX, op=ALU.add), inc=1,
                waits=[pe])
            rr = p.op("vector", lambda h, tt=tt: h.reciprocal(
                vcols[:, 12 + tt:13 + tt], vcols[:, 12 + tt:13 + tt]), inc=1,
                waits=[ps_])
            g1 = p.op("vector", lambda h, eq=eq, tt=tt: h.tensor_scalar(
                eq, eq, vcols[:, 12 + tt:13 + tt], None, ALU.mult), inc=1,
                waits=[rr])
            g2 = p.op("vector", lambda h, zt_ap=zt_ap, v2=v2: h.tensor_scalar(
                zt_ap, zt_ap, v2, None, ALU.is_ge), inc=1, waits=[g1])
            g3 = p.op("vector", lambda h, eq=eq, zt_ap=zt_ap: h.tensor_tensor(
                eq, eq, zt_ap, ALU.mult), inc=1, waits=[g2])
            psU, iU = pool.get()
            tU = p.op("tensor", lambda h, psU=psU, eq=eq, tt=tt: h.transpose(
                psU[0:8, 0:128], eq, C.ident[:]), inc=1, waits=[g3])
            cT = p.op("vector", lambda h, psU=psU, tg=tg, tt=tt: h.tensor_copy(
                comb_all[0:8, tg * TG + tt * 128: tg * TG + (tt + 1) * 128],
                psU[0:8, 0:128]), inc=1, waits=[tU])
            pool.done(iU, cT)
            ctoks.append(cT)
        comb_toks[tg] = ctoks[-1]
        C.xt_free = C.xt_free + [yc, tR]

    xn_done = [p.tok("sp_xn"), p.tok("st_y0"), p.tok("tensor"),
               p.tok("vector"), p.tok("scalar")]

    xnt_free = [None, None]
    combe_free = [None, None]
    htld_free = None
    yld_free = [None, None]
    out_free = [None, None]
    ystore = {}
    w1_free = None
    w2_free = None
    for e in range(8):
        w = list(xn_done) + ([w1_free] if w1_free else [])
        w1tok = load_slabs(p, "gpsimd", W1, W1h, H1, NCH, toff=e * DM, sem="ld_w1",
                           waits=w)
        w = [w2_free] if w2_free else []
        w2tok = load_slabs(p, "gpsimd", W2, W2h, DM, NHC1, toff=e * H1, sem="ld_w2",
                           waits=w)
        for tg in range(NTG):
            par = tg % 2
            for fc in range(NCH):
                ap = bass.AP(xn_sp.tensor, fc * 128 * S + tg * TG, [[S, 128], [1, TG]])
                wx = list(xn_done) + ([xnt_free[0]] if xnt_free[0] else [])
                xtok = p.op("sync", lambda h, ap=ap, fc=fc: h.dma_start(
                    out=xnt[:, fc * TG:(fc + 1) * TG],
                    in_=ap), inc=16, sem="ld_xn0",
                    waits=wx if fc == 0 else ())
            psE, iE = pool.get()
            tE = p.op("tensor", lambda h, psE=psE, e=e, tg=tg: h.matmul(
                psE[:, :], sels[:, e * 128:(e + 1) * 128],
                comb_all[0:8, tg * TG:(tg + 1) * TG], start=True, stop=True),
                inc=1, waits=[comb_toks[tg], wt_g])
            wc = [tE] + ([combe_free[0]] if combe_free[0] else [])
            cbe = p.op("vector", lambda h, psE=psE: h.tensor_copy(
                combe[:, :], psE[:, :]), inc=1, waits=wc)
            pool.done(iE, cbe)
            for hc in range(NHC1):
                psQ, iQ = pool.get()
                for fc in range(NCH):
                    tQ = p.op("tensor", lambda h, psQ=psQ, fc=fc, hc=hc: h.matmul(
                        psQ[:, :], W1[:, fc * H1 + hc * 128: fc * H1 + (hc + 1) * 128],
                        xnt[:, fc * TG:(fc + 1) * TG],
                        start=(fc == 0), stop=(fc == NCH - 1)),
                        inc=(1 if fc == NCH - 1 else 0),
                        waits=[xtok, w1tok] if fc == 0 else ())
                dst = htld[:, hc * TG:(hc + 1) * TG]
                w = [tQ] + ([htld_free] if (hc == 0 and htld_free) else [])
                eQ = p.op("scalar", lambda h, psQ=psQ, dst=dst: h.activation(
                    dst, psQ[:, :], AF.Silu), inc=1, waits=w)
                pool.done(iQ, eQ)
                mQ = p.op("vector", lambda h, dst=dst: h.tensor_tensor(
                    dst, dst, combe[:, :], ALU.mult), inc=1,
                    waits=[eQ, cbe])
            ht_all = mQ
            for dc in range(NCH):
                ypar = dc % 2
                yap = bass.AP(y_hbm.tensor, dc * 128 * S + tg * TG, [[S, 128], [1, TG]])
                wly = [ycp_toks[(tg, dc)]] + ([yld_free[ypar]] if yld_free[ypar] else [])
                if (tg, dc) in ystore:
                    wly.append(ystore[(tg, dc)])
                yl = p.op("sync", lambda h, yap=yap, ypar=ypar: h.dma_start(
                    out=ytile[:, ypar * TG:(ypar + 1) * TG], in_=yap), inc=16,
                    sem=f"ld_y{ypar}", waits=wly)
                psY, iY = pool.get()
                for hc in range(NHC1):
                    tY = p.op("tensor", lambda h, psY=psY, hc=hc, dc=dc: h.matmul(
                        psY[:, :], W2[:, hc * DM + dc * 128: hc * DM + (dc + 1) * 128],
                        htld[:, hc * TG:(hc + 1) * TG], start=(hc == 0),
                        stop=(hc == NHC1 - 1)), inc=(1 if hc == NHC1 - 1 else 0),
                        waits=[ht_all, w2tok] if hc == 0 else ())
                ob = outb[:, ypar * TG:(ypar + 1) * TG]
                w = [tY, yl] + ([out_free[ypar]] if out_free[ypar] else [])
                eY = p.op("vector", lambda h, psY=psY, ob=ob, ypar=ypar: h.tensor_tensor(
                    ob, psY[:, :], ytile[:, ypar * TG:(ypar + 1) * TG], ALU.add),
                    inc=1, waits=w)
                pool.done(iY, eY)
                yld_free[ypar] = eY
                sT = p.op("sync", lambda h, yap=yap, ob=ob: h.dma_start(
                    out=yap, in_=ob), inc=16, sem=f"st_y{ypar + 1}", waits=[eY])
                out_free[ypar] = sT
                ystore[(tg, dc)] = sT
            htld_free = tY
            xnt_free[0] = tQ
            combe_free[0] = mQ
        w1_free = tQ
        w2_free = tY
    return None


def build_ffn1():
    L = Launch("ffn1")
    x_hbm = L.dram_in("x", [DM, S])
    y_hbm = L.dram_out("y", [DM, S])
    emit_ffn1(L, x_hbm, y_hbm, declare_ffn1(L))
    L.p.wait("sync", L.p.tok("st_y1"))
    L.p.wait("sync", L.p.tok("st_y2"))
    return L.done()


def ffn1_host_inputs(inp):
    g = lambda k: np.asarray(inp["L1_ffn_" + k], dtype=np.float32)
    sels = np.zeros((8, 1024), dtype=np.float32)
    for e in range(8):
        sels[e, e * 128:(e + 1) * 128] = 1.0
    return {
        "W1": g("W1"), "W2": g("W2"),
        "rWT1": np.ascontiguousarray(g("router_w").T),
        "gaml": g("gamma").reshape(8, 128).T.copy(),
        "betl": g("beta").reshape(8, 128).T.copy(),
        "sels1": sels,
        "c_identin": np.eye(128, dtype=np.float32),
    }



def declare_l1b(L):
    sv, L.prefix = L.prefix, ""
    D = {
        "Wq": L.dram_in("Wq", [DM, DM]),
        "Wk": L.dram_in("Wk", [DM, DM]),
        "Wv": L.dram_in("Wv", [DM, 2048]),
        "Wg": L.dram_in("Wg", [DM, 2048]),
        "Wo": L.dram_in("Wo", [2048, DM]),
        "gnl": L.dram_in("gnl", [128, 16]),
        "ropeC2": L.dram_in("ropeC2", [128, S]),
        "ropeS2": L.dram_in("ropeS2", [128, S]),
        "dmaskT": L.dram_in("dmaskT", [64, 4 * 64]),
        "cqcol": L.dram_in("cqcol", [64, 4]),
        "ckcol": L.dram_in("ckcol", [64, 4]),
    }
    L.prefix = sv
    return D


def emit_l1b(L, x_hbm, y_hbm, D):
    nc, p = L.nc, L.p
    DV = 2048
    Wqh, Wkh, Wvh, Wgh, Woh = D["Wq"], D["Wk"], D["Wv"], D["Wg"], D["Wo"]
    gnlh, Cth, Sth = D["gnl"], D["ropeC2"], D["ropeS2"]
    dmh, cqh, ckh = D["dmaskT"], D["cqcol"], D["ckcol"]

    q_hbm = L.dram_scratch("q_sp", [DM, S])
    k_hbm = L.dram_scratch("k_sp", [DM, S])
    v_hbm = L.dram_scratch("v_sp", [4, 32, 128, 512])
    g_hbm = L.dram_scratch("g_sp", [4, 32, 128, 512])
    o2_hbm = L.dram_scratch("o2_sp", [DV, S])

    GAM = [1.0 - 2.0 ** (-5.0 - hh) for hh in range(4)]

    Wq = L.sb("s_Wq", [128, NCH * DM], BF16)
    Wk = L.sb("s_Wk", [128, NCH * DM], BF16)
    arenaA = L.sb("arenaA", [128, 32768], BF16)
    Wv = arenaA[:, 0:16384]
    Wg = arenaA[:, 16384:32768]
    Wo = arenaA[:, 0:16384]
    o2t = arenaA[:, 16384:24576]
    Ct = L.sb("s_ropeC", [128, S], BF16)
    St = L.sb("s_ropeS", [128, S], BF16)
    dmk = L.sb("s_dmaskT", [64, 4 * 64], F32)
    cqc = L.sb("s_cqcol", [64, 4], F32)
    ckc = L.sb("s_ckcol", [64, 4], F32)
    gnl = L.sb("s_gnl", [128, 16], F32)

    load_slabs(p, "gpsimd", Wq, Wqh, DM, NCH, sem="ld_wg")
    load_slabs(p, "gpsimd", Wk, Wkh, DM, NCH, sem="ld_wg")
    load_slabs(p, "gpsimd", Wv, Wvh, DV, NCH, sem="ld_wg")
    load_slabs(p, "gpsimd", Wg, Wgh, DV, NCH, sem="ld_wg")
    p.op("gpsimd", lambda h: h.dma_start(out=Ct[:], in_=Cth[:, :]), inc=16, sem="ld_wg")
    p.op("gpsimd", lambda h: h.dma_start(out=St[:], in_=Sth[:, :]), inc=16, sem="ld_wg")
    p.op("sync", lambda h: h.dma_start(out=dmk[:], in_=dmh[:, :]), inc=16, sem="ld_w")
    p.op("sync", lambda h: h.dma_start(out=cqc[:], in_=cqh[:, :]), inc=16, sem="ld_w")
    p.op("sync", lambda h: h.dma_start(out=ckc[:], in_=ckh[:, :]), inc=16, sem="ld_w")
    p.op("sync", lambda h: h.dma_start(out=gnl[:], in_=gnlh[:, :]), inc=16, sem="ld_w")

    C = Common(L, x_hbm)
    wt_s = p.tok("ld_w")
    C.ident_tok = wt_s
    wt_g = p.tok("ld_wg")
    L.init_pool()
    pool = L.pool

    r1s = L.sb("pj_r1", [128, 2 * TG], F32)
    r2s = L.sb("pj_r2", [128, 2 * TG], F32)
    qout = L.sb("pj_qout", [128, 2 * TG], BF16)
    vgst = L.sb("pj_vgst", [128, 2 * TG], BF16)
    r1_free = [None, None]
    qout_free = [None, None]
    vg_free = [None, None]

    for tg in range(NTG):
        C.load_x(tg)
        C.ln(tg, pool)
        for part, Wsb, dsth, pfx, rope in ((0, Wq, q_hbm, "q", True),
                                           (1, Wk, k_hbm, "k", True)):
            for hh in range(4):
                psA, iA = pool.get()
                psB, iB = pool.get()
                for half, psx in ((0, psA), (1, psB)):
                    c = 2 * hh + half
                    for fc in range(NCH):
                        tq = p.op("tensor", lambda h, psx=psx, fc=fc, c=c, Wsb=Wsb: h.matmul(
                            psx[:, :], Wsb[:, fc * DM + c * 128: fc * DM + (c + 1) * 128],
                            C.xnslice(fc, tg), start=(fc == 0), stop=(fc == NCH - 1)),
                            inc=(1 if fc == NCH - 1 else 0),
                            waits=[C.xn_tok, wt_g] if fc == 0 else ())
                ctg = Ct[:, tg * TG:(tg + 1) * TG]
                stg = St[:, tg * TG:(tg + 1) * TG]
                par = hh % 2
                r1a = r1s[:, par * TG:(par + 1) * TG]
                r2a = r2s[:, par * TG:(par + 1) * TG]
                qo = qout[:, par * TG:(par + 1) * TG]
                w0 = [tq] + ([r1_free[par]] if r1_free[par] else [])
                e1 = p.op("vector", lambda h, psA=psA, r1a=r1a, ctg=ctg: h.tensor_tensor(
                    r1a, psA[:, :], ctg, ALU.mult), inc=1, waits=w0)
                e2 = p.op("vector", lambda h, psB=psB, r2a=r2a, stg=stg: h.tensor_tensor(
                    r2a, psB[:, :], stg, ALU.mult), inc=1, waits=[e1])
                w1 = [e2] + ([qout_free[par]] if qout_free[par] else [])
                e3 = p.op("vector", lambda h, r1a=r1a, r2a=r2a, qo=qo: h.tensor_tensor(
                    qo, r1a, r2a, ALU.subtract), inc=1, waits=w1)
                ap1 = bass.AP(dsth.tensor, 2 * hh * 128 * S + tg * TG, [[S, 128], [1, TG]])
                s1 = p.op("sync", lambda h, ap1=ap1, qo=qo: h.dma_start(
                    out=ap1, in_=qo), inc=16, sem=f"sp_{pfx}{par}", waits=[e3])
                e4 = p.op("vector", lambda h, psB=psB, r1a=r1a, ctg=ctg: h.tensor_tensor(
                    r1a, psB[:, :], ctg, ALU.mult), inc=1, waits=[e3])
                pool.done(iB, e4)
                e5 = p.op("vector", lambda h, psA=psA, r2a=r2a, stg=stg: h.tensor_tensor(
                    r2a, psA[:, :], stg, ALU.mult), inc=1, waits=[e4])
                pool.done(iA, e5)
                w2 = [e5, s1]
                e6 = p.op("vector", lambda h, r1a=r1a, r2a=r2a, qo=qo: h.tensor_tensor(
                    qo, r1a, r2a, ALU.add), inc=1, waits=w2)
                ap2 = bass.AP(dsth.tensor, (2 * hh + 1) * 128 * S + tg * TG,
                              [[S, 128], [1, TG]])
                s2 = p.op("sync", lambda h, ap2=ap2, qo=qo: h.dma_start(
                    out=ap2, in_=qo), inc=16, sem=f"sp_{pfx}{par}", waits=[e6])
                qout_free[par] = s2
                r1_free[par] = e6
        for j2 in range(4):
            for hh in range(4):
                for which, Wsb2, dsth2, act in ((0, Wv, v_hbm, False),
                                                (1, Wg, g_hbm, True)):
                    par = (j2 * 4 + hh * 2 + which) % 2
                    psv, iv = pool.get()
                    for fc in range(NCH):
                        tv = p.op("tensor", lambda h, psv=psv, fc=fc, hh=hh, j2=j2,
                                  Wsb2=Wsb2: h.matmul(
                            psv[:, :], C.xnslice(fc, tg, width=128, off=j2 * 128),
                            Wsb2[:, fc * DV + hh * 512: fc * DV + (hh + 1) * 512],
                            start=(fc == 0), stop=(fc == NCH - 1)),
                            inc=(1 if fc == NCH - 1 else 0),
                            waits=[C.xn_tok] if fc == 0 else ())
                    vg = vgst[:, par * TG:(par + 1) * TG]
                    w = [tv] + ([vg_free[par]] if vg_free[par] else [])
                    if act:
                        ev = p.op("scalar", lambda h, psv=psv, vg=vg: h.activation(
                            vg, psv[:, :], AF.Silu), inc=1, waits=w)
                    else:
                        ev = p.op("vector", lambda h, psv=psv, vg=vg: h.tensor_copy(
                            vg, psv[:, :]), inc=1, waits=w)
                    pool.done(iv, ev)
                    jj2 = tg * 4 + j2
                    ap = bass.AP(dsth2.tensor, hh * 32 * 128 * 512 + jj2 * 128 * 512,
                                 [[512, 128], [1, 512]])
                    sv = p.op("sync", lambda h, ap=ap, vg=vg: h.dma_start(
                        out=ap, in_=vg), inc=16, sem=f"sp_v{par}", waits=[ev])
                    vg_free[par] = sv

    spill = [p.tok("sp_q0"), p.tok("sp_q1"), p.tok("sp_k0"), p.tok("sp_k1"),
             p.tok("sp_v0"), p.tok("sp_v1")]

    qh = Wq
    kh = Wk
    vgt = L.sb("sc_vgt", [64, 4 * TG], BF16)
    Ssb = L.sb("sc_S", [128, 2 * TG], F32)
    Sbf = L.sb("sc_Sbf", [128, 2 * TG], BF16)
    abuf = L.sb("sc_abuf", [64, 64], BF16)
    ktok_sb = L.sb("sc_ktok", [64, 256], BF16)
    vk_sb = L.sb("sc_vk", [64, TG], BF16)
    of32 = L.sb("sc_of32", [64, TG], F32)
    sqs = L.sb("sc_sqs", [64, TG], F32)
    scol = L.sb("sc_scol", [64, 4], F32)
    o2bf = L.sb("sc_o2", [64, TG], BF16)
    o2st = L.sb("sc_o2st", [128, 2 * 4 * 64], BF16)

    qh_free = None
    kh_free = None
    vg_free2 = [None, None]
    st_o2 = [None, None]
    prev_sc = None
    for hh in range(4):
        w = list(spill) + ([qh_free] if qh_free else [])
        for half in range(2):
            ap = bass.AP(q_hbm.tensor, (2 * hh + half) * 128 * S, [[S, 128], [1, S]])
            qtok = p.op("sync", lambda h, ap=ap, half=half: h.dma_start(
                out=qh[:, half * S:(half + 1) * S], in_=ap), inc=16, sem="ld_qh",
                waits=w if half == 0 else ())
        w = [kh_free] if kh_free else []
        for half in range(2):
            ap = bass.AP(k_hbm.tensor, (2 * hh + half) * 128 * S, [[S, 128], [1, S]])
            ktok2 = p.op("sync", lambda h, ap=ap, half=half: h.dma_start(
                out=kh[:, half * S:(half + 1) * S], in_=ap), inc=16, sem="ld_kh",
                waits=w if half == 0 else ())
        z1 = p.op("gpsimd", lambda h: h.memset(Ssb[:, :], 0.0), inc=1,
                  waits=[prev_sc] if prev_sc else [])
        z2 = p.op("gpsimd", lambda h: h.memset(Sbf[:, :], 0.0), inc=1)
        sbf_tok = p.tok("gpsimd")
        gC = GAM[hh] ** 64
        dmsl = dmk[0:64, hh * 64:(hh + 1) * 64]
        cqsl = cqc[0:64, hh:hh + 1]
        cksl = ckc[0:64, hh:hh + 1]
        st_upd = None
        sbf_upd = sbf_tok
        for j in range(64):
            j2 = j // 2
            hlf = j % 2
            vpar = j % 2
            for wh, src in ((0, v_hbm), (1, g_hbm)):
                ap = bass.AP(src.tensor,
                             hh * 32 * 128 * 512 + j2 * 128 * 512 + hlf * 64 * 512,
                             [[512, 64], [1, 512]])
                wv = list(spill) + ([vg_free2[vpar]] if vg_free2[vpar] else [])
                vgtok = p.op("sync", lambda h, ap=ap, wh=wh, vpar=vpar: h.dma_start(
                    out=vgt[:, (vpar * 2 + wh) * TG:(vpar * 2 + wh + 1) * TG],
                    in_=ap), inc=16, sem=f"ld_vg{vpar}",
                    waits=wv if wh == 0 else ())
            cur_vg = vgtok
            vsl = vgt[:, vpar * 2 * TG:(vpar * 2 + 1) * TG]
            gsl = vgt[:, (vpar * 2 + 1) * TG:(vpar * 2 + 2) * TG]
            qsl = [qh[:, half * S + j * 64:half * S + (j + 1) * 64] for half in range(2)]
            ksl = [kh[:, half * S + j * 64:half * S + (j + 1) * 64] for half in range(2)]
            ps1, i1 = pool.get()
            for half in range(2):
                t1 = p.op("tensor", lambda h, ps1=ps1, half=half, j=j, qsl=qsl,
                          ksl=ksl: h.matmul(
                    ps1[0:64, 0:64], ksl[half], qsl[half], start=(half == 0),
                    stop=(half == 1)), inc=(1 if half == 1 else 0),
                    waits=[qtok, ktok2, vgtok] if (j == 0 and half == 0) else ())
            am = p.op("vector", lambda h, ps1=ps1, dmsl=dmsl: h.tensor_tensor(
                abuf[:, :], ps1[0:64, 0:64], dmsl, ALU.mult), inc=1,
                waits=[t1, wt_s])
            pool.done(i1, am)
            ps2, i2 = pool.get()
            t2 = p.op("tensor", lambda h, ps2=ps2, vsl=vsl: h.matmul(
                ps2[0:64, :], abuf[:, :], vsl, start=True, stop=True), inc=1,
                waits=[am, cur_vg])
            ps3, i3 = pool.get()
            for half in range(2):
                t3 = p.op("tensor", lambda h, ps3=ps3, half=half, qsl=qsl: h.matmul(
                    ps3[0:64, :], qsl[half], Sbf[:, half * TG:(half + 1) * TG],
                    start=(half == 0), stop=(half == 1)),
                    inc=(1 if half == 1 else 0),
                    waits=[sbf_upd] if half == 0 else ())
            c1 = p.op("vector", lambda h, ps2=ps2: h.tensor_copy(of32[:, :],
                                                                 ps2[0:64, :]),
                      inc=1, waits=[t2])
            pool.done(i2, c1)
            c2 = p.op("vector", lambda h, ps3=ps3, cqsl=cqsl: h.scalar_tensor_tensor(
                of32[:, :], ps3[0:64, :], cqsl, of32[:, :], ALU.mult, ALU.add),
                inc=1, waits=[t3, c1])
            pool.done(i3, c2)
            sq = p.op("scalar", lambda h: h.activation(sqs[:, :], of32[:, :],
                                                       AF.Square), inc=1, waits=[c2])
            rs = p.op("vector", lambda h: h.tensor_reduce(
                scol[:, 0:1], sqs[:, :], axis=AX, op=ALU.add), inc=1, waits=[sq])
            ms = p.op("vector", lambda h: h.tensor_scalar(
                scol[:, 1:2], scol[:, 0:1], 1.0 / 512.0, None, ALU.mult), inc=1,
                waits=[rs])
            epsap = C.constant(1e-5, parts=64)
            sr = p.op("scalar", lambda h, epsap=epsap: h.activation(
                scol[:, 2:3], scol[:, 1:2], AF.Sqrt, bias=epsap), inc=1,
                waits=[ms, C.const_tok])
            rc = p.op("vector", lambda h: h.reciprocal(scol[:, 3:4], scol[:, 2:3]),
                      inc=1, waits=[sr])
            n1 = p.op("vector", lambda h: h.tensor_scalar(
                of32[:, :], of32[:, :], scol[:, 3:4], None, ALU.mult), inc=1,
                waits=[rc])
            g1 = p.op("vector", lambda h, gsl=gsl: h.tensor_tensor(
                o2bf[:, :], of32[:, :], gsl, ALU.mult), inc=1, waits=[n1])
            opar = j % 2
            ps4, i4 = pool.get()
            ps4b = ps4[:, :].bitcast(BF16)
            ev_last = None
            for sub in range(4):
                t4 = p.op("tensor", lambda h, ps4b=ps4b, sub=sub: h.transpose(
                    ps4b[:, sub * 64:(sub + 1) * 64],
                    o2bf[:, sub * 128:(sub + 1) * 128], C.ident_bf[0:64, 0:64]),
                    inc=1, waits=[g1, C.identbf_tok])
            w = [t4] + ([st_o2[opar]] if st_o2[opar] else [])
            ev = p.op("vector", lambda h, ps4b=ps4b, opar=opar: h.tensor_copy(
                o2st[:, opar * 256:(opar + 1) * 256], ps4b[:, 0:256]), inc=1,
                waits=w)
            pool.done(i4, ev)
            for sub in range(4):
                ap = bass.AP(o2_hbm.tensor, (hh * 512 + sub * 128) * S + j * 64,
                             [[S, 128], [1, 64]])
                so = p.op("sync", lambda h, ap=ap, sub=sub, opar=opar: h.dma_start(
                    out=ap, in_=o2st[:, opar * 256 + sub * 64:opar * 256 + (sub + 1) * 64]),
                    inc=16, sem=f"st_o2{opar}", waits=[ev] if sub == 0 else ())
            st_o2[opar] = so
            ps5, i5 = pool.get()
            ps5b = ps5[:, :].bitcast(BF16)
            for half in range(2):
                t5 = p.op("tensor", lambda h, ps5b=ps5b, half=half, ksl=ksl: h.transpose(
                    ps5b[0:64, half * 128:(half + 1) * 128], ksl[half],
                    C.ident_bf[:, :]), inc=1,
                    waits=[am, C.identbf_tok] if half == 0 else [am])
            kt_ev = p.op("vector", lambda h, ps5b=ps5b: h.tensor_copy(
                ktok_sb[:, :], ps5b[0:64, 0:256]), inc=1, waits=[t5])
            pool.done(i5, kt_ev)
            vk = p.op("vector", lambda h, vsl=vsl, cksl=cksl: h.tensor_scalar(
                vk_sb[:, :], vsl, cksl, None, ALU.mult), inc=1,
                waits=[kt_ev, wt_s, cur_vg])
            ps6, i6 = pool.get()
            ps7, i7 = pool.get()
            for half, psx in ((0, ps6), (1, ps7)):
                t6 = p.op("tensor", lambda h, psx=psx, half=half: h.matmul(
                    psx[:, :], ktok_sb[:, half * 128:(half + 1) * 128], vk_sb[:, :],
                    start=True, stop=True), inc=1, waits=[kt_ev, vk])
            su1 = p.op("vector", lambda h, ps6=ps6, gC=gC: h.scalar_tensor_tensor(
                Ssb[:, 0:TG], Ssb[:, 0:TG], gC, ps6[:, :], ALU.mult, ALU.add),
                inc=1, waits=[t6])
            pool.done(i6, su1)
            su2 = p.op("vector", lambda h, ps7=ps7, gC=gC: h.scalar_tensor_tensor(
                Ssb[:, TG:2 * TG], Ssb[:, TG:2 * TG], gC, ps7[:, :], ALU.mult,
                ALU.add), inc=1, waits=[t6])
            pool.done(i7, su2)
            sb1_ = p.op("vector", lambda h: h.tensor_copy(Sbf[:, 0:TG], Ssb[:, 0:TG]),
                        inc=1, waits=[su1, t3])
            sb2_ = p.op("vector", lambda h: h.tensor_copy(Sbf[:, TG:2 * TG],
                                                          Ssb[:, TG:2 * TG]),
                        inc=1, waits=[su2])
            sbf_upd = sb2_
            if hlf == 1:
                vg_free2[vpar] = vk
        qh_free = t3
        kh_free = t6
        prev_sc = sb2_

    o2_done = [p.tok("st_o20"), p.tok("st_o21")]

    wo_tok = load_slabs(p, "gpsimd", Wo, Woh, DM, 16, sem="ld_wo",
                        waits=list(o2_done) + [p.tok("vector"), p.tok("tensor")])
    gn_toks = []
    for vc in range(16):
        t = p.op("vector", lambda h, vc=vc: h.tensor_scalar(
            Wo[:, vc * DM:(vc + 1) * DM], Wo[:, vc * DM:(vc + 1) * DM],
            gnl[:, vc:vc + 1], None, ALU.mult), inc=1,
            waits=[wo_tok, wt_s])
        gn_toks.append(t)
    wo_ready = gn_toks[-1]

    ot_free = None
    out_free = [None, None]
    for tg in range(NTG):
        C.load_x(tg)
        w0 = [wo_ready] + list(o2_done) + ([ot_free] if ot_free else [])
        for vc in range(16):
            oap = bass.AP(o2_hbm.tensor, vc * 128 * S + tg * TG, [[S, 128], [1, TG]])
            otok = p.op("sync", lambda h, oap=oap, vc=vc: h.dma_start(
                out=o2t[:, vc * TG:(vc + 1) * TG], in_=oap), inc=16, sem="ld_o2",
                waits=w0 if vc == 0 else ())
        for dc in range(NCH):
            psY, iY = pool.get()
            for vc in range(16):
                tY = p.op("tensor", lambda h, psY=psY, vc=vc, dc=dc: h.matmul(
                    psY[:, :], Wo[:, vc * DM + dc * 128: vc * DM + (dc + 1) * 128],
                    o2t[:, vc * TG:(vc + 1) * TG], start=(vc == 0), stop=(vc == 15)),
                    inc=(1 if vc == 15 else 0),
                    waits=[otok, wo_ready, C.x_tok] if vc == 0 else ())
            opar = dc % 2
            ob = r1s[:, opar * TG:(opar + 1) * TG]
            w = [tY] + ([out_free[opar]] if out_free[opar] else [])
            eY = p.op("vector", lambda h, psY=psY, ob=ob, dc=dc: h.tensor_tensor(
                ob, psY[:, :], C.xslice(dc), ALU.add), inc=1, waits=w)
            pool.done(iY, eY)
            yap = bass.AP(y_hbm.tensor, dc * 128 * S + tg * TG, [[S, 128], [1, TG]])
            sT = p.op("sync", lambda h, yap=yap, ob=ob: h.dma_start(out=yap, in_=ob),
                      inc=16, sem=f"st{opar}", waits=[eY])
            out_free[opar] = sT
        C.xt_free = [eY]
        ot_free = tY
    return None


def build_attn_l1b():
    L = Launch("l1b")
    x_hbm = L.dram_in("x", [DM, S])
    y_hbm = L.dram_out("y", [DM, S])
    emit_l1b(L, x_hbm, y_hbm, declare_l1b(L))
    L.p.wait("sync", L.p.tok("st0"))
    L.p.wait("sync", L.p.tok("st1"))
    return L.done()


def l1b_host_inputs(inp):
    g = lambda k: np.asarray(inp["L1_attn_b_" + k], dtype=np.float32)
    inv = 1.0 / (10000.0 ** (np.arange(0, 256, 2, dtype=np.float64) / 256))
    t = np.arange(S, dtype=np.float64)
    fr = np.outer(inv, t)
    GAM = [1.0 - 2.0 ** (-5.0 - hh) for hh in range(4)]
    idx = np.arange(64)
    dm = np.zeros((64, 4 * 64), dtype=np.float32)
    cq = np.zeros((64, 4), dtype=np.float32)
    ck = np.zeros((64, 4), dtype=np.float32)
    for hh in range(4):
        rel = idx[None, :] - idx[:, None]
        D = np.where(rel >= 0, GAM[hh] ** np.maximum(rel, 0), 0.0)
        dm[:, hh * 64:(hh + 1) * 64] = (D / 16.0).astype(np.float32)
        cq[:, hh] = (GAM[hh] ** (idx + 1) / 16.0).astype(np.float32)
        ck[:, hh] = (GAM[hh] ** (63 - idx)).astype(np.float32)
    gn = g("gn_weight")
    gnl = np.zeros((128, 16), dtype=np.float32)
    for vc in range(16):
        gnl[:, vc] = gn[(vc % 4) * 128:(vc % 4 + 1) * 128]
    return {
        "Wq": g("Wq"), "Wk": g("Wk"), "Wv": g("Wv"), "Wg": g("Wg"), "Wo": g("Wo"),
        "gnl": gnl,
        "ropeC2": np.cos(fr).astype(np.float32), "ropeS2": np.sin(fr).astype(np.float32),
        "dmaskT": dm, "cqcol": cq, "ckcol": ck,
        "c_identin": np.eye(128, dtype=np.float32),
    }



def build_fused():
    L = Launch("fused")
    x0 = L.dram_in("x", [DM, S])
    y8 = L.dram_out("y", [DM, S])
    acts = [L.dram_scratch(f"act{i}", [DM, S], F32) for i in range(7)]
    D_l0a = declare_l0a(L)
    D_ffn0 = declare_ffn0(L)
    D_l0b = declare_l0b(L)
    D_l1a = declare_l1a(L)
    D_ffn1 = declare_ffn1(L)
    D_l1b = declare_l1b(L)
    L.use_arena(106000)
    seq = [(emit_l0a, D_l0a), (emit_ffn0, D_ffn0), (emit_l0b, D_l0b),
           (emit_ffn0, D_ffn0), (emit_l1a, D_l1a), (emit_ffn1, D_ffn1),
           (emit_l1b, D_l1b), (emit_ffn1, D_ffn1)]
    cur = x0
    for i, (fn, D) in enumerate(seq):
        L.prefix = f"s{i}_"
        out = y8 if i == 7 else acts[i]
        fn(L, cur, out, D)
        L.arena_reset()
        L.prefix = ""
        L.fence()
        cur = out
    return L.done()


def fused_host_inputs(inputs):
    hm = {}
    for d in (l0a_host_inputs(inputs), ffn0_host_inputs(inputs),
              l0b_host_inputs(inputs), l1a_host_inputs(inputs),
              ffn1_host_inputs(inputs), l1b_host_inputs(inputs)):
        for k, v in d.items():
            if k in ("c_identin", "maskb_in"):
                continue
            hm[k] = v
    eye = np.eye(128, dtype=np.float32)
    for i in range(8):
        hm[f"s{i}_c_identin"] = eye
    hm["s0_maskb_in"] = host_maskB(anti=False)
    hm["s4_maskb_in"] = host_maskB(anti=False)
    return hm


def kernel(**inputs):
    x = np.asarray(inputs["L0_attn_a_x"], dtype=np.float32)
    nc = build_fused()
    shared = fused_host_inputs(inputs)
    in_maps = []
    for i in range(8):
        m = dict(shared)
        m["x"] = np.ascontiguousarray(x[i].T)
        in_maps.append(m)
    res = run_bass_kernel_spmd(nc, in_maps, core_ids=list(range(8)))
    return np.stack([np.ascontiguousarray(res.results[i]["y"]).T
                     for i in range(8)]).astype(np.float32)


def kernel_unfused(**inputs):
    x = np.asarray(inputs["L0_attn_a_x"], dtype=np.float32)
    cur = [np.ascontiguousarray(x[i].T) for i in range(8)]

    nc_l0a = build_attn_l0a()
    nc_ffn0 = build_ffn0()
    nc_l0b = build_attn_l0b()
    nc_l1a = build_attn_l1a()
    nc_ffn1 = build_ffn1()
    nc_l1b = build_attn_l1b()

    steps = [
        (nc_l0a, l0a_host_inputs(inputs)),
        (nc_ffn0, ffn0_host_inputs(inputs)),
        (nc_l0b, l0b_host_inputs(inputs)),
        (nc_ffn0, ffn0_host_inputs(inputs)),
        (nc_l1a, l1a_host_inputs(inputs)),
        (nc_ffn1, ffn1_host_inputs(inputs)),
        (nc_l1b, l1b_host_inputs(inputs)),
        (nc_ffn1, ffn1_host_inputs(inputs)),
    ]
    for nc, shared in steps:
        in_maps = []
        for i in range(8):
            m = dict(shared)
            m["x"] = cur[i]
            in_maps.append(m)
        res = run_bass_kernel_spmd(nc, in_maps, core_ids=list(range(8)))
        cur = [np.ascontiguousarray(res.results[i]["y"]) for i in range(8)]
    return np.stack([c.T for c in cur]).astype(np.float32)


# version 05
# speedup vs baseline: 1.2839x; 1.0294x over previous
import numpy as np
import ml_dtypes
from contextlib import ExitStack
import concourse.bass as bass
import concourse.mybir as mybir
from concourse.bass_utils import run_bass_kernel_spmd
from concourse.alu_op_type import AluOpType as ALU

F32 = mybir.dt.float32
BF16 = mybir.dt.bfloat16
AF = mybir.ActivationFunctionType
AX = mybir.AxisListType.X
BF = ml_dtypes.bfloat16

S = 4096
DM = 1024
NCH = 8
TG = 512
NTG = S // TG
NEG = -1e9

ENGS = ("sync", "gpsimd", "vector", "scalar", "tensor")


class Prog:

    def __init__(self):
        self.q = {e: [] for e in ENGS}
        self.cnt = {}
        self.waited = {e: {} for e in ENGS}
        self.sems = set(ENGS)

    def wait(self, eng, tok):
        if tok is None:
            return
        o, v = tok
        if v <= self.waited[eng].get(o, 0):
            return
        self.waited[eng][o] = v
        self.q[eng].append(lambda h, SEM, o=o, v=v: h.wait_ge(SEM[o], v))

    def op(self, eng, build, inc=0, sem=None, waits=()):
        sem = sem or eng
        self.sems.add(sem)
        for t in waits:
            self.wait(eng, t)
        if inc:
            self.cnt[sem] = self.cnt.get(sem, 0) + inc
        v = self.cnt.get(sem, 0)

        def f(h, SEM, build=build, inc=inc, sem=sem):
            ins = build(h)
            if inc:
                ins.then_inc(SEM[sem], inc)
        self.q[eng].append(f)
        return (sem, v)

    def tok(self, sem):
        return (sem, self.cnt.get(sem, 0))


def finalize(nc, p, ctx):
    SEM = {e: ctx.enter_context(nc.semaphore(f"S_{e}")) for e in sorted(p.sems)}
    block = ctx.enter_context(nc.Block())
    for e in ENGS:
        if not p.q[e]:
            continue
        queue = p.q[e]

        def body(h, queue=queue):
            for f in queue:
                f(h, SEM)
        getattr(block, e)(body)


class PsumPool:
    def __init__(self, p, ctx, nc, n=8):
        self.tiles = [ctx.enter_context(nc.psum_tensor(f"pb{i}", [128, 512], F32))
                      for i in range(n)]
        self.release = [None] * n
        self.held = set()
        self.i = 0
        self.p = p

    def get(self, hold=False):
        for _ in range(len(self.tiles)):
            i = self.i
            self.i = (self.i + 1) % len(self.tiles)
            if i not in self.held:
                break
        else:
            raise RuntimeError("all psum banks held")
        if hold:
            self.held.add(i)
        if self.release[i] is not None:
            self.p.wait("tensor", self.release[i])
            self.release[i] = None
        return self.tiles[i], i

    def done(self, i, token):
        self.release[i] = token
        self.held.discard(i)


class Launch:

    def __init__(self, name):
        self.nc = bass.Bass("TRN2", target_bir_lowering=False)
        self.ctx = ExitStack()
        self.ctx.__enter__()
        self.p = Prog()
        self.pool = None
        self.name = name
        self.prefix = ""
        self.arena = None
        self.arena_off = 0
        self.arena_max = 0
        self._sb_names = set()

    def dram_in(self, name, shape, dtype=F32):
        return self.nc.dram_tensor(self.prefix + name, list(shape), dtype,
                                   kind="ExternalInput").ap()

    def dram_out(self, name, shape, dtype=F32):
        return self.nc.dram_tensor(self.prefix + name, list(shape), dtype,
                                   kind="ExternalOutput").ap()

    def dram_scratch(self, name, shape, dtype=BF16):
        return self.nc.dram_tensor(self.prefix + name, list(shape), dtype).ap()

    def use_arena(self, cols_bf16):
        self.arena = self.sb("shared_arena", [128, cols_bf16], BF16)
        self.arena_cols = cols_bf16

    def arena_reset(self):
        self.arena_max = max(self.arena_max, self.arena_off)
        self.arena_off = 0

    def sb(self, name, shape, dtype):
        name = self.prefix + name
        assert name not in self._sb_names, name
        self._sb_names.add(name)
        if self.arena is not None:
            P, Cc = shape
            nb = 2 if dtype == BF16 else 4
            bcols = Cc * nb // 2
            off = self.arena_off
            self.arena_off = off + bcols + (bcols & 1)
            assert self.arena_off <= self.arena_cols, (
                f"arena overflow at {name}: {self.arena_off} > {self.arena_cols}")
            ap = self.arena[:, off:off + bcols]
            if dtype != BF16:
                ap = ap.bitcast(dtype)
            return ap[0:P, :]
        return self.ctx.enter_context(self.nc.sbuf_tensor(name, list(shape), dtype))

    def init_pool(self):
        if self.pool is None:
            self.pool = PsumPool(self.p, self.ctx, self.nc)

    def fence(self):
        p = self.p
        if not hasattr(self, "_fb"):
            self.prefix, sv = "", self.prefix
            self._fb = self.ctx.enter_context(
                self.nc.sbuf_tensor("fence_buf", [1, 8], F32))
            self._fbb = self.ctx.enter_context(
                self.nc.sbuf_tensor("fence_bufb", [1, 8], BF16))
            p.op("gpsimd", lambda h: h.memset(self._fbb[:], 1.0), inc=1)
            self.prefix = sv
        fb, fbb = self._fb, self._fbb
        tv = p.op("vector", lambda h: h.tensor_copy(fb[0:1, 0:1], fb[0:1, 1:2]), inc=1)
        ts_ = p.op("scalar", lambda h: h.copy(fb[0:1, 2:3], fb[0:1, 3:4]), inc=1)
        fps, ifp = self.pool.get()
        tt_ = p.op("tensor", lambda h, fps=fps: h.matmul(fps[0:1, 0:1], fbb[0:1, 0:1],
                                                         fbb[0:1, 1:2], start=True,
                                                         stop=True), inc=1)
        self.pool.done(ifp, tt_)
        joins = [tv, ts_, tt_]
        for sem_name in sorted(p.cnt.keys()):
            if sem_name not in ENGS:
                joins.append((sem_name, p.cnt[sem_name]))
        tg_ = p.op("gpsimd", lambda h: h.memset(fb[0:1, 4:5], 0.0), inc=1,
                   waits=joins)
        for e in ENGS:
            p.wait(e, tg_)
        return tg_

    def done(self):
        finalize(self.nc, self.p, self.ctx)
        self.ctx.__exit__(None, None, None)
        return self.nc


def load_slabs(p, eng, sb, hbm, n, nk, toff=0, col0=0, waits=(), sem="ld_w"):
    tok = None
    for fc in range(nk):
        ap = bass.AP(hbm.tensor, (toff + fc * 128) * n, [[n, 128], [1, n]])
        tok = p.op(eng, lambda h, ap=ap, fc=fc: h.dma_start(
            out=sb[:, col0 + fc * n:col0 + (fc + 1) * n], in_=ap), inc=16,
            sem=sem, waits=waits if fc == 0 else ())
    return tok



class Common:

    def __init__(self, L, x_hbm, with_xn_full=False, xt_ap=None, xsq_ap=None,
                 rows_cols=3080):
        self.L = L
        p, nc = L.p, L.nc
        self.x_hbm = x_hbm
        self.xt = xt_ap if xt_ap is not None else L.sb("c_xt", [128, NCH * TG], F32)
        if xsq_ap is not None:
            self.xsq = xsq_ap
            self.sxn = None
        else:
            self.xsq = L.sb("c_xsq", [128, NCH * TG], BF16)
            self.sxn = L.sb("c_sxn", [128, TG], F32)
        self.with_xn_full = with_xn_full
        if with_xn_full:
            self.xn = L.sb("c_xn", [128, NCH * S], BF16)
        else:
            self.xn = L.sb("c_xn", [128, NCH * TG], BF16)
        self.rows = L.sb("c_rows", [1, rows_cols], F32)
        self.ab = L.sb("c_ab", [128, 2 * TG], F32)
        self.ones_col = L.sb("c_onescol", [128, 1], F32)
        self.ones_row = L.sb("c_onesrow", [1, 128], F32)
        self.ident = L.sb("c_ident", [128, 128], F32)
        self.xt_free = []
        self.xn_free = None
        self.xsq_free = None
        self.x_tok = None
        self.mu_tok = None
        self.ab_tok = None
        self.xn_tok = None
        self.cpool = L.sb("c_cpool", [128, 8], F32)
        self.ones_col_bf = L.sb("c_onescolbf2", [128, 1], BF16)
        self._cvals = {}
        p.op("gpsimd", lambda h: h.memset(self.ones_col_bf[:], 1.0), inc=1)
        p.op("gpsimd", lambda h: h.memset(self.ones_col[:], 1.0), inc=1)
        p.op("gpsimd", lambda h: h.memset(self.ones_row[:], 1.0), inc=1)
        self.const_tok = p.tok("gpsimd")
        ident_h = L.dram_in("c_identin", [128, 128], F32)
        self.ident_tok = p.op("sync", lambda h: h.dma_start(out=self.ident[:], in_=ident_h[:, :]), inc=16, sem="ld_w")
        self.ident_bf = L.sb("c_identbf", [128, 128], BF16)
        self.identbf_tok = p.op("gpsimd", lambda h: h.dma_start(out=self.ident_bf[:], in_=ident_h[:, :]), inc=16, sem="ld_idbf")

    def constant(self, value, parts=128):
        if value not in self._cvals:
            col = len(self._cvals)
            assert col < 8
            self._cvals[value] = col
            tok = self.L.p.op("gpsimd", lambda h, col=col, value=value: h.memset(
                self.cpool[:, col:col + 1], value), inc=1)
            self.const_tok = tok
        col = self._cvals[value]
        return self.cpool[0:parts, col:col + 1]

    def xslice(self, fc, width=TG):
        return self.xt[:, fc * TG:fc * TG + width]

    def xnslice(self, fc, tg=None, width=TG, off=0):
        if self.with_xn_full:
            return self.xn[:, fc * S + tg * TG + off: fc * S + tg * TG + off + width]
        return self.xn[:, fc * TG + off: fc * TG + off + width]

    def load_x(self, tg):
        p = self.L.p
        w = list(self.xt_free)
        for fc in range(NCH):
            ap = bass.AP(self.x_hbm.tensor, fc * 128 * S + tg * TG,
                         [[S, 128], [1, TG]])
            self.x_tok = p.op("sync", lambda h, ap=ap, fc=fc: h.dma_start(
                out=self.xt[:, fc * TG:(fc + 1) * TG], in_=ap), inc=16,
                sem="ld_x", waits=w if fc == 0 else ())
        return self.x_tok

    def ln(self, tg, pool, affine=None):
        p = self.L.p
        R = self.rows
        mu, e2, m2 = R[0:1, 0:512], R[0:1, 512:1024], R[0:1, 1024:1536]
        var, std = e2, m2
        rstd, bneg = R[0:1, 1536:2048], R[0:1, 2048:2560]
        sq_toks = []
        w0 = [self.x_tok, self.xsq_free] if self.xsq_free else [self.x_tok]
        for fc in range(NCH):
            t = p.op("scalar", lambda h, fc=fc: h.activation(
                self.xsq[:, fc * TG:(fc + 1) * TG], self.xslice(fc), AF.Square),
                inc=1, waits=w0 if fc == 0 else [self.x_tok])
            sq_toks.append(t)
        psA, iA = pool.get()
        for fc in range(NCH):
            tokA = p.op("tensor", lambda h, fc=fc, psA=psA: h.matmul(
                psA[0:1, :], self.ones_col[:], self.xslice(fc),
                start=(fc == 0), stop=(fc == NCH - 1)),
                inc=(1 if fc == NCH - 1 else 0),
                waits=[self.x_tok, self.const_tok] if fc == 0 else ())
        ocol = self.ones_col_bf if self.sxn is not None else self.ones_col
        psB, iB = pool.get()
        for fc in range(NCH):
            tokB = p.op("tensor", lambda h, fc=fc, psB=psB, ocol=ocol: h.matmul(
                psB[0:1, :], ocol[:], self.xsq[:, fc * TG:(fc + 1) * TG],
                start=(fc == 0), stop=(fc == NCH - 1)),
                inc=(1 if fc == NCH - 1 else 0),
                waits=[sq_toks[fc]])
        self.xsq_free = tokB
        t1 = p.op("vector", lambda h: h.tensor_scalar(mu, psA[0:1, :], 1.0 / DM, None, ALU.mult),
                  inc=1, waits=[tokA])
        t2 = p.op("vector", lambda h: h.tensor_scalar(e2, psB[0:1, :], 1.0 / DM, None, ALU.mult),
                  inc=1, waits=[tokB])
        pool.done(iA, t1)
        pool.done(iB, t2)
        t3 = p.op("vector", lambda h: h.tensor_tensor(m2, mu, mu, ALU.mult), inc=1,
                  waits=[t1])
        t4 = p.op("vector", lambda h: h.tensor_tensor(var, e2, m2, ALU.subtract), inc=1,
                  waits=[t2, t3])
        epsap = self.constant(1e-5, parts=1)
        t5 = p.op("scalar", lambda h, epsap=epsap: h.activation(
            std, var, AF.Sqrt, bias=epsap), inc=1, waits=[t4, self.const_tok])
        t6 = p.op("vector", lambda h: h.reciprocal(rstd, std), inc=1, waits=[t5])
        t7 = p.op("vector", lambda h: h.scalar_tensor_tensor(bneg, mu, -1.0, rstd,
                                                            ALU.mult, ALU.mult),
                  inc=1, waits=[t6])
        self.mu_tok = t7
        psC, iC = pool.get()
        tC = p.op("tensor", lambda h: h.matmul(psC[:, :], self.ones_row[:], rstd,
                                               start=True, stop=True), inc=1,
                  waits=[t6, self.ident_tok])
        psD, iD = pool.get()
        tD = p.op("tensor", lambda h: h.matmul(psD[:, :], self.ones_row[:], bneg,
                                               start=True, stop=True), inc=1,
                  waits=[t7])
        ea = p.op("vector", lambda h: h.tensor_copy(self.ab[:, 0:TG], psC[:, :]), inc=1,
                  waits=[tC, self.ab_tok] if self.ab_tok else [tC])
        eb = p.op("vector", lambda h: h.tensor_copy(self.ab[:, TG:2 * TG], psD[:, :]),
                  inc=1, waits=[tD])
        pool.done(iC, ea)
        pool.done(iD, eb)
        a_b = self.ab[:, 0:TG]
        b_b = self.ab[:, TG:2 * TG]
        xn_toks = []
        prev_tB = None
        for fc in range(NCH):
            scr = (self.sxn[:, :] if self.sxn is not None
                   else self.xsq[:, fc * TG:(fc + 1) * TG])
            wA = [ea, eb, self.xsq_free, self.x_tok] if fc == 0 else []
            if prev_tB is not None:
                wA = list(wA) + [prev_tB]
            tA = p.op("vector", lambda h, fc=fc, scr=scr: h.tensor_tensor(
                scr, self.xslice(fc), a_b, ALU.mult), inc=1,
                waits=wA)
            dst = self.xnslice(fc, tg)
            if affine is None:
                tB = p.op("vector", lambda h, dst=dst, scr=scr: h.tensor_tensor(
                    dst, scr, b_b, ALU.add), inc=1,
                    waits=[tA] + ([self.xn_free] if (fc == 0 and self.xn_free) else []))
            else:
                gam, bet, gtok = affine
                tB0 = p.op("vector", lambda h, scr=scr: h.tensor_tensor(
                    scr, scr, b_b, ALU.add), inc=1,
                    waits=[tA] + ([self.xn_free] if (fc == 0 and self.xn_free) else []))
                tB = p.op("vector", lambda h, dst=dst, scr=scr, fc=fc, gam=gam,
                          bet=bet: h.tensor_scalar(
                    dst, scr, gam[:, fc:fc + 1], bet[:, fc:fc + 1], ALU.mult,
                    ALU.add), inc=1, waits=[tB0, gtok])
            prev_tB = tB
            xn_toks.append(tB)
        self.xsq_free = xn_toks[-1]
        self.xn_tok = xn_toks[-1]
        self.ab_tok = xn_toks[-1]
        self.xt_free = [xn_toks[-1]]
        return xn_toks



def declare_ffn0(L):
    sv, L.prefix = L.prefix, ""
    D = {
        "sW1": L.dram_in("sW1", [DM, 512]),
        "sW2": L.dram_in("sW2", [512, DM]),
        "rW1": L.dram_in("rW1", [8, DM, 512]),
        "rW2": L.dram_in("rW2", [8, 512, DM]),
        "rb2": L.dram_in("rb2", [8, DM]),
        "sb1l": L.dram_in("sb1l", [128, 4]),
        "sb2l": L.dram_in("sb2l", [128, 8]),
        "rb1l": L.dram_in("rb1l", [128, 32]),
        "rWT": L.dram_in("rWT", [DM, 8]),
        "ebl": L.dram_in("ebl", [8, 1]),
        "sels": L.dram_in("sels", [8, 1024]),
    }
    L.prefix = sv
    return D


def emit_ffn0(L, x_hbm, y_hbm, D):
    nc, p = L.nc, L.p
    H = 512
    NHC = H // 128
    sW1h, sW2h, rW1h, rW2h, rb2h = D["sW1"], D["sW2"], D["rW1"], D["rW2"], D["rb2"]
    sb1h, sb2h, rb1h, rWTh, ebh, selsh = (D["sb1l"], D["sb2l"], D["rb1l"],
                                          D["rWT"], D["ebl"], D["sels"])

    sW1 = L.sb("s_sW1", [128, NCH * H], BF16)
    sW2 = L.sb("s_sW2", [128, NHC * DM], BF16)
    rW1 = L.sb("s_rW1", [128, 2 * NCH * H], BF16)
    rW2 = L.sb("s_rW2", [128, 8 * NHC * DM], BF16)
    rb2 = L.sb("s_rb2", [8, DM], BF16)
    sb1 = L.sb("s_sb1", [128, 4], F32)
    sb2 = L.sb("s_sb2", [128, 8], F32)
    rb1 = L.sb("s_rb1", [128, 32], F32)
    rWT = L.sb("s_rWT", [128, NCH * 8], F32)
    eb = L.sb("s_eb", [8, 1], F32)
    sels = L.sb("s_sels", [8, 1024], BF16)
    colsum = L.sb("s_colsum", [8, 1], F32)

    load_slabs(p, "gpsimd", sW1, sW1h, H, NCH, sem="ld_wg")
    load_slabs(p, "gpsimd", sW2, sW2h, DM, NHC, sem="ld_wg")
    for e in range(8):
        load_slabs(p, "gpsimd", rW2, rW2h, DM, NHC, toff=e * H, col0=e * NHC * DM, sem="ld_wg")
    p.op("gpsimd", lambda h: h.dma_start(out=rb2[:], in_=rb2h[:, :]), inc=16, sem="ld_wg")
    p.op("sync", lambda h: h.dma_start(out=sb1[:], in_=sb1h[:, :]), inc=16, sem="ld_w")
    p.op("sync", lambda h: h.dma_start(out=sb2[:], in_=sb2h[:, :]), inc=16, sem="ld_w")
    p.op("sync", lambda h: h.dma_start(out=rb1[:], in_=rb1h[:, :]), inc=16, sem="ld_w")
    load_slabs(p, "sync", rWT, rWTh, 8, NCH)
    p.op("sync", lambda h: h.dma_start(out=eb[:], in_=ebh[:, :]), inc=16, sem="ld_w")
    p.op("gpsimd", lambda h: h.dma_start(out=sels[:], in_=selsh[:, :]), inc=16, sem="ld_wg")
    wt_gp = p.tok("ld_wg")

    C = Common(L, x_hbm)
    wt_sync = p.tok("ld_w")
    C.ident_tok = wt_sync
    L.init_pool()
    pool = L.pool

    psK, iK = pool.get()
    for fc in range(NCH):
        tk = p.op("tensor", lambda h, fc=fc: h.matmul(
            psK[0:1, 0:8], C.ones_col[:], rWT[:, fc * 8:(fc + 1) * 8],
            start=(fc == 0), stop=(fc == NCH - 1)),
            inc=(1 if fc == NCH - 1 else 0),
            waits=[wt_sync, wt_gp, C.const_tok] if fc == 0 else ())
    csr = C.rows[0:1, 2560:2568]
    e1 = p.op("vector", lambda h: h.tensor_copy(csr, psK[0:1, 0:8]), inc=1, waits=[tk])
    pool.done(iK, e1)
    psK2, iK2 = pool.get()
    tk2 = p.op("tensor", lambda h: h.transpose(psK2[0:8, 0:1], csr, C.ident[0:1, 0:1]),
               inc=1, waits=[e1, C.ident_tok])
    e2c = p.op("vector", lambda h: h.tensor_copy(colsum[:], psK2[0:8, 0:1]), inc=1,
               waits=[tk2])
    pool.done(iK2, e2c)
    colsum_tok = e2c

    htld = L.sb("htld", [128, 32 * TG], BF16)
    h1b = L.sb("h1b", [128, NHC * TG], BF16)
    combe = L.sb("combe", [128, 2 * TG], BF16)
    zbufs = L.sb("zb", [8, 5 * TG], F32)
    combbf = L.sb("combbf", [8, TG], BF16)
    ztok = L.sb("ztok", [128, 4 * 8], F32)
    vcols = L.sb("vcols", [128, 8], F32)
    eqb = L.sb("eqb", [128, 4 * 8], F32)
    outb = L.sb("outb", [128, 2 * 512], F32)

    z = zbufs[0:8, 0:TG]
    relu = zbufs[0:8, TG:2 * TG]
    ach = zbufs[0:8, 2 * TG:3 * TG]
    sp = zbufs[0:8, 3 * TG:4 * TG]
    act = zbufs[0:8, 4 * TG:5 * TG]
    combf = zbufs[0:8, 3 * TG:4 * TG]
    maskb = relu

    out_free = [None, None]
    rw_frees = {0: None, 1: None}
    htld_free = None
    combe_free = [None, None]

    for tg in range(NTG):
        C.load_x(tg)
        C.ln(tg, pool)
        mu = C.rows[0:1, 0:512]
        rstd = C.rows[0:1, 1536:2048]

        psR, iR = pool.get()
        for fc in range(NCH):
            tR = p.op("tensor", lambda h, fc=fc, psR=psR: h.matmul(
                psR[0:8, :], rWT[:, fc * 8:(fc + 1) * 8], C.xslice(fc),
                start=(fc == 0), stop=(fc == NCH - 1)),
                inc=(1 if fc == NCH - 1 else 0),
                waits=[C.x_tok, wt_sync] if fc == 0 else ())
        psM, iM = pool.get()
        tM = p.op("tensor", lambda h, psM=psM: h.matmul(
            psM[0:8, :], C.ones_row[0:1, 0:8], mu, start=True, stop=True), inc=1,
            waits=[C.mu_tok])
        psS, iS = pool.get()
        tS = p.op("tensor", lambda h, psS=psS: h.matmul(
            psS[0:8, :], C.ones_row[0:1, 0:8], rstd, start=True, stop=True), inc=1,
            waits=[C.mu_tok])
        zt1 = p.op("vector", lambda h, psM=psM: h.tensor_scalar(ach, psM[0:8, :], colsum[:], None,
                                                       ALU.mult), inc=1,
                   waits=[tM, colsum_tok])
        pool.done(iM, zt1)
        zt2 = p.op("vector", lambda h, psR=psR: h.tensor_tensor(sp, psR[0:8, :], ach, ALU.subtract),
                   inc=1, waits=[tR, zt1])
        pool.done(iR, zt2)
        zt3 = p.op("vector", lambda h, psS=psS: h.tensor_tensor(relu, sp, psS[0:8, :], ALU.mult),
                   inc=1, waits=[tS, zt2])
        pool.done(iS, zt3)
        zt4 = p.op("vector", lambda h: h.tensor_scalar(z, relu, eb[:], None, ALU.add),
                   inc=1, waits=[zt3, wt_sync])
        st1 = p.op("scalar", lambda h: h.activation(ach, z, AF.Abs), inc=1, waits=[zt4])
        st2 = p.op("scalar", lambda h: h.activation(ach, ach, AF.Exp, scale=-1.0), inc=1,
                   waits=[st1])
        st3 = p.op("scalar", lambda h: h.activation(ach, ach, AF.Ln, bias=1.0), inc=1,
                   waits=[st2])
        rt = p.op("vector", lambda h: h.tensor_scalar(relu, z, 0.0, None, ALU.max),
                  inc=1, waits=[zt4])
        sp1 = p.op("vector", lambda h: h.tensor_tensor(sp, relu, ach, ALU.add), inc=1,
                   waits=[rt, st3])
        st4 = p.op("scalar", lambda h: h.activation(act, sp, AF.Ln), inc=1, waits=[sp1])
        halfap = C.constant(0.5, parts=8)
        st5 = p.op("scalar", lambda h, halfap=halfap: h.activation(
            act, act, AF.Exp, scale=halfap), inc=1, waits=[st4, C.const_tok])
        zt_toks = []
        for tt in range(4):
            psT, iT = pool.get()
            tT = p.op("tensor", lambda h, tt=tt, psT=psT: h.transpose(
                psT[:, 0:8], z[:, tt * 128:(tt + 1) * 128], C.ident[0:8, 0:8]),
                inc=1, waits=[zt4, C.ident_tok])
            eT = p.op("vector", lambda h, tt=tt, psT=psT: h.tensor_copy(
                ztok[:, tt * 8:(tt + 1) * 8], psT[:, 0:8]), inc=1, waits=[tT])
            pool.done(iT, eT)
            zt_toks.append(eT)
        v2toks = []
        for tt in range(4):
            zt_ap = ztok[:, tt * 8:(tt + 1) * 8]
            v1 = vcols[:, tt:tt + 1]
            v2 = vcols[:, 4 + tt:5 + tt]
            eq = eqb[:, tt * 8:(tt + 1) * 8]
            a = p.op("vector", lambda h, zt_ap=zt_ap, v1=v1: h.tensor_reduce(
                v1, zt_ap, axis=AX, op=ALU.max), inc=1, waits=[zt_toks[tt]])
            b = p.op("vector", lambda h, zt_ap=zt_ap, v1=v1, eq=eq: h.tensor_scalar(
                eq, zt_ap, v1, None, ALU.is_equal), inc=1, waits=[a])
            c = p.op("vector", lambda h, zt_ap=zt_ap, eq=eq: h.scalar_tensor_tensor(
                eq, eq, -1e30, zt_ap, ALU.mult, ALU.add), inc=1, waits=[b])
            d = p.op("vector", lambda h, eq=eq, v2=v2: h.tensor_reduce(
                v2, eq, axis=AX, op=ALU.max), inc=1, waits=[c])
            v2toks.append(d)
        psV, iV = pool.get()
        for tt in range(4):
            tV = p.op("tensor", lambda h, tt=tt, psV=psV: h.transpose(
                psV[0:1, tt * 128:(tt + 1) * 128], vcols[:, 4 + tt:5 + tt], C.ident[:]),
                inc=1, waits=[v2toks[tt]])
        v2r = C.rows[0:1, 2568:3080]
        eV = p.op("vector", lambda h, psV=psV: h.tensor_copy(v2r, psV[0:1, 0:512]), inc=1,
                  waits=[tV])
        pool.done(iV, eV)
        psV2, iV2 = pool.get()
        tV2 = p.op("tensor", lambda h, psV2=psV2: h.matmul(psV2[0:8, :], C.ones_row[0:1, 0:8], v2r,
                                                start=True, stop=True), inc=1,
                   waits=[eV])
        mk = p.op("vector", lambda h, psV2=psV2: h.tensor_tensor(maskb, z, psV2[0:8, :], ALU.is_ge),
                  inc=1, waits=[tV2, st5])
        pool.done(iV2, mk)
        cb1 = p.op("vector", lambda h: h.tensor_tensor(combf[:, :], maskb, act, ALU.mult),
                   inc=1, waits=[mk])
        cb2 = p.op("vector", lambda h: h.tensor_copy(combbf[:, :], combf[:, :]), inc=1, waits=[cb1])

        for hc in range(NHC):
            psH, iH = pool.get()
            for fc in range(NCH):
                tH = p.op("tensor", lambda h, hc=hc, fc=fc, psH=psH: h.matmul(
                    psH[:, :], sW1[:, fc * H + hc * 128: fc * H + (hc + 1) * 128],
                    C.xnslice(fc, tg), start=(fc == 0), stop=(fc == NCH - 1)),
                    inc=(1 if fc == NCH - 1 else 0),
                    waits=[C.xn_tok, wt_gp] if fc == 0 else ())
            eH = p.op("scalar", lambda h, hc=hc, psH=psH: h.activation(
                h1b[:, hc * TG:(hc + 1) * TG], psH[:, :], AF.Silu,
                bias=sb1[:, hc:hc + 1]), inc=1, waits=[tH, wt_sync])
            pool.done(iH, eH)
        h1_tok = eH

        ht_toks = []
        rwtoks = {}
        for e in range(8):
            epar = e % 2
            if e not in rwtoks:
                rwtoks[e] = load_slabs(p, "gpsimd", rW1, rW1h, H, NCH, toff=e * DM,
                                       col0=epar * NCH * H,
                                       waits=[rw_frees[epar]] if rw_frees[epar] else (),
                                       sem=f"ld_r1{epar}")
            rW1_tok = rwtoks[e]
            par = e % 2
            cslc = combe[:, par * TG:(par + 1) * TG]
            psE, iE = pool.get()
            tE = p.op("tensor", lambda h, e=e, psE=psE: h.matmul(
                psE[:, :], sels[:, e * 128:(e + 1) * 128], combbf[:, :],
                start=True, stop=True), inc=1, waits=[cb2, wt_gp])
            w = [tE] + ([combe_free[par]] if combe_free[par] else [])
            cbe = p.op("vector", lambda h, psE=psE, cslc=cslc: h.tensor_copy(
                cslc, psE[:, :]), inc=1, waits=w)
            pool.done(iE, cbe)
            for hc in range(NHC):
                psQ, iQ = pool.get()
                for fc in range(NCH):
                    tQ = p.op("tensor", lambda h, hc=hc, fc=fc, psQ=psQ, epar=epar: h.matmul(
                        psQ[:, :],
                        rW1[:, (epar * NCH + fc) * H + hc * 128: (epar * NCH + fc) * H + (hc + 1) * 128],
                        C.xnslice(fc, tg), start=(fc == 0), stop=(fc == NCH - 1)),
                        inc=(1 if fc == NCH - 1 else 0),
                        waits=[rW1_tok, C.xn_tok] if fc == 0 else ())
                dst = htld[:, (e * NHC + hc) * TG:(e * NHC + hc + 1) * TG]
                w = [tQ] + ([htld_free] if (e == 0 and hc == 0 and htld_free) else [])
                eQ = p.op("scalar", lambda h, e=e, hc=hc, psQ=psQ, dst=dst: h.activation(
                    dst, psQ[:, :], AF.Silu, bias=rb1[:, e * 4 + hc:e * 4 + hc + 1]),
                    inc=1, waits=w)
                pool.done(iQ, eQ)
                mQ = p.op("vector", lambda h, dst=dst, cslc=cslc: h.tensor_tensor(
                    dst, dst, cslc, ALU.mult), inc=1,
                    waits=[eQ, cbe])
                ht_toks.append(mQ)
                combe_free[par] = mQ
            rw_frees[epar] = tQ
            if e + 1 < 8 and (e + 1) not in rwtoks:
                rwtoks[e + 1] = load_slabs(
                    p, "gpsimd", rW1, rW1h, H, NCH, toff=(e + 1) * DM,
                    col0=((e + 1) % 2) * NCH * H,
                    waits=[rw_frees[(e + 1) % 2]] if rw_frees[(e + 1) % 2] else (),
                    sem=f"ld_r1{(e + 1) % 2}")
        ht_all = ht_toks[-1]

        st_toks = []
        for dc in range(NCH):
            psY, iY = pool.get()
            first = True
            for hc in range(NHC):
                tY = p.op("tensor", lambda h, dc=dc, hc=hc, psY=psY, first=first: h.matmul(
                    psY[:, :], sW2[:, hc * DM + dc * 128: hc * DM + (dc + 1) * 128],
                    h1b[:, hc * TG:(hc + 1) * TG], start=first, stop=False),
                    waits=[h1_tok] if first else ())
                first = False
            for e in range(8):
                for hc in range(NHC):
                    tY = p.op("tensor", lambda h, dc=dc, e=e, hc=hc, psY=psY: h.matmul(
                        psY[:, :],
                        rW2[:, (e * NHC + hc) * DM + dc * 128: (e * NHC + hc) * DM + (dc + 1) * 128],
                        htld[:, (e * NHC + hc) * TG:(e * NHC + hc + 1) * TG],
                        start=False, stop=False),
                        waits=[ht_all] if (e == 0 and hc == 0) else ())
            tY = p.op("tensor", lambda h, dc=dc, psY=psY: h.matmul(
                psY[:, :], rb2[:, dc * 128:(dc + 1) * 128], combbf[:, :],
                start=False, stop=True), inc=1, waits=[])
            par = dc % 2
            ob = outb[:, par * 512:(par + 1) * 512]
            w = [tY] + ([out_free[par]] if out_free[par] else [])
            eY = p.op("vector", lambda h, dc=dc, psY=psY, ob=ob: h.scalar_tensor_tensor(
                ob, psY[:, :], sb2[:, dc:dc + 1], C.xslice(dc), ALU.add, ALU.add),
                inc=1, waits=w)
            pool.done(iY, eY)
            yap = bass.AP(y_hbm.tensor, dc * 128 * S + tg * TG, [[S, 128], [1, TG]])
            sT = p.op("sync", lambda h, yap=yap, ob=ob: h.dma_start(out=yap, in_=ob),
                      inc=16, sem=f"st{par}", waits=[eY])
            out_free[par] = sT
            st_toks.append(sT)
        C.xt_free = [eY]
        C.xn_free = tQ
        htld_free = tY

    return None


def build_ffn0():
    L = Launch("ffn0")
    x_hbm = L.dram_in("x", [DM, S])
    y_hbm = L.dram_out("y", [DM, S])
    D = declare_ffn0(L)
    emit_ffn0(L, x_hbm, y_hbm, D)
    L.p.wait("sync", L.p.tok("st0"))
    L.p.wait("sync", L.p.tok("st1"))
    return L.done()



def ffn0_host_inputs(inp, prefix="L0_ffn_"):
    g = lambda k: np.asarray(inp[prefix + k], dtype=np.float32)
    sb1 = g("sb1").reshape(4, 128).T.copy()
    sb2 = g("sb2").reshape(8, 128).T.copy()
    rb1 = g("rb1").reshape(8, 4, 128).transpose(2, 0, 1).reshape(128, 32).copy()
    sels = np.zeros((8, 1024), dtype=np.float32)
    for e in range(8):
        sels[e, e * 128:(e + 1) * 128] = 1.0
    return {
        "sW1": g("sW1"), "sW2": g("sW2"), "rW1": g("rW1"), "rW2": g("rW2"),
        "rb2": g("rb2"), "sb1l": sb1, "sb2l": sb2, "rb1l": rb1,
        "rWT": np.ascontiguousarray(g("routing_W").T),
        "ebl": g("expert_bias").reshape(8, 1),
        "sels": sels, "c_identin": np.eye(128, dtype=np.float32),
    }


def run_launch(nc, shared, per_core, n=8):
    in_maps = []
    for i in range(n):
        m = dict(shared)
        m.update(per_core[i])
        in_maps.append(m)
    res = run_bass_kernel_spmd(nc, in_maps, core_ids=list(range(n)))
    return res.results






def emit_attn_heads(L, C, pool, cfg):
    p = L.p
    NH = cfg["n_heads"]
    NT = S // 128
    NQG = S // TG
    kv_of = cfg["kv_of"]
    q_hbm, k_hbm, v_hbm, o_hbm = cfg["q_hbm"], cfg["k_hbm"], cfg["v_hbm"], cfg["o_hbm"]
    maskB = cfg["maskB"]
    maskB_tok = cfg["maskB_tok"]
    spill_toks = cfg["spill_toks"]
    scale_ap = C.constant(cfg["scale"])

    qb = [L.sb("at_q0", [65, S], BF16), L.sb("at_q1", [65, S], BF16)]
    kb = [L.sb("at_k0", [65, S], BF16), L.sb("at_k1", [65, S], BF16)]
    vb = [L.sb("at_v0", [128, NT * 65], BF16), L.sb("at_v1", [128, NT * 65], BF16)]
    esb = cfg.get("esb")
    if esb is None:
        esb = L.sb("at_esb", [128, NT * TG], BF16)
    nsq = L.sb("at_nsq", [64, S], BF16)
    nrow = L.sb("at_nrow", [1, TG], F32)
    kcols = L.sb("at_kcols", [1, 16], F32)
    ost = [L.sb("at_o0", [64, TG], BF16), L.sb("at_o1", [64, TG], BF16)]
    rrow = L.sb("at_rrow", [1, TG], F32)
    rb_sb = L.sb("at_rbsb", [64, TG], F32)
    onescol_bf = L.sb("at_onescolbf", [64, 1], BF16)

    for par in range(2):
        p.op("gpsimd", lambda h, par=par: h.memset(kb[par][64:65, :], 1.0), inc=1)
    p.op("gpsimd", lambda h: h.memset(onescol_bf[:], 1.0), inc=1)
    ones_tok = p.tok("gpsimd")

    q_free = [None, None]
    k_free = [None, None]
    v_free = [None, None]
    e_free = None
    o_free = [None, None]
    rrow_free = None
    rb_free = None
    nsq_free = None
    nrow_free = None
    kmax_tok = None
    prev_g = None

    for hh in range(NH):
        par = hh % 2
        g = kv_of(hh)
        qap = bass.AP(q_hbm.tensor, hh * 64 * S, [[S, 64], [1, S]])
        kap = bass.AP(k_hbm.tensor, g * 64 * S, [[S, 64], [1, S]])
        vap = bass.AP(v_hbm.tensor, g * 128 * (NT * 65), [[NT * 65, 128], [1, NT * 65]])
        w0 = list(spill_toks) + ([q_free[par]] if q_free[par] else [])
        qtok = p.op("sync", lambda h, qap=qap, par=par: h.dma_start(
            out=qb[par][0:64, :], in_=qap), inc=16, sem=f"ld_q{par}", waits=w0)
        w1 = [k_free[par]] if k_free[par] else []
        ktok = p.op("sync", lambda h, kap=kap, par=par: h.dma_start(
            out=kb[par][0:64, :], in_=kap), inc=16, sem=f"ld_k{par}", waits=w1)
        w2 = [v_free[par]] if v_free[par] else []
        vtok = p.op("sync", lambda h, vap=vap, par=par: h.dma_start(
            out=vb[par][:, :], in_=vap), inc=16, sem=f"ld_v{par}", waits=w2)

        if g != prev_g:
            prev_g = g
            wk = [ktok] + ([nsq_free] if nsq_free else [])
            ksq = p.op("scalar", lambda h, par=par: h.activation(
                nsq[:, :], kb[par][0:64, :], AF.Square), inc=1, waits=wk)
            rtoks = []
            for g8 in range(NCH):
                psK, iK = pool.get()
                tk = p.op("tensor", lambda h, psK=psK, g8=g8: h.matmul(
                    psK[0:1, :], onescol_bf[:, :], nsq[:, g8 * TG:(g8 + 1) * TG],
                    start=True, stop=True), inc=1,
                    waits=[ksq, ones_tok] if g8 == 0 else [ksq])
                w = [tk] + ([nrow_free] if (g8 == 0 and nrow_free) else [])
                rk = p.op("vector", lambda h, psK=psK, g8=g8: h.tensor_reduce(
                    kcols[0:1, g8:g8 + 1], psK[0:1, :], axis=AX, op=ALU.max),
                    inc=1, waits=w)
                pool.done(iK, rk)
                rtoks.append(rk)
            km1 = p.op("vector", lambda h: h.tensor_reduce(
                kcols[0:1, 8:9], kcols[0:1, 0:8], axis=AX, op=ALU.max), inc=1,
                waits=[rtoks[-1]])
            kmax_tok = p.op("scalar", lambda h: h.activation(
                kcols[0:1, 9:10], kcols[0:1, 8:9], AF.Sqrt), inc=1, waits=[km1])
            nsq_free = rtoks[-1]
        kmaxap = kcols[0:1, 9:10]

        wq = [qtok] + ([nsq_free] if nsq_free else [])
        qsq = p.op("scalar", lambda h, par=par: h.activation(
            nsq[:, :], qb[par][0:64, :], AF.Square), inc=1, waits=wq)
        brows = [None] * NQG
        for g8 in range(NQG):
            psQ, iQ = pool.get()
            tq = p.op("tensor", lambda h, psQ=psQ, g8=g8: h.matmul(
                psQ[0:1, :], onescol_bf[:, :], nsq[:, g8 * TG:(g8 + 1) * TG],
                start=True, stop=True), inc=1, waits=[qsq])
            w = [tq] + ([nrow_free] if nrow_free else [])
            sq_ = p.op("scalar", lambda h, psQ=psQ: h.activation(
                nrow[0:1, :], psQ[0:1, :], AF.Sqrt), inc=1, waits=w)
            pool.done(iQ, sq_)
            br = p.op("vector", lambda h, par=par, g8=g8, kmaxap=kmaxap: h.tensor_scalar(
                qb[par][64:65, g8 * TG:(g8 + 1) * TG], nrow[0:1, :], kmaxap, -1.0,
                ALU.mult, ALU.mult), inc=1, waits=[sq_, kmax_tok])
            nrow_free = br
            brows[g8] = br
        nsq_free = tq

        for qg in range(NQG):
            if cfg.get("anti"):
                kts = list(range(4 * qg, NT))
            else:
                kts = list(range(0, 4 * qg + 4))
            ex_toks = []
            for ki, kt in enumerate(kts):
                psb, ib = pool.get()
                wM = [brows[qg]] if ki == 0 else []
                m1 = p.op("tensor", lambda h, psb=psb, par=par, kt=kt, qg=qg: h.matmul(
                    psb[:, :], kb[par][0:65, kt * 128:(kt + 1) * 128],
                    qb[par][0:65, qg * TG:(qg + 1) * TG], start=True, stop=True),
                    inc=1, waits=wM)
                dj = kt - 4 * qg
                if 0 <= dj < 4:
                    msk = p.op("vector", lambda h, psb=psb, dj=dj: h.tensor_tensor(
                        psb[:, :], psb[:, :], maskB[:, dj * TG:(dj + 1) * TG],
                        ALU.add), inc=1, waits=[m1, maskB_tok])
                    ew = msk
                else:
                    ew = m1
                wE = [ew, C.const_tok] + ([e_free] if (ki == 0 and e_free) else [])
                ex = p.op("scalar", lambda h, psb=psb, ki=ki: h.activation(
                    esb[:, ki * TG:(ki + 1) * TG], psb[:, :], AF.Exp,
                    scale=scale_ap), inc=1, waits=wE)
                pool.done(ib, ex)
                ex_toks.append(ex)
            po, io = pool.get(hold=True)
            vm = None
            for ki, kt in enumerate(kts):
                vm = p.op("tensor", lambda h, po=po, par=par, kt=kt, ki=ki,
                          st=(ki == 0), sp=(ki == len(kts) - 1): h.matmul(
                    po[0:65, :], vb[par][:, kt * 65:(kt + 1) * 65],
                    esb[:, ki * TG:(ki + 1) * TG], start=st, stop=sp), inc=1,
                    waits=[ex_toks[ki], vtok] if ki == 0 else [ex_toks[ki]])
            e_free = vm
            wRc = [vm] + ([rrow_free] if rrow_free else [])
            rc = p.op("vector", lambda h, po=po: h.reciprocal(rrow[0:1, :],
                                                             po[64:65, :]),
                      inc=1, waits=wRc)
            psr, ir = pool.get()
            tb = p.op("tensor", lambda h, psr=psr: h.matmul(
                psr[0:64, :], C.ones_row[0:1, 0:64], rrow[0:1, :],
                start=True, stop=True), inc=1, waits=[rc, C.const_tok])
            rrow_free = tb
            wC1 = [tb] + ([rb_free] if rb_free else [])
            c1 = p.op("vector", lambda h, psr=psr: h.tensor_copy(rb_sb[:, :],
                                                                psr[0:64, :]),
                      inc=1, waits=wC1)
            pool.done(ir, c1)
            opar = qg % 2
            wC2 = [c1] + ([o_free[opar]] if o_free[opar] else [])
            c2 = p.op("vector", lambda h, po=po, opar=opar: h.tensor_tensor(
                ost[opar][:, :], po[0:64, :], rb_sb[:, :], ALU.mult), inc=1,
                waits=wC2)
            pool.done(io, c2)
            rb_free = c2
            oap = bass.AP(o_hbm.tensor, hh * 64 * S + qg * TG, [[S, 64], [1, TG]])
            so = p.op("sync", lambda h, oap=oap, opar=opar: h.dma_start(
                out=oap, in_=ost[opar][:, :]), inc=16, sem=f"st_o{opar}",
                waits=[c2])
            o_free[opar] = so
        q_free[par] = m1
        k_free[par] = m1
        v_free[par] = vm
    return [p.tok("st_o0"), p.tok("st_o1")]


def emit_wout(L, C, pool, cfg):
    p = L.p
    o_hbm, y_hbm = cfg["o_hbm"], cfg["y_hbm"]
    Wo_sb, Wo_tok = cfg["Wo_sb"], cfg["Wo_tok"]
    bout_sb, bout_tok = cfg.get("bout_sb"), cfg.get("bout_tok")
    o_done = cfg["o_done"]
    ot = cfg.get("ot_ap")
    if ot is None:
        ot = L.sb("wo_ot", [128, NCH * TG], BF16)
        outb = L.sb("wo_outb", [128, 2 * TG], F32)
    else:
        outb = cfg["outb_ap"]
    ot_free = None
    out_free = [None, None]
    last = None
    for tg in range(NTG):
        C.load_x(tg)
        w0 = list(o_done) + ([ot_free] if ot_free else [])
        for fc in range(NCH):
            oap = bass.AP(o_hbm.tensor, fc * 128 * S + tg * TG, [[S, 128], [1, TG]])
            otok = p.op("sync", lambda h, oap=oap, fc=fc: h.dma_start(
                out=ot[:, fc * TG:(fc + 1) * TG], in_=oap), inc=16, sem="ld_o",
                waits=w0 if fc == 0 else ())
        for dc in range(NCH):
            psY, iY = pool.get()
            for fc in range(NCH):
                tY = p.op("tensor", lambda h, dc=dc, fc=fc, psY=psY: h.matmul(
                    psY[:, :], Wo_sb[:, fc * DM + dc * 128: fc * DM + (dc + 1) * 128],
                    ot[:, fc * TG:(fc + 1) * TG], start=(fc == 0), stop=(fc == NCH - 1)),
                    inc=(1 if fc == NCH - 1 else 0),
                    waits=[otok, Wo_tok, C.x_tok] if fc == 0 else ())
            par = dc % 2
            ob = outb[:, par * TG:(par + 1) * TG]
            w = [tY] + ([out_free[par]] if out_free[par] else [])
            if bout_sb is not None:
                w.append(bout_tok)
                eY = p.op("vector", lambda h, dc=dc, psY=psY, ob=ob: h.scalar_tensor_tensor(
                    ob, psY[:, :], bout_sb[:, dc:dc + 1], C.xslice(dc), ALU.add,
                    ALU.add), inc=1, waits=w)
            else:
                eY = p.op("vector", lambda h, dc=dc, psY=psY, ob=ob: h.tensor_tensor(
                    ob, psY[:, :], C.xslice(dc), ALU.add), inc=1, waits=w)
            pool.done(iY, eY)
            yap = bass.AP(y_hbm.tensor, dc * 128 * S + tg * TG, [[S, 128], [1, TG]])
            sT = p.op("sync", lambda h, yap=yap, ob=ob: h.dma_start(out=yap, in_=ob),
                      inc=16, sem=f"st{par}", waits=[eY])
            out_free[par] = sT
            last = eY
        C.xt_free = [last]
        ot_free = tY
    p.wait("sync", p.tok("st0"))
    p.wait("sync", p.tok("st1"))


def make_maskB(L, C):
    mh = L.dram_in("maskb_in", [128, 4 * TG], F32)
    msb = L.sb("c_maskB", [128, 4 * TG], F32)
    tok = L.p.op("sync", lambda h: h.dma_start(out=msb[:], in_=mh[:, :]), inc=16,
                 sem="ld_w")
    return msb, tok


def host_maskB(anti=False):
    m = np.zeros((128, 4 * TG), dtype=np.float32)
    for j in range(4):
        tile = np.zeros((128, TG), dtype=np.float32)
        for sub in range(4):
            if sub < j:
                tile[:, sub * 128:(sub + 1) * 128] = 0.0 if anti else NEG
            elif sub == j:
                kk = np.arange(128)[:, None]
                qq = np.arange(128)[None, :]
                keep = (kk >= qq) if anti else (kk <= qq)
                tile[:, sub * 128:(sub + 1) * 128] = np.where(keep, 0.0, NEG)
            else:
                tile[:, sub * 128:(sub + 1) * 128] = NEG if anti else 0.0
        m[:, j * TG:(j + 1) * TG] = tile
    return m



def declare_l0a(L):
    sv, L.prefix = L.prefix, ""
    D = {
        "Wqa": L.dram_in("Wqa", [DM, 64]),
        "Wqb": L.dram_in("Wqb", [64, DM]),
        "Wkva": L.dram_in("Wkva", [DM, 32]),
        "Wkp": L.dram_in("Wkp", [32, 256]),
        "Wvp": L.dram_in("Wvp", [32, 256]),
        "Wout": L.dram_in("Wout", [DM, DM]),
        "boutl": L.dram_in("boutl", [128, 8]),
        "Pperm": L.dram_in("Pperm", [128, 128]),
        "ropeC": L.dram_in("ropeC", [64, S]),
        "ropeS": L.dram_in("ropeS", [64, S]),
    }
    L.prefix = sv
    return D


def emit_l0a(L, x_hbm, y_hbm, D):
    nc, p = L.nc, L.p
    QL, KVL, KVH = 64, 32, 4
    Wqah, Wqbh, Wkvah, Wkh, Wvh = (D["Wqa"], D["Wqb"], D["Wkva"], D["Wkp"],
                                   D["Wvp"])
    Wouth, bouth, Ph, Cth, Sth = (D["Wout"], D["boutl"], D["Pperm"], D["ropeC"],
                                  D["ropeS"])

    q_hbm = L.dram_scratch("q_sp", [DM, S])
    k_hbm = L.dram_scratch("k_sp", [KVH * 64, S])
    v_hbm = L.dram_scratch("v_sp", [KVH, 128, (S // 128) * 65])
    o_hbm = L.dram_scratch("o_sp", [DM, S])
    arenaE = L.sb("arenaE", [128, 16384], BF16)

    Wqa = L.sb("s_Wqa", [128, NCH * QL], BF16)
    Wqb = L.sb("s_Wqb", [64, DM], BF16)
    Wkva = L.sb("s_Wkva", [128, NCH * KVL], BF16)
    Wk = L.sb("s_Wk", [32, 256], BF16)
    Wv = L.sb("s_Wv", [32, 256], BF16)
    Wout = L.sb("s_Wout", [128, NCH * DM], BF16)
    bout = L.sb("s_bout", [128, 8], F32)
    Pp = L.sb("s_P", [128, 128], BF16)
    Ct = L.sb("s_ropeC", [64, S], BF16)
    St = L.sb("s_ropeS", [64, S], BF16)

    load_slabs(p, "gpsimd", Wqa, Wqah, QL, NCH, sem="ld_wg")
    p.op("gpsimd", lambda h: h.dma_start(out=Wqb[:], in_=Wqbh[:, :]), inc=16, sem="ld_wg")
    load_slabs(p, "gpsimd", Wkva, Wkvah, KVL, NCH, sem="ld_wg")
    p.op("gpsimd", lambda h: h.dma_start(out=Wk[:], in_=Wkh[:, :]), inc=16, sem="ld_wg")
    p.op("gpsimd", lambda h: h.dma_start(out=Wv[:], in_=Wvh[:, :]), inc=16, sem="ld_wg")
    load_slabs(p, "gpsimd", Wout, Wouth, DM, NCH, sem="ld_wg")
    p.op("gpsimd", lambda h: h.dma_start(out=Pp[:], in_=Ph[:, :]), inc=16, sem="ld_wg")
    p.op("sync", lambda h: h.dma_start(out=bout[:], in_=bouth[:, :]), inc=16, sem="ld_w")
    p.op("gpsimd", lambda h: h.dma_start(out=Ct[:, :], in_=Cth[:, :]), inc=16, sem="ld_wg")
    p.op("gpsimd", lambda h: h.dma_start(out=St[:, :], in_=Sth[:, :]), inc=16, sem="ld_wg")

    C = Common(L, x_hbm, xsq_ap=arenaE[:, 0:8192].bitcast(F32))
    maskB, maskB_tok = make_maskB(L, C)
    wt_s = p.tok("ld_w")
    C.ident_tok = wt_s
    wt_g = p.tok("ld_wg")
    L.init_pool()
    pool = L.pool
    cfg_esb = arenaE[:, 0:16384]

    qlb = L.sb("pj_qlb", [64, TG], BF16)
    kvlb = L.sb("pj_kvlb", [32, TG], BF16)
    qtmp = L.sb("pj_qtmp", [128, 2 * TG], BF16)
    r1 = L.sb("pj_r1", [128, 2 * TG], F32)
    r2 = L.sb("pj_r2", [128, 2 * TG], F32)
    qout = L.sb("pj_qout", [128, 2 * TG], BF16)
    vstage = L.sb("pj_vst", [128, 2 * 260], BF16)
    for pr in range(2):
        for g in range(4):
            p.op("gpsimd", lambda h, pr=pr, g=g: h.memset(
                vstage[:, pr * 260 + g * 65 + 64: pr * 260 + g * 65 + 65], 1.0), inc=1)
    vst_const = p.tok("gpsimd")

    qlb_free = None
    kvlb_free = None
    qtmp_free = [None, None]
    r1_free = [None, None]
    r2_free = [None, None]
    qout_free = [None, None]
    vst_free = [None, None]

    def rope_chunk(psq_pair, dst_ap, tgc, par, extra_w=()):
        psq, iq, tokq = psq_pair
        qt_ap = qtmp[:, par * TG:(par + 1) * TG]
        w = [tokq] + ([qtmp_free[par]] if qtmp_free[par] else [])
        ev = p.op("vector", lambda h, psq=psq, qt_ap=qt_ap: h.tensor_copy(qt_ap, psq[:, :]),
                  inc=1, waits=w)
        psp, ip = pool.get()
        tp = p.op("tensor", lambda h, psp=psp, qt_ap=qt_ap: h.matmul(
            psp[:, :], Pp[:, :], qt_ap, start=True, stop=True), inc=1,
            waits=[ev, wt_g])
        qtmp_free[par] = tp
        r1a = r1[:, par * TG:(par + 1) * TG]
        r2a = r2[:, par * TG:(par + 1) * TG]
        ct = Ct[:, tgc * TG:(tgc + 1) * TG]
        st_ = St[:, tgc * TG:(tgc + 1) * TG]
        w1 = [tokq, wt_s, wt_g] + ([r1_free[par]] if r1_free[par] else []) + list(extra_w)
        ta = p.op("vector", lambda h, psq=psq, r1a=r1a, ct=ct: h.tensor_tensor(
            r1a[0:64, :], psq[0:64, :], ct, ALU.mult), inc=1, waits=w1)
        tb_ = p.op("vector", lambda h, psq=psq, r1a=r1a, ct=ct: h.tensor_tensor(
            r1a[64:128, :], psq[64:128, :], ct, ALU.mult), inc=1, waits=[ta])
        pool.done(iq, tb_)
        w2 = [tp] + ([r2_free[par]] if r2_free[par] else [])
        tc = p.op("vector", lambda h, psp=psp, r2a=r2a, st_=st_: h.tensor_tensor(
            r2a[0:64, :], psp[0:64, :], st_, ALU.mult), inc=1, waits=w2)
        td = p.op("vector", lambda h, psp=psp, r2a=r2a, st_=st_: h.tensor_tensor(
            r2a[64:128, :], psp[64:128, :], st_, ALU.mult), inc=1, waits=[tc])
        pool.done(ip, td)
        te = p.op("vector", lambda h, r1a=r1a, r2a=r2a, dst_ap=dst_ap: h.tensor_tensor(
            dst_ap, r1a, r2a, ALU.add), inc=1, waits=[tb_, td])
        r1_free[par] = te
        r2_free[par] = te
        return te

    for tg in range(NTG):
        C.load_x(tg)
        C.ln(tg, pool)
        psql, iql = pool.get()
        for fc in range(NCH):
            tql = p.op("tensor", lambda h, psql=psql, fc=fc: h.matmul(
                psql[0:64, :], Wqa[:, fc * QL:(fc + 1) * QL], C.xnslice(fc, tg),
                start=(fc == 0), stop=(fc == NCH - 1)),
                inc=(1 if fc == NCH - 1 else 0),
                waits=[C.xn_tok, wt_g] if fc == 0 else ())
        w = [tql] + ([qlb_free] if qlb_free else [])
        eql = p.op("vector", lambda h, psql=psql: h.tensor_copy(qlb[:, :], psql[0:64, :]),
                   inc=1, waits=w)
        pool.done(iql, eql)
        pskl, ikl = pool.get()
        for fc in range(NCH):
            tkl = p.op("tensor", lambda h, pskl=pskl, fc=fc: h.matmul(
                pskl[0:32, :], Wkva[:, fc * KVL:(fc + 1) * KVL], C.xnslice(fc, tg),
                start=(fc == 0), stop=(fc == NCH - 1)),
                inc=(1 if fc == NCH - 1 else 0))
        w = [tkl] + ([kvlb_free] if kvlb_free else [])
        ekl = p.op("vector", lambda h, pskl=pskl: h.tensor_copy(kvlb[:, :], pskl[0:32, :]),
                   inc=1, waits=w)
        pool.done(ikl, ekl)
        for c in range(NCH):
            par = c % 2
            psq, iq = pool.get()
            tq = p.op("tensor", lambda h, psq=psq, c=c: h.matmul(
                psq[:, :], Wqb[:, c * 128:(c + 1) * 128], qlb[:, :],
                start=True, stop=True), inc=1, waits=[eql])
            dst = qout[:, par * TG:(par + 1) * TG]
            wq = [qout_free[par]] if qout_free[par] else []
            te = rope_chunk((psq, iq, tq), dst, tg, par, extra_w=wq)
            qap2 = bass.AP(q_hbm.tensor, c * 128 * S + tg * TG, [[S, 128], [1, TG]])
            sq = p.op("sync", lambda h, qap2=qap2, dst=dst: h.dma_start(
                out=qap2, in_=dst), inc=16, sem=f"sp_q{par}", waits=[te])
            qout_free[par] = sq
            qlb_free = tq
        for c2 in range(2):
            par = c2 % 2
            psk, ik = pool.get()
            tk = p.op("tensor", lambda h, psk=psk, c2=c2: h.matmul(
                psk[:, :], Wk[:, c2 * 128:(c2 + 1) * 128], kvlb[:, :],
                start=True, stop=True), inc=1, waits=[ekl])
            dst = qout[:, par * TG:(par + 1) * TG]
            wq = [qout_free[par]] if qout_free[par] else []
            te = rope_chunk((psk, ik, tk), dst, tg, par, extra_w=wq)
            kap2 = bass.AP(k_hbm.tensor, c2 * 128 * S + tg * TG, [[S, 128], [1, TG]])
            sk = p.op("sync", lambda h, kap2=kap2, dst=dst: h.dma_start(
                out=kap2, in_=dst), inc=16, sem=f"sp_q{par}", waits=[te])
            qout_free[par] = sk
        for tt_ in range(4):
            par = tt_ % 2
            psv, iv = pool.get()
            tv = p.op("tensor", lambda h, psv=psv, tt_=tt_: h.matmul(
                psv[:, 0:256], kvlb[:, tt_ * 128:(tt_ + 1) * 128], Wv[:, :],
                start=True, stop=True), inc=1, waits=[ekl, wt_g])
            kt = tg * 4 + tt_
            wv = [tv, vst_const] + ([vst_free[par]] if vst_free[par] else [])
            for g in range(4):
                ev = p.op("vector", lambda h, psv=psv, g=g, par=par: h.tensor_copy(
                    vstage[:, par * 260 + g * 65: par * 260 + g * 65 + 64],
                    psv[:, g * 64:(g + 1) * 64]), inc=1,
                    waits=wv if g == 0 else ())
            pool.done(iv, ev)
            for g in range(4):
                vap2 = bass.AP(v_hbm.tensor, g * 128 * (32 * 65) + kt * 65,
                               [[32 * 65, 128], [1, 65]])
                sv = p.op("sync", lambda h, vap2=vap2, g=g, par=par: h.dma_start(
                    out=vap2, in_=vstage[:, par * 260 + g * 65: par * 260 + (g + 1) * 65]),
                    inc=16, sem=f"sp_v{par}", waits=[ev] if g == 0 else ())
            vst_free[par] = sv
            kvlb_free = tv

    spill_toks = [p.tok("sp_q0"), p.tok("sp_q1"), p.tok("sp_v0"), p.tok("sp_v1")]
    o_done = emit_attn_heads(L, C, pool, dict(
        n_heads=16, kv_of=lambda hh: hh // 4, scale=0.125,
        q_hbm=q_hbm, k_hbm=k_hbm, v_hbm=v_hbm, o_hbm=o_hbm, esb=cfg_esb,
        maskB=maskB, maskB_tok=maskB_tok, spill_toks=spill_toks))
    emit_wout(L, C, pool, dict(o_hbm=o_hbm, y_hbm=y_hbm, Wo_sb=Wout, Wo_tok=wt_g,
                               bout_sb=bout, bout_tok=wt_s, o_done=o_done, ot_ap=cfg_esb[:, 0:4096],
                               outb_ap=cfg_esb[:, 4096:8192].bitcast(F32)))
    return None


def build_attn_l0a():
    L = Launch("l0a")
    x_hbm = L.dram_in("x", [DM, S])
    y_hbm = L.dram_out("y", [DM, S])
    emit_l0a(L, x_hbm, y_hbm, declare_l0a(L))
    return L.done()


def l0a_host_inputs(inp):
    g = lambda k: np.asarray(inp["L0_attn_a_" + k], dtype=np.float32)
    Wkvb = g("Wkvb").reshape(32, 4, 64, 2)
    inv = 1.0 / (10000.0 ** (np.arange(0, 64, 2, dtype=np.float64) / 64))
    t = np.arange(S, dtype=np.float64)
    fr = np.outer(t, inv)
    Crep = np.repeat(np.cos(fr).T, 2, axis=0).astype(np.float32)
    Srep = np.repeat(np.sin(fr).T, 2, axis=0).astype(np.float32)
    sign = np.tile(np.array([-1.0, 1.0], dtype=np.float32), 32)[:, None]
    P = np.zeros((128, 128), dtype=np.float32)
    for i in range(128):
        j = (i // 64) * 64 + ((i % 64) ^ 1)
        P[i, j] = 1.0
    return {
        "Wqa": g("Wqa"), "Wqb": g("Wqb"), "Wkva": g("Wkva"),
        "Wkp": np.ascontiguousarray(Wkvb[:, :, :, 0].reshape(32, 256)),
        "Wvp": np.ascontiguousarray(Wkvb[:, :, :, 1].reshape(32, 256)),
        "Wout": g("Wout"),
        "boutl": g("bout").reshape(8, 128).T.copy(),
        "Pperm": P, "ropeC": Crep, "ropeS": Srep * sign,
        "maskb_in": host_maskB(anti=False),
        "c_identin": np.eye(128, dtype=np.float32),
    }



def declare_l1a(L):
    sv, L.prefix = L.prefix, ""
    D = {
        "WQ": L.dram_in("WQ", [DM, DM]),
        "WK": L.dram_in("WK", [DM, DM]),
        "WV": L.dram_in("WV", [DM, DM]),
        "WOt": L.dram_in("WOt", [DM, DM]),
        "bql": L.dram_in("bql", [128, 8]),
        "bkl": L.dram_in("bkl", [128, 8]),
        "bvrow": L.dram_in("bvrow", [1, DM]),
        "bol": L.dram_in("bol", [128, 8]),
    }
    L.prefix = sv
    return D


def emit_l1a(L, x_hbm, y_hbm, D):
    nc, p = L.nc, L.p
    WQh, WKh, WVh, WOh = D["WQ"], D["WK"], D["WV"], D["WOt"]
    bqh, bkh, bvh, boh = D["bql"], D["bkl"], D["bvrow"], D["bol"]

    q_hbm = L.dram_scratch("q_sp", [DM, S])
    k_hbm = L.dram_scratch("k_sp", [DM, S])
    v_hbm = L.dram_scratch("v_sp", [16, 128, (S // 128) * 65])
    o_hbm = L.dram_scratch("o_sp", [DM, S])
    arenaE = L.sb("arenaE", [128, 16384], BF16)

    WQ = L.sb("s_WQ", [128, NCH * DM], BF16)
    WK = L.sb("s_WK", [128, NCH * DM], BF16)
    WV = L.sb("s_WV", [128, NCH * DM], BF16)
    WO = L.sb("s_WO", [128, NCH * DM], BF16)
    bq = L.sb("s_bq", [128, 8], F32)
    bk = L.sb("s_bk", [128, 8], F32)
    bo = L.sb("s_bo", [128, 8], F32)
    bvr = L.sb("s_bvr", [1, DM], BF16)
    onesbf = L.sb("s_onesbf", [1, 128], BF16)

    load_slabs(p, "gpsimd", WQ, WQh, DM, NCH, sem="ld_wg")
    load_slabs(p, "gpsimd", WK, WKh, DM, NCH, sem="ld_wg")
    load_slabs(p, "gpsimd", WV, WVh, DM, NCH, sem="ld_wg")
    load_slabs(p, "gpsimd", WO, WOh, DM, NCH, sem="ld_wg")
    p.op("gpsimd", lambda h: h.dma_start(out=bvr[:], in_=bvh[:, :]), inc=16, sem="ld_wg")
    p.op("sync", lambda h: h.dma_start(out=bq[:], in_=bqh[:, :]), inc=16, sem="ld_w")
    p.op("sync", lambda h: h.dma_start(out=bk[:], in_=bkh[:, :]), inc=16, sem="ld_w")
    p.op("sync", lambda h: h.dma_start(out=bo[:], in_=boh[:, :]), inc=16, sem="ld_w")
    p.op("gpsimd", lambda h: h.memset(onesbf[:], 1.0), inc=1)
    ones_tok = p.tok("gpsimd")

    C = Common(L, x_hbm, xsq_ap=arenaE[:, 0:8192].bitcast(F32), rows_cols=2568)
    maskB, maskB_tok = make_maskB(L, C)
    wt_s = p.tok("ld_w")
    C.ident_tok = wt_s
    wt_g = p.tok("ld_wg")
    L.init_pool()
    pool = L.pool
    cfg_esb = arenaE[:, 0:16384]

    qst = L.sb("pj_qst", [128, 2 * TG], BF16)
    vstage = L.sb("pj_vst", [128, 2 * 520], BF16)
    for pr in range(2):
        for g in range(8):
            p.op("gpsimd", lambda h, pr=pr, g=g: h.memset(
                vstage[:, pr * 520 + g * 65 + 64: pr * 520 + g * 65 + 65], 1.0), inc=1)
    vst_const = p.tok("gpsimd")

    qst_free = [None, None]
    vst_free = [None, None]
    for tg in range(NTG):
        C.load_x(tg)
        C.ln(tg, pool)
        for (W_sb, b_sb, dsth, pfx) in ((WQ, bq, q_hbm, "q"), (WK, bk, k_hbm, "k")):
            for c in range(NCH):
                par = c % 2
                ps, ib = pool.get()
                for fc in range(NCH):
                    tq = p.op("tensor", lambda h, ps=ps, fc=fc, c=c, W_sb=W_sb: h.matmul(
                        ps[:, :], W_sb[:, fc * DM + c * 128: fc * DM + (c + 1) * 128],
                        C.xnslice(fc, tg), start=(fc == 0), stop=(fc == NCH - 1)),
                        inc=(1 if fc == NCH - 1 else 0),
                        waits=[C.xn_tok, wt_g] if fc == 0 else ())
                dst = qst[:, par * TG:(par + 1) * TG]
                w = [tq, wt_s] + ([qst_free[par]] if qst_free[par] else [])
                ev = p.op("vector", lambda h, ps=ps, dst=dst, c=c, b_sb=b_sb: h.tensor_scalar(
                    dst, ps[:, :], b_sb[:, c:c + 1], None, ALU.add), inc=1, waits=w)
                pool.done(ib, ev)
                qap2 = bass.AP(dsth.tensor, c * 128 * S + tg * TG, [[S, 128], [1, TG]])
                sq = p.op("sync", lambda h, qap2=qap2, dst=dst: h.dma_start(
                    out=qap2, in_=dst), inc=16, sem=f"sp_{pfx}{par}", waits=[ev])
                qst_free[par] = sq
        for tt_ in range(4):
            for half in range(2):
                par = (tt_ * 2 + half) % 2
                psv, iv = pool.get()
                for fc in range(NCH):
                    tv = p.op("tensor", lambda h, psv=psv, fc=fc, half=half, tt_=tt_: h.matmul(
                        psv[:, :], C.xnslice(fc, tg, width=128, off=tt_ * 128),
                        WV[:, fc * DM + half * 512: fc * DM + (half + 1) * 512],
                        start=(fc == 0), stop=False),
                        waits=[C.xn_tok, wt_g] if fc == 0 else ())
                tv = p.op("tensor", lambda h, psv=psv, half=half: h.matmul(
                    psv[:, :], onesbf[0:1, 0:128], bvr[0:1, half * 512:(half + 1) * 512],
                    start=False, stop=True), inc=1, waits=[ones_tok])
                kt = tg * 4 + tt_
                wv = [tv, vst_const] + ([vst_free[par]] if vst_free[par] else [])
                for gg in range(8):
                    ev = p.op("vector", lambda h, psv=psv, gg=gg, par=par: h.tensor_copy(
                        vstage[:, par * 520 + gg * 65: par * 520 + gg * 65 + 64],
                        psv[:, gg * 64:(gg + 1) * 64]), inc=1,
                        waits=wv if gg == 0 else ())
                pool.done(iv, ev)
                for gg in range(8):
                    g = half * 8 + gg
                    vap2 = bass.AP(v_hbm.tensor, g * 128 * (32 * 65) + kt * 65,
                                   [[32 * 65, 128], [1, 65]])
                    sv = p.op("sync", lambda h, vap2=vap2, gg=gg, par=par: h.dma_start(
                        out=vap2,
                        in_=vstage[:, par * 520 + gg * 65: par * 520 + (gg + 1) * 65]),
                        inc=16, sem=f"sp_v{par}", waits=[ev] if gg == 0 else ())
                vst_free[par] = sv

    spill_toks = [p.tok("sp_q0"), p.tok("sp_q1"), p.tok("sp_k0"), p.tok("sp_k1"),
                  p.tok("sp_v0"), p.tok("sp_v1")]
    o_done = emit_attn_heads(L, C, pool, dict(
        n_heads=16, kv_of=lambda hh: hh, scale=0.125,
        q_hbm=q_hbm, k_hbm=k_hbm, v_hbm=v_hbm, o_hbm=o_hbm, esb=cfg_esb,
        maskB=maskB, maskB_tok=maskB_tok, spill_toks=spill_toks))
    emit_wout(L, C, pool, dict(o_hbm=o_hbm, y_hbm=y_hbm, Wo_sb=WO, Wo_tok=wt_g,
                               bout_sb=bo, bout_tok=wt_s, o_done=o_done, ot_ap=cfg_esb[:, 0:4096],
                               outb_ap=cfg_esb[:, 4096:8192].bitcast(F32)))
    return None


def build_attn_l1a():
    L = Launch("l1a")
    x_hbm = L.dram_in("x", [DM, S])
    y_hbm = L.dram_out("y", [DM, S])
    emit_l1a(L, x_hbm, y_hbm, declare_l1a(L))
    return L.done()


def l1a_host_inputs(inp):
    g = lambda k: np.asarray(inp["L1_attn_a_" + k], dtype=np.float32)
    WQ = np.ascontiguousarray(g("W_q").transpose(1, 0, 2).reshape(DM, DM))
    WK = np.ascontiguousarray(g("W_k").transpose(1, 0, 2).reshape(DM, DM))
    WV = np.ascontiguousarray(g("W_v").transpose(1, 0, 2).reshape(DM, DM))
    return {
        "WQ": WQ, "WK": WK, "WV": WV,
        "WOt": np.ascontiguousarray(g("W_o").T),
        "bql": g("b_q").reshape(-1).reshape(8, 128).T.copy(),
        "bkl": g("b_k").reshape(-1).reshape(8, 128).T.copy(),
        "bvrow": g("b_v").reshape(1, DM).copy(),
        "bol": g("b_o").reshape(8, 128).T.copy(),
        "maskb_in": host_maskB(anti=False),
        "c_identin": np.eye(128, dtype=np.float32),
    }



def declare_l0b(L):
    sv, L.prefix = L.prefix, ""
    D = {
        "Wqkv": L.dram_in("Wqkv", [DM, 3 * DM]),
        "maskb_anti": L.dram_in("maskb_anti", [128, 4 * TG]),
    }
    L.prefix = sv
    return D


def emit_l0b(L, x_hbm, y_hbm, D):
    nc, p = L.nc, L.p
    Wh = D["Wqkv"]
    NT = S // 128

    q_hbm = L.dram_scratch("q_sp", [DM, S])
    k_hbm = L.dram_scratch("k_sp", [DM, S])
    v_hbm = L.dram_scratch("v_sp", [NT, 128, 1025])

    W = L.sb("s_W", [128, NCH * 3 * DM], BF16)
    load_slabs(p, "gpsimd", W, Wh, 3 * DM, NCH, sem="ld_wg")
    onesbf = L.sb("s_onesbf", [1, 128], BF16)
    onescolbf = L.sb("s_onescolbf", [128, 1], BF16)
    p.op("gpsimd", lambda h: h.memset(onesbf[:], 1.0), inc=1)
    p.op("gpsimd", lambda h: h.memset(onescolbf[:], 1.0), inc=1)

    C = Common(L, x_hbm)
    msb = L.sb("c_maskB", [128, 4 * TG], F32)
    maskB_tok = p.op("sync", lambda h: h.dma_start(out=msb[:], in_=D["maskb_anti"][:, :]),
                     inc=16, sem="ld_w")
    maskB = msb
    wt_s = p.tok("ld_w")
    C.ident_tok = wt_s
    wt_g = p.tok("ld_wg")
    scale_ap = C.constant(1.0 / 32.0)
    ones_tok = p.tok("gpsimd")
    L.init_pool()
    pool = L.pool

    qst = L.sb("pj_qst", [128, 2 * TG], BF16)
    vstage = L.sb("pj_vst", [128, 2 * 1025], BF16)
    for pr in range(2):
        p.op("gpsimd", lambda h, pr=pr: h.memset(
            vstage[:, pr * 1025 + 1024: pr * 1025 + 1025], 1.0), inc=1)
    vst_const = p.tok("gpsimd")

    qst_free = [None, None]
    vst_free = [None, None]
    for tg in range(NTG):
        C.load_x(tg)
        C.ln(tg, pool)
        for part, dsth, pfx in ((0, q_hbm, "q"), (1, k_hbm, "k")):
            for c in range(NCH):
                par = c % 2
                ps, ib = pool.get()
                for fc in range(NCH):
                    tq = p.op("tensor", lambda h, ps=ps, fc=fc, c=c, part=part: h.matmul(
                        ps[:, :],
                        W[:, fc * 3 * DM + part * DM + c * 128: fc * 3 * DM + part * DM + (c + 1) * 128],
                        C.xnslice(fc, tg), start=(fc == 0), stop=(fc == NCH - 1)),
                        inc=(1 if fc == NCH - 1 else 0),
                        waits=[C.xn_tok, wt_g] if fc == 0 else ())
                dst = qst[:, par * TG:(par + 1) * TG]
                w = [tq] + ([qst_free[par]] if qst_free[par] else [])
                ev = p.op("vector", lambda h, ps=ps, dst=dst: h.tensor_copy(dst, ps[:, :]),
                          inc=1, waits=w)
                pool.done(ib, ev)
                qap2 = bass.AP(dsth.tensor, c * 128 * S + tg * TG, [[S, 128], [1, TG]])
                sq = p.op("sync", lambda h, qap2=qap2, dst=dst: h.dma_start(
                    out=qap2, in_=dst), inc=16, sem=f"sp_{pfx}{par}", waits=[ev])
                qst_free[par] = sq
        for tt_ in range(4):
            par = tt_ % 2
            psv, iv = pool.get()
            psv2, iv2 = pool.get()
            evs = []
            for half in range(2):
                psx = psv if half == 0 else psv2
                for fc in range(NCH):
                    tv = p.op("tensor", lambda h, psx=psx, fc=fc, half=half, tt_=tt_: h.matmul(
                        psx[:, :], C.xnslice(fc, tg, width=128, off=tt_ * 128),
                        W[:, fc * 3 * DM + 2 * DM + half * 512: fc * 3 * DM + 2 * DM + (half + 1) * 512],
                        start=(fc == 0), stop=(fc == NCH - 1)),
                        inc=(1 if fc == NCH - 1 else 0),
                        waits=[C.xn_tok] if fc == 0 else ())
                w = [tv, vst_const] + ([vst_free[par]] if (half == 0 and vst_free[par]) else [])
                ev = p.op("vector", lambda h, psx=psx, half=half, par=par: h.tensor_copy(
                    vstage[:, par * 1025 + half * 512: par * 1025 + (half + 1) * 512],
                    psx[:, :]), inc=1, waits=w)
                evs.append(ev)
            pool.done(iv, evs[0])
            pool.done(iv2, evs[1])
            kt = tg * 4 + tt_
            vap2 = bass.AP(v_hbm.tensor, kt * 128 * 1025, [[1025, 128], [1, 1025]])
            sv = p.op("sync", lambda h, vap2=vap2, par=par: h.dma_start(
                out=vap2, in_=vstage[:, par * 1025:(par + 1) * 1025]), inc=16,
                sem=f"sp_v{par}", waits=[evs[0], evs[1]])
            vst_free[par] = sv

    spill = [p.tok("sp_q0"), p.tok("sp_q1"), p.tok("sp_k0"), p.tok("sp_k1"),
             p.tok("sp_v0"), p.tok("sp_v1")]

    qg_b = [L.sb("at_q0", [128, NCH * TG], BF16), L.sb("at_q1", [128, NCH * TG], BF16)]
    kt_b = [L.sb("at_k0", [128, NCH * 128], BF16), L.sb("at_k1", [128, NCH * 128], BF16)]
    vt_b = [L.sb("at_v0", [128, 1025], BF16), L.sb("at_v1", [128, 1025], BF16)]
    esb = L.sb("at_e", [128, NT * TG], BF16)
    negm = L.sb("at_negm", [1, S], BF16)
    mcols = L.sb("at_mcols", [128, NT], F32)
    scol = L.sb("at_scol", [128, 1], F32)
    rrow = L.sb("at_rrow", [1, TG], F32)
    rb_sb = L.sb("at_rb", [128, TG], F32)
    outb = L.sb("at_outb", [128, 2 * TG], F32)

    qb_free = [None, None]
    kb_free = [None, None]
    vb_free = [None, None]
    scol_free = None
    mcols_free = None

    def kload(kt, src, dst_b, free, pfx):
        par = kt % 2
        ap = bass.AP(src.tensor, kt * 128, [[128 * S, NCH], [S, 128], [1, 128]])
        tok = None
        for fc in range(NCH):
            a2 = bass.AP(src.tensor, fc * 128 * S + kt * 128, [[S, 128], [1, 128]])
            w = list(spill) + ([free[par]] if free[par] else [])
            tok = p.op("sync", lambda h, a2=a2, fc=fc, par=par: h.dma_start(
                out=dst_b[par][:, fc * 128:(fc + 1) * 128], in_=a2), inc=16,
                sem=f"ld_{pfx}{par}", waits=w if fc == 0 else ())
        return tok, par

    red_tok = {}
    for kg in range(NCH):
        par = kg % 2
        ktoks = []
        for fc in range(NCH):
            a2 = bass.AP(k_hbm.tensor, fc * 128 * S + kg * TG, [[S, 128], [1, TG]])
            w = list(spill) + ([qb_free[par]] if qb_free[par] else [])
            ktok = p.op("sync", lambda h, a2=a2, fc=fc, par=par: h.dma_start(
                out=qg_b[par][:, fc * TG:(fc + 1) * TG], in_=a2), inc=16,
                sem=f"ld_kg{par}", waits=w if fc == 0 else ())
        for qt in range(4 * kg + 4):
            qpar = qt % 2
            qtoks = []
            for fc in range(NCH):
                a2 = bass.AP(q_hbm.tensor, fc * 128 * S + qt * 128, [[S, 128], [1, 128]])
                w = list(spill) + ([kb_free[qpar]] if kb_free[qpar] else [])
                qtok = p.op("sync", lambda h, a2=a2, fc=fc, qpar=qpar: h.dma_start(
                    out=kt_b[qpar][:, fc * 128:(fc + 1) * 128], in_=a2), inc=16,
                    sem=f"ld_qt{qpar}", waits=w if fc == 0 else ())
            ps, ib = pool.get()
            for fc in range(NCH):
                t1 = p.op("tensor", lambda h, ps=ps, fc=fc, par=par, qpar=qpar: h.matmul(
                    ps[:, :], kt_b[qpar][:, fc * 128:(fc + 1) * 128],
                    qg_b[par][:, fc * TG:(fc + 1) * TG],
                    start=(fc == 0), stop=(fc == NCH - 1)),
                    inc=(1 if fc == NCH - 1 else 0),
                    waits=[qtok, ktok] if fc == 0 else ())
            kb_free[qpar] = t1
            w = [t1] + ([scol_free] if scol_free else [])
            rtok = p.op("vector", lambda h, ps=ps: h.tensor_reduce(
                scol[:, :], ps[:, :], axis=AX, op=ALU.max), inc=1, waits=w)
            pool.done(ib, rtok)
            if kg == qt // 4:
                w = [rtok] + ([mcols_free] if mcols_free else [])
                mtok = p.op("vector", lambda h, qt=qt: h.tensor_copy(
                    mcols[:, qt:qt + 1], scol[:, :]), inc=1, waits=w)
            else:
                mtok = p.op("vector", lambda h, qt=qt: h.tensor_tensor(
                    mcols[:, qt:qt + 1], mcols[:, qt:qt + 1], scol[:, :], ALU.max),
                    inc=1, waits=[rtok])
            scol_free = mtok
            red_tok[qt] = mtok
        qb_free[par] = t1
    ngm_toks = [None] * NTG
    for qg in range(NTG):
        psT, iT = pool.get()
        for j in range(4):
            tT = p.op("tensor", lambda h, psT=psT, j=j, qg=qg: h.transpose(
                psT[0:1, j * 128:(j + 1) * 128], mcols[:, 4 * qg + j:4 * qg + j + 1],
                C.ident[:]), inc=1, waits=[red_tok[4 * qg + j], C.ident_tok])
        mcols_free = tT
        ngm = p.op("vector", lambda h, psT=psT, qg=qg: h.tensor_scalar(
            negm[0:1, qg * TG:(qg + 1) * TG], psT[0:1, 0:TG], -1.0, None, ALU.mult),
            inc=1, waits=[tT])
        pool.done(iT, ngm)
        ngm_toks[qg] = ngm

    qb_free = [None, None]
    kb_free = [None, None]
    e_free = None
    out_free = [None, None]
    rb_free = None
    rrow_free = None
    for qg in range(NTG):
        par = qg % 2
        for fc in range(NCH):
            a2 = bass.AP(q_hbm.tensor, fc * 128 * S + qg * TG, [[S, 128], [1, TG]])
            w = [ngm_toks[qg]] + ([qb_free[par]] if qb_free[par] else [])
            qtok = p.op("sync", lambda h, a2=a2, fc=fc, par=par: h.dma_start(
                out=qg_b[par][:, fc * TG:(fc + 1) * TG], in_=a2), inc=16,
                sem=f"ld_qg{par}", waits=w if fc == 0 else ())
        kts = list(range(4 * qg, NT))
        psl, il = pool.get(hold=True)
        for ki, kt in enumerate(kts):
            ktok, kpar = kload(kt, k_hbm, kt_b, kb_free, "kb")
            ps, ib = pool.get()
            for fc in range(NCH):
                m1 = p.op("tensor", lambda h, ps=ps, fc=fc, kpar=kpar, par=par: h.matmul(
                    ps[:, :], kt_b[kpar][:, fc * 128:(fc + 1) * 128],
                    qg_b[par][:, fc * TG:(fc + 1) * TG], start=(fc == 0), stop=False),
                    waits=[ktok, qtok] if fc == 0 else ())
            m2 = p.op("tensor", lambda h, ps=ps, qg=qg: h.matmul(
                ps[:, :], onesbf[0:1, 0:128], negm[0:1, qg * TG:(qg + 1) * TG],
                start=False, stop=True), inc=1, waits=[ones_tok])
            kb_free[kpar] = m2
            if kt < 4 * qg + 4:
                j = kt - 4 * qg
                msk = p.op("vector", lambda h, ps=ps, j=j: h.tensor_tensor(
                    ps[:, :], ps[:, :], maskB[:, j * TG:(j + 1) * TG], ALU.add),
                    inc=1, waits=[m2, maskB_tok])
                ew = msk
            else:
                ew = m2
            wE = [ew, C.const_tok] + ([e_free] if (ki == 0 and e_free) else [])
            ex = p.op("scalar", lambda h, ps=ps, ki=ki: h.activation(
                esb[:, ki * TG:(ki + 1) * TG], ps[:, :], AF.Exp, scale=scale_ap),
                inc=1, waits=wE)
            pool.done(ib, ex)
            tl = p.op("tensor", lambda h, psl=psl, ki=ki, last=(ki == len(kts) - 1): h.matmul(
                psl[0:1, :], onescolbf[:, :], esb[:, ki * TG:(ki + 1) * TG],
                start=(ki == 0), stop=last), inc=(1 if ki == len(kts) - 1 else 0),
                waits=[ex])
        wR = [tl] + ([rrow_free] if rrow_free else [])
        rc = p.op("vector", lambda h, psl=psl: h.reciprocal(rrow[0:1, :], psl[0:1, :]),
                  inc=1, waits=wR)
        pool.done(il, rc)
        psr, ir = pool.get()
        tb = p.op("tensor", lambda h, psr=psr: h.matmul(
            psr[:, :], C.ones_row[:, :], rrow[0:1, :], start=True, stop=True),
            inc=1, waits=[rc])
        rrow_free = tb
        wC = [tb] + ([rb_free] if rb_free else [])
        c1 = p.op("vector", lambda h, psr=psr: h.tensor_copy(rb_sb[:, :], psr[:, :]),
                  inc=1, waits=wC)
        pool.done(ir, c1)
        C.load_x(qg)
        ops = []
        for dc in range(NCH):
            po, io = pool.get(hold=True)
            ops.append((po, io))
        vm = None
        for ki, kt in enumerate(kts):
            vpar = kt % 2
            vap2 = bass.AP(v_hbm.tensor, kt * 128 * 1025, [[1025, 128], [1, 1025]])
            w = list(spill) + ([vb_free[vpar]] if vb_free[vpar] else [])
            vtok = p.op("sync", lambda h, vap2=vap2, vpar=vpar: h.dma_start(
                out=vt_b[vpar][:, :], in_=vap2), inc=16, sem=f"ld_v{vpar}", waits=w)
            for dc in range(NCH):
                po, io = ops[dc]
                vm = p.op("tensor", lambda h, po=po, vpar=vpar, dc=dc, ki=ki,
                          st=(ki == 0), sp=(ki == len(kts) - 1): h.matmul(
                    po[:, :], vt_b[vpar][:, dc * 128:(dc + 1) * 128],
                    esb[:, ki * TG:(ki + 1) * TG], start=st, stop=sp),
                    inc=(1 if dc == NCH - 1 else 0), waits=[vtok] if dc == 0 else ())
            vb_free[vpar] = vm
        e_free = vm
        for dc in range(NCH):
            po, io = ops[dc]
            opar = dc % 2
            ob = outb[:, opar * TG:(opar + 1) * TG]
            w = [vm, c1, C.x_tok] + ([out_free[opar]] if out_free[opar] else [])
            e1 = p.op("vector", lambda h, po=po, ob=ob: h.tensor_tensor(
                ob, po[:, :], rb_sb[:, :], ALU.mult), inc=1, waits=w)
            pool.done(io, e1)
            e2 = p.op("vector", lambda h, ob=ob, dc=dc: h.tensor_tensor(
                ob, ob, C.xslice(dc), ALU.add), inc=1, waits=[e1])
            rb_free = e1
            yap = bass.AP(y_hbm.tensor, dc * 128 * S + qg * TG, [[S, 128], [1, TG]])
            sT = p.op("sync", lambda h, yap=yap, ob=ob: h.dma_start(out=yap, in_=ob),
                      inc=16, sem=f"st{opar}", waits=[e2])
            out_free[opar] = sT
        C.xt_free = [e2]
        qb_free[par] = m1
    return None


def build_attn_l0b():
    L = Launch("l0b")
    x_hbm = L.dram_in("x", [DM, S])
    y_hbm = L.dram_out("y", [DM, S])
    emit_l0b(L, x_hbm, y_hbm, declare_l0b(L))
    L.p.wait("sync", L.p.tok("st0"))
    L.p.wait("sync", L.p.tok("st1"))
    return L.done()


def l0b_host_inputs(inp):
    return {
        "Wqkv": np.asarray(inp["L0_attn_b_W"], dtype=np.float32),
        "maskb_anti": host_maskB(anti=True),
        "c_identin": np.eye(128, dtype=np.float32),
    }



def declare_ffn1(L):
    sv, L.prefix = L.prefix, ""
    D = {
        "W1": L.dram_in("W1", [8, DM, 4096]),
        "W2": L.dram_in("W2", [8, 4096, DM]),
        "rWT1": L.dram_in("rWT1", [DM, 8]),
        "gaml": L.dram_in("gaml", [128, 8]),
        "betl": L.dram_in("betl", [128, 8]),
        "sels1": L.dram_in("sels1", [8, 1024]),
        "iota32": L.dram_in("iota32", [128, 64], mybir.dt.int32),
        "ectl": L.dram_in("ectl", [8, 1]),
        "identin2": L.dram_in("identin2", [128, 128]),
    }
    L.prefix = sv
    return D


def emit_ffn1(L, x_hbm, y_hbm, D):
    nc, p = L.nc, L.p
    H1 = 4096
    NHC1 = H1 // 128
    W1h, W2h, rWTh = D["W1"], D["W2"], D["rWT1"]
    gamh, beth, selsh = D["gaml"], D["betl"], D["sels1"]
    xn_sp = L.dram_scratch("xn_sp", [DM, S])

    rWT = L.sb("s_rWT", [128, NCH * 8], F32)
    rGT = L.sb("s_rGT", [128, NCH * 8], F32)
    gam = L.sb("s_gam", [128, 8], F32)
    bet = L.sb("s_bet", [128, 8], F32)
    sels = L.sb("s_sels", [8, 1024], BF16)
    colsg = L.sb("s_colsg", [8, 1], F32)
    cbeta = L.sb("s_cbeta", [8, 1], F32)
    W1 = L.sb("s_W1", [128, NCH * H1], BF16)
    W2 = L.sb("s_W2", [128, NHC1 * DM], BF16)
    arenaA = L.sb("arenaA", [128, 16384], BF16)
    arenaB = L.sb("arenaB", [128, 4096], BF16)

    load_slabs(p, "sync", rWT, rWTh, 8, NCH)
    p.op("sync", lambda h: h.dma_start(out=gam[:], in_=gamh[:, :]), inc=16, sem="ld_w")
    p.op("sync", lambda h: h.dma_start(out=bet[:], in_=beth[:, :]), inc=16, sem="ld_w")
    p.op("gpsimd", lambda h: h.dma_start(out=sels[:], in_=selsh[:, :]), inc=16, sem="ld_wg")
    wt_g = p.tok("ld_wg")

    C = Common(L, x_hbm,
               xt_ap=arenaA[:, 0:8192].bitcast(F32),
               xsq_ap=arenaA[:, 8192:16384].bitcast(F32))
    wt_s = p.tok("ld_w")
    C.ident_tok = wt_s
    L.init_pool()
    pool = L.pool

    rg_toks = []
    for fc in range(NCH):
        t = p.op("vector", lambda h, fc=fc: h.tensor_scalar(
            rGT[:, fc * 8:(fc + 1) * 8], rWT[:, fc * 8:(fc + 1) * 8],
            gam[:, fc:fc + 1], None, ALU.mult), inc=1, waits=[wt_s])
        rg_toks.append(t)
    psK, iK = pool.get()
    for fc in range(NCH):
        tk = p.op("tensor", lambda h, psK=psK, fc=fc: h.matmul(
            psK[0:1, 0:8], C.ones_col[:], rGT[:, fc * 8:(fc + 1) * 8],
            start=(fc == 0), stop=(fc == NCH - 1)),
            inc=(1 if fc == NCH - 1 else 0),
            waits=[rg_toks[-1], C.const_tok] if fc == 0 else ())
    csr = C.rows[0:1, 2560:2568]
    e1 = p.op("vector", lambda h: h.tensor_copy(csr, psK[0:1, 0:8]), inc=1, waits=[tk])
    pool.done(iK, e1)
    psK2, iK2 = pool.get()
    tk2 = p.op("tensor", lambda h, psK2=psK2: h.transpose(
        psK2[0:8, 0:1], csr, C.ident[0:1, 0:1]), inc=1, waits=[e1, C.ident_tok])
    e2c = p.op("vector", lambda h, psK2=psK2: h.tensor_copy(colsg[:], psK2[0:8, 0:1]),
               inc=1, waits=[tk2])
    pool.done(iK2, e2c)
    rb_toks = []
    for fc in range(NCH):
        t = p.op("vector", lambda h, fc=fc: h.tensor_scalar(
            C.xsq[:, fc * 8:fc * 8 + 8], rWT[:, fc * 8:(fc + 1) * 8],
            bet[:, fc:fc + 1], None, ALU.mult), inc=1, waits=[wt_s, e2c])
        rb_toks.append(t)
    psK3, iK3 = pool.get()
    for fc in range(NCH):
        tk3 = p.op("tensor", lambda h, psK3=psK3, fc=fc: h.matmul(
            psK3[0:1, 0:8], C.ones_col[:], C.xsq[:, fc * 8:fc * 8 + 8],
            start=(fc == 0), stop=(fc == NCH - 1)),
            inc=(1 if fc == NCH - 1 else 0),
            waits=[rb_toks[-1]] if fc == 0 else ())
    e3 = p.op("vector", lambda h, psK3=psK3: h.tensor_copy(csr, psK3[0:1, 0:8]),
              inc=1, waits=[tk3])
    pool.done(iK3, e3)
    psK4, iK4 = pool.get()
    tk4 = p.op("tensor", lambda h, psK4=psK4: h.transpose(
        psK4[0:8, 0:1], csr, C.ident[0:1, 0:1]), inc=1, waits=[e3])
    e4 = p.op("vector", lambda h, psK4=psK4: h.tensor_copy(cbeta[:], psK4[0:8, 0:1]),
              inc=1, waits=[tk4])
    pool.done(iK4, e4)
    pre_tok = e4
    C.xsq_free = tk3

    zview = arenaB[:, 0:4096].bitcast(F32)
    z = zview[0:8, 0:TG]
    relu = zview[0:8, TG:2 * TG]
    ach = zview[0:8, 2 * TG:3 * TG]
    comb_all = L.sb("comb_all", [8, S], BF16)
    ztok = L.sb("ztok", [128, 4 * 8], F32)
    vcols = L.sb("vcols", [128, 16], F32)
    eqb = L.sb("eqb", [128, 4 * 8], F32)
    xnt = C.xn
    combe = L.sb("combe", [128, TG], BF16)
    htld = arenaA[:, 0:16384]
    ytile = arenaB[:, 0:2048].bitcast(F32)
    outb = arenaB[:, 2048:4096].bitcast(F32)
    inv15 = C.constant(1.0 / 1.5, parts=128)

    xn_st = [None] * NTG
    comb_toks = [None] * NTG
    ycp_toks = {}
    for tg in range(NTG):
        C.load_x(tg)
        toks = C.ln(tg, pool, affine=(gam, bet, wt_s))
        mu = C.rows[0:1, 0:512]
        rstd = C.rows[0:1, 1536:2048]
        for fc in range(NCH):
            ap = bass.AP(xn_sp.tensor, fc * 128 * S + tg * TG, [[S, 128], [1, TG]])
            st_x = p.op("sync", lambda h, ap=ap, fc=fc: h.dma_start(
                out=ap, in_=C.xnslice(fc, tg)), inc=16, sem="sp_xn",
                waits=[toks[fc]])
        xn_st[tg] = st_x
        C.xn_free = st_x
        for fc in range(NCH):
            scr = C.xsq[:, fc * TG:(fc + 1) * TG]
            tYI = p.op("vector", lambda h, scr=scr, fc=fc: h.tensor_tensor(
                scr, scr, C.xslice(fc), ALU.add), inc=1, waits=[toks[fc]])
            ap = bass.AP(y_hbm.tensor, fc * 128 * S + tg * TG, [[S, 128], [1, TG]])
            yc = p.op("sync", lambda h, ap=ap, scr=scr: h.dma_start(
                out=ap, in_=scr), inc=16, sem="st_y0", waits=[tYI])
            ycp_toks[(tg, fc)] = yc
        C.xsq_free = yc
        psR, iR = pool.get()
        for fc in range(NCH):
            tR = p.op("tensor", lambda h, psR=psR, fc=fc: h.matmul(
                psR[0:8, :], rGT[:, fc * 8:(fc + 1) * 8], C.xslice(fc),
                start=(fc == 0), stop=(fc == NCH - 1)),
                inc=(1 if fc == NCH - 1 else 0),
                waits=[C.x_tok, rg_toks[-1]] if fc == 0 else ())
        psM, iM = pool.get()
        tM = p.op("tensor", lambda h, psM=psM: h.matmul(
            psM[0:8, :], C.ones_row[0:1, 0:8], mu, start=True, stop=True), inc=1,
            waits=[C.mu_tok])
        psS, iS = pool.get()
        tS = p.op("tensor", lambda h, psS=psS: h.matmul(
            psS[0:8, :], C.ones_row[0:1, 0:8], rstd, start=True, stop=True), inc=1,
            waits=[C.mu_tok])
        zt1 = p.op("vector", lambda h, psM=psM: h.tensor_scalar(
            ach, psM[0:8, :], colsg[:], None, ALU.mult), inc=1, waits=[tM, pre_tok])
        pool.done(iM, zt1)
        zt2 = p.op("vector", lambda h, psR=psR: h.tensor_tensor(
            relu, psR[0:8, :], ach, ALU.subtract), inc=1, waits=[tR, zt1])
        pool.done(iR, zt2)
        zt3 = p.op("vector", lambda h, psS=psS: h.tensor_tensor(
            ach, relu, psS[0:8, :], ALU.mult), inc=1, waits=[tS, zt2])
        pool.done(iS, zt3)
        zt4 = p.op("vector", lambda h: h.tensor_scalar(z, ach, cbeta[:], None,
                                                       ALU.add), inc=1, waits=[zt3])
        ctoks = []
        for tt in range(4):
            psT, iT = pool.get()
            tT = p.op("tensor", lambda h, psT=psT, tt=tt: h.transpose(
                psT[:, 0:8], z[:, tt * 128:(tt + 1) * 128], C.ident[0:8, 0:8]),
                inc=1, waits=[zt4, C.ident_tok])
            eT = p.op("vector", lambda h, psT=psT, tt=tt: h.tensor_copy(
                ztok[:, tt * 8:(tt + 1) * 8], psT[:, 0:8]), inc=1, waits=[tT])
            pool.done(iT, eT)
            zt_ap = ztok[:, tt * 8:(tt + 1) * 8]
            v1 = vcols[:, tt:tt + 1]
            v2 = vcols[:, 8 + tt:9 + tt]
            nv1 = vcols[:, 4 + tt:5 + tt]
            eq = eqb[:, tt * 8:(tt + 1) * 8]
            a = p.op("vector", lambda h, zt_ap=zt_ap, v1=v1: h.tensor_reduce(
                v1, zt_ap, axis=AX, op=ALU.max), inc=1, waits=[eT])
            b = p.op("vector", lambda h, zt_ap=zt_ap, v1=v1, eq=eq: h.tensor_scalar(
                eq, zt_ap, v1, None, ALU.is_equal), inc=1, waits=[a])
            c = p.op("vector", lambda h, zt_ap=zt_ap, eq=eq: h.scalar_tensor_tensor(
                eq, eq, -1e30, zt_ap, ALU.mult, ALU.add), inc=1, waits=[b])
            d = p.op("vector", lambda h, eq=eq, v2=v2: h.tensor_reduce(
                v2, eq, axis=AX, op=ALU.max), inc=1, waits=[c])
            nb = p.op("vector", lambda h, v1=v1, nv1=nv1: h.tensor_scalar(
                nv1, v1, -1.0 / 1.5, None, ALU.mult), inc=1, waits=[a])
            pe = p.op("scalar", lambda h, zt_ap=zt_ap, eq=eq, nv1=nv1: h.activation(
                eq, zt_ap, AF.Exp, bias=nv1, scale=inv15), inc=1,
                waits=[d, nb, C.const_tok])
            ps_ = p.op("vector", lambda h, eq=eq, tt=tt: h.tensor_reduce(
                vcols[:, 12 + tt:13 + tt], eq, axis=AX, op=ALU.add), inc=1,
                waits=[pe])
            rr = p.op("vector", lambda h, tt=tt: h.reciprocal(
                vcols[:, 12 + tt:13 + tt], vcols[:, 12 + tt:13 + tt]), inc=1,
                waits=[ps_])
            g1 = p.op("vector", lambda h, eq=eq, tt=tt: h.tensor_scalar(
                eq, eq, vcols[:, 12 + tt:13 + tt], None, ALU.mult), inc=1,
                waits=[rr])
            g2 = p.op("vector", lambda h, zt_ap=zt_ap, v2=v2: h.tensor_scalar(
                zt_ap, zt_ap, v2, None, ALU.is_ge), inc=1, waits=[g1])
            g3 = p.op("vector", lambda h, eq=eq, zt_ap=zt_ap: h.tensor_tensor(
                eq, eq, zt_ap, ALU.mult), inc=1, waits=[g2])
            psU, iU = pool.get()
            tU = p.op("tensor", lambda h, psU=psU, eq=eq, tt=tt: h.transpose(
                psU[0:8, 0:128], eq, C.ident[:]), inc=1, waits=[g3])
            cT = p.op("vector", lambda h, psU=psU, tg=tg, tt=tt: h.tensor_copy(
                comb_all[0:8, tg * TG + tt * 128: tg * TG + (tt + 1) * 128],
                psU[0:8, 0:128]), inc=1, waits=[tU])
            pool.done(iU, cT)
            ctoks.append(cT)
        comb_toks[tg] = ctoks[-1]
        C.xt_free = C.xt_free + [yc, tR]

    xn_done = [p.tok("sp_xn"), p.tok("st_y0"), p.tok("tensor"),
               p.tok("vector"), p.tok("scalar")]

    xnt_free = [None, None]
    combe_free = [None, None]
    htld_free = None
    yld_free = [None, None]
    out_free = [None, None]
    ystore = {}
    w1_free = None
    w2_free = None
    for e in range(8):
        w = list(xn_done) + ([w1_free] if w1_free else [])
        w1tok = load_slabs(p, "gpsimd", W1, W1h, H1, NCH, toff=e * DM, sem="ld_w1",
                           waits=w)
        w = [w2_free] if w2_free else []
        w2tok = load_slabs(p, "gpsimd", W2, W2h, DM, NHC1, toff=e * H1, sem="ld_w2",
                           waits=w)
        for tg in range(NTG):
            par = tg % 2
            for fc in range(NCH):
                ap = bass.AP(xn_sp.tensor, fc * 128 * S + tg * TG, [[S, 128], [1, TG]])
                wx = list(xn_done) + ([xnt_free[0]] if xnt_free[0] else [])
                xtok = p.op("sync", lambda h, ap=ap, fc=fc: h.dma_start(
                    out=xnt[:, fc * TG:(fc + 1) * TG],
                    in_=ap), inc=16, sem="ld_xn0",
                    waits=wx if fc == 0 else ())
            psE, iE = pool.get()
            tE = p.op("tensor", lambda h, psE=psE, e=e, tg=tg: h.matmul(
                psE[:, :], sels[:, e * 128:(e + 1) * 128],
                comb_all[0:8, tg * TG:(tg + 1) * TG], start=True, stop=True),
                inc=1, waits=[comb_toks[tg], wt_g])
            wc = [tE] + ([combe_free[0]] if combe_free[0] else [])
            cbe = p.op("vector", lambda h, psE=psE: h.tensor_copy(
                combe[:, :], psE[:, :]), inc=1, waits=wc)
            pool.done(iE, cbe)
            for hc in range(NHC1):
                psQ, iQ = pool.get()
                for fc in range(NCH):
                    tQ = p.op("tensor", lambda h, psQ=psQ, fc=fc, hc=hc: h.matmul(
                        psQ[:, :], W1[:, fc * H1 + hc * 128: fc * H1 + (hc + 1) * 128],
                        xnt[:, fc * TG:(fc + 1) * TG],
                        start=(fc == 0), stop=(fc == NCH - 1)),
                        inc=(1 if fc == NCH - 1 else 0),
                        waits=[xtok, w1tok] if fc == 0 else ())
                dst = htld[:, hc * TG:(hc + 1) * TG]
                w = [tQ] + ([htld_free] if (hc == 0 and htld_free) else [])
                eQ = p.op("scalar", lambda h, psQ=psQ, dst=dst: h.activation(
                    dst, psQ[:, :], AF.Silu), inc=1, waits=w)
                pool.done(iQ, eQ)
                mQ = p.op("vector", lambda h, dst=dst: h.tensor_tensor(
                    dst, dst, combe[:, :], ALU.mult), inc=1,
                    waits=[eQ, cbe])
            ht_all = mQ
            for dc in range(NCH):
                ypar = dc % 2
                yap = bass.AP(y_hbm.tensor, dc * 128 * S + tg * TG, [[S, 128], [1, TG]])
                wly = [ycp_toks[(tg, dc)]] + ([yld_free[ypar]] if yld_free[ypar] else [])
                if (tg, dc) in ystore:
                    wly.append(ystore[(tg, dc)])
                yl = p.op("sync", lambda h, yap=yap, ypar=ypar: h.dma_start(
                    out=ytile[:, ypar * TG:(ypar + 1) * TG], in_=yap), inc=16,
                    sem=f"ld_y{ypar}", waits=wly)
                psY, iY = pool.get()
                for hc in range(NHC1):
                    tY = p.op("tensor", lambda h, psY=psY, hc=hc, dc=dc: h.matmul(
                        psY[:, :], W2[:, hc * DM + dc * 128: hc * DM + (dc + 1) * 128],
                        htld[:, hc * TG:(hc + 1) * TG], start=(hc == 0),
                        stop=(hc == NHC1 - 1)), inc=(1 if hc == NHC1 - 1 else 0),
                        waits=[ht_all, w2tok] if hc == 0 else ())
                ob = outb[:, ypar * TG:(ypar + 1) * TG]
                w = [tY, yl] + ([out_free[ypar]] if out_free[ypar] else [])
                eY = p.op("vector", lambda h, psY=psY, ob=ob, ypar=ypar: h.tensor_tensor(
                    ob, psY[:, :], ytile[:, ypar * TG:(ypar + 1) * TG], ALU.add),
                    inc=1, waits=w)
                pool.done(iY, eY)
                yld_free[ypar] = eY
                sT = p.op("sync", lambda h, yap=yap, ob=ob: h.dma_start(
                    out=yap, in_=ob), inc=16, sem=f"st_y{ypar + 1}", waits=[eY])
                out_free[ypar] = sT
                ystore[(tg, dc)] = sT
            htld_free = tY
            xnt_free[0] = tQ
            combe_free[0] = mQ
        w1_free = tQ
        w2_free = tY
    return None


def build_ffn1():
    L = Launch("ffn1")
    x_hbm = L.dram_in("x", [DM, S])
    y_hbm = L.dram_out("y", [DM, S])
    emit_ffn1(L, x_hbm, y_hbm, declare_ffn1(L))
    L.p.wait("sync", L.p.tok("st_y1"))
    L.p.wait("sync", L.p.tok("st_y2"))
    return L.done()


def ffn1_host_inputs(inp):
    g = lambda k: np.asarray(inp["L1_ffn_" + k], dtype=np.float32)
    sels = np.zeros((8, 1024), dtype=np.float32)
    for e in range(8):
        sels[e, e * 128:(e + 1) * 128] = 1.0
    return {
        "W1": g("W1"), "W2": g("W2"),
        "rWT1": np.ascontiguousarray(g("router_w").T),
        "gaml": g("gamma").reshape(8, 128).T.copy(),
        "betl": g("beta").reshape(8, 128).T.copy(),
        "sels1": sels,
        "iota32": np.repeat(np.arange(32)[None, :] * 128 + np.arange(128)[:, None],
                            2, axis=1).astype(np.int32),
        "ectl": (np.arange(8, dtype=np.float32) * (2 * 768)).reshape(8, 1),
        "identin2": np.eye(128, dtype=np.float32),
        "c_identin": np.eye(128, dtype=np.float32),
    }



def declare_l1b(L):
    sv, L.prefix = L.prefix, ""
    D = {
        "Wq": L.dram_in("Wq", [DM, DM]),
        "Wk": L.dram_in("Wk", [DM, DM]),
        "Wv": L.dram_in("Wv", [DM, 2048]),
        "Wg": L.dram_in("Wg", [DM, 2048]),
        "Wo": L.dram_in("Wo", [2048, DM]),
        "gnl": L.dram_in("gnl", [128, 16]),
        "ropeC2": L.dram_in("ropeC2", [128, S]),
        "ropeS2": L.dram_in("ropeS2", [128, S]),
        "dmaskT": L.dram_in("dmaskT", [64, 4 * 64]),
        "cqcol": L.dram_in("cqcol", [64, 4]),
        "ckcol": L.dram_in("ckcol", [64, 4]),
    }
    L.prefix = sv
    return D


def emit_l1b(L, x_hbm, y_hbm, D):
    nc, p = L.nc, L.p
    DV = 2048
    Wqh, Wkh, Wvh, Wgh, Woh = D["Wq"], D["Wk"], D["Wv"], D["Wg"], D["Wo"]
    gnlh, Cth, Sth = D["gnl"], D["ropeC2"], D["ropeS2"]
    dmh, cqh, ckh = D["dmaskT"], D["cqcol"], D["ckcol"]

    q_hbm = L.dram_scratch("q_sp", [DM, S])
    k_hbm = L.dram_scratch("k_sp", [DM, S])
    v_hbm = L.dram_scratch("v_sp", [4, 32, 128, 512])
    g_hbm = L.dram_scratch("g_sp", [4, 32, 128, 512])
    o2_hbm = L.dram_scratch("o2_sp", [DV, S])

    GAM = [1.0 - 2.0 ** (-5.0 - hh) for hh in range(4)]

    Wq = L.sb("s_Wq", [128, NCH * DM], BF16)
    Wk = L.sb("s_Wk", [128, NCH * DM], BF16)
    arenaA = L.sb("arenaA", [128, 32768], BF16)
    Wv = arenaA[:, 0:16384]
    Wg = arenaA[:, 16384:32768]
    Wo = arenaA[:, 0:16384]
    o2t = arenaA[:, 16384:24576]
    Ct = L.sb("s_ropeC", [128, S], BF16)
    St = L.sb("s_ropeS", [128, S], BF16)
    dmk = L.sb("s_dmaskT", [64, 4 * 64], F32)
    cqc = L.sb("s_cqcol", [64, 4], F32)
    ckc = L.sb("s_ckcol", [64, 4], F32)
    gnl = L.sb("s_gnl", [128, 16], F32)

    load_slabs(p, "gpsimd", Wq, Wqh, DM, NCH, sem="ld_wg")
    load_slabs(p, "gpsimd", Wk, Wkh, DM, NCH, sem="ld_wg")
    load_slabs(p, "gpsimd", Wv, Wvh, DV, NCH, sem="ld_wg")
    load_slabs(p, "gpsimd", Wg, Wgh, DV, NCH, sem="ld_wg")
    p.op("gpsimd", lambda h: h.dma_start(out=Ct[:], in_=Cth[:, :]), inc=16, sem="ld_wg")
    p.op("gpsimd", lambda h: h.dma_start(out=St[:], in_=Sth[:, :]), inc=16, sem="ld_wg")
    p.op("sync", lambda h: h.dma_start(out=dmk[:], in_=dmh[:, :]), inc=16, sem="ld_w")
    p.op("sync", lambda h: h.dma_start(out=cqc[:], in_=cqh[:, :]), inc=16, sem="ld_w")
    p.op("sync", lambda h: h.dma_start(out=ckc[:], in_=ckh[:, :]), inc=16, sem="ld_w")
    p.op("sync", lambda h: h.dma_start(out=gnl[:], in_=gnlh[:, :]), inc=16, sem="ld_w")

    C = Common(L, x_hbm)
    wt_s = p.tok("ld_w")
    C.ident_tok = wt_s
    wt_g = p.tok("ld_wg")
    L.init_pool()
    pool = L.pool

    r1s = L.sb("pj_r1", [128, 2 * TG], F32)
    r2s = L.sb("pj_r2", [128, 2 * TG], F32)
    qout = L.sb("pj_qout", [128, 2 * TG], BF16)
    vgst = L.sb("pj_vgst", [128, 2 * TG], BF16)
    r1_free = [None, None]
    qout_free = [None, None]
    vg_free = [None, None]

    for tg in range(NTG):
        C.load_x(tg)
        C.ln(tg, pool)
        for part, Wsb, dsth, pfx, rope in ((0, Wq, q_hbm, "q", True),
                                           (1, Wk, k_hbm, "k", True)):
            for hh in range(4):
                psA, iA = pool.get()
                psB, iB = pool.get()
                for half, psx in ((0, psA), (1, psB)):
                    c = 2 * hh + half
                    for fc in range(NCH):
                        tq = p.op("tensor", lambda h, psx=psx, fc=fc, c=c, Wsb=Wsb: h.matmul(
                            psx[:, :], Wsb[:, fc * DM + c * 128: fc * DM + (c + 1) * 128],
                            C.xnslice(fc, tg), start=(fc == 0), stop=(fc == NCH - 1)),
                            inc=(1 if fc == NCH - 1 else 0),
                            waits=[C.xn_tok, wt_g] if fc == 0 else ())
                ctg = Ct[:, tg * TG:(tg + 1) * TG]
                stg = St[:, tg * TG:(tg + 1) * TG]
                par = hh % 2
                r1a = r1s[:, par * TG:(par + 1) * TG]
                r2a = r2s[:, par * TG:(par + 1) * TG]
                qo = qout[:, par * TG:(par + 1) * TG]
                w0 = [tq] + ([r1_free[par]] if r1_free[par] else [])
                e1 = p.op("vector", lambda h, psA=psA, r1a=r1a, ctg=ctg: h.tensor_tensor(
                    r1a, psA[:, :], ctg, ALU.mult), inc=1, waits=w0)
                e2 = p.op("vector", lambda h, psB=psB, r2a=r2a, stg=stg: h.tensor_tensor(
                    r2a, psB[:, :], stg, ALU.mult), inc=1, waits=[e1])
                w1 = [e2] + ([qout_free[par]] if qout_free[par] else [])
                e3 = p.op("vector", lambda h, r1a=r1a, r2a=r2a, qo=qo: h.tensor_tensor(
                    qo, r1a, r2a, ALU.subtract), inc=1, waits=w1)
                ap1 = bass.AP(dsth.tensor, 2 * hh * 128 * S + tg * TG, [[S, 128], [1, TG]])
                s1 = p.op("sync", lambda h, ap1=ap1, qo=qo: h.dma_start(
                    out=ap1, in_=qo), inc=16, sem=f"sp_{pfx}{par}", waits=[e3])
                e4 = p.op("vector", lambda h, psB=psB, r1a=r1a, ctg=ctg: h.tensor_tensor(
                    r1a, psB[:, :], ctg, ALU.mult), inc=1, waits=[e3])
                pool.done(iB, e4)
                e5 = p.op("vector", lambda h, psA=psA, r2a=r2a, stg=stg: h.tensor_tensor(
                    r2a, psA[:, :], stg, ALU.mult), inc=1, waits=[e4])
                pool.done(iA, e5)
                w2 = [e5, s1]
                e6 = p.op("vector", lambda h, r1a=r1a, r2a=r2a, qo=qo: h.tensor_tensor(
                    qo, r1a, r2a, ALU.add), inc=1, waits=w2)
                ap2 = bass.AP(dsth.tensor, (2 * hh + 1) * 128 * S + tg * TG,
                              [[S, 128], [1, TG]])
                s2 = p.op("sync", lambda h, ap2=ap2, qo=qo: h.dma_start(
                    out=ap2, in_=qo), inc=16, sem=f"sp_{pfx}{par}", waits=[e6])
                qout_free[par] = s2
                r1_free[par] = e6
        for j2 in range(4):
            for hh in range(4):
                for which, Wsb2, dsth2, act in ((0, Wv, v_hbm, False),
                                                (1, Wg, g_hbm, True)):
                    par = (j2 * 4 + hh * 2 + which) % 2
                    psv, iv = pool.get()
                    for fc in range(NCH):
                        tv = p.op("tensor", lambda h, psv=psv, fc=fc, hh=hh, j2=j2,
                                  Wsb2=Wsb2: h.matmul(
                            psv[:, :], C.xnslice(fc, tg, width=128, off=j2 * 128),
                            Wsb2[:, fc * DV + hh * 512: fc * DV + (hh + 1) * 512],
                            start=(fc == 0), stop=(fc == NCH - 1)),
                            inc=(1 if fc == NCH - 1 else 0),
                            waits=[C.xn_tok] if fc == 0 else ())
                    vg = vgst[:, par * TG:(par + 1) * TG]
                    w = [tv] + ([vg_free[par]] if vg_free[par] else [])
                    if act:
                        ev = p.op("scalar", lambda h, psv=psv, vg=vg: h.activation(
                            vg, psv[:, :], AF.Silu), inc=1, waits=w)
                    else:
                        ev = p.op("vector", lambda h, psv=psv, vg=vg: h.tensor_copy(
                            vg, psv[:, :]), inc=1, waits=w)
                    pool.done(iv, ev)
                    jj2 = tg * 4 + j2
                    ap = bass.AP(dsth2.tensor, hh * 32 * 128 * 512 + jj2 * 128 * 512,
                                 [[512, 128], [1, 512]])
                    sv = p.op("sync", lambda h, ap=ap, vg=vg: h.dma_start(
                        out=ap, in_=vg), inc=16, sem=f"sp_v{par}", waits=[ev])
                    vg_free[par] = sv

    spill = [p.tok("sp_q0"), p.tok("sp_q1"), p.tok("sp_k0"), p.tok("sp_k1"),
             p.tok("sp_v0"), p.tok("sp_v1")]

    qh = Wq
    kh = Wk
    vgt = L.sb("sc_vgt", [64, 4 * TG], BF16)
    Ssb = L.sb("sc_S", [128, 2 * TG], F32)
    Sbf = L.sb("sc_Sbf", [128, 2 * TG], BF16)
    abuf = L.sb("sc_abuf", [64, 64], BF16)
    ktok_sb = L.sb("sc_ktok", [64, 256], BF16)
    vk_sb = L.sb("sc_vk", [64, TG], BF16)
    of32 = L.sb("sc_of32", [64, TG], F32)
    sqs = L.sb("sc_sqs", [64, TG], F32)
    scol = L.sb("sc_scol", [64, 4], F32)
    o2bf = L.sb("sc_o2", [64, TG], BF16)
    o2st = L.sb("sc_o2st", [128, 2 * 4 * 64], BF16)

    qh_free = None
    kh_free = None
    vg_free2 = [None, None]
    st_o2 = [None, None]
    prev_sc = None
    for hh in range(4):
        w = list(spill) + ([qh_free] if qh_free else [])
        for half in range(2):
            ap = bass.AP(q_hbm.tensor, (2 * hh + half) * 128 * S, [[S, 128], [1, S]])
            qtok = p.op("sync", lambda h, ap=ap, half=half: h.dma_start(
                out=qh[:, half * S:(half + 1) * S], in_=ap), inc=16, sem="ld_qh",
                waits=w if half == 0 else ())
        w = [kh_free] if kh_free else []
        for half in range(2):
            ap = bass.AP(k_hbm.tensor, (2 * hh + half) * 128 * S, [[S, 128], [1, S]])
            ktok2 = p.op("sync", lambda h, ap=ap, half=half: h.dma_start(
                out=kh[:, half * S:(half + 1) * S], in_=ap), inc=16, sem="ld_kh",
                waits=w if half == 0 else ())
        z1 = p.op("gpsimd", lambda h: h.memset(Ssb[:, :], 0.0), inc=1,
                  waits=[prev_sc] if prev_sc else [])
        z2 = p.op("gpsimd", lambda h: h.memset(Sbf[:, :], 0.0), inc=1)
        sbf_tok = p.tok("gpsimd")
        gC = GAM[hh] ** 64
        dmsl = dmk[0:64, hh * 64:(hh + 1) * 64]
        cqsl = cqc[0:64, hh:hh + 1]
        cksl = ckc[0:64, hh:hh + 1]
        st_upd = None
        sbf_upd = sbf_tok
        for j in range(64):
            j2 = j // 2
            hlf = j % 2
            vpar = j % 2
            for wh, src in ((0, v_hbm), (1, g_hbm)):
                ap = bass.AP(src.tensor,
                             hh * 32 * 128 * 512 + j2 * 128 * 512 + hlf * 64 * 512,
                             [[512, 64], [1, 512]])
                wv = list(spill) + ([vg_free2[vpar]] if vg_free2[vpar] else [])
                vgtok = p.op("sync", lambda h, ap=ap, wh=wh, vpar=vpar: h.dma_start(
                    out=vgt[:, (vpar * 2 + wh) * TG:(vpar * 2 + wh + 1) * TG],
                    in_=ap), inc=16, sem=f"ld_vg{vpar}",
                    waits=wv if wh == 0 else ())
            cur_vg = vgtok
            vsl = vgt[:, vpar * 2 * TG:(vpar * 2 + 1) * TG]
            gsl = vgt[:, (vpar * 2 + 1) * TG:(vpar * 2 + 2) * TG]
            qsl = [qh[:, half * S + j * 64:half * S + (j + 1) * 64] for half in range(2)]
            ksl = [kh[:, half * S + j * 64:half * S + (j + 1) * 64] for half in range(2)]
            ps1, i1 = pool.get()
            for half in range(2):
                t1 = p.op("tensor", lambda h, ps1=ps1, half=half, j=j, qsl=qsl,
                          ksl=ksl: h.matmul(
                    ps1[0:64, 0:64], ksl[half], qsl[half], start=(half == 0),
                    stop=(half == 1)), inc=(1 if half == 1 else 0),
                    waits=[qtok, ktok2, vgtok] if (j == 0 and half == 0) else ())
            am = p.op("vector", lambda h, ps1=ps1, dmsl=dmsl: h.tensor_tensor(
                abuf[:, :], ps1[0:64, 0:64], dmsl, ALU.mult), inc=1,
                waits=[t1, wt_s])
            pool.done(i1, am)
            ps2, i2 = pool.get()
            t2 = p.op("tensor", lambda h, ps2=ps2, vsl=vsl: h.matmul(
                ps2[0:64, :], abuf[:, :], vsl, start=True, stop=True), inc=1,
                waits=[am, cur_vg])
            ps3, i3 = pool.get()
            for half in range(2):
                t3 = p.op("tensor", lambda h, ps3=ps3, half=half, qsl=qsl: h.matmul(
                    ps3[0:64, :], qsl[half], Sbf[:, half * TG:(half + 1) * TG],
                    start=(half == 0), stop=(half == 1)),
                    inc=(1 if half == 1 else 0),
                    waits=[sbf_upd] if half == 0 else ())
            c1 = p.op("vector", lambda h, ps2=ps2: h.tensor_copy(of32[:, :],
                                                                 ps2[0:64, :]),
                      inc=1, waits=[t2])
            pool.done(i2, c1)
            c2 = p.op("vector", lambda h, ps3=ps3, cqsl=cqsl: h.scalar_tensor_tensor(
                of32[:, :], ps3[0:64, :], cqsl, of32[:, :], ALU.mult, ALU.add),
                inc=1, waits=[t3, c1])
            pool.done(i3, c2)
            sq = p.op("scalar", lambda h: h.activation(sqs[:, :], of32[:, :],
                                                       AF.Square), inc=1, waits=[c2])
            rs = p.op("vector", lambda h: h.tensor_reduce(
                scol[:, 0:1], sqs[:, :], axis=AX, op=ALU.add), inc=1, waits=[sq])
            ms = p.op("vector", lambda h: h.tensor_scalar(
                scol[:, 1:2], scol[:, 0:1], 1.0 / 512.0, None, ALU.mult), inc=1,
                waits=[rs])
            epsap = C.constant(1e-5, parts=64)
            sr = p.op("scalar", lambda h, epsap=epsap: h.activation(
                scol[:, 2:3], scol[:, 1:2], AF.Sqrt, bias=epsap), inc=1,
                waits=[ms, C.const_tok])
            rc = p.op("vector", lambda h: h.reciprocal(scol[:, 3:4], scol[:, 2:3]),
                      inc=1, waits=[sr])
            n1 = p.op("vector", lambda h: h.tensor_scalar(
                of32[:, :], of32[:, :], scol[:, 3:4], None, ALU.mult), inc=1,
                waits=[rc])
            g1 = p.op("vector", lambda h, gsl=gsl: h.tensor_tensor(
                o2bf[:, :], of32[:, :], gsl, ALU.mult), inc=1, waits=[n1])
            opar = j % 2
            ps4, i4 = pool.get()
            ps4b = ps4[:, :].bitcast(BF16)
            ev_last = None
            for sub in range(4):
                t4 = p.op("tensor", lambda h, ps4b=ps4b, sub=sub: h.transpose(
                    ps4b[:, sub * 64:(sub + 1) * 64],
                    o2bf[:, sub * 128:(sub + 1) * 128], C.ident_bf[0:64, 0:64]),
                    inc=1, waits=[g1, C.identbf_tok])
            w = [t4] + ([st_o2[opar]] if st_o2[opar] else [])
            ev = p.op("vector", lambda h, ps4b=ps4b, opar=opar: h.tensor_copy(
                o2st[:, opar * 256:(opar + 1) * 256], ps4b[:, 0:256]), inc=1,
                waits=w)
            pool.done(i4, ev)
            for sub in range(4):
                ap = bass.AP(o2_hbm.tensor, (hh * 512 + sub * 128) * S + j * 64,
                             [[S, 128], [1, 64]])
                so = p.op("sync", lambda h, ap=ap, sub=sub, opar=opar: h.dma_start(
                    out=ap, in_=o2st[:, opar * 256 + sub * 64:opar * 256 + (sub + 1) * 64]),
                    inc=16, sem=f"st_o2{opar}", waits=[ev] if sub == 0 else ())
            st_o2[opar] = so
            ps5, i5 = pool.get()
            ps5b = ps5[:, :].bitcast(BF16)
            for half in range(2):
                t5 = p.op("tensor", lambda h, ps5b=ps5b, half=half, ksl=ksl: h.transpose(
                    ps5b[0:64, half * 128:(half + 1) * 128], ksl[half],
                    C.ident_bf[:, :]), inc=1,
                    waits=[am, C.identbf_tok] if half == 0 else [am])
            kt_ev = p.op("vector", lambda h, ps5b=ps5b: h.tensor_copy(
                ktok_sb[:, :], ps5b[0:64, 0:256]), inc=1, waits=[t5])
            pool.done(i5, kt_ev)
            vk = p.op("vector", lambda h, vsl=vsl, cksl=cksl: h.tensor_scalar(
                vk_sb[:, :], vsl, cksl, None, ALU.mult), inc=1,
                waits=[kt_ev, wt_s, cur_vg])
            ps6, i6 = pool.get()
            ps7, i7 = pool.get()
            for half, psx in ((0, ps6), (1, ps7)):
                t6 = p.op("tensor", lambda h, psx=psx, half=half: h.matmul(
                    psx[:, :], ktok_sb[:, half * 128:(half + 1) * 128], vk_sb[:, :],
                    start=True, stop=True), inc=1, waits=[kt_ev, vk])
            su1 = p.op("vector", lambda h, ps6=ps6, gC=gC: h.scalar_tensor_tensor(
                Ssb[:, 0:TG], Ssb[:, 0:TG], gC, ps6[:, :], ALU.mult, ALU.add),
                inc=1, waits=[t6])
            pool.done(i6, su1)
            su2 = p.op("vector", lambda h, ps7=ps7, gC=gC: h.scalar_tensor_tensor(
                Ssb[:, TG:2 * TG], Ssb[:, TG:2 * TG], gC, ps7[:, :], ALU.mult,
                ALU.add), inc=1, waits=[t6])
            pool.done(i7, su2)
            sb1_ = p.op("vector", lambda h: h.tensor_copy(Sbf[:, 0:TG], Ssb[:, 0:TG]),
                        inc=1, waits=[su1, t3])
            sb2_ = p.op("vector", lambda h: h.tensor_copy(Sbf[:, TG:2 * TG],
                                                          Ssb[:, TG:2 * TG]),
                        inc=1, waits=[su2])
            sbf_upd = sb2_
            if hlf == 1:
                vg_free2[vpar] = vk
        qh_free = t3
        kh_free = t6
        prev_sc = sb2_

    o2_done = [p.tok("st_o20"), p.tok("st_o21")]

    wo_tok = load_slabs(p, "gpsimd", Wo, Woh, DM, 16, sem="ld_wo",
                        waits=list(o2_done) + [p.tok("vector"), p.tok("tensor")])
    gn_toks = []
    for vc in range(16):
        t = p.op("vector", lambda h, vc=vc: h.tensor_scalar(
            Wo[:, vc * DM:(vc + 1) * DM], Wo[:, vc * DM:(vc + 1) * DM],
            gnl[:, vc:vc + 1], None, ALU.mult), inc=1,
            waits=[wo_tok, wt_s])
        gn_toks.append(t)
    wo_ready = gn_toks[-1]

    ot_free = None
    out_free = [None, None]
    for tg in range(NTG):
        C.load_x(tg)
        w0 = [wo_ready] + list(o2_done) + ([ot_free] if ot_free else [])
        for vc in range(16):
            oap = bass.AP(o2_hbm.tensor, vc * 128 * S + tg * TG, [[S, 128], [1, TG]])
            otok = p.op("sync", lambda h, oap=oap, vc=vc: h.dma_start(
                out=o2t[:, vc * TG:(vc + 1) * TG], in_=oap), inc=16, sem="ld_o2",
                waits=w0 if vc == 0 else ())
        for dc in range(NCH):
            psY, iY = pool.get()
            for vc in range(16):
                tY = p.op("tensor", lambda h, psY=psY, vc=vc, dc=dc: h.matmul(
                    psY[:, :], Wo[:, vc * DM + dc * 128: vc * DM + (dc + 1) * 128],
                    o2t[:, vc * TG:(vc + 1) * TG], start=(vc == 0), stop=(vc == 15)),
                    inc=(1 if vc == 15 else 0),
                    waits=[otok, wo_ready, C.x_tok] if vc == 0 else ())
            opar = dc % 2
            ob = r1s[:, opar * TG:(opar + 1) * TG]
            w = [tY] + ([out_free[opar]] if out_free[opar] else [])
            eY = p.op("vector", lambda h, psY=psY, ob=ob, dc=dc: h.tensor_tensor(
                ob, psY[:, :], C.xslice(dc), ALU.add), inc=1, waits=w)
            pool.done(iY, eY)
            yap = bass.AP(y_hbm.tensor, dc * 128 * S + tg * TG, [[S, 128], [1, TG]])
            sT = p.op("sync", lambda h, yap=yap, ob=ob: h.dma_start(out=yap, in_=ob),
                      inc=16, sem=f"st{opar}", waits=[eY])
            out_free[opar] = sT
        C.xt_free = [eY]
        ot_free = tY
    return None


def build_attn_l1b():
    L = Launch("l1b")
    x_hbm = L.dram_in("x", [DM, S])
    y_hbm = L.dram_out("y", [DM, S])
    emit_l1b(L, x_hbm, y_hbm, declare_l1b(L))
    L.p.wait("sync", L.p.tok("st0"))
    L.p.wait("sync", L.p.tok("st1"))
    return L.done()


def l1b_host_inputs(inp):
    g = lambda k: np.asarray(inp["L1_attn_b_" + k], dtype=np.float32)
    inv = 1.0 / (10000.0 ** (np.arange(0, 256, 2, dtype=np.float64) / 256))
    t = np.arange(S, dtype=np.float64)
    fr = np.outer(inv, t)
    GAM = [1.0 - 2.0 ** (-5.0 - hh) for hh in range(4)]
    idx = np.arange(64)
    dm = np.zeros((64, 4 * 64), dtype=np.float32)
    cq = np.zeros((64, 4), dtype=np.float32)
    ck = np.zeros((64, 4), dtype=np.float32)
    for hh in range(4):
        rel = idx[None, :] - idx[:, None]
        D = np.where(rel >= 0, GAM[hh] ** np.maximum(rel, 0), 0.0)
        dm[:, hh * 64:(hh + 1) * 64] = (D / 16.0).astype(np.float32)
        cq[:, hh] = (GAM[hh] ** (idx + 1) / 16.0).astype(np.float32)
        ck[:, hh] = (GAM[hh] ** (63 - idx)).astype(np.float32)
    gn = g("gn_weight")
    gnl = np.zeros((128, 16), dtype=np.float32)
    for vc in range(16):
        gnl[:, vc] = gn[(vc % 4) * 128:(vc % 4 + 1) * 128]
    return {
        "Wq": g("Wq"), "Wk": g("Wk"), "Wv": g("Wv"), "Wg": g("Wg"), "Wo": g("Wo"),
        "gnl": gnl,
        "ropeC2": np.cos(fr).astype(np.float32), "ropeS2": np.sin(fr).astype(np.float32),
        "dmaskT": dm, "cqcol": cq, "ckcol": ck,
        "c_identin": np.eye(128, dtype=np.float32),
    }



def build_fused():
    L = Launch("fused")
    x0 = L.dram_in("x", [DM, S])
    y8 = L.dram_out("y", [DM, S])
    acts = [L.dram_scratch(f"act{i}", [DM, S], F32) for i in range(7)]
    D_l0a = declare_l0a(L)
    D_ffn0 = declare_ffn0(L)
    D_l0b = declare_l0b(L)
    D_l1a = declare_l1a(L)
    D_ffn1 = declare_ffn1(L)
    D_l1b = declare_l1b(L)
    L.use_arena(106000)
    seq = [(emit_l0a, D_l0a), (emit_ffn0, D_ffn0), (emit_l0b, D_l0b),
           (emit_ffn0, D_ffn0), (emit_l1a, D_l1a), (emit_ffn1, D_ffn1),
           (emit_l1b, D_l1b), (emit_ffn1, D_ffn1)]
    cur = x0
    for i, (fn, D) in enumerate(seq):
        L.prefix = f"s{i}_"
        out = y8 if i == 7 else acts[i]
        fn(L, cur, out, D)
        L.arena_reset()
        L.prefix = ""
        L.fence()
        cur = out
    return L.done()


def fused_host_inputs(inputs):
    hm = {}
    for d in (l0a_host_inputs(inputs), ffn0_host_inputs(inputs),
              l0b_host_inputs(inputs), l1a_host_inputs(inputs),
              ffn1_host_inputs(inputs), l1b_host_inputs(inputs)):
        for k, v in d.items():
            if k in ("c_identin", "maskb_in"):
                continue
            hm[k] = v
    eye = np.eye(128, dtype=np.float32)
    for i in range(8):
        hm[f"s{i}_c_identin"] = eye
    hm["s0_maskb_in"] = host_maskB(anti=False)
    hm["s4_maskb_in"] = host_maskB(anti=False)
    return hm


def kernel(**inputs):
    x = np.asarray(inputs["L0_attn_a_x"], dtype=np.float32)
    nc = build_fused()
    shared = fused_host_inputs(inputs)
    in_maps = []
    for i in range(8):
        m = dict(shared)
        m["x"] = np.ascontiguousarray(x[i].T)
        in_maps.append(m)
    res = run_bass_kernel_spmd(nc, in_maps, core_ids=list(range(8)))
    return np.stack([np.ascontiguousarray(res.results[i]["y"]).T
                     for i in range(8)]).astype(np.float32)


def kernel_unfused(**inputs):
    x = np.asarray(inputs["L0_attn_a_x"], dtype=np.float32)
    cur = [np.ascontiguousarray(x[i].T) for i in range(8)]

    nc_l0a = build_attn_l0a()
    nc_ffn0 = build_ffn0()
    nc_l0b = build_attn_l0b()
    nc_l1a = build_attn_l1a()
    nc_ffn1 = build_ffn1()
    nc_l1b = build_attn_l1b()

    steps = [
        (nc_l0a, l0a_host_inputs(inputs)),
        (nc_ffn0, ffn0_host_inputs(inputs)),
        (nc_l0b, l0b_host_inputs(inputs)),
        (nc_ffn0, ffn0_host_inputs(inputs)),
        (nc_l1a, l1a_host_inputs(inputs)),
        (nc_ffn1, ffn1_host_inputs(inputs)),
        (nc_l1b, l1b_host_inputs(inputs)),
        (nc_ffn1, ffn1_host_inputs(inputs)),
    ]
    for nc, shared in steps:
        in_maps = []
        for i in range(8):
            m = dict(shared)
            m["x"] = cur[i]
            in_maps.append(m)
        res = run_bass_kernel_spmd(nc, in_maps, core_ids=list(range(8)))
        cur = [np.ascontiguousarray(res.results[i]["y"]) for i in range(8)]
    return np.stack([c.T for c in cur]).astype(np.float32)



CT_SLOT = 768
CT_E = 2 * CT_SLOT


def emit_ffn1_sparse(L, x_hbm, y_hbm, D):
    nc, p = L.nc, L.p
    H1 = 4096
    NHC1 = H1 // 128
    NTILE = S // 128
    W1h, W2h, rWTh = D["W1"], D["W2"], D["rWT1"]
    gamh, beth = D["gaml"], D["betl"]
    iotah = D["iota32"]
    ecth = D["ectl"]

    xn_tok = L.dram_scratch("xn_tok", [S, DM])
    comb_sp = L.dram_scratch("comb_sp", [8, S], F32)
    lists_sp = L.dram_scratch("lists_sp", [8 * CT_E, 2], mybir.dt.int32)
    acc0 = L.dram_scratch("acc0", [S, DM], F32)
    accA = L.dram_scratch("accA", [S, DM], F32)
    accB = L.dram_scratch("accB", [S, DM], F32)

    rWT = L.sb("s_rWT", [128, NCH * 8], F32)
    rGT = L.sb("s_rGT", [128, NCH * 8], F32)
    gam = L.sb("s_gam", [128, 8], F32)
    bet = L.sb("s_bet", [128, 8], F32)
    colsg = L.sb("s_colsg", [8, 1], F32)
    cbeta = L.sb("s_cbeta", [8, 1], F32)
    ectc = L.sb("s_ect", [8, 1], F32)
    load_slabs(p, "sync", rWT, rWTh, 8, NCH)
    p.op("sync", lambda h: h.dma_start(out=gam[:], in_=gamh[:, :]), inc=16, sem="ld_w")
    p.op("sync", lambda h: h.dma_start(out=bet[:], in_=beth[:, :]), inc=16, sem="ld_w")
    p.op("sync", lambda h: h.dma_start(out=ectc[:], in_=ecth[:, :]), inc=16, sem="ld_w")

    C = Common(L, x_hbm, rows_cols=2568)
    wt_s = p.tok("ld_w")
    C.ident_tok = wt_s
    L.init_pool()
    pool = L.pool

    rg_toks = []
    for fc in range(NCH):
        t = p.op("vector", lambda h, fc=fc: h.tensor_scalar(
            rGT[:, fc * 8:(fc + 1) * 8], rWT[:, fc * 8:(fc + 1) * 8],
            gam[:, fc:fc + 1], None, ALU.mult), inc=1, waits=[wt_s])
        rg_toks.append(t)
    psK, iK = pool.get()
    for fc in range(NCH):
        tk = p.op("tensor", lambda h, psK=psK, fc=fc: h.matmul(
            psK[0:1, 0:8], C.ones_col[:], rGT[:, fc * 8:(fc + 1) * 8],
            start=(fc == 0), stop=(fc == NCH - 1)),
            inc=(1 if fc == NCH - 1 else 0),
            waits=[rg_toks[-1], C.const_tok] if fc == 0 else ())
    csr = C.rows[0:1, 2560:2568]
    e1 = p.op("vector", lambda h: h.tensor_copy(csr, psK[0:1, 0:8]), inc=1, waits=[tk])
    pool.done(iK, e1)
    psK2, iK2 = pool.get()
    tk2 = p.op("tensor", lambda h, psK2=psK2: h.transpose(
        psK2[0:8, 0:1], csr, C.ident[0:1, 0:1]), inc=1, waits=[e1, C.ident_tok])
    e2c = p.op("vector", lambda h, psK2=psK2: h.tensor_copy(colsg[:], psK2[0:8, 0:1]),
               inc=1, waits=[tk2])
    pool.done(iK2, e2c)
    rb_toks = []
    for fc in range(NCH):
        t = p.op("vector", lambda h, fc=fc: h.tensor_scalar(
            C.xsq[:, fc * 8:fc * 8 + 8], rWT[:, fc * 8:(fc + 1) * 8],
            bet[:, fc:fc + 1], None, ALU.mult), inc=1, waits=[wt_s, e2c])
        rb_toks.append(t)
    psK3, iK3 = pool.get()
    for fc in range(NCH):
        tk3 = p.op("tensor", lambda h, psK3=psK3, fc=fc: h.matmul(
            psK3[0:1, 0:8], C.ones_col[:], C.xsq[:, fc * 8:fc * 8 + 8],
            start=(fc == 0), stop=(fc == NCH - 1)),
            inc=(1 if fc == NCH - 1 else 0),
            waits=[rb_toks[-1]] if fc == 0 else ())
    e3 = p.op("vector", lambda h, psK3=psK3: h.tensor_copy(csr, psK3[0:1, 0:8]),
              inc=1, waits=[tk3])
    pool.done(iK3, e3)
    psK4, iK4 = pool.get()
    tk4 = p.op("tensor", lambda h, psK4=psK4: h.transpose(
        psK4[0:8, 0:1], csr, C.ident[0:1, 0:1]), inc=1, waits=[e3])
    e4 = p.op("vector", lambda h, psK4=psK4: h.tensor_copy(cbeta[:], psK4[0:8, 0:1]),
              inc=1, waits=[tk4])
    pool.done(iK4, e4)
    pre_tok = e4
    C.xsq_free = tk3

    zb = L.sb("zbA", [8, 3 * TG], F32)
    z = zb[0:8, 0:TG]
    relu = zb[0:8, TG:2 * TG]
    ach = zb[0:8, 2 * TG:3 * TG]
    m0fm = L.sb("m0fm", [8, S], F32)
    m1fm = L.sb("m1fm", [8, S], F32)
    combfm = L.sb("combfm", [8, S], F32)
    ztok = L.sb("ztokA", [128, 4 * 8], F32)
    vcols = L.sb("vcolsA", [128, 16], F32)
    eqb = L.sb("eqbA", [128, 4 * 8], F32)
    m0t = L.sb("m0tA", [128, 8], F32)
    xstage = L.sb("xstageA", [128, 2 * DM], BF16)
    astage = L.sb("astageA", [128, 2 * DM], F32)
    inv15 = C.constant(1.0 / 1.5, parts=128)
    idbf_tok = C.identbf_tok

    xs_free = [None, None]
    as_free = [None, None]
    for tg in range(NTG):
        C.load_x(tg)
        toks = C.ln(tg, pool, affine=(gam, bet, wt_s))
        mu = C.rows[0:1, 0:512]
        rstd = C.rows[0:1, 1536:2048]
        for tt in range(4):
            par = tt % 2
            evx = None
            eva = None
            for fc in range(NCH):
                psT, iT = pool.get()
                psTb = psT[:, :].bitcast(BF16)
                tT = p.op("tensor", lambda h, psTb=psTb, fc=fc, tt=tt: h.transpose(
                    psTb[:, 0:128], C.xnslice(fc, tg, width=128, off=tt * 128),
                    C.ident_bf[:, :]), inc=1, waits=[toks[fc], idbf_tok])
                w = [tT] + ([xs_free[par]] if (fc == 0 and xs_free[par]) else [])
                evx = p.op("vector", lambda h, psTb=psTb, fc=fc, par=par: h.tensor_copy(
                    xstage[:, par * DM + fc * 128:par * DM + (fc + 1) * 128],
                    psTb[:, 0:128]), inc=1, waits=w)
                pool.done(iT, evx)
            ap = bass.AP(xn_tok.tensor, (tg * 4 + tt) * 128 * DM, [[DM, 128], [1, DM]])
            sx = p.op("sync", lambda h, ap=ap, par=par: h.dma_start(
                out=ap, in_=xstage[:, par * DM:(par + 1) * DM]), inc=16,
                sem=f"sp_xt{par}", waits=[evx])
            xs_free[par] = sx
            for fc in range(NCH):
                scr = C.xsq[:, fc * TG + tt * 128: fc * TG + tt * 128 + 128]
                tadd = p.op("vector", lambda h, scr=scr, fc=fc: h.tensor_tensor(
                    scr, scr, C.xslice(fc, width=TG)[:, tt * 128:tt * 128 + 128],
                    ALU.add), inc=1, waits=[toks[fc]])
                psT, iT = pool.get()
                tT = p.op("tensor", lambda h, psT=psT, scr=scr: h.transpose(
                    psT[:, 0:128], scr, C.ident[:, :]), inc=1, waits=[tadd])
                w = [tT] + ([as_free[par]] if (fc == 0 and as_free[par]) else [])
                eva = p.op("vector", lambda h, psT=psT, fc=fc, par=par: h.tensor_copy(
                    astage[:, par * DM + fc * 128:par * DM + (fc + 1) * 128],
                    psT[:, 0:128]), inc=1, waits=w)
                pool.done(iT, eva)
            ap = bass.AP(acc0.tensor, (tg * 4 + tt) * 128 * DM, [[DM, 128], [1, DM]])
            sa = p.op("sync", lambda h, ap=ap, par=par: h.dma_start(
                out=ap, in_=astage[:, par * DM:(par + 1) * DM]), inc=16,
                sem=f"sp_a{par}", waits=[eva])
            as_free[par] = sa
        C.xsq_free = tT
        psR, iR = pool.get()
        for fc in range(NCH):
            tR = p.op("tensor", lambda h, psR=psR, fc=fc: h.matmul(
                psR[0:8, :], rGT[:, fc * 8:(fc + 1) * 8], C.xslice(fc),
                start=(fc == 0), stop=(fc == NCH - 1)),
                inc=(1 if fc == NCH - 1 else 0),
                waits=[C.x_tok, rg_toks[-1]] if fc == 0 else ())
        psM, iM = pool.get()
        tM = p.op("tensor", lambda h, psM=psM: h.matmul(
            psM[0:8, :], C.ones_row[0:1, 0:8], mu, start=True, stop=True), inc=1,
            waits=[C.mu_tok])
        psS, iS = pool.get()
        tS = p.op("tensor", lambda h, psS=psS: h.matmul(
            psS[0:8, :], C.ones_row[0:1, 0:8], rstd, start=True, stop=True), inc=1,
            waits=[C.mu_tok])
        zt1 = p.op("vector", lambda h, psM=psM: h.tensor_scalar(
            ach, psM[0:8, :], colsg[:], None, ALU.mult), inc=1, waits=[tM, pre_tok])
        pool.done(iM, zt1)
        zt2 = p.op("vector", lambda h, psR=psR: h.tensor_tensor(
            relu, psR[0:8, :], ach, ALU.subtract), inc=1, waits=[tR, zt1])
        pool.done(iR, zt2)
        zt3 = p.op("vector", lambda h, psS=psS: h.tensor_tensor(
            ach, relu, psS[0:8, :], ALU.mult), inc=1, waits=[tS, zt2])
        pool.done(iS, zt3)
        zt4 = p.op("vector", lambda h: h.tensor_scalar(z, ach, cbeta[:], None,
                                                       ALU.add), inc=1, waits=[zt3])
        for tt in range(4):
            psT, iT = pool.get()
            tT = p.op("tensor", lambda h, psT=psT, tt=tt: h.transpose(
                psT[:, 0:8], z[:, tt * 128:(tt + 1) * 128], C.ident[0:8, 0:8]),
                inc=1, waits=[zt4, C.ident_tok])
            eT = p.op("vector", lambda h, psT=psT, tt=tt: h.tensor_copy(
                ztok[:, tt * 8:(tt + 1) * 8], psT[:, 0:8]), inc=1, waits=[tT])
            pool.done(iT, eT)
            zt_ap = ztok[:, tt * 8:(tt + 1) * 8]
            v1 = vcols[:, tt:tt + 1]
            v2 = vcols[:, 8 + tt:9 + tt]
            nv1 = vcols[:, 4 + tt:5 + tt]
            eq = eqb[:, tt * 8:(tt + 1) * 8]
            a = p.op("vector", lambda h, zt_ap=zt_ap, v1=v1: h.tensor_reduce(
                v1, zt_ap, axis=AX, op=ALU.max), inc=1, waits=[eT])
            b = p.op("vector", lambda h, zt_ap=zt_ap, v1=v1, eq=eq: h.tensor_scalar(
                eq, zt_ap, v1, None, ALU.is_equal), inc=1, waits=[a])
            c = p.op("vector", lambda h, zt_ap=zt_ap, eq=eq: h.scalar_tensor_tensor(
                eq, eq, -1e30, zt_ap, ALU.mult, ALU.add), inc=1, waits=[b])
            d = p.op("vector", lambda h, eq=eq, v2=v2: h.tensor_reduce(
                v2, eq, axis=AX, op=ALU.max), inc=1, waits=[c])
            t0 = p.op("vector", lambda h, zt_ap=zt_ap, v1=v1: h.tensor_scalar(
                m0t[:, :], zt_ap, v1, None, ALU.is_equal), inc=1, waits=[d])
            nb = p.op("vector", lambda h, v1=v1, nv1=nv1: h.tensor_scalar(
                nv1, v1, -1.0 / 1.5, None, ALU.mult), inc=1, waits=[t0])
            pe = p.op("scalar", lambda h, zt_ap=zt_ap, eq=eq, nv1=nv1: h.activation(
                eq, zt_ap, AF.Exp, bias=nv1, scale=inv15), inc=1,
                waits=[nb, C.const_tok])
            ps_ = p.op("vector", lambda h, eq=eq, tt=tt: h.tensor_reduce(
                vcols[:, 12 + tt:13 + tt], eq, axis=AX, op=ALU.add), inc=1,
                waits=[pe])
            rr = p.op("vector", lambda h, tt=tt: h.reciprocal(
                vcols[:, 12 + tt:13 + tt], vcols[:, 12 + tt:13 + tt]), inc=1,
                waits=[ps_])
            g1 = p.op("vector", lambda h, eq=eq, tt=tt: h.tensor_scalar(
                eq, eq, vcols[:, 12 + tt:13 + tt], None, ALU.mult), inc=1,
                waits=[rr])
            g2 = p.op("vector", lambda h, zt_ap=zt_ap, v2=v2: h.tensor_scalar(
                zt_ap, zt_ap, v2, None, ALU.is_ge), inc=1, waits=[g1])
            g3 = p.op("vector", lambda h, eq=eq, zt_ap=zt_ap: h.tensor_tensor(
                eq, eq, zt_ap, ALU.mult), inc=1, waits=[g2])
            mm1 = p.op("vector", lambda h, zt_ap=zt_ap: h.tensor_tensor(
                zt_ap, zt_ap, m0t[:, :], ALU.subtract), inc=1, waits=[g3])
            for src, dst in ((eq, combfm), (zt_ap, m1fm), (m0t[:, :], m0fm)):
                psU, iU = pool.get()
                tU = p.op("tensor", lambda h, psU=psU, src=src: h.transpose(
                    psU[0:8, 0:128], src, C.ident[:, :]), inc=1, waits=[mm1])
                cT = p.op("vector", lambda h, psU=psU, dst=dst, tg=tg, tt=tt: h.tensor_copy(
                    dst[0:8, tg * TG + tt * 128: tg * TG + (tt + 1) * 128],
                    psU[0:8, 0:128]), inc=1, waits=[tU])
                pool.done(iU, cT)
        C.xt_free = C.xt_free + [tR, tadd]
    s1 = cT
    cbs = p.op("sync", lambda h: h.dma_start(out=comb_sp[:, :], in_=combfm[:, :]),
               inc=16, sem="sp_cb", waits=[cT])
    oobt = L.sb("oobtA", [128, 192], mybir.dt.int32)
    ot1 = p.op("gpsimd", lambda h: h.memset(oobt[:, :], 1000000000), inc=1)
    lf = p.op("gpsimd", lambda h: h.dma_start(
        out=bass.AP(lists_sp.tensor, 0, [[192, 128], [1, 192]]), in_=oobt[:, :]),
        inc=16, sem="st_lf", waits=[ot1])
    onesS = L.sb("onesSA", [8, S], F32)
    pos = L.sb("posA", [8, 2 * S], F32)
    o1_ = p.op("gpsimd", lambda h: h.memset(onesS[:, :], 1.0), inc=1)
    offt = []
    for sl, msrc in ((0, m0fm), (1, m1fm)):
        pslice = pos[0:8, sl * S:(sl + 1) * S]
        sc = p.op("vector", lambda h, pslice=pslice, msrc=msrc: h.tensor_tensor_scan(
            pslice, onesS[:, :], msrc[:, :], 0.0, ALU.mult, ALU.add), inc=1,
            waits=[s1, o1_])
        a1_ = p.op("vector", lambda h, pslice=pslice: h.tensor_scalar(
            pslice, pslice, -1.0, None, ALU.add), inc=1, waits=[sc])
        a2_ = p.op("vector", lambda h, pslice=pslice, msrc=msrc: h.tensor_tensor(
            pslice, pslice, msrc[:, :], ALU.mult), inc=1, waits=[a1_])
        a3_ = p.op("vector", lambda h, pslice=pslice, msrc=msrc: h.scalar_tensor_tensor(
            pslice, msrc[:, :], -1e9, pslice, ALU.mult, ALU.add), inc=1, waits=[a2_])
        a4_ = p.op("vector", lambda h, pslice=pslice, sl=sl: h.tensor_scalar(
            pslice, pslice, 1e9 + sl * CT_SLOT, None, ALU.add), inc=1, waits=[a3_])
        a5_ = p.op("vector", lambda h, pslice=pslice: h.tensor_scalar(
            pslice, pslice, ectc[:, :], None, ALU.add), inc=1, waits=[a4_, wt_s])
        offt.append(a5_)
    offi = L.sb("offiA", [128, 16], mybir.dt.int32)
    iotac = L.sb("iotacA", [128, 64], mybir.dt.int32)
    p.op("sync", lambda h: h.dma_start(out=iotac[:, :], in_=iotah[:, :]), inc=16,
         sem="ld_w")
    wt2 = p.tok("ld_w")
    sc_cnt = 0
    for j in range(NTILE):
        for sl in range(2):
            psU, iU = pool.get()
            tU = p.op("tensor", lambda h, psU=psU, j=j, sl=sl: h.transpose(
                psU[:, 0:8], pos[0:8, sl * S + j * 128: sl * S + (j + 1) * 128],
                C.ident[0:8, 0:8]), inc=1, waits=[offt[sl]])
            ci = p.op("vector", lambda h, psU=psU, sl=sl: h.tensor_copy(
                offi[:, sl * 8:(sl + 1) * 8], psU[:, 0:8]), inc=1,
                waits=[tU, p.tok("gpsimd")])
            pool.done(iU, ci)
            for e in range(8):
                sct = p.op("gpsimd", lambda h, e=e, j=j, sl=sl: h.indirect_dma_start(
                    out=lists_sp[:, :],
                    out_offset=bass.IndirectOffsetOnAxis(
                        offi[:, sl * 8 + e:sl * 8 + e + 1], 0),
                    in_=iotac[:, 2 * j:2 * j + 2], in_offset=None,
                    bounds_check=8 * CT_E - 1, oob_is_err=False), inc=16,
                    sem="sc_ls", waits=[ci, wt2, p.tok("st_lf")] if e == 0 else [ci])
                sc_cnt += 16

    L.arena_reset()
    ph_a = L.fence()

    W1 = L.sb("s_W1", [128, NCH * H1], BF16)
    W2 = L.sb("s_W2", [128, NHC1 * DM], BF16)
    htld = L.sb("htldB", [128, NHC1 * TG], BF16)
    xg = L.sb("xgB", [128, 2 * DM], BF16)
    xe = L.sb("xeB", [128, NCH * TG], BF16)
    idxb = L.sb("idxB", [128, 24], mybir.dt.int32)
    cvals = L.sb("cvalsB", [128, 12], F32)
    yfm = L.sb("yfmB", [128, NCH * TG], BF16)
    ytr = L.sb("ytrB", [128, 2 * DM], BF16)
    ysc = L.sb("yscB", [128, 2 * DM], F32)
    identb2 = L.sb("identb2B", [128, 128], BF16)
    p.op("gpsimd", lambda h: h.dma_start(out=identb2[:], in_=D["identin2"][:, :]),
         inc=16, sem="ld_idb2")
    id2 = p.tok("ld_idb2")

    w1_free = None
    w2_free = None
    xg_free = [None, None]
    ysc_free = [None, None]
    idx_free = None
    cv_free = None
    xe_free = None
    yfm_free = None
    scy_tok = []
    for e in range(8):
        w = [ph_a] + ([w1_free] if w1_free else [])
        w1tok = load_slabs(p, "gpsimd", W1, W1h, H1, NCH, toff=e * DM, sem="ld_w1",
                           waits=w)
        w = [w2_free] if w2_free else []
        w2tok = load_slabs(p, "gpsimd", W2, W2h, DM, NHC1, toff=e * H1, sem="ld_w2",
                           waits=w)
        wi = [ph_a] + ([idx_free] if idx_free else [])
        for j in range(12):
            ap = bass.AP(lists_sp.tensor, (e * CT_E + j * 128) * 2, [[2, 128], [1, 2]])
            it = p.op("sync", lambda h, ap=ap, j=j: h.dma_start(
                out=idxb[:, 2 * j:2 * j + 2], in_=ap), inc=16, sem="ld_ix",
                waits=wi if j == 0 else ())
        wc = [it] + ([cv_free] if cv_free else [])
        for j in range(12):
            cvap = bass.AP(comb_sp.tensor, e * S, [[1, S], [1, 1]])
            ct = p.op("gpsimd", lambda h, cvap=cvap, j=j: h.indirect_dma_start(
                out=cvals[:, j:j + 1], out_offset=None,
                in_=cvap, in_offset=bass.IndirectOffsetOnAxis(idxb[:, 2 * j:2 * j + 1], 0),
                bounds_check=S - 1, oob_is_err=False), inc=16, sem="ld_cv",
                waits=wc if j == 0 else [it])
        for grp in range(3):
            for t4 in range(4):
                j = grp * 4 + t4
                gpar = j % 2
                wg = [it] + ([xg_free[gpar]] if xg_free[gpar] else [])
                gt = p.op("gpsimd", lambda h, j=j, gpar=gpar: h.indirect_dma_start(
                    out=xg[:, gpar * DM:(gpar + 1) * DM], out_offset=None,
                    in_=xn_tok[:, :],
                    in_offset=bass.IndirectOffsetOnAxis(idxb[:, 2 * j:2 * j + 1], 0),
                    bounds_check=S - 1, oob_is_err=False), inc=16,
                    sem=f"ld_g{gpar}", waits=wg)
                for fc in range(NCH):
                    psT, iT = pool.get()
                    psTb = psT[:, :].bitcast(BF16)
                    tT = p.op("tensor", lambda h, psTb=psTb, fc=fc, gpar=gpar: h.transpose(
                        psTb[:, 0:128], xg[:, gpar * DM + fc * 128:gpar * DM + (fc + 1) * 128],
                        identb2[:, :]), inc=1, waits=[gt, id2])
                    w = [tT] + ([xe_free] if (fc == 0 and t4 == 0 and xe_free) else [])
                    ev = p.op("vector", lambda h, psTb=psTb, fc=fc, t4=t4: h.tensor_copy(
                        xe[:, fc * TG + t4 * 128: fc * TG + (t4 + 1) * 128],
                        psTb[:, 0:128]), inc=1, waits=w)
                    pool.done(iT, ev)
                xg_free[gpar] = tT
            xe_tok = ev
            for hc in range(NHC1):
                psQ, iQ = pool.get()
                for fc in range(NCH):
                    tQ = p.op("tensor", lambda h, psQ=psQ, fc=fc, hc=hc: h.matmul(
                        psQ[:, :], W1[:, fc * H1 + hc * 128: fc * H1 + (hc + 1) * 128],
                        xe[:, fc * TG:(fc + 1) * TG],
                        start=(fc == 0), stop=(fc == NCH - 1)),
                        inc=(1 if fc == NCH - 1 else 0),
                        waits=[xe_tok, w1tok] if fc == 0 else ())
                dst = htld[:, hc * TG:(hc + 1) * TG]
                w = [tQ] + ([yfm_free] if (hc == 0 and yfm_free) else [])
                eQ = p.op("scalar", lambda h, psQ=psQ, dst=dst: h.activation(
                    dst, psQ[:, :], AF.Silu), inc=1, waits=w)
                pool.done(iQ, eQ)
            ht_all = eQ
            for dc in range(NCH):
                psY, iY = pool.get()
                for hc in range(NHC1):
                    tY = p.op("tensor", lambda h, psY=psY, hc=hc, dc=dc: h.matmul(
                        psY[:, :], W2[:, hc * DM + dc * 128: hc * DM + (dc + 1) * 128],
                        htld[:, hc * TG:(hc + 1) * TG], start=(hc == 0),
                        stop=(hc == NHC1 - 1)), inc=(1 if hc == NHC1 - 1 else 0),
                        waits=[ht_all, w2tok] if hc == 0 else ())
                eY = p.op("vector", lambda h, psY=psY, dc=dc: h.tensor_copy(
                    yfm[:, dc * TG:(dc + 1) * TG], psY[:, :]), inc=1, waits=[tY])
                pool.done(iY, eY)
            yfm_tok = eY
            for t4 in range(4):
                j = grp * 4 + t4
                ypar = j % 2
                for dc in range(NCH):
                    psT, iT = pool.get()
                    psTb = psT[:, :].bitcast(BF16)
                    tT = p.op("tensor", lambda h, psTb=psTb, dc=dc, t4=t4: h.transpose(
                        psTb[:, 0:128], yfm[:, dc * TG + t4 * 128: dc * TG + (t4 + 1) * 128],
                        identb2[:, :]), inc=1, waits=[yfm_tok])
                    w = [tT] + ([ysc_free[ypar]] if (dc == 0 and ysc_free[ypar]) else [])
                    ev2 = p.op("vector", lambda h, psTb=psTb, dc=dc, ypar=ypar: h.tensor_copy(
                        ytr[:, ypar * DM + dc * 128:ypar * DM + (dc + 1) * 128],
                        psTb[:, 0:128]), inc=1, waits=w)
                    pool.done(iT, ev2)
                sc2 = p.op("vector", lambda h, j=j, ypar=ypar: h.tensor_scalar(
                    ysc[:, ypar * DM:(ypar + 1) * DM], ytr[:, ypar * DM:(ypar + 1) * DM],
                    cvals[:, j:j + 1], None, ALU.mult), inc=1, waits=[ev2, ct])
                dstacc = accA if j < 6 else accB
                sy = p.op("gpsimd", lambda h, j=j, ypar=ypar, dstacc=dstacc: h.indirect_dma_start(
                    out=dstacc[:, :],
                    out_offset=bass.IndirectOffsetOnAxis(idxb[:, 2 * j:2 * j + 1], 0),
                    in_=ysc[:, ypar * DM:(ypar + 1) * DM], in_offset=None,
                    bounds_check=S - 1, oob_is_err=False), inc=16,
                    sem=f"sc_y{ypar}", waits=[sc2, p.tok("sc_ls")]), 
                sy = sy[0]
                ysc_free[ypar] = sy
            yfm_free = tT
            xe_free = tQ
        idx_free = sy
        cv_free = sc2
        w1_free = tQ
        w2_free = tY
    scy = [p.tok("sc_y0"), p.tok("sc_y1")]

    L.arena_reset()
    ph_b = L.fence()

    f0 = L.sb("f0C", [128, 2 * DM], F32)
    fA = L.sb("fAC", [128, 2 * DM], F32)
    fB = L.sb("fBC", [128, 2 * DM], F32)
    ost2 = L.sb("ostC", [128, 2 * DM], F32)
    fr = [[None, None], [None, None], [None, None]]
    os_free = [None, None]
    last_st = None
    for r in range(NTILE):
        par = r % 2
        toks3 = []
        for bi, (buf, acc) in enumerate(((f0, acc0), (fA, accA), (fB, accB))):
            ap = bass.AP(acc.tensor, r * 128 * DM, [[DM, 128], [1, DM]])
            w = [ph_b] + ([fr[bi][par]] if fr[bi][par] else [])
            lt = p.op("sync", lambda h, ap=ap, buf=buf, par=par: h.dma_start(
                out=buf[:, par * DM:(par + 1) * DM], in_=ap), inc=16,
                sem=f"ld_f{bi}{par}", waits=w)
            toks3.append(lt)
        ad1 = p.op("vector", lambda h, par=par: h.tensor_tensor(
            f0[:, par * DM:(par + 1) * DM], f0[:, par * DM:(par + 1) * DM],
            fA[:, par * DM:(par + 1) * DM], ALU.add), inc=1, waits=toks3)
        ad2 = p.op("vector", lambda h, par=par: h.tensor_tensor(
            f0[:, par * DM:(par + 1) * DM], f0[:, par * DM:(par + 1) * DM],
            fB[:, par * DM:(par + 1) * DM], ALU.add), inc=1, waits=[ad1])
        fr[1][par] = ad1
        fr[2][par] = ad2
        for fc in range(NCH):
            psT, iT = pool.get()
            tT = p.op("tensor", lambda h, psT=psT, fc=fc, par=par: h.transpose(
                psT[:, 0:128], f0[:, par * DM + fc * 128:par * DM + (fc + 1) * 128],
                C.ident[:, :]), inc=1, waits=[ad2, C.ident_tok])
            w = [tT] + ([os_free[par]] if (fc == 0 and os_free[par]) else [])
            ev3 = p.op("vector", lambda h, psT=psT, fc=fc, par=par: h.tensor_copy(
                ost2[:, par * DM + fc * 128:par * DM + (fc + 1) * 128],
                psT[:, 0:128]), inc=1, waits=w)
            pool.done(iT, ev3)
        fr[0][par] = ev3
        for fc in range(NCH):
            yap = bass.AP(y_hbm.tensor, fc * 128 * S + r * 128, [[S, 128], [1, 128]])
            st_ = p.op("sync", lambda h, yap=yap, fc=fc, par=par: h.dma_start(
                out=yap, in_=ost2[:, par * DM + fc * 128:par * DM + (fc + 1) * 128]),
                inc=16, sem=f"st{par}", waits=[ev3] if fc == 0 else ())
        os_free[par] = st_
    return None
